# version 05
# baseline (speedup 1.0000x reference)
import math
import numpy as np
import ml_dtypes
from contextlib import ExitStack

import concourse.bass as bass
import concourse.mybir as mybir
from concourse.bass_utils import run_bass_kernel_spmd

F32 = mybir.dt.float32
BF16 = mybir.dt.bfloat16
AF = mybir.ActivationFunctionType
ALU = mybir.AluOpType

B, S, D = 4, 4096, 2048
T = 2048
E, I = 8, 1024
NG = 4
SLOPE = 0.0625
SCALE = 1.0 / math.sqrt(128.0)
NEG = -1e30
LN_EPS = 1e-5
RMS_EPS = 1e-6

MLA_KTS = [list(range(4 * g + 4)) + list(range(16, 32)) for g in range(NG)]
FFN_KTS = [list(range(20 + 4 * g)) for g in range(NG)]
FFN_CAND = [[4 * g + s for s in range(4)] + [16 + 4 * g + s for s in range(4)]
            for g in range(NG)]
MLA_NC = sum(len(k) for k in MLA_KTS)
FFN_NC = sum(len(k) for k in FFN_KTS)
VAW = 16 * 528
KVW = 4 * 2048 + VAW

N_PHASES = 3
DBG_P2 = False
DBG_P3 = False
ENGINES = ("sync", "scalar", "vector", "tensor", "gpsimd")


class KB:
    def __init__(self, nc, stack):
        self.nc = nc
        self.stack = stack
        self.sems = {}
        self.dma_sems = set()
        self.q = None

    def sem(self, name):
        if name not in self.sems:
            h = self.stack.enter_context(self.nc.semaphore(name))
            self.sems[name] = [h, 0]
        return self.sems[name]

    def on(self, eng, fn):
        self.q[eng].append(fn)

    def wait(self, eng, name, val=None):
        s = self.sem(name)
        v = s[1] if val is None else val
        if v <= 0:
            return
        self.on(eng, lambda e, h=s[0], v=v: e.wait_ge(h, v))

    def inc(self, eng, fn, name, k=1):
        s = self.sem(name)
        s[1] += k
        if k == 16:
            self.dma_sems.add(name)
        self.on(eng, lambda e, fn=fn, h=s[0], k=k: fn(e).then_inc(h, k))
        return s[1]

    def do(self, eng, fn):
        self.on(eng, fn)

    def val(self, name):
        return self.sem(name)[1]

    def begin(self):
        self.q = {e: [] for e in ENGINES}

    def flush(self):
        for name in sorted(self.dma_sems):
            self.wait("sync", name)
        with self.nc.Block() as blk:
            for e in ENGINES:
                fns = self.q[e]
                if fns:
                    getattr(blk, e)(lambda eng, fns=fns: [f(eng) for f in fns])
        self.q = None


class Alloc:
    def __init__(self, nc):
        self.nc = nc
        self.groups = {}
        self.order = []

    def sb(self, group, name, shape, dt):
        cm = self.nc.sbuf_tensor(name, list(shape), dt)
        h = cm.__enter__()
        if group not in self.groups:
            self.groups[group] = []
            self.order.append(group)
        self.groups[group].append(cm)
        return h

    def free(self, group):
        assert self.order and self.order[-1] == group, (self.order, group)
        for cm in reversed(self.groups.pop(group)):
            cm.__exit__(None, None, None)
        self.order.pop()


def _stats(kb, xt, width, st, junk, eps, mean, b):
    sem = "sst%d" % b
    if mean:
        kb.inc("vector", lambda e: e.reduce_sum(st[:, 0:1], xt,
                                                axis=mybir.AxisListType.X), sem)
    kb.do("scalar", lambda e: e.activation(junk[:, 0:width], xt, AF.Square,
                                           accum_out=st[:, 2:3]))
    kb.do("scalar", lambda e: e.activation(st[:, 3:4], st[:, 2:3],
                                           AF.Identity, scale=1.0 / width))
    if mean:
        kb.wait("scalar", sem)
        kb.do("scalar", lambda e: e.activation(st[:, 1:2], st[:, 0:1],
                                               AF.Identity, scale=1.0 / width))
        kb.inc("scalar", lambda e: e.activation(st[:, 7:8], st[:, 1:2],
                                                AF.Square), sem)
        kb.wait("vector", sem)
        kb.inc("vector", lambda e: e.tensor_scalar(
            st[:, 3:4], st[:, 3:4], st[:, 7:8], eps,
            op0=ALU.subtract, op1=ALU.add), sem)
    else:
        kb.inc("scalar", lambda e: e.activation(st[:, 3:4], st[:, 3:4],
                                                AF.Identity), sem)
        kb.wait("vector", sem)
        kb.inc("vector", lambda e: e.tensor_scalar_add(st[:, 3:4], st[:, 3:4],
                                                       eps), sem)
    kb.wait("scalar", sem)
    kb.do("scalar", lambda e: e.activation(st[:, 4:5], st[:, 3:4], AF.Ln))
    kb.inc("scalar", lambda e: e.activation(st[:, 4:5], st[:, 4:5], AF.Exp,
                                            scale=-0.5), sem)
    if mean:
        kb.wait("vector", sem)
        kb.inc("vector", lambda e: e.tensor_tensor(st[:, 5:6], st[:, 1:2],
                                                   st[:, 4:5], op=ALU.mult),
               sem)
        kb.wait("scalar", sem)
        kb.inc("scalar", lambda e: e.activation(st[:, 6:7], st[:, 5:6],
                                                AF.Identity), sem)


def build_program(n_phases=N_PHASES):
    nc = bass.Bass("TRN2", target_bir_lowering=False,
                   detect_race_conditions=False)
    stack = ExitStack()
    kb = KB(nc, stack)
    al = Alloc(nc)

    def din(name, shape, dt=F32):
        return nc.dram_tensor(name, list(shape), dt, kind="ExternalInput").ap()

    x0 = din("x0", (S, D))
    w_dkv = din("w_dkv", (128, D))
    w_q = din("w_q", (D, D))
    w_uk = din("w_uk", (D, 128))
    w_uv = din("w_uv", (D, 128))
    w_o = din("w_o", (D, D))
    wq = din("wq", (D, D))
    wk = din("wk", (D, 512))
    wv = din("wv", (D, 512))
    wo = din("wo", (D, D))
    rw = din("router_w", (D, 8))
    wg_r = din("wg_r", (E, 8, 128, 16 * 128))
    wu_r = din("wu_r", (E, 8, 128, 16 * 128))
    wdp = din("wd", (E, I, D))
    identf = din("identf", (128, 128))
    tri01 = din("tri01", (128, 128), BF16)
    negu = din("negu", (128, 4 * 512), BF16)
    seli = din("sel", (128, 32))
    amla = din("amla", (128, MLA_NC))
    vffn = din("vffn", (128, FFN_NC))
    cossin = din("cossin", (128, 16 * 64))
    ln1b = din("ln1b", (128, D))
    ln2b = din("ln2b", (128, D))
    qnb = din("qnb", (128, D))
    knb = din("knb", (128, 512))
    rbb = din("rbb", (128, 8))

    y = nc.dram_tensor("y", [T, D], F32, kind="ExternalOutput").ap()

    wqk_d = nc.dram_tensor("wqk_d", [16, 128, 16 * 128], BF16).ap()
    x1_d = nc.dram_tensor("x1_d", [T, D], F32).ap()
    x2_d = nc.dram_tensor("x2_d", [T, D], F32).ap()
    kvin_d = nc.dram_tensor("kvin_d", [128, KVW], BF16).ap()
    kvout_d = nc.dram_tensor("kvout_d", [2, 128, KVW], BF16).ap()
    if DBG_P3:
        dbg_cb = nc.dram_tensor("dbg_cb", [128, 128], F32,
                                kind="ExternalOutput").ap()
        dbg_sg = nc.dram_tensor("dbg_sg", [128, 32], F32,
                                kind="ExternalOutput").ap()
        dbg_h2 = nc.dram_tensor("dbg_h2", [128, 16 * 1024], BF16,
                                kind="ExternalOutput").ap()
    if DBG_P2:
        dbg_kt = nc.dram_tensor("dbg_kt", [128, 4 * 4096], BF16,
                                kind="ExternalOutput").ap()
        dbg_va = nc.dram_tensor("dbg_va", [128, 32 * 528], BF16,
                                kind="ExternalOutput").ap()
        dbg_q2 = nc.dram_tensor("dbg_q2", [4, 128, 16 * 512], BF16,
                                kind="ExternalOutput").ap()
        dbg_o2 = nc.dram_tensor("dbg_o2", [4, 128, 16 * 512], BF16,
                                kind="ExternalOutput").ap()

    bsb = lambda n, sh, dt: al.sb("base", n, sh, dt)
    xb = [bsb("xb%d" % i, (128, D), F32) for i in range(2)]
    zb = [bsb("zb%d" % i, (128, D), BF16) for i in range(2)]
    stat = [bsb("st%d" % i, (128, 8), F32) for i in range(2)]
    junkb = bsb("junkb", (128, D), BF16)
    asb = lambda n, sh, dt: al.sb("attnsh", n, sh, dt)
    amla_s = asb("amla_s", (128, MLA_NC), F32)
    tri_s = asb("tri_s", (128, 128), BF16)
    Pb = [asb("Pb%d" % i, (128, 512), BF16) for i in range(3)]
    Pscr = asb("Pscr", (128, 512), BF16)
    rcol = asb("rcol", (128, 8), F32)
    onb = asb("onb", (128, 512), BF16)
    ropescr = asb("ropescr", (128, 2048), BF16)

    pbk = [stack.enter_context(nc.psum_tensor("pbk%d" % i, [128, 512], F32))
           for i in range(8)]

    kb.begin()
    wdkvT_s = al.sb("p1per", "wdkvT_s", (128, 16 * 128), BF16)
    w2_s = al.sb("p1per", "w2_s", (128, 16 * 2048), BF16)
    ckvT_s = al.sb("p1per", "ckvT_s", (128, 4096), BF16)
    ckvaug_s = al.sb("p1per", "ckvaug_s", (128, 32 * 144), BF16)
    lnT_g = al.sb("p1per", "lnT_g", (128, 16 * 512), BF16)
    qtg = al.sb("p1per", "qtg", (128, 16 * 512), BF16)
    wqks = [al.sb("p1per", "wqks%d" % i, (128, 4096), BF16) for i in range(2)]
    oTg = al.sb("p1per", "oTg", (128, 16 * 512), BF16)
    x1t = [al.sb("p1per", "x1t0", (128, D), F32)] * 2

    wuv_s = al.sb("p0tmp", "p0_wuv", (128, 16 * 128), BF16)
    wuk_s = wuv_s
    evtile = al.sb("p0tmp", "p0_ev", (128, 128), BF16)
    t128 = [al.sb("p0tmp", "p0_t%d" % i, (128, 128), BF16) for i in range(2)]
    wslab = [al.sb("p0tmp", "p0_ws%d" % i, (128, 2048), BF16) for i in range(2)]
    woth = wslab[0]

    for s_, d_ in ((amla, amla_s), (tri01, tri_s)):
        kb.inc("sync", lambda e, s=s_, d=d_: e.dma_start(out=d[:], in_=s),
               "p0m", 16)
    kb.inc("vector", lambda e: e.memset(ckvaug_s[:], 2.0 ** -24), "p0aug", 1)

    for h in range(16):
        kb.inc("gpsimd", lambda e, h=h: e.dma_start(
            out=wuv_s[:, h * 128:(h + 1) * 128],
            in_=w_uv[h * 128:(h + 1) * 128, :]), "p0g", 16)

    kb.wait("tensor", "p0g")
    for h in range(16):
        for ci in range(16):
            sl_ = ci % 2
            tb = t128[sl_]
            kb.wait("gpsimd", "p0wot%d" % sl_, kb.val("p0wot%d" % sl_))
            kb.inc("gpsimd", lambda e, h=h, ci=ci, tb=tb: e.dma_start(
                out=tb[:], in_=w_o[ci * 128:(ci + 1) * 128,
                                   h * 128:(h + 1) * 128]), "p0wol%d" % sl_, 16)
            kb.wait("sync", "p0wol%d" % sl_)
            kb.wait("sync", "p0w2m", h * 4)
            kb.inc("sync", lambda e, ci=ci, tb=tb: e.dma_start_transpose(
                woth[:, ci * 128:(ci + 1) * 128], tb[:]), "p0wot%d" % sl_, 16)
        kb.wait("tensor", "p0wot0")
        kb.wait("tensor", "p0wot1")
        for db in range(4):
            bank = pbk[db % 2]
            kb.wait("tensor", "p0w2e", kb.val("p0w2e") - 1)
            kb.inc("tensor", lambda e, h=h, db=db, bank=bank: e.matmul(
                bank[:], wuv_s[:, h * 128:(h + 1) * 128],
                woth[:, db * 512:(db + 1) * 512],
                start=True, stop=True), "p0w2m")
            kb.wait("vector", "p0w2m")
            kb.inc("vector", lambda e, h=h, db=db, bank=bank: e.tensor_copy(
                w2_s[:, h * 2048 + db * 512: h * 2048 + (db + 1) * 512], bank[:]),
                "p0w2e")

    kb.wait("gpsimd", "p0w2m", kb.val("p0w2m"))
    for h in range(16):
        kb.inc("gpsimd", lambda e, h=h: e.dma_start(
            out=wuk_s[:, h * 128:(h + 1) * 128],
            in_=w_uk[h * 128:(h + 1) * 128, :]), "p0g", 16)
    for h in range(16):
        sl_ = h % 2
        sl = wslab[sl_]
        kb.wait("gpsimd", "p0qkm", (h - 1) * 16)
        kb.inc("gpsimd", lambda e, h=h, sl=sl: e.dma_start(
            out=sl[:], in_=w_q[h * 128:(h + 1) * 128, :]), "p0wq%d" % sl_, 16)
        kb.wait("tensor", "p0wq%d" % sl_)
        kb.wait("tensor", "p0g")
        for dch in range(16):
            bank = pbk[2 + dch % 2]
            kb.wait("tensor", "p0qke", kb.val("p0qke") - 1)
            kb.inc("tensor", lambda e, h=h, dch=dch, sl=sl, bank=bank: e.matmul(
                bank[:, 0:128], sl[:, dch * 128:(dch + 1) * 128],
                wuk_s[:, h * 128:(h + 1) * 128], start=True, stop=True), "p0qkm")
            kb.wait("vector", "p0qkm")
            kb.wait("vector", "p0qks", kb.val("p0qks"))
            kb.inc("vector", lambda e, bank=bank: e.tensor_copy(
                evtile[:], bank[:, 0:128]), "p0qke")
            kb.wait("sync", "p0qke")
            kb.inc("sync", lambda e, h=h, dch=dch: e.dma_start(
                out=wqk_d[h, :, dch * 128:(dch + 1) * 128], in_=evtile[:]),
                "p0qks", 16)

    kb.wait("gpsimd", "p0qkm", kb.val("p0qkm"))
    kb.inc("gpsimd", lambda e: e.dma_start(out=wslab[0][:], in_=w_dkv[:, :]),
           "p0g", 16)
    kb.wait("sync", "p0g")
    for dch in range(16):
        kb.inc("sync", lambda e, dch=dch: e.dma_start_transpose(
            wdkvT_s[:, dch * 128:(dch + 1) * 128],
            wslab[0][:, dch * 128:(dch + 1) * 128]), "p0m", 16)
    al.free("p0tmp")

    state = {"n": 0}

    def ln_tile(t, tsub):
        n = state["n"]
        state["n"] += 1
        b = n % 2
        kb.wait("sync", "paz", n - 1)
        kb.wait("sync", "patp", (n - 1) * 256)
        kb.inc("sync", lambda e, t=t, b=b: e.dma_start(
            out=xb[b][:], in_=x0[t * 128:(t + 1) * 128, :]), "pald", 16)
        kb.wait("vector", "pald")
        kb.wait("scalar", "pald")
        _stats(kb, xb[b][:], D, stat[b], junkb, LN_EPS, True, b)
        kb.wait("vector", "sst%d" % b)
        kb.inc("vector", lambda e, b=b: e.tensor_scalar(
            zb[b][:], xb[b][:], stat[b][:, 1:2], stat[b][:, 4:5],
            op0=ALU.subtract, op1=ALU.mult), "paz")
        kb.wait("sync", "paz")
        for dch in range(16):
            kb.inc("sync", lambda e, b=b, dch=dch, tsub=tsub: e.dma_start_transpose(
                lnT_g[:, dch * 512 + tsub * 128: dch * 512 + (tsub + 1) * 128],
                zb[b][:, dch * 128:(dch + 1) * 128]), "patp", 16)

    kb.wait("tensor", "p0m")
    kb.wait("vector", "p0m")
    kb.wait("scalar", "p0m")
    for ga in range(8):
        kb.wait("sync", "pack", kb.val("pack"))
        for ts4 in range(4):
            ln_tile(ga * 4 + ts4, ts4)
        kb.wait("tensor", "patp", state["n"] * 256)
        bank = pbk[ga % 2]
        kb.wait("tensor", "packe", kb.val("packe") - 1)
        for dch in range(16):
            fn = lambda e, dch=dch, bank=bank: e.matmul(
                bank[:], wdkvT_s[:, dch * 128:(dch + 1) * 128],
                lnT_g[:, dch * 512:(dch + 1) * 512],
                start=(dch == 0), stop=(dch == 15))
            kb.inc("tensor", fn, "pack") if dch == 15 else kb.do("tensor", fn)
        kb.wait("vector", "pack")
        kb.inc("vector", lambda e, ga=ga, bank=bank: e.tensor_copy(
            ckvT_s[:, ga * 512:(ga + 1) * 512], bank[:]), "packe")
        kb.wait("sync", "packe")
        kb.wait("sync", "p0aug", 1)
        for ks in range(4):
            kt = ga * 4 + ks
            kb.inc("sync", lambda e, ga=ga, ks=ks, kt=kt: e.dma_start_transpose(
                ckvaug_s[:, kt * 144: kt * 144 + 128],
                ckvT_s[:, ga * 512 + ks * 128: ga * 512 + (ks + 1) * 128]),
                "paaug", 16)
    kb.wait("tensor", "paaug")
    kb.wait("vector", "paaug")
    kb.wait("sync", "pack", kb.val("pack"))

    kb.wait("gpsimd", "p0qks")
    kb.wait("tensor", "p0w2e")
    for g in range(NG):
        kb.wait("sync", "pbqm", kb.val("pbqm"))
        kb.wait("sync", "p1xe", kb.val("p1xe"))
        for ts4 in range(4):
            ln_tile(g * 4 + ts4, ts4)
        kb.wait("tensor", "patp", state["n"] * 256)
        kb.wait("tensor", "p1xe", kb.val("p1xe"))
        for cp in range(8):
            sl_ = cp % 2
            sl = wqks[sl_]
            kb.wait("gpsimd", "pbqm", kb.val("pbqm") - 2)
            kb.inc("gpsimd", lambda e, cp=cp, sl=sl: e.dma_start(
                out=bass.AP(sl, 0, [[4096, 128], [2048, 2], [1, 2048]]),
                in_=wqk_d[2 * cp:2 * cp + 2, :, :].rearrange(
                    "two p c -> p two c")), "pbqk%d" % sl_, 16)
            kb.wait("tensor", "pbqk%d" % sl_)
            for half in range(2):
                cch = 2 * cp + half
                bank = pbk[2 + cch % 2]
                kb.wait("tensor", "pbqe", kb.val("pbqe") - 1)
                for dch in range(16):
                    fn = lambda e, dch=dch, sl=sl, half=half, bank=bank: \
                        e.matmul(
                        bank[:],
                        sl[:, half * 2048 + dch * 128: half * 2048 + (dch + 1) * 128],
                        lnT_g[:, dch * 512:(dch + 1) * 512],
                        start=(dch == 0), stop=(dch == 15))
                    kb.inc("tensor", fn, "pbqm") if dch == 15 \
                        else kb.do("tensor", fn)
                kb.wait("vector", "pbqm")
                kb.wait("vector", "p1sm", kb.val("p1sm"))
                kb.inc("vector", lambda e, cch=cch, bank=bank: e.tensor_copy(
                    qtg[:, cch * 512:(cch + 1) * 512], bank[:]), "pbqe")
        kb.wait("tensor", "pbqe")
        kb.wait("sync", "p1s2m", kb.val("p1s2m"))
        kb.wait("tensor", "p1xe", kb.val("p1xe"))

        kts = MLA_KTS[g]
        nit = len(kts)
        colbase = sum(len(MLA_KTS[gg]) for gg in range(g))
        for h in range(16):
            expv = {}
            maskv = {}
            pend = [None]

            def emit_pv(j, kts=kts, nit=nit, expv=expv, maskv=maskv):
                if j in maskv:
                    kb.wait("tensor", "p1mk", maskv[j])
                else:
                    kb.wait("tensor", "p1ex", expv[j])
                if j == 0:
                    kb.wait("tensor", "p1on", kb.val("p1on"))
                kt = kts[j]
                for sub in range(4):
                    fn = lambda e, j=j, sub=sub, kt=kt, nit=nit: e.matmul(
                        pbk[2 + sub][:, 0:129],
                        Pb[j % 3][:, sub * 128:(sub + 1) * 128],
                        ckvaug_s[:, kt * 144: kt * 144 + 129],
                        start=(j == 0), stop=(j == nit - 1))
                    kb.inc("tensor", fn, "p1pv") if sub == 3 else kb.do("tensor", fn)

            for it, kt in enumerate(kts):
                sbk = pbk[it % 2]
                if it >= 2:
                    kb.wait("tensor", "p1ex", expv[it - 2])
                kb.inc("tensor", lambda e, kt=kt, h=h, sbk=sbk: e.matmul(
                    sbk[:], ckvT_s[:, kt * 128:(kt + 1) * 128],
                    qtg[:, h * 512:(h + 1) * 512], start=True, stop=True),
                    "p1sm")
                pbuf = Pb[it % 3]
                kb.wait("scalar", "p1sm")
                kb.wait("scalar", "p1pv", kb.val("p1pv") - 1)
                col = colbase + it
                expv[it] = kb.inc("scalar", lambda e, sbk=sbk, pbuf=pbuf,
                                  col=col: e.activation(
                    pbuf[:], sbk[:], AF.Exp, bias=amla_s[:, col:col + 1],
                    scale=SCALE), "p1ex")
                if 4 * g <= kt < 4 * g + 4:
                    s_ = kt - 4 * g
                    kb.wait("vector", "p1ex", expv[it])
                    if s_ > 0:
                        kb.do("vector", lambda e, pbuf=pbuf, s_=s_: e.memset(
                            pbuf[:, 0:s_ * 128], 0.0))
                    maskv[it] = kb.inc("vector", lambda e, pbuf=pbuf, s_=s_:
                                       e.tensor_tensor(
                        pbuf[:, s_ * 128:(s_ + 1) * 128],
                        pbuf[:, s_ * 128:(s_ + 1) * 128], tri_s[:],
                        op=ALU.mult), "p1mk")
                if pend[0] is not None:
                    emit_pv(pend[0])
                pend[0] = it
            emit_pv(pend[0])
            kb.wait("vector", "p1pv")
            for sub in range(3):
                kb.do("vector", lambda e, sub=sub: e.reciprocal(
                    rcol[:, sub:sub + 1], pbk[2 + sub][:, 128:129]))
            kb.inc("vector", lambda e: e.reciprocal(
                rcol[:, 3:4], pbk[5][:, 128:129]), "p1rc")
            kb.wait("scalar", "p1rc")
            kb.inc("scalar", lambda e: e.activation(rcol[:, 4:8], rcol[:, 0:4],
                                                    AF.Identity), "p1lnc")
            kb.wait("vector", "p1lnc")
            kb.wait("vector", "p1oT", kb.val("p1oT") - 64)
            for sub in range(4):
                kb.inc("vector", lambda e, sub=sub: e.tensor_scalar(
                    onb[:, sub * 128:(sub + 1) * 128], pbk[2 + sub][:, 0:128],
                    rcol[:, 4 + sub:5 + sub], 2.0 ** -24,
                    op0=ALU.mult, op1=ALU.mult), "p1on")
            kb.wait("sync", "p1on")
            for sub in range(4):
                kb.inc("sync", lambda e, h=h, sub=sub: e.dma_start_transpose(
                    oTg[:, h * 512 + sub * 128: h * 512 + (sub + 1) * 128],
                    onb[:, sub * 128:(sub + 1) * 128]), "p1oT", 16)
        kb.wait("tensor", "p1oT")
        for tsub in range(4):
            t = g * 4 + tsub
            bx = tsub % 2
            kb.wait("sync", "p1xe", kb.val("p1xe") - 8)
            kb.inc("sync", lambda e, t=t, bx=bx: e.dma_start(
                out=xb[bx][:], in_=x0[t * 128:(t + 1) * 128, :]), "p1xr", 16)
            for db in range(4):
                bank = pbk[2 + db]
                kb.wait("tensor", "p1xe", kb.val("p1xe"))
                for h in range(16):
                    fn = lambda e, h=h, db=db, tsub=tsub, bank=bank: e.matmul(
                        bank[:],
                        oTg[:, h * 512 + tsub * 128: h * 512 + (tsub + 1) * 128],
                        w2_s[:, h * 2048 + db * 512: h * 2048 + (db + 1) * 512],
                        start=(h == 0), stop=(h == 15))
                    kb.inc("tensor", fn, "p1s2m") if h == 15 else kb.do("tensor", fn)
            kb.wait("vector", "p1s2m")
            kb.wait("vector", "p1xr")
            bo = 0
            kb.wait("vector", "p1xs", kb.val("p1xs"))
            for db in range(4):
                kb.inc("vector", lambda e, db=db, bo=bo, bx=bx: e.tensor_tensor(
                    x1t[bo][:, db * 512:(db + 1) * 512], pbk[2 + db][:],
                    xb[bx][:, db * 512:(db + 1) * 512], op=ALU.add), "p1xe")
            kb.wait("sync", "p1xe")
            kb.inc("sync", lambda e, t=t, bo=bo: e.dma_start(
                out=x1_d[t * 128:(t + 1) * 128, :], in_=x1t[bo][:]), "p1xs", 16)
    kb.flush()
    al.free("p1per")

    if n_phases == 1:
        al.free("attnsh")
        _copy_out(kb, al, x1_d, y)
        stack.close()
        return nc

    kb.begin()
    negu_s = al.sb("p2tb", "negu_s", (128, 4 * 512), BF16)
    sel_s = al.sb("p2tb", "sel_s", (128, 32), F32)
    vffn_s = al.sb("p2tb", "vffn_s", (128, FFN_NC), F32)
    cs_s = al.sb("p2tb", "cs_s", (128, 16 * 64), F32)
    ln1b_s = al.sb("p2tb", "ln1b_s", (128, D), BF16)
    qnb_s = al.sb("p2tb", "qnb_s", (128, D), BF16)
    knb_s = al.sb("p2tb", "knb_s", (128, 512), BF16)
    for s_, d_ in ((negu, negu_s), (seli, sel_s), (vffn, vffn_s),
                   (cossin, cs_s)):
        kb.inc("sync", lambda e, s=s_, d=d_: e.dma_start(out=d[:], in_=s),
               "p2tb", 16)
    for s_, d_ in ((ln1b, ln1b_s), (qnb, qnb_s), (knb, knb_s)):
        kb.inc("gpsimd", lambda e, s=s_, d=d_: e.dma_start(out=d[:], in_=s),
               "p2tg", 16)
    wkv_s = al.sb("p2a", "wkv_s", (128, 2 * 16 * 512), BF16)
    hTt = al.sb("p2a", "hTt", (128, 16 * 128), BF16)
    ktf = al.sb("p2a", "ktf", (128, 512), F32)
    krb = al.sb("p2a", "krb", (128, 512), BF16)
    kTown = al.sb("p2a", "kTown", (128, 4 * 2048), BF16)
    vaugo = al.sb("p2a", "vaugo", (128, VAW), BF16)

    for ci in range(16):
        kb.inc("gpsimd", lambda e, ci=ci: e.dma_start(
            out=wkv_s[:, ci * 512:(ci + 1) * 512],
            in_=wk[ci * 128:(ci + 1) * 128, :]), "p2tg", 16)
        kb.inc("gpsimd", lambda e, ci=ci: e.dma_start(
            out=wkv_s[:, 8192 + ci * 512: 8192 + (ci + 1) * 512],
            in_=wv[ci * 128:(ci + 1) * 128, :]), "p2tg", 16)
    kb.inc("vector", lambda e: e.memset(vaugo[:], 1.0), "p2vone", 1)

    def h_tile(tag, t, b):
        kb.inc("sync", lambda e, t=t, b=b: e.dma_start(
            out=xb[b][:], in_=x1_d[t * 128:(t + 1) * 128, :]), tag + "ld", 16)
        kb.wait("vector", tag + "ld")
        kb.wait("scalar", tag + "ld")
        _stats(kb, xb[b][:], D, stat[b], junkb, LN_EPS, True, b)
        kb.wait("vector", "sst%d" % b)
        kb.inc("vector", lambda e, b=b: e.tensor_scalar(
            zb[b][:], xb[b][:], stat[b][:, 1:2], stat[b][:, 4:5],
            op0=ALU.subtract, op1=ALU.mult), tag + "z")
        kb.wait("scalar", tag + "z")
        _stats(kb, zb[b][:], D, stat[b], junkb, RMS_EPS, False, b)
        kb.wait("vector", "sst%d" % b)
        kb.do("vector", lambda e, b=b: e.tensor_scalar(
            zb[b][:], zb[b][:], stat[b][:, 4:5], None, op0=ALU.mult))
        kb.inc("vector", lambda e, b=b: e.tensor_tensor(
            zb[b][:], zb[b][:], ln1b_s[:], op=ALU.mult), tag + "h")

    def rope_emit(src, srcw, soff, dst, dstw, doff, nh, tcol, tag):
        cosap = bass.AP(cs_s, tcol * 64, [[1024, 128], [0, nh], [1, 32]])
        sinap = bass.AP(cs_s, tcol * 64 + 32, [[1024, 128], [0, nh], [1, 32]])
        sc = [bass.AP(ropescr, 32 * nh * i, [[2048, 128], [32, nh], [1, 32]])
              for i in range(4)]
        q1 = bass.AP(src, soff, [[srcw, 128], [128, nh], [1, 32]])
        q2 = bass.AP(src, soff + 32, [[srcw, 128], [128, nh], [1, 32]])
        d1 = bass.AP(dst, doff, [[dstw, 128], [128, nh], [1, 32]])
        d2 = bass.AP(dst, doff + 32, [[dstw, 128], [128, nh], [1, 32]])
        kb.do("vector", lambda e: e.tensor_tensor(sc[0], q1, cosap, op=ALU.mult))
        kb.do("vector", lambda e: e.tensor_tensor(sc[1], q2, sinap, op=ALU.mult))
        kb.do("vector", lambda e: e.tensor_tensor(sc[2], q2, cosap, op=ALU.mult))
        kb.do("vector", lambda e: e.tensor_tensor(sc[3], q1, sinap, op=ALU.mult))
        kb.do("vector", lambda e: e.tensor_tensor(d1, sc[0], sc[1],
                                                  op=ALU.subtract))
        kb.do("vector", lambda e: e.tensor_tensor(d2, sc[2], sc[3], op=ALU.add))
        if src is not dst or soff != doff:
            pasrc = bass.AP(src, soff + 64, [[srcw, 128], [128, nh], [1, 64]])
            padst = bass.AP(dst, doff + 64, [[dstw, 128], [128, nh], [1, 64]])
            kb.do("vector", lambda e: e.tensor_copy(padst, pasrc))
        kb.inc("vector", lambda e: e.tensor_copy(
            bass.AP(dst, doff, [[dstw, 128], [1, 1]]),
            bass.AP(dst, doff, [[dstw, 128], [1, 1]])), tag)

    kb.wait("vector", "p2tg")
    kb.wait("vector", "p2tb")
    for t in range(16):
        b = t % 2
        kb.wait("sync", "p2ahT", (t - 1) * 256)
        kb.wait("sync", "p2az", t - 1)
        h_tile("p2a", t, b)
        kb.wait("sync", "p2ah")
        kb.wait("sync", "p2akm", kb.val("p2akm"))
        kb.wait("sync", "p2avm", kb.val("p2avm"))
        for dch in range(16):
            kb.inc("sync", lambda e, b=b, dch=dch: e.dma_start_transpose(
                hTt[:, dch * 128:(dch + 1) * 128],
                zb[b][:, dch * 128:(dch + 1) * 128]), "p2ahT", 16)
        kb.wait("tensor", "p2ahT")
        kb.wait("tensor", "p2tg")
        kb.wait("tensor", "p2ake", kb.val("p2ake") - 1)
        for dch in range(16):
            fn = lambda e, dch=dch: e.matmul(
                pbk[0][:], hTt[:, dch * 128:(dch + 1) * 128],
                wkv_s[:, dch * 512:(dch + 1) * 512],
                start=(dch == 0), stop=(dch == 15))
            kb.inc("tensor", fn, "p2akm") if dch == 15 else kb.do("tensor", fn)
        kb.wait("tensor", "p2ave", kb.val("p2ave") - 1)
        for dch in range(16):
            fn = lambda e, dch=dch: e.matmul(
                pbk[1][:], hTt[:, dch * 128:(dch + 1) * 128],
                wkv_s[:, 8192 + dch * 512: 8192 + (dch + 1) * 512],
                start=(dch == 0), stop=(dch == 15))
            kb.inc("tensor", fn, "p2avm") if dch == 15 else kb.do("tensor", fn)
        kb.wait("vector", "p2akm")
        kb.wait("vector", "p2akT", t * 64)
        kb.inc("vector", lambda e: e.tensor_copy(ktf[:], pbk[0][:]), "p2ake")
        kb.wait("scalar", "p2ake")
        _stats(kb, ktf[:], 512, stat[b], junkb, RMS_EPS, False, b)
        kb.wait("vector", "sst%d" % b)
        kb.do("vector", lambda e, b=b: e.tensor_scalar(
            ktf[:], ktf[:], stat[b][:, 4:5], None, op0=ALU.mult))
        kb.inc("vector", lambda e: e.tensor_tensor(
            ktf[:], ktf[:], knb_s[:], op=ALU.mult), "p2akn")
        rope_emit(ktf, 512, 0, krb, 512, 0, 4, t, "p2akr")
        kb.wait("sync", "p2akr")
        for kvh in range(4):
            kb.inc("sync", lambda e, kvh=kvh, t=t: e.dma_start_transpose(
                kTown[:, kvh * 2048 + t * 128: kvh * 2048 + (t + 1) * 128],
                krb[:, kvh * 128:(kvh + 1) * 128]), "p2akT", 16)
        kb.wait("vector", "p2avm")
        kb.wait("vector", "p2vone", 1)
        vdst = bass.AP(vaugo, t * 528, [[VAW, 128], [132, 4], [1, 128]])
        vsrc = bass.AP(pbk[1], 0, [[512, 128], [128, 4], [1, 128]])
        kb.inc("vector", lambda e, vdst=vdst, vsrc=vsrc: e.tensor_copy(
            vdst, vsrc), "p2ave")
    kb.wait("sync", "p2akT")
    kb.wait("sync", "p2ave")
    kb.inc("sync", lambda e: e.dma_start(out=kvin_d[:, 0:8192], in_=kTown[:]),
           "p2kvw", 16)
    kb.inc("sync", lambda e: e.dma_start(out=kvin_d[:, 8192:KVW], in_=vaugo[:]),
           "p2kvw", 16)
    kb.wait("gpsimd", "p2kvw")
    kb.inc("gpsimd", lambda e: e.collective_compute(
        "AllGather", ALU.bypass,
        replica_groups=[[0, 1], [2, 3], [4, 5], [6, 7]],
        ins=[kvin_d.opt()], outs=[kvout_d.opt()]), "p2cc", 1)
    kb.flush()
    al.free("p2a")

    kb.begin()
    kT = al.sb("p2b", "kT", (128, 4 * 4096), BF16)
    vaug = al.sb("p2b", "vaug", (128, 32 * 528), BF16)
    hTg2 = al.sb("p2b", "hTg2", (128, 16 * 512), BF16)
    o2Tg = hTg2
    qf4 = al.sb("p2b", "qf4", (128, 4 * 2048), BF16)
    q2Tg = al.sb("p2b", "q2Tg", (128, 16 * 512), BF16)
    wqo2 = [al.sb("p2b", "wqo%d" % i, (128, 16 * 512), BF16) for i in range(2)]
    x2s = al.sb("p2b", "x2s", (128, 1024), F32)
    s2st = al.sb("p2b", "s2st", (128, 8), F32)

    kb.wait("sync", "p2cc", 1)
    for r in range(2):
        for kvh in range(4):
            kb.inc("sync", lambda e, r=r, kvh=kvh: e.dma_start(
                out=kT[:, kvh * 4096 + r * 2048: kvh * 4096 + (r + 1) * 2048],
                in_=kvout_d[r, :, kvh * 2048:(kvh + 1) * 2048]), "p2kvr", 16)
        kb.inc("sync", lambda e, r=r: e.dma_start(
            out=vaug[:, r * VAW:(r + 1) * VAW],
            in_=kvout_d[r, :, 8192:KVW]), "p2kvr", 16)
    kb.wait("tensor", "p2kvr")
    if DBG_P2:
        kb.wait("sync", "p2kvr")
        kb.inc("sync", lambda e: e.dma_start(out=dbg_kt, in_=kT[:]), "dbg", 16)
        kb.inc("sync", lambda e: e.dma_start(out=dbg_va, in_=vaug[:]), "dbg", 16)

    st2 = {"n": 0}
    for g in range(NG):
        kb.wait("sync", "p2qm", kb.val("p2qm"))
        kb.wait("sync", "p2s2m", kb.val("p2s2m"))
        kb.wait("sync", "p2xe", kb.val("p2xe"))
        for ts4 in range(4):
            n2 = st2["n"]
            st2["n"] += 1
            t = g * 4 + ts4
            b = n2 % 2
            kb.wait("sync", "p2bhT", (n2 - 1) * 256)
            kb.wait("sync", "p2bz", n2 - 1)
            h_tile("p2b", t, b)
            kb.wait("sync", "p2bh")
            for dch in range(16):
                kb.inc("sync", lambda e, b=b, dch=dch, ts4=ts4: e.dma_start_transpose(
                    hTg2[:, dch * 512 + ts4 * 128: dch * 512 + (ts4 + 1) * 128],
                    zb[b][:, dch * 128:(dch + 1) * 128]), "p2bhT", 16)
        kb.wait("tensor", "p2bhT")
        kb.wait("vector", "p2qT", kb.val("p2qT"))
        for fb in range(4):
            wsl = wqo2[fb % 2]
            sname = "p2wq%d" % (fb % 2)
            kb.wait("gpsimd", "p2qm", kb.val("p2qm") - 4)
            kb.wait("gpsimd", "p2s2m", kb.val("p2s2m"))
            for dch in range(16):
                kb.inc("gpsimd", lambda e, dch=dch, fb=fb, wsl=wsl: e.dma_start(
                    out=wsl[:, dch * 512:(dch + 1) * 512],
                    in_=wq[dch * 128:(dch + 1) * 128, fb * 512:(fb + 1) * 512]),
                    sname, 16)
            kb.wait("tensor", sname)
            for ts4 in range(4):
                bank = pbk[ts4 % 2]
                kb.wait("tensor", "p2qe", kb.val("p2qe") - 1)
                for dch in range(16):
                    fn = lambda e, dch=dch, ts4=ts4, bank=bank, wsl=wsl: \
                        e.matmul(
                        bank[:],
                        hTg2[:, dch * 512 + ts4 * 128: dch * 512 + (ts4 + 1) * 128],
                        wsl[:, dch * 512:(dch + 1) * 512],
                        start=(dch == 0), stop=(dch == 15))
                    kb.inc("tensor", fn, "p2qm") if dch == 15 else kb.do("tensor", fn)
                kb.wait("vector", "p2qm")
                kb.inc("vector", lambda e, ts4=ts4, fb=fb, bank=bank: e.tensor_copy(
                    qf4[:, ts4 * 2048 + fb * 512: ts4 * 2048 + (fb + 1) * 512],
                    bank[:]), "p2qe")
        kb.wait("sync", "p1sm", kb.val("p1sm"))
        for ts4 in range(4):
            b = ts4 % 2
            t = g * 4 + ts4
            qsl = qf4[:, ts4 * 2048:(ts4 + 1) * 2048]
            kb.wait("vector", "p2qe", kb.val("p2qe"))
            kb.wait("scalar", "p2qe", kb.val("p2qe"))
            _stats(kb, qsl, D, stat[b], junkb, RMS_EPS, False, b)
            kb.wait("vector", "sst%d" % b)
            kb.do("vector", lambda e, qsl=qsl, b=b: e.tensor_scalar(
                qsl, qsl, stat[b][:, 4:5], None, op0=ALU.mult))
            kb.inc("vector", lambda e, qsl=qsl: e.tensor_tensor(
                qsl, qsl, qnb_s[:], op=ALU.mult), "p2qn")
            rope_emit(qf4, 4 * 2048, ts4 * 2048, qf4, 4 * 2048, ts4 * 2048,
                      16, t, "p2qr")
            kb.wait("sync", "p2qr")
            for hh in range(16):
                kb.inc("sync", lambda e, hh=hh, ts4=ts4: e.dma_start_transpose(
                    q2Tg[:, hh * 512 + ts4 * 128: hh * 512 + (ts4 + 1) * 128],
                    qf4[:, ts4 * 2048 + hh * 128: ts4 * 2048 + (hh + 1) * 128]),
                    "p2qT", 16)
        kb.wait("tensor", "p2qT")
        if DBG_P2:
            kb.wait("sync", "p2qT")
            kb.inc("sync", lambda e, g=g: e.dma_start(
                out=dbg_q2[g], in_=q2Tg[:]), "dbg", 16)
        kb.wait("tensor", "p2qe")
        kb.wait("sync", "p2s2m", kb.val("p2s2m"))
        kb.wait("tensor", "p2xe", kb.val("p2xe"))

        kts = FFN_KTS[g]
        nit = len(kts)
        colbase = sum(len(FFN_KTS[gg]) for gg in range(g))
        cand = FFN_CAND[g]
        kb.wait("sync", "p2qm", kb.val("p2qm"))
        for h in range(16):
            kvh = h // 4
            expv = {}
            maskv = {}
            pend = [None]

            def emit_pv2(j2_, kts=kts, nit=nit, expv=expv, maskv=maskv,
                         kvh=kvh):
                if j2_ in maskv:
                    kb.wait("tensor", "p1mk", maskv[j2_])
                else:
                    kb.wait("tensor", "p1ex", expv[j2_])
                if j2_ == 0:
                    kb.wait("tensor", "p1on", kb.val("p1on"))
                kt = kts[j2_]
                for sub in range(4):
                    fn = lambda e, j2_=j2_, sub=sub, kt=kt, nit=nit, kvh=kvh: \
                        e.matmul(
                        pbk[2 + sub][:, 0:129],
                        Pb[j2_ % 3][:, sub * 128:(sub + 1) * 128],
                        vaug[:, kt * 528 + kvh * 132: kt * 528 + kvh * 132 + 129],
                        start=(j2_ == 0), stop=(j2_ == nit - 1))
                    kb.inc("tensor", fn, "p1pv") if sub == 3 else kb.do("tensor", fn)

            for it, kt in enumerate(kts):
                sbk = pbk[it % 2]
                if it >= 2:
                    kb.wait("tensor", "p1ex", expv[it - 2])
                kb.inc("tensor", lambda e, kt=kt, h=h, kvh=kvh, sbk=sbk:
                       e.matmul(
                    sbk[:],
                    kT[:, kvh * 4096 + kt * 128: kvh * 4096 + (kt + 1) * 128],
                    q2Tg[:, h * 512:(h + 1) * 512], start=True, stop=True),
                    "p1sm")
                pbuf = Pb[it % 3]
                kb.wait("scalar", "p1sm")
                kb.wait("scalar", "p1pv", kb.val("p1pv") - 1)
                col = colbase + it
                expv[it] = kb.inc("scalar", lambda e, sbk=sbk, pbuf=pbuf,
                                  col=col: e.activation(
                    pbuf[:], sbk[:], AF.Exp, bias=vffn_s[:, col:col + 1],
                    scale=SCALE), "p1ex")
                if kt in cand:
                    jj = cand.index(kt)
                    s_ = jj % 4
                    slot = g * 8 + jj
                    kb.wait("vector", "p1ex", expv[it])
                    kb.do("vector", lambda e, pbuf=pbuf, s_=s_: e.tensor_tensor(
                        Pscr[:], pbuf[:], negu_s[:, s_ * 512:(s_ + 1) * 512],
                        op=ALU.mult))
                    maskv[it] = kb.inc("vector", lambda e, pbuf=pbuf, slot=slot:
                                       e.scalar_tensor_tensor(
                        pbuf[:], Pscr[:], sel_s[:, slot:slot + 1], pbuf[:],
                        op0=ALU.mult, op1=ALU.add), "p1mk")
                if pend[0] is not None:
                    emit_pv2(pend[0])
                pend[0] = it
            emit_pv2(pend[0])
            kb.wait("vector", "p1pv")
            for sub in range(3):
                kb.do("vector", lambda e, sub=sub: e.reciprocal(
                    rcol[:, sub:sub + 1], pbk[2 + sub][:, 128:129]))
            kb.inc("vector", lambda e: e.reciprocal(
                rcol[:, 3:4], pbk[5][:, 128:129]), "p1rc")
            kb.wait("scalar", "p1rc")
            kb.inc("scalar", lambda e: e.activation(rcol[:, 4:8], rcol[:, 0:4],
                                                    AF.Identity), "p1lnc")
            kb.wait("vector", "p1lnc")
            kb.wait("vector", "p1oT", kb.val("p1oT") - 64)
            for sub in range(4):
                kb.inc("vector", lambda e, sub=sub: e.tensor_scalar(
                    onb[:, sub * 128:(sub + 1) * 128], pbk[2 + sub][:, 0:128],
                    rcol[:, 4 + sub:5 + sub], None, op0=ALU.mult), "p1on")
            kb.wait("sync", "p1on")
            for sub in range(4):
                kb.inc("sync", lambda e, h=h, sub=sub: e.dma_start_transpose(
                    o2Tg[:, h * 512 + sub * 128: h * 512 + (sub + 1) * 128],
                    onb[:, sub * 128:(sub + 1) * 128]), "p1oT", 16)
        kb.wait("tensor", "p1oT")
        if DBG_P2:
            kb.wait("sync", "p1oT")
            kb.inc("sync", lambda e, g=g: e.dma_start(
                out=dbg_o2[g], in_=o2Tg[:]), "dbg", 16)
            kb.wait("tensor", "dbg")
        for tsub in range(4):
            t = g * 4 + tsub
            bx = tsub % 2
            kb.wait("sync", "p2st", kb.val("p2st"))
            kb.inc("sync", lambda e, t=t, bx=bx: e.dma_start(
                out=xb[bx][:], in_=x1_d[t * 128:(t + 1) * 128, :]), "p2xr", 16)
            kb.wait("vector", "p2xr")
            kb.wait("scalar", "p2xr")
            _stats(kb, xb[bx][:], D, stat[bx], junkb, LN_EPS, True, bx)
            kb.wait("scalar", "sst%d" % bx)
            kb.inc("scalar", lambda e, tsub=tsub, bx=bx: e.activation(
                s2st[:, tsub * 2:tsub * 2 + 1], stat[bx][:, 4:5], AF.Identity),
                "p2st")
            kb.inc("scalar", lambda e, tsub=tsub, bx=bx: e.activation(
                s2st[:, tsub * 2 + 1:tsub * 2 + 2], stat[bx][:, 6:7],
                AF.Identity), "p2st")
        for db in range(4):
            wsl = wqo2[db % 2]
            sname = "p2wq%d" % (db % 2)
            kb.wait("gpsimd", "p2s2m", kb.val("p2s2m") - 4)
            kb.wait("gpsimd", "p2qm", kb.val("p2qm"))
            for hh in range(16):
                kb.inc("gpsimd", lambda e, hh=hh, db=db, wsl=wsl: e.dma_start(
                    out=wsl[:, hh * 512:(hh + 1) * 512],
                    in_=wo[hh * 128:(hh + 1) * 128, db * 512:(db + 1) * 512]),
                    sname, 16)
            kb.wait("tensor", sname)
            for tsub in range(4):
                t = g * 4 + tsub
                i2 = db * 4 + tsub
                bank = pbk[2 + i2 % 2]
                xsl = slice((i2 % 2) * 512, (i2 % 2) * 512 + 512)
                kb.wait("sync", "p2xs", kb.val("p2xs") - 16)
                kb.inc("sync", lambda e, t=t, db=db, xsl=xsl: e.dma_start(
                    out=x2s[:, xsl], in_=x1_d[t * 128:(t + 1) * 128,
                                              db * 512:(db + 1) * 512]),
                    "p2xr", 16)
                kb.wait("tensor", "p2xe", kb.val("p2xe") - 1)
                for hh in range(16):
                    fn = lambda e, hh=hh, db=db, tsub=tsub, bank=bank, wsl=wsl: \
                        e.matmul(
                        bank[:],
                        o2Tg[:, hh * 512 + tsub * 128: hh * 512 + (tsub + 1) * 128],
                        wsl[:, hh * 512:(hh + 1) * 512],
                        start=(hh == 0), stop=(hh == 15))
                    kb.inc("tensor", fn, "p2s2m") if hh == 15 else kb.do("tensor", fn)
                kb.wait("vector", "p2s2m")
                kb.wait("vector", "p2xr")
                kb.wait("vector", "p2st", kb.val("p2st"))
                kb.wait("vector", "p2xs", kb.val("p2xs") - 16)
                kb.do("vector", lambda e, xsl=xsl, tsub=tsub, bank=bank:
                      e.scalar_tensor_tensor(
                    x2s[:, xsl], x2s[:, xsl], s2st[:, tsub * 2:tsub * 2 + 1],
                    bank[:], op0=ALU.mult, op1=ALU.add))
                kb.inc("vector", lambda e, xsl=xsl, tsub=tsub: e.tensor_scalar(
                    x2s[:, xsl], x2s[:, xsl],
                    s2st[:, tsub * 2 + 1:tsub * 2 + 2], None,
                    op0=ALU.subtract), "p2xe")
                kb.wait("sync", "p2xe")
                kb.inc("sync", lambda e, t=t, db=db, xsl=xsl: e.dma_start(
                    out=x2_d[t * 128:(t + 1) * 128, db * 512:(db + 1) * 512],
                    in_=x2s[:, xsl]), "p2xs", 16)
    kb.flush()
    al.free("p2b")
    al.free("p2tb")
    al.free("attnsh")

    if n_phases == 2:
        _copy_out(kb, al, x2_d, y)
        stack.close()
        return nc

    kb.begin()
    identf_s = al.sb("p3", "identf_s", (128, 128), F32)
    ln2b_s = al.sb("p3", "ln2b_s", (128, D), F32)
    rbb_s = al.sb("p3", "rbb_s", (128, 8), F32)
    comb_s = al.sb("p3", "comb_s", (128, 16 * 8), F32)
    for s_, d_ in ((identf, identf_s), (ln2b, ln2b_s), (rbb, rbb_s)):
        kb.inc("sync", lambda e, s=s_, d=d_: e.dma_start(out=d[:], in_=s),
               "p3tab", 16)
    x2m = al.sb("p3", "x2m", (128, 8 * 2048), F32)
    h2T = al.sb("p3", "h2T", (128, 16 * 1024), BF16)
    h2f = xb[0]
    h2Tf = al.sb("p3", "h2Tf", (128, 128), F32)
    rws = al.sb("p3", "rws", (128, 16 * 8), F32)
    sig = al.sb("p3", "sig", (128, 8), F32)
    sg2 = al.sb("p3", "sg2", (128, 32), F32)
    actT = al.sb("p3", "actT", (128, 8 * 1024), BF16)
    gsil = al.sb("p3", "gsil", (128, 1024), BF16)
    wgs = [al.sb("p3", "wgs%d" % i, (128, 16 * 128), BF16) for i in range(2)]
    wus = [al.sb("p3", "wus%d" % i, (128, 16 * 128), BF16) for i in range(2)]
    wds = al.sb("p3", "wds", (128, 8 * 2048), BF16)

    for ch in range(16):
        kb.inc("sync", lambda e, ch=ch: e.dma_start(
            out=rws[:, ch * 8:(ch + 1) * 8],
            in_=rw[ch * 128:(ch + 1) * 128, :]), "p3rw", 16)

    for hf in range(2):
        kb.wait("vector", "p3gm", kb.val("p3gm"))
        for tl in range(8):
            t = hf * 8 + tl
            kb.wait("sync", "p3ys", hf * 8 * 16)
            kb.inc("sync", lambda e, t=t, tl=tl: e.dma_start(
                out=x2m[:, tl * 2048:(tl + 1) * 2048],
                in_=x2_d[t * 128:(t + 1) * 128, :]), "p3xl", 16)
        for tl in range(8):
            t = hf * 8 + tl
            b = tl % 2
            kb.wait("vector", "p3xl", (hf * 8 + tl + 1) * 16)
            kb.wait("scalar", "p3xl", (hf * 8 + tl + 1) * 16)
            xt = x2m[:, tl * 2048:(tl + 1) * 2048]
            _stats(kb, xt, D, stat[b], junkb, RMS_EPS, False, b)
            kb.wait("vector", "sst%d" % b)
            kb.wait("vector", "p3tp", kb.val("p3tp"))
            kb.do("vector", lambda e, xt=xt, b=b: e.tensor_scalar(
                h2f[:], xt, stat[b][:, 4:5], None, op0=ALU.mult))
            kb.inc("vector", lambda e: e.tensor_tensor(
                h2f[:], h2f[:], ln2b_s[:], op=ALU.mult), "p3h2")
            kb.wait("tensor", "p3h2")
            kb.wait("tensor", "p3rw", 256)
            kb.wait("tensor", "p3tab", 48)
            kb.wait("tensor", "p3sg", kb.val("p3sg"))
            for dch in range(16):
                kb.wait("tensor", "p3he", kb.val("p3he"))
                kb.inc("tensor", lambda e, dch=dch: e.transpose(
                    pbk[6][:, 0:128], h2f[:, dch * 128:(dch + 1) * 128],
                    identf_s[:]), "p3tp")
                kb.wait("vector", "p3tp")
                kb.wait("vector", "p3rm", kb.val("p3rm"))
                kb.inc("vector", lambda e: e.tensor_copy(h2Tf[:],
                                                         pbk[6][:, 0:128]),
                       "p3he")
                kb.inc("vector", lambda e, dch=dch, tl=tl: e.tensor_copy(
                    h2T[:, dch * 1024 + tl * 128: dch * 1024 + (tl + 1) * 128],
                    h2Tf[:]), "p3hT")
                kb.wait("tensor", "p3he")
                kb.inc("tensor", lambda e, dch=dch: e.matmul(
                    pbk[7][:, 0:8], h2Tf[:], rws[:, dch * 8:(dch + 1) * 8],
                    start=(dch == 0), stop=(dch == 15)), "p3rm")
            kb.wait("scalar", "p3rm")
            if DBG_P3 and hf == 0 and tl == 7:
                kb.inc("scalar", lambda e: e.activation(
                    sg2[:, 18:26], pbk[7][:, 0:8], AF.Identity), "p3sg")
            kb.inc("scalar", lambda e: e.activation(sig[:], pbk[7][:, 0:8],
                                                    AF.Sigmoid), "p3sg")
            kb.wait("vector", "p3sg")
            A = sg2[:, 0:8]
            Bc = sg2[:, 8:16]
            Cc = sg2[:, 16:24]

            def vstep(fn):
                kb.inc("vector", fn, "p3sg")
                kb.wait("scalar", "p3sg")

            def scopy(dst, srcc):
                kb.inc("scalar", lambda e, d=dst, s=srcc: e.activation(
                    d, s, AF.Identity), "p3sg")
                kb.wait("vector", "p3sg")

            vstep(lambda e: e.tensor_tensor(A, sig[:], rbb_s[:], op=ALU.add))
            scopy(Bc, A)
            vstep(lambda e: e.reduce_max(sg2[:, 24:25], Bc,
                                         axis=mybir.AxisListType.X))
            scopy(sg2[:, 25:26], sg2[:, 24:25])
            vstep(lambda e: e.tensor_scalar(A, Bc, sg2[:, 25:26], None,
                                            op0=ALU.is_ge))
            scopy(Cc, A)
            vstep(lambda e: e.scalar_tensor_tensor(A, Cc, NEG, Bc,
                                                   op0=ALU.mult, op1=ALU.add))
            scopy(Bc, A)
            vstep(lambda e: e.reduce_max(sg2[:, 24:25], Bc,
                                         axis=mybir.AxisListType.X))
            scopy(sg2[:, 25:26], sg2[:, 24:25])
            vstep(lambda e: e.tensor_scalar(A, Bc, sg2[:, 25:26], None,
                                            op0=ALU.is_ge))
            scopy(Bc, A)
            vstep(lambda e: e.tensor_tensor(A, Cc, Bc, op=ALU.add))
            scopy(Cc, A)
            vstep(lambda e: e.tensor_tensor(A, Cc, sig[:], op=ALU.mult))
            scopy(Bc, A)
            vstep(lambda e: e.reduce_sum(sg2[:, 24:25], Bc,
                                         axis=mybir.AxisListType.X))
            scopy(sg2[:, 25:26], sg2[:, 24:25])
            vstep(lambda e: e.reciprocal(sg2[:, 26:27], sg2[:, 25:26]))
            scopy(sg2[:, 27:28], sg2[:, 26:27])
            vstep(lambda e: e.tensor_scalar(A, Bc, sg2[:, 27:28], None,
                                            op0=ALU.mult))
            kb.inc("scalar", lambda e, t=t: e.activation(
                comb_s[:, t * 8:(t + 1) * 8], A, AF.Identity), "p3cb")
        kb.wait("tensor", "p3hT", kb.val("p3hT"))
        if DBG_P3 and hf == 0:
            kb.wait("sync", "p3cb", kb.val("p3cb"))
            kb.wait("sync", "p3sg", kb.val("p3sg"))
            kb.inc("sync", lambda e: e.dma_start(
                out=dbg_sg[:, 0:24], in_=sg2[:, 0:24]), "p3x1", 16)
            kb.inc("sync", lambda e: e.dma_start(
                out=dbg_sg[:, 24:32], in_=sig[:]), "p3x1", 16)
            kb.wait("sync", "p3hT", kb.val("p3hT"))
            kb.inc("sync", lambda e: e.dma_start(out=dbg_cb, in_=comb_s[:]),
                   "p3x1", 16)
            kb.inc("sync", lambda e: e.dma_start(out=dbg_h2, in_=h2T[:]),
                   "p3x1", 16)
        for e_ in range(E):
            kb.wait("gpsimd", "p3dn", kb.val("p3dn"))
            for ich in range(8):
                kb.inc("gpsimd", lambda e, e_=e_, ich=ich: e.dma_start(
                    out=wds[:, ich * 2048:(ich + 1) * 2048],
                    in_=wdp[e_, ich * 128:(ich + 1) * 128, :]), "p3wd", 16)
            kb.wait("vector", "p3dn", kb.val("p3dn"))
            for i128 in range(8):
                sl_ = i128 % 2
                wb_ = wgs[sl_]
                wub = wus[sl_]
                kb.wait("gpsimd", "p3gm", kb.val("p3gm") - 4)
                kb.inc("gpsimd", lambda e, e_=e_, i128=i128, wb_=wb_: e.dma_start(
                    out=wb_[:], in_=wg_r[e_, i128, :, :]), "p3wg%d" % sl_, 16)
                kb.inc("gpsimd", lambda e, e_=e_, i128=i128, wub=wub: e.dma_start(
                    out=wub[:], in_=wu_r[e_, i128, :, :]), "p3wg%d" % sl_, 16)
                kb.wait("tensor", "p3wg%d" % sl_)
                kb.wait("tensor", "p3sl", kb.val("p3sl"))
                kb.wait("tensor", "p3ae", kb.val("p3ae"))
                for tg in range(2):
                    for dch in range(16):
                        fn = lambda e, dch=dch, tg=tg, wb_=wb_: e.matmul(
                            pbk[tg][:], wb_[:, dch * 128:(dch + 1) * 128],
                            h2T[:, dch * 1024 + tg * 512: dch * 1024 + (tg + 1) * 512],
                            start=(dch == 0), stop=(dch == 15))
                        kb.inc("tensor", fn, "p3gm") if dch == 15 \
                            else kb.do("tensor", fn)
                        fn2 = lambda e, dch=dch, tg=tg, wub=wub: e.matmul(
                            pbk[2 + tg][:], wub[:, dch * 128:(dch + 1) * 128],
                            h2T[:, dch * 1024 + tg * 512: dch * 1024 + (tg + 1) * 512],
                            start=(dch == 0), stop=(dch == 15))
                        kb.inc("tensor", fn2, "p3gm") if dch == 15 \
                            else kb.do("tensor", fn2)
                kb.wait("scalar", "p3gm")
                for tg in range(2):
                    kb.inc("scalar", lambda e, tg=tg: e.activation(
                        gsil[:, tg * 512:(tg + 1) * 512], pbk[tg][:], AF.Silu),
                        "p3sl")
                kb.wait("vector", "p3sl")
                for tg in range(2):
                    kb.inc("vector", lambda e, tg=tg, i128=i128: e.tensor_tensor(
                        actT[:, i128 * 1024 + tg * 512: i128 * 1024 + (tg + 1) * 512],
                        gsil[:, tg * 512:(tg + 1) * 512], pbk[2 + tg][:],
                        op=ALU.mult), "p3ae")
            kb.wait("tensor", "p3ae")
            kb.wait("tensor", "p3wd")
            for tl in range(8):
                t = hf * 8 + tl
                for db in range(4):
                    bank = pbk[4 + (tl * 4 + db) % 2]
                    kb.wait("tensor", "p3de", kb.val("p3de") - 1)
                    for ich in range(8):
                        fn = lambda e, ich=ich, tl=tl, db=db, bank=bank: \
                            e.matmul(
                            bank[:],
                            actT[:, ich * 1024 + tl * 128: ich * 1024 + (tl + 1) * 128],
                            wds[:, ich * 2048 + db * 512: ich * 2048 + (db + 1) * 512],
                            start=(ich == 0), stop=(ich == 7))
                        kb.inc("tensor", fn, "p3dn") if ich == 7 \
                            else kb.do("tensor", fn)
                    kb.wait("vector", "p3dn")
                    kb.wait("vector", "p3cb", (hf + 1) * 8)
                    kb.inc("vector", lambda e, tl=tl, db=db, t=t, e_=e_,
                           bank=bank: e.scalar_tensor_tensor(
                        x2m[:, tl * 2048 + db * 512: tl * 2048 + (db + 1) * 512],
                        bank[:], comb_s[:, t * 8 + e_: t * 8 + e_ + 1],
                        x2m[:, tl * 2048 + db * 512: tl * 2048 + (db + 1) * 512],
                        op0=ALU.mult, op1=ALU.add), "p3de")
        for tl in range(8):
            t = hf * 8 + tl
            kb.wait("sync", "p3xa", kb.val("p3xa"))
            kb.inc("sync", lambda e, t=t: e.dma_start(
                out=xb[1][:], in_=x1_d[t * 128:(t + 1) * 128, :]), "p3x1", 16)
            kb.wait("vector", "p3x1")
            kb.wait("vector", "p3de", kb.val("p3de"))
            kb.inc("vector", lambda e, tl=tl: e.tensor_tensor(
                x2m[:, tl * 2048:(tl + 1) * 2048],
                x2m[:, tl * 2048:(tl + 1) * 2048], xb[1][:], op=ALU.add),
                "p3xa")
            kb.wait("sync", "p3xa")
            kb.inc("sync", lambda e, t=t, tl=tl: e.dma_start(
                out=y[t * 128:(t + 1) * 128, :],
                in_=x2m[:, tl * 2048:(tl + 1) * 2048]), "p3ys", 16)
    kb.flush()
    al.free("p3")
    al.free("base")
    stack.close()
    return nc


def _copy_out(kb, al, src_d, y):
    kb.begin()
    dbg = al.sb("dbg", "dbgt", (128, D), F32)
    for t in range(16):
        kb.wait("sync", "dst", (t - 1) * 16)
        kb.inc("sync", lambda e, t=t: e.dma_start(
            out=dbg[:], in_=src_d[t * 128:(t + 1) * 128, :]), "dld", 16)
        kb.wait("sync", "dld")
        kb.inc("sync", lambda e, t=t: e.dma_start(
            out=y[t * 128:(t + 1) * 128, :], in_=dbg[:]), "dst", 16)
    kb.flush()
    al.free("dbg")
    al.free("base")



def _host_tables(H):
    t = {}
    t["identf"] = np.eye(128, dtype=np.float32)
    i = np.arange(128)[:, None]
    j = np.arange(128)[None, :]
    t["tri01"] = (i <= j).astype(ml_dtypes.bfloat16)
    negu = np.zeros((128, 4, 512), np.float32)
    for s in range(4):
        m = np.ones((128, 512), np.float32)
        m[:, : s * 128] = 0.0
        m[:, s * 128:(s + 1) * 128] = (i <= j).astype(np.float32)
        negu[:, s, :] = -(1.0 - m)
    t["negu"] = negu.reshape(128, 4 * 512).astype(ml_dtypes.bfloat16)
    sel = np.zeros((128, 32), np.float32)
    for g in range(NG):
        for fam in range(2):
            for s in range(4):
                sel[:, g * 8 + fam * 4 + s] = 1.0 if fam == H else 0.0
    t["sel"] = sel
    amla_t = np.full((128, MLA_NC), NEG, np.float32)
    ii = np.arange(128)
    colb = 0
    for g in range(NG):
        Q0 = 2048 * H + 512 * g
        for it, kt in enumerate(MLA_KTS[g]):
            if kt < 4 * g + 4:
                kpos = kt * 128 + ii + 2048 * H
                amla_t[:, colb + it] = SLOPE * (kpos - Q0)
            elif H == 1:
                kpos = (kt * 128 + ii + 2048) % 4096
                amla_t[:, colb + it] = SLOPE * (kpos - Q0)
        colb += len(MLA_KTS[g])
    t["amla"] = amla_t
    vffn_t = np.full((128, FFN_NC), NEG, np.float32)
    colb = 0
    for g in range(NG):
        for it, kt in enumerate(FFN_KTS[g]):
            if kt <= 16 * H + 4 * g + 3:
                vffn_t[:, colb + it] = 0.0
        colb += len(FFN_KTS[g])
    t["vffn"] = vffn_t
    pos = (np.arange(T) + 2048 * H).astype(np.float64)
    invf = 1.0 / (1e6 ** (np.arange(0, 64, 2) / 64.0))
    ang = pos[:, None] * invf[None, :]
    cs = np.zeros((128, 16 * 64), np.float32)
    for tt in range(16):
        cs[:, tt * 64: tt * 64 + 32] = np.cos(ang[tt * 128:(tt + 1) * 128])
        cs[:, tt * 64 + 32: tt * 64 + 64] = np.sin(ang[tt * 128:(tt + 1) * 128])
    t["cossin"] = cs
    return t


_NC_CACHE = {}


def kernel(**inp):
    g = lambda k: np.ascontiguousarray(np.asarray(inp[k], dtype=np.float32))
    x0f = g("L0_attn_hidden_states")

    def restage(w):
        return np.ascontiguousarray(
            w.reshape(E, 16, 128, 8, 128).transpose(0, 3, 2, 1, 4)
            .reshape(E, 8, 128, 16 * 128))

    common = {
        "w_dkv": g("L0_attn_w_dkv"), "w_q": g("L0_attn_w_q"),
        "w_uk": g("L0_attn_w_uk"), "w_uv": g("L0_attn_w_uv"),
        "w_o": g("L0_attn_w_o"), "wq": g("L0_ffn_wq"), "wk": g("L0_ffn_wk"),
        "wv": g("L0_ffn_wv"), "wo": g("L0_ffn_wo"),
        "router_w": g("L0_ffn_router_w"),
        "wg_r": restage(g("L0_ffn_wg")), "wu_r": restage(g("L0_ffn_wu")),
        "wd": g("L0_ffn_wd"),
        "ln1b": np.ascontiguousarray(
            np.broadcast_to(g("L0_ffn_ln1_w"), (128, D))),
        "ln2b": np.ascontiguousarray(
            np.broadcast_to(g("L0_ffn_ln2_w"), (128, D))),
        "qnb": np.ascontiguousarray(
            np.broadcast_to(g("L0_ffn_qnorm_w"), (128, D))),
        "knb": np.ascontiguousarray(
            np.broadcast_to(g("L0_ffn_knorm_w"), (128, 512))),
        "rbb": np.ascontiguousarray(
            np.broadcast_to(g("L0_ffn_router_bias"), (128, 8))),
    }
    tabs = [_host_tables(H) for H in range(2)]

    in_maps = []
    for c in range(8):
        b, H = c // 2, c % 2
        m = dict(common)
        m.update(tabs[H])
        m["x0"] = np.ascontiguousarray(np.roll(x0f[b], -2048 * H, axis=0))
        in_maps.append(m)

    if "nc" not in _NC_CACHE:
        _NC_CACHE["nc"] = build_program()
    nc = _NC_CACHE["nc"]
    res = run_bass_kernel_spmd(nc, in_maps, core_ids=list(range(8)))
    out = np.zeros((B, S, D), np.float32)
    for c in range(8):
        b, H = c // 2, c % 2
        out[b, 2048 * H: 2048 * (H + 1), :] = res.results[c]["y"]
    return out


# version 06
# speedup vs baseline: 1.0440x; 1.0440x over previous
import math
import numpy as np
import ml_dtypes
from contextlib import ExitStack

import concourse.bass as bass
import concourse.mybir as mybir
from concourse.bass_utils import run_bass_kernel_spmd

F32 = mybir.dt.float32
BF16 = mybir.dt.bfloat16
AF = mybir.ActivationFunctionType
ALU = mybir.AluOpType

B, S, D = 4, 4096, 2048
T = 2048
E, I = 8, 1024
NG = 4
SLOPE = 0.0625
SCALE = 1.0 / math.sqrt(128.0)
NEG = -1e30
LN_EPS = 1e-5
RMS_EPS = 1e-6

MLA_KTS = [list(range(4 * g + 4)) + list(range(16, 32)) for g in range(NG)]
FFN_KTS = [list(range(20 + 4 * g)) for g in range(NG)]
FFN_CAND = [[4 * g + s for s in range(4)] + [16 + 4 * g + s for s in range(4)]
            for g in range(NG)]
MLA_NC = sum(len(k) for k in MLA_KTS)
FFN_NC = sum(len(k) for k in FFN_KTS)
VAW = 16 * 528
KVW = 4 * 2048 + VAW

N_PHASES = 3
DBG_P2 = False
DBG_P3 = False
ENGINES = ("sync", "scalar", "vector", "tensor", "gpsimd")


class KB:
    def __init__(self, nc, stack):
        self.nc = nc
        self.stack = stack
        self.sems = {}
        self.dma_sems = set()
        self.q = None

    def sem(self, name):
        if name not in self.sems:
            h = self.stack.enter_context(self.nc.semaphore(name))
            self.sems[name] = [h, 0]
        return self.sems[name]

    def on(self, eng, fn):
        self.q[eng].append(fn)

    def wait(self, eng, name, val=None):
        s = self.sem(name)
        v = s[1] if val is None else val
        if v <= 0:
            return
        self.on(eng, lambda e, h=s[0], v=v: e.wait_ge(h, v))

    def inc(self, eng, fn, name, k=1):
        s = self.sem(name)
        s[1] += k
        if k == 16:
            self.dma_sems.add(name)
        self.on(eng, lambda e, fn=fn, h=s[0], k=k: fn(e).then_inc(h, k))
        return s[1]

    def do(self, eng, fn):
        self.on(eng, fn)

    def val(self, name):
        return self.sem(name)[1]

    def begin(self):
        self.q = {e: [] for e in ENGINES}

    def flush(self):
        for name in sorted(self.dma_sems):
            self.wait("sync", name)
        with self.nc.Block() as blk:
            for e in ENGINES:
                fns = self.q[e]
                if fns:
                    getattr(blk, e)(lambda eng, fns=fns: [f(eng) for f in fns])
        self.q = None


class Alloc:
    def __init__(self, nc):
        self.nc = nc
        self.groups = {}
        self.order = []

    def sb(self, group, name, shape, dt):
        cm = self.nc.sbuf_tensor(name, list(shape), dt)
        h = cm.__enter__()
        if group not in self.groups:
            self.groups[group] = []
            self.order.append(group)
        self.groups[group].append(cm)
        return h

    def free(self, group):
        assert self.order and self.order[-1] == group, (self.order, group)
        for cm in reversed(self.groups.pop(group)):
            cm.__exit__(None, None, None)
        self.order.pop()


def _stats(kb, xt, width, st, junk, eps, mean, b):
    sem = "sst%d" % b
    if mean:
        kb.inc("vector", lambda e: e.reduce_sum(st[:, 0:1], xt,
                                                axis=mybir.AxisListType.X), sem)
    kb.do("scalar", lambda e: e.activation(junk[:, 0:width], xt, AF.Square,
                                           accum_out=st[:, 2:3]))
    kb.do("scalar", lambda e: e.activation(st[:, 3:4], st[:, 2:3],
                                           AF.Identity, scale=1.0 / width))
    if mean:
        kb.wait("scalar", sem)
        kb.do("scalar", lambda e: e.activation(st[:, 1:2], st[:, 0:1],
                                               AF.Identity, scale=1.0 / width))
        kb.inc("scalar", lambda e: e.activation(st[:, 7:8], st[:, 1:2],
                                                AF.Square), sem)
        kb.wait("vector", sem)
        kb.inc("vector", lambda e: e.tensor_scalar(
            st[:, 3:4], st[:, 3:4], st[:, 7:8], eps,
            op0=ALU.subtract, op1=ALU.add), sem)
    else:
        kb.inc("scalar", lambda e: e.activation(st[:, 3:4], st[:, 3:4],
                                                AF.Identity), sem)
        kb.wait("vector", sem)
        kb.inc("vector", lambda e: e.tensor_scalar_add(st[:, 3:4], st[:, 3:4],
                                                       eps), sem)
    kb.wait("scalar", sem)
    kb.do("scalar", lambda e: e.activation(st[:, 4:5], st[:, 3:4], AF.Ln))
    kb.inc("scalar", lambda e: e.activation(st[:, 4:5], st[:, 4:5], AF.Exp,
                                            scale=-0.5), sem)
    if mean:
        kb.wait("vector", sem)
        kb.inc("vector", lambda e: e.tensor_tensor(st[:, 5:6], st[:, 1:2],
                                                   st[:, 4:5], op=ALU.mult),
               sem)
        kb.wait("scalar", sem)
        kb.inc("scalar", lambda e: e.activation(st[:, 6:7], st[:, 5:6],
                                                AF.Identity), sem)


def build_program(n_phases=N_PHASES):
    nc = bass.Bass("TRN2", target_bir_lowering=False,
                   detect_race_conditions=False)
    stack = ExitStack()
    kb = KB(nc, stack)
    al = Alloc(nc)

    def din(name, shape, dt=F32):
        return nc.dram_tensor(name, list(shape), dt, kind="ExternalInput").ap()

    x0 = din("x0", (S, D))
    w_dkv = din("w_dkv", (128, D))
    w_q = din("w_q", (D, D))
    w_uk = din("w_uk", (D, 128))
    w_uv = din("w_uv", (D, 128))
    w_o = din("w_o", (D, D))
    wq = din("wq", (D, D))
    wk = din("wk", (D, 512))
    wv = din("wv", (D, 512))
    wo = din("wo", (D, D))
    rw = din("router_w", (D, 8))
    wg_r = din("wg_r", (E, 8, 128, 16 * 128))
    wu_r = din("wu_r", (E, 8, 128, 16 * 128))
    wdp = din("wd", (E, I, D))
    identf = din("identf", (128, 128))
    tri01 = din("tri01", (128, 128), BF16)
    negu = din("negu", (128, 4 * 512), BF16)
    seli = din("sel", (128, 32))
    amla = din("amla", (128, MLA_NC))
    vffn = din("vffn", (128, FFN_NC))
    cossin = din("cossin", (128, 16 * 64))
    ln1b = din("ln1b", (128, D))
    ln2b = din("ln2b", (128, D))
    qnb = din("qnb", (128, D))
    knb = din("knb", (128, 512))
    rbb = din("rbb", (128, 8))

    y = nc.dram_tensor("y", [T, D], F32, kind="ExternalOutput").ap()

    wqk_d = nc.dram_tensor("wqk_d", [16, 128, 16 * 128], BF16).ap()
    x1_d = nc.dram_tensor("x1_d", [T, D], F32).ap()
    x2_d = nc.dram_tensor("x2_d", [T, D], F32).ap()
    kvin_d = nc.dram_tensor("kvin_d", [128, KVW], BF16).ap()
    kvout_d = nc.dram_tensor("kvout_d", [2, 128, KVW], BF16).ap()
    if DBG_P3:
        dbg_cb = nc.dram_tensor("dbg_cb", [128, 128], F32,
                                kind="ExternalOutput").ap()
        dbg_sg = nc.dram_tensor("dbg_sg", [128, 32], F32,
                                kind="ExternalOutput").ap()
        dbg_h2 = nc.dram_tensor("dbg_h2", [128, 16 * 1024], BF16,
                                kind="ExternalOutput").ap()
    if DBG_P2:
        dbg_kt = nc.dram_tensor("dbg_kt", [128, 4 * 4096], BF16,
                                kind="ExternalOutput").ap()
        dbg_va = nc.dram_tensor("dbg_va", [128, 32 * 528], BF16,
                                kind="ExternalOutput").ap()
        dbg_q2 = nc.dram_tensor("dbg_q2", [4, 128, 16 * 512], BF16,
                                kind="ExternalOutput").ap()
        dbg_o2 = nc.dram_tensor("dbg_o2", [4, 128, 16 * 512], BF16,
                                kind="ExternalOutput").ap()

    bsb = lambda n, sh, dt: al.sb("base", n, sh, dt)
    xb = [bsb("xb%d" % i, (128, D), F32) for i in range(2)]
    zb = [bsb("zb%d" % i, (128, D), BF16) for i in range(2)]
    stat = [bsb("st%d" % i, (128, 8), F32) for i in range(2)]
    junkb = bsb("junkb", (128, D), BF16)
    asb = lambda n, sh, dt: al.sb("attnsh", n, sh, dt)
    amla_s = asb("amla_s", (128, MLA_NC), F32)
    tri_s = asb("tri_s", (128, 128), BF16)
    Pb = [asb("Pb%d" % i, (128, 512), BF16) for i in range(3)]
    Pscr = asb("Pscr", (128, 512), BF16)
    rcol = asb("rcol", (128, 8), F32)
    onb = asb("onb", (128, 512), BF16)
    ropescr = asb("ropescr", (128, 2048), BF16)

    pbk = [stack.enter_context(nc.psum_tensor("pbk%d" % i, [128, 512], F32))
           for i in range(8)]

    kb.begin()
    wdkvT_s = al.sb("p1per", "wdkvT_s", (128, 16 * 128), BF16)
    w2_s = al.sb("p1per", "w2_s", (128, 16 * 2048), BF16)
    ckvT_s = al.sb("p1per", "ckvT_s", (128, 4096), BF16)
    ckvaug_s = al.sb("p1per", "ckvaug_s", (128, 32 * 144), BF16)
    lnT_g = al.sb("p1per", "lnT_g", (128, 16 * 512), BF16)
    qtg = al.sb("p1per", "qtg", (128, 16 * 512), BF16)
    wqks = [al.sb("p1per", "wqks%d" % i, (128, 4096), BF16) for i in range(2)]
    oTg = al.sb("p1per", "oTg", (128, 16 * 512), BF16)
    x1t = [al.sb("p1per", "x1t0", (128, D), F32)] * 2

    wuv_s = al.sb("p0tmp", "p0_wuv", (128, 16 * 128), BF16)
    wuk_s = wuv_s
    evtile = al.sb("p0tmp", "p0_ev", (128, 128), BF16)
    t128 = [al.sb("p0tmp", "p0_t%d" % i, (128, 128), BF16) for i in range(2)]
    wslab = [al.sb("p0tmp", "p0_ws%d" % i, (128, 2048), BF16) for i in range(2)]
    woth = wslab[0]

    for s_, d_ in ((amla, amla_s), (tri01, tri_s)):
        kb.inc("sync", lambda e, s=s_, d=d_: e.dma_start(out=d[:], in_=s),
               "p0m", 16)
    kb.inc("vector", lambda e: e.memset(ckvaug_s[:], 2.0 ** -24), "p0aug", 1)

    for h in range(16):
        kb.inc("gpsimd", lambda e, h=h: e.dma_start(
            out=wuv_s[:, h * 128:(h + 1) * 128],
            in_=w_uv[h * 128:(h + 1) * 128, :]), "p0g", 16)

    kb.wait("tensor", "p0g")
    for h in range(16):
        for ci in range(16):
            sl_ = ci % 2
            tb = t128[sl_]
            kb.wait("gpsimd", "p0wot%d" % sl_, kb.val("p0wot%d" % sl_))
            kb.inc("gpsimd", lambda e, h=h, ci=ci, tb=tb: e.dma_start(
                out=tb[:], in_=w_o[ci * 128:(ci + 1) * 128,
                                   h * 128:(h + 1) * 128]), "p0wol%d" % sl_, 16)
            kb.wait("sync", "p0wol%d" % sl_)
            kb.wait("sync", "p0w2m", h * 4)
            kb.inc("sync", lambda e, ci=ci, tb=tb: e.dma_start_transpose(
                woth[:, ci * 128:(ci + 1) * 128], tb[:]), "p0wot%d" % sl_, 16)
        kb.wait("tensor", "p0wot0")
        kb.wait("tensor", "p0wot1")
        for db in range(4):
            bank = pbk[db % 2]
            kb.wait("tensor", "p0w2e", kb.val("p0w2e") - 1)
            kb.inc("tensor", lambda e, h=h, db=db, bank=bank: e.matmul(
                bank[:], wuv_s[:, h * 128:(h + 1) * 128],
                woth[:, db * 512:(db + 1) * 512],
                start=True, stop=True), "p0w2m")
            kb.wait("vector", "p0w2m")
            kb.inc("vector", lambda e, h=h, db=db, bank=bank: e.tensor_copy(
                w2_s[:, h * 2048 + db * 512: h * 2048 + (db + 1) * 512], bank[:]),
                "p0w2e")

    kb.wait("gpsimd", "p0w2m", kb.val("p0w2m"))
    for h in range(16):
        kb.inc("gpsimd", lambda e, h=h: e.dma_start(
            out=wuk_s[:, h * 128:(h + 1) * 128],
            in_=w_uk[h * 128:(h + 1) * 128, :]), "p0g", 16)
    for h in range(16):
        sl_ = h % 2
        sl = wslab[sl_]
        kb.wait("gpsimd", "p0qkm", (h - 1) * 16)
        kb.inc("gpsimd", lambda e, h=h, sl=sl: e.dma_start(
            out=sl[:], in_=w_q[h * 128:(h + 1) * 128, :]), "p0wq%d" % sl_, 16)
        kb.wait("tensor", "p0wq%d" % sl_)
        kb.wait("tensor", "p0g")
        for dch in range(16):
            bank = pbk[2 + dch % 2]
            kb.wait("tensor", "p0qke", kb.val("p0qke") - 1)
            kb.inc("tensor", lambda e, h=h, dch=dch, sl=sl, bank=bank: e.matmul(
                bank[:, 0:128], sl[:, dch * 128:(dch + 1) * 128],
                wuk_s[:, h * 128:(h + 1) * 128], start=True, stop=True), "p0qkm")
            kb.wait("vector", "p0qkm")
            kb.wait("vector", "p0qks", kb.val("p0qks"))
            kb.inc("vector", lambda e, bank=bank: e.tensor_copy(
                evtile[:], bank[:, 0:128]), "p0qke")
            kb.wait("sync", "p0qke")
            kb.inc("sync", lambda e, h=h, dch=dch: e.dma_start(
                out=wqk_d[h, :, dch * 128:(dch + 1) * 128], in_=evtile[:]),
                "p0qks", 16)

    kb.wait("gpsimd", "p0qkm", kb.val("p0qkm"))
    kb.inc("gpsimd", lambda e: e.dma_start(out=wslab[0][:], in_=w_dkv[:, :]),
           "p0g", 16)
    kb.wait("sync", "p0g")
    for dch in range(16):
        kb.inc("sync", lambda e, dch=dch: e.dma_start_transpose(
            wdkvT_s[:, dch * 128:(dch + 1) * 128],
            wslab[0][:, dch * 128:(dch + 1) * 128]), "p0m", 16)
    al.free("p0tmp")

    state = {"n": 0}

    def ln_tile(t, tsub):
        n = state["n"]
        state["n"] += 1
        b = n % 2
        kb.wait("sync", "paz", n - 1)
        kb.wait("sync", "patp", (n - 1) * 256)
        kb.inc("sync", lambda e, t=t, b=b: e.dma_start(
            out=xb[b][:], in_=x0[t * 128:(t + 1) * 128, :]), "pald", 16)
        kb.wait("vector", "pald")
        kb.wait("scalar", "pald")
        _stats(kb, xb[b][:], D, stat[b], junkb, LN_EPS, True, b)
        kb.wait("vector", "sst%d" % b)
        kb.inc("vector", lambda e, b=b: e.tensor_scalar(
            zb[b][:], xb[b][:], stat[b][:, 1:2], stat[b][:, 4:5],
            op0=ALU.subtract, op1=ALU.mult), "paz")
        kb.wait("sync", "paz")
        for dch in range(16):
            kb.inc("sync", lambda e, b=b, dch=dch, tsub=tsub: e.dma_start_transpose(
                lnT_g[:, dch * 512 + tsub * 128: dch * 512 + (tsub + 1) * 128],
                zb[b][:, dch * 128:(dch + 1) * 128]), "patp", 16)

    kb.wait("tensor", "p0m")
    kb.wait("vector", "p0m")
    kb.wait("scalar", "p0m")
    for ga in range(8):
        kb.wait("sync", "pack", kb.val("pack"))
        for ts4 in range(4):
            ln_tile(ga * 4 + ts4, ts4)
        kb.wait("tensor", "patp", state["n"] * 256)
        bank = pbk[ga % 2]
        kb.wait("tensor", "packe", kb.val("packe") - 1)
        for dch in range(16):
            fn = lambda e, dch=dch, bank=bank: e.matmul(
                bank[:], wdkvT_s[:, dch * 128:(dch + 1) * 128],
                lnT_g[:, dch * 512:(dch + 1) * 512],
                start=(dch == 0), stop=(dch == 15))
            kb.inc("tensor", fn, "pack") if dch == 15 else kb.do("tensor", fn)
        kb.wait("vector", "pack")
        kb.inc("vector", lambda e, ga=ga, bank=bank: e.tensor_copy(
            ckvT_s[:, ga * 512:(ga + 1) * 512], bank[:]), "packe")
        kb.wait("sync", "packe")
        kb.wait("sync", "p0aug", 1)
        for ks in range(4):
            kt = ga * 4 + ks
            kb.inc("sync", lambda e, ga=ga, ks=ks, kt=kt: e.dma_start_transpose(
                ckvaug_s[:, kt * 144: kt * 144 + 128],
                ckvT_s[:, ga * 512 + ks * 128: ga * 512 + (ks + 1) * 128]),
                "paaug", 16)
    kb.wait("tensor", "paaug")
    kb.wait("vector", "paaug")
    kb.wait("sync", "pack", kb.val("pack"))

    kb.wait("gpsimd", "p0qks")
    kb.wait("tensor", "p0w2e")
    for g in range(NG):
        kb.wait("sync", "pbqm", kb.val("pbqm"))
        kb.wait("sync", "p1xe", kb.val("p1xe"))
        for ts4 in range(4):
            ln_tile(g * 4 + ts4, ts4)
        kb.wait("tensor", "patp", state["n"] * 256)
        kb.wait("tensor", "p1xe", kb.val("p1xe"))
        for cp in range(8):
            sl_ = cp % 2
            sl = wqks[sl_]
            kb.wait("gpsimd", "pbqm", kb.val("pbqm") - 2)
            kb.inc("gpsimd", lambda e, cp=cp, sl=sl: e.dma_start(
                out=bass.AP(sl, 0, [[4096, 128], [2048, 2], [1, 2048]]),
                in_=wqk_d[2 * cp:2 * cp + 2, :, :].rearrange(
                    "two p c -> p two c")), "pbqk%d" % sl_, 16)
            kb.wait("tensor", "pbqk%d" % sl_)
            for half in range(2):
                cch = 2 * cp + half
                bank = pbk[2 + cch % 2]
                kb.wait("tensor", "pbqe", kb.val("pbqe") - 1)
                for dch in range(16):
                    fn = lambda e, dch=dch, sl=sl, half=half, bank=bank: \
                        e.matmul(
                        bank[:],
                        sl[:, half * 2048 + dch * 128: half * 2048 + (dch + 1) * 128],
                        lnT_g[:, dch * 512:(dch + 1) * 512],
                        start=(dch == 0), stop=(dch == 15))
                    kb.inc("tensor", fn, "pbqm") if dch == 15 \
                        else kb.do("tensor", fn)
                kb.wait("vector", "pbqm")
                kb.wait("vector", "p1sm", kb.val("p1sm"))
                kb.inc("vector", lambda e, cch=cch, bank=bank: e.tensor_copy(
                    qtg[:, cch * 512:(cch + 1) * 512], bank[:]), "pbqe")
        kb.wait("tensor", "pbqe")
        kb.wait("sync", "p1s2m", kb.val("p1s2m"))
        kb.wait("tensor", "p1xe", kb.val("p1xe"))

        kts = MLA_KTS[g]
        nit = len(kts)
        colbase = sum(len(MLA_KTS[gg]) for gg in range(g))
        for h in range(16):
            expv = {}
            maskv = {}
            pend = [None]

            def emit_pv(j, kts=kts, nit=nit, expv=expv, maskv=maskv):
                if j in maskv:
                    kb.wait("tensor", "p1mk", maskv[j])
                else:
                    kb.wait("tensor", "p1ex", expv[j])
                if j == 0:
                    kb.wait("tensor", "p1on", kb.val("p1on"))
                kt = kts[j]
                for sub in range(4):
                    fn = lambda e, j=j, sub=sub, kt=kt, nit=nit: e.matmul(
                        pbk[2 + sub][:, 0:129],
                        Pb[j % 3][:, sub * 128:(sub + 1) * 128],
                        ckvaug_s[:, kt * 144: kt * 144 + 129],
                        start=(j == 0), stop=(j == nit - 1))
                    kb.inc("tensor", fn, "p1pv") if sub == 3 else kb.do("tensor", fn)

            for it, kt in enumerate(kts):
                sbk = pbk[it % 2]
                if it >= 2:
                    kb.wait("tensor", "p1ex", expv[it - 2])
                kb.inc("tensor", lambda e, kt=kt, h=h, sbk=sbk: e.matmul(
                    sbk[:], ckvT_s[:, kt * 128:(kt + 1) * 128],
                    qtg[:, h * 512:(h + 1) * 512], start=True, stop=True),
                    "p1sm")
                pbuf = Pb[it % 3]
                kb.wait("scalar", "p1sm")
                kb.wait("scalar", "p1pv", kb.val("p1pv"))
                col = colbase + it
                expv[it] = kb.inc("scalar", lambda e, sbk=sbk, pbuf=pbuf,
                                  col=col: e.activation(
                    pbuf[:], sbk[:], AF.Exp, bias=amla_s[:, col:col + 1],
                    scale=SCALE), "p1ex")
                if 4 * g <= kt < 4 * g + 4:
                    s_ = kt - 4 * g
                    kb.wait("vector", "p1ex", expv[it])
                    if s_ > 0:
                        kb.do("vector", lambda e, pbuf=pbuf, s_=s_: e.memset(
                            pbuf[:, 0:s_ * 128], 0.0))
                    maskv[it] = kb.inc("vector", lambda e, pbuf=pbuf, s_=s_:
                                       e.tensor_tensor(
                        pbuf[:, s_ * 128:(s_ + 1) * 128],
                        pbuf[:, s_ * 128:(s_ + 1) * 128], tri_s[:],
                        op=ALU.mult), "p1mk")
                pend.append(it)
                if len(pend) > 3:
                    emit_pv(pend.pop(1))
            for j_ in pend[1:]:
                emit_pv(j_)
            kb.wait("vector", "p1pv")
            for sub in range(3):
                kb.do("vector", lambda e, sub=sub: e.reciprocal(
                    rcol[:, sub:sub + 1], pbk[2 + sub][:, 128:129]))
            kb.inc("vector", lambda e: e.reciprocal(
                rcol[:, 3:4], pbk[5][:, 128:129]), "p1rc")
            kb.wait("scalar", "p1rc")
            kb.inc("scalar", lambda e: e.activation(rcol[:, 4:8], rcol[:, 0:4],
                                                    AF.Identity), "p1lnc")
            kb.wait("vector", "p1lnc")
            kb.wait("vector", "p1oT", kb.val("p1oT") - 64)
            for sub in range(4):
                kb.inc("vector", lambda e, sub=sub: e.tensor_scalar(
                    onb[:, sub * 128:(sub + 1) * 128], pbk[2 + sub][:, 0:128],
                    rcol[:, 4 + sub:5 + sub], 2.0 ** -24,
                    op0=ALU.mult, op1=ALU.mult), "p1on")
            kb.wait("sync", "p1on")
            for sub in range(4):
                kb.inc("sync", lambda e, h=h, sub=sub: e.dma_start_transpose(
                    oTg[:, h * 512 + sub * 128: h * 512 + (sub + 1) * 128],
                    onb[:, sub * 128:(sub + 1) * 128]), "p1oT", 16)
        kb.wait("tensor", "p1oT")
        for tsub in range(4):
            t = g * 4 + tsub
            bx = tsub % 2
            kb.wait("sync", "p1xe", kb.val("p1xe") - 8)
            kb.inc("sync", lambda e, t=t, bx=bx: e.dma_start(
                out=xb[bx][:], in_=x0[t * 128:(t + 1) * 128, :]), "p1xr", 16)
            for db in range(4):
                bank = pbk[2 + db]
                kb.wait("tensor", "p1xe", kb.val("p1xe"))
                for h in range(16):
                    fn = lambda e, h=h, db=db, tsub=tsub, bank=bank: e.matmul(
                        bank[:],
                        oTg[:, h * 512 + tsub * 128: h * 512 + (tsub + 1) * 128],
                        w2_s[:, h * 2048 + db * 512: h * 2048 + (db + 1) * 512],
                        start=(h == 0), stop=(h == 15))
                    kb.inc("tensor", fn, "p1s2m") if h == 15 else kb.do("tensor", fn)
            kb.wait("vector", "p1s2m")
            kb.wait("vector", "p1xr")
            bo = 0
            kb.wait("vector", "p1xs", kb.val("p1xs"))
            for db in range(4):
                kb.inc("vector", lambda e, db=db, bo=bo, bx=bx: e.tensor_tensor(
                    x1t[bo][:, db * 512:(db + 1) * 512], pbk[2 + db][:],
                    xb[bx][:, db * 512:(db + 1) * 512], op=ALU.add), "p1xe")
            kb.wait("sync", "p1xe")
            kb.inc("sync", lambda e, t=t, bo=bo: e.dma_start(
                out=x1_d[t * 128:(t + 1) * 128, :], in_=x1t[bo][:]), "p1xs", 16)
    kb.flush()
    al.free("p1per")

    if n_phases == 1:
        al.free("attnsh")
        _copy_out(kb, al, x1_d, y)
        stack.close()
        return nc

    kb.begin()
    negu_s = al.sb("p2tb", "negu_s", (128, 4 * 512), BF16)
    sel_s = al.sb("p2tb", "sel_s", (128, 32), F32)
    vffn_s = al.sb("p2tb", "vffn_s", (128, FFN_NC), F32)
    cs_s = al.sb("p2tb", "cs_s", (128, 16 * 64), F32)
    ln1b_s = al.sb("p2tb", "ln1b_s", (128, D), BF16)
    qnb_s = al.sb("p2tb", "qnb_s", (128, D), BF16)
    knb_s = al.sb("p2tb", "knb_s", (128, 512), BF16)
    for s_, d_ in ((negu, negu_s), (seli, sel_s), (vffn, vffn_s),
                   (cossin, cs_s)):
        kb.inc("sync", lambda e, s=s_, d=d_: e.dma_start(out=d[:], in_=s),
               "p2tb", 16)
    for s_, d_ in ((ln1b, ln1b_s), (qnb, qnb_s), (knb, knb_s)):
        kb.inc("gpsimd", lambda e, s=s_, d=d_: e.dma_start(out=d[:], in_=s),
               "p2tg", 16)
    wkv_s = al.sb("p2a", "wkv_s", (128, 2 * 16 * 512), BF16)
    hTt = al.sb("p2a", "hTt", (128, 16 * 128), BF16)
    ktf = al.sb("p2a", "ktf", (128, 512), F32)
    krb = al.sb("p2a", "krb", (128, 512), BF16)
    kTown = al.sb("p2a", "kTown", (128, 4 * 2048), BF16)
    vaugo = al.sb("p2a", "vaugo", (128, VAW), BF16)

    for ci in range(16):
        kb.inc("gpsimd", lambda e, ci=ci: e.dma_start(
            out=wkv_s[:, ci * 512:(ci + 1) * 512],
            in_=wk[ci * 128:(ci + 1) * 128, :]), "p2tg", 16)
        kb.inc("gpsimd", lambda e, ci=ci: e.dma_start(
            out=wkv_s[:, 8192 + ci * 512: 8192 + (ci + 1) * 512],
            in_=wv[ci * 128:(ci + 1) * 128, :]), "p2tg", 16)
    kb.inc("vector", lambda e: e.memset(vaugo[:], 1.0), "p2vone", 1)

    def h_tile(tag, t, b):
        kb.inc("sync", lambda e, t=t, b=b: e.dma_start(
            out=xb[b][:], in_=x1_d[t * 128:(t + 1) * 128, :]), tag + "ld", 16)
        kb.wait("vector", tag + "ld")
        kb.wait("scalar", tag + "ld")
        _stats(kb, xb[b][:], D, stat[b], junkb, LN_EPS, True, b)
        kb.wait("vector", "sst%d" % b)
        kb.inc("vector", lambda e, b=b: e.tensor_scalar(
            zb[b][:], xb[b][:], stat[b][:, 1:2], stat[b][:, 4:5],
            op0=ALU.subtract, op1=ALU.mult), tag + "z")
        kb.wait("scalar", tag + "z")
        _stats(kb, zb[b][:], D, stat[b], junkb, RMS_EPS, False, b)
        kb.wait("vector", "sst%d" % b)
        kb.do("vector", lambda e, b=b: e.tensor_scalar(
            zb[b][:], zb[b][:], stat[b][:, 4:5], None, op0=ALU.mult))
        kb.inc("vector", lambda e, b=b: e.tensor_tensor(
            zb[b][:], zb[b][:], ln1b_s[:], op=ALU.mult), tag + "h")

    def rope_emit(src, srcw, soff, dst, dstw, doff, nh, tcol, tag):
        cosap = bass.AP(cs_s, tcol * 64, [[1024, 128], [0, nh], [1, 32]])
        sinap = bass.AP(cs_s, tcol * 64 + 32, [[1024, 128], [0, nh], [1, 32]])
        sc = [bass.AP(ropescr, 32 * nh * i, [[2048, 128], [32, nh], [1, 32]])
              for i in range(4)]
        q1 = bass.AP(src, soff, [[srcw, 128], [128, nh], [1, 32]])
        q2 = bass.AP(src, soff + 32, [[srcw, 128], [128, nh], [1, 32]])
        d1 = bass.AP(dst, doff, [[dstw, 128], [128, nh], [1, 32]])
        d2 = bass.AP(dst, doff + 32, [[dstw, 128], [128, nh], [1, 32]])
        kb.do("vector", lambda e: e.tensor_tensor(sc[0], q1, cosap, op=ALU.mult))
        kb.do("vector", lambda e: e.tensor_tensor(sc[1], q2, sinap, op=ALU.mult))
        kb.do("vector", lambda e: e.tensor_tensor(sc[2], q2, cosap, op=ALU.mult))
        kb.do("vector", lambda e: e.tensor_tensor(sc[3], q1, sinap, op=ALU.mult))
        kb.do("vector", lambda e: e.tensor_tensor(d1, sc[0], sc[1],
                                                  op=ALU.subtract))
        kb.do("vector", lambda e: e.tensor_tensor(d2, sc[2], sc[3], op=ALU.add))
        if src is not dst or soff != doff:
            pasrc = bass.AP(src, soff + 64, [[srcw, 128], [128, nh], [1, 64]])
            padst = bass.AP(dst, doff + 64, [[dstw, 128], [128, nh], [1, 64]])
            kb.do("vector", lambda e: e.tensor_copy(padst, pasrc))
        kb.inc("vector", lambda e: e.tensor_copy(
            bass.AP(dst, doff, [[dstw, 128], [1, 1]]),
            bass.AP(dst, doff, [[dstw, 128], [1, 1]])), tag)

    kb.wait("vector", "p2tg")
    kb.wait("vector", "p2tb")
    for t in range(16):
        b = t % 2
        kb.wait("sync", "p2ahT", (t - 1) * 256)
        kb.wait("sync", "p2az", t - 1)
        h_tile("p2a", t, b)
        kb.wait("sync", "p2ah")
        kb.wait("sync", "p2akm", kb.val("p2akm"))
        kb.wait("sync", "p2avm", kb.val("p2avm"))
        for dch in range(16):
            kb.inc("sync", lambda e, b=b, dch=dch: e.dma_start_transpose(
                hTt[:, dch * 128:(dch + 1) * 128],
                zb[b][:, dch * 128:(dch + 1) * 128]), "p2ahT", 16)
        kb.wait("tensor", "p2ahT")
        kb.wait("tensor", "p2tg")
        kb.wait("tensor", "p2ake", kb.val("p2ake") - 1)
        for dch in range(16):
            fn = lambda e, dch=dch: e.matmul(
                pbk[0][:], hTt[:, dch * 128:(dch + 1) * 128],
                wkv_s[:, dch * 512:(dch + 1) * 512],
                start=(dch == 0), stop=(dch == 15))
            kb.inc("tensor", fn, "p2akm") if dch == 15 else kb.do("tensor", fn)
        kb.wait("tensor", "p2ave", kb.val("p2ave") - 1)
        for dch in range(16):
            fn = lambda e, dch=dch: e.matmul(
                pbk[1][:], hTt[:, dch * 128:(dch + 1) * 128],
                wkv_s[:, 8192 + dch * 512: 8192 + (dch + 1) * 512],
                start=(dch == 0), stop=(dch == 15))
            kb.inc("tensor", fn, "p2avm") if dch == 15 else kb.do("tensor", fn)
        kb.wait("vector", "p2akm")
        kb.wait("vector", "p2akT", t * 64)
        kb.inc("vector", lambda e: e.tensor_copy(ktf[:], pbk[0][:]), "p2ake")
        kb.wait("scalar", "p2ake")
        _stats(kb, ktf[:], 512, stat[b], junkb, RMS_EPS, False, b)
        kb.wait("vector", "sst%d" % b)
        kb.do("vector", lambda e, b=b: e.tensor_scalar(
            ktf[:], ktf[:], stat[b][:, 4:5], None, op0=ALU.mult))
        kb.inc("vector", lambda e: e.tensor_tensor(
            ktf[:], ktf[:], knb_s[:], op=ALU.mult), "p2akn")
        rope_emit(ktf, 512, 0, krb, 512, 0, 4, t, "p2akr")
        kb.wait("sync", "p2akr")
        for kvh in range(4):
            kb.inc("sync", lambda e, kvh=kvh, t=t: e.dma_start_transpose(
                kTown[:, kvh * 2048 + t * 128: kvh * 2048 + (t + 1) * 128],
                krb[:, kvh * 128:(kvh + 1) * 128]), "p2akT", 16)
        kb.wait("vector", "p2avm")
        kb.wait("vector", "p2vone", 1)
        vdst = bass.AP(vaugo, t * 528, [[VAW, 128], [132, 4], [1, 128]])
        vsrc = bass.AP(pbk[1], 0, [[512, 128], [128, 4], [1, 128]])
        kb.inc("vector", lambda e, vdst=vdst, vsrc=vsrc: e.tensor_copy(
            vdst, vsrc), "p2ave")
    kb.wait("sync", "p2akT")
    kb.wait("sync", "p2ave")
    kb.inc("sync", lambda e: e.dma_start(out=kvin_d[:, 0:8192], in_=kTown[:]),
           "p2kvw", 16)
    kb.inc("sync", lambda e: e.dma_start(out=kvin_d[:, 8192:KVW], in_=vaugo[:]),
           "p2kvw", 16)
    kb.wait("gpsimd", "p2kvw")
    kb.inc("gpsimd", lambda e: e.collective_compute(
        "AllGather", ALU.bypass,
        replica_groups=[[0, 1], [2, 3], [4, 5], [6, 7]],
        ins=[kvin_d.opt()], outs=[kvout_d.opt()]), "p2cc", 1)
    kb.flush()
    al.free("p2a")

    kb.begin()
    kT = al.sb("p2b", "kT", (128, 4 * 4096), BF16)
    vaug = al.sb("p2b", "vaug", (128, 32 * 528), BF16)
    hTg2 = al.sb("p2b", "hTg2", (128, 16 * 512), BF16)
    o2Tg = hTg2
    qf4 = al.sb("p2b", "qf4", (128, 4 * 2048), BF16)
    q2Tg = al.sb("p2b", "q2Tg", (128, 16 * 512), BF16)
    wqo2 = [al.sb("p2b", "wqo%d" % i, (128, 16 * 512), BF16) for i in range(2)]
    x2s = al.sb("p2b", "x2s", (128, 1024), F32)
    s2st = al.sb("p2b", "s2st", (128, 8), F32)

    kb.wait("sync", "p2cc", 1)
    for r in range(2):
        for kvh in range(4):
            kb.inc("sync", lambda e, r=r, kvh=kvh: e.dma_start(
                out=kT[:, kvh * 4096 + r * 2048: kvh * 4096 + (r + 1) * 2048],
                in_=kvout_d[r, :, kvh * 2048:(kvh + 1) * 2048]), "p2kvr", 16)
        kb.inc("sync", lambda e, r=r: e.dma_start(
            out=vaug[:, r * VAW:(r + 1) * VAW],
            in_=kvout_d[r, :, 8192:KVW]), "p2kvr", 16)
    kb.wait("tensor", "p2kvr")
    if DBG_P2:
        kb.wait("sync", "p2kvr")
        kb.inc("sync", lambda e: e.dma_start(out=dbg_kt, in_=kT[:]), "dbg", 16)
        kb.inc("sync", lambda e: e.dma_start(out=dbg_va, in_=vaug[:]), "dbg", 16)

    st2 = {"n": 0}
    for g in range(NG):
        kb.wait("sync", "p2qm", kb.val("p2qm"))
        kb.wait("sync", "p2s2m", kb.val("p2s2m"))
        kb.wait("sync", "p2xe", kb.val("p2xe"))
        for ts4 in range(4):
            n2 = st2["n"]
            st2["n"] += 1
            t = g * 4 + ts4
            b = n2 % 2
            kb.wait("sync", "p2bhT", (n2 - 1) * 256)
            kb.wait("sync", "p2bz", n2 - 1)
            h_tile("p2b", t, b)
            kb.wait("sync", "p2bh")
            for dch in range(16):
                kb.inc("sync", lambda e, b=b, dch=dch, ts4=ts4: e.dma_start_transpose(
                    hTg2[:, dch * 512 + ts4 * 128: dch * 512 + (ts4 + 1) * 128],
                    zb[b][:, dch * 128:(dch + 1) * 128]), "p2bhT", 16)
        kb.wait("tensor", "p2bhT")
        kb.wait("vector", "p2qT", kb.val("p2qT"))
        for fb in range(4):
            wsl = wqo2[fb % 2]
            sname = "p2wq%d" % (fb % 2)
            kb.wait("gpsimd", "p2qm", kb.val("p2qm") - 4)
            kb.wait("gpsimd", "p2s2m", kb.val("p2s2m"))
            for dch in range(16):
                kb.inc("gpsimd", lambda e, dch=dch, fb=fb, wsl=wsl: e.dma_start(
                    out=wsl[:, dch * 512:(dch + 1) * 512],
                    in_=wq[dch * 128:(dch + 1) * 128, fb * 512:(fb + 1) * 512]),
                    sname, 16)
            kb.wait("tensor", sname)
            for ts4 in range(4):
                bank = pbk[ts4 % 2]
                kb.wait("tensor", "p2qe", kb.val("p2qe") - 1)
                for dch in range(16):
                    fn = lambda e, dch=dch, ts4=ts4, bank=bank, wsl=wsl: \
                        e.matmul(
                        bank[:],
                        hTg2[:, dch * 512 + ts4 * 128: dch * 512 + (ts4 + 1) * 128],
                        wsl[:, dch * 512:(dch + 1) * 512],
                        start=(dch == 0), stop=(dch == 15))
                    kb.inc("tensor", fn, "p2qm") if dch == 15 else kb.do("tensor", fn)
                kb.wait("vector", "p2qm")
                kb.inc("vector", lambda e, ts4=ts4, fb=fb, bank=bank: e.tensor_copy(
                    qf4[:, ts4 * 2048 + fb * 512: ts4 * 2048 + (fb + 1) * 512],
                    bank[:]), "p2qe")
        kb.wait("sync", "p1sm", kb.val("p1sm"))
        for ts4 in range(4):
            b = ts4 % 2
            t = g * 4 + ts4
            qsl = qf4[:, ts4 * 2048:(ts4 + 1) * 2048]
            kb.wait("vector", "p2qe", kb.val("p2qe"))
            kb.wait("scalar", "p2qe", kb.val("p2qe"))
            _stats(kb, qsl, D, stat[b], junkb, RMS_EPS, False, b)
            kb.wait("vector", "sst%d" % b)
            kb.do("vector", lambda e, qsl=qsl, b=b: e.tensor_scalar(
                qsl, qsl, stat[b][:, 4:5], None, op0=ALU.mult))
            kb.inc("vector", lambda e, qsl=qsl: e.tensor_tensor(
                qsl, qsl, qnb_s[:], op=ALU.mult), "p2qn")
            rope_emit(qf4, 4 * 2048, ts4 * 2048, qf4, 4 * 2048, ts4 * 2048,
                      16, t, "p2qr")
            kb.wait("sync", "p2qr")
            for hh in range(16):
                kb.inc("sync", lambda e, hh=hh, ts4=ts4: e.dma_start_transpose(
                    q2Tg[:, hh * 512 + ts4 * 128: hh * 512 + (ts4 + 1) * 128],
                    qf4[:, ts4 * 2048 + hh * 128: ts4 * 2048 + (hh + 1) * 128]),
                    "p2qT", 16)
        kb.wait("tensor", "p2qT")
        if DBG_P2:
            kb.wait("sync", "p2qT")
            kb.inc("sync", lambda e, g=g: e.dma_start(
                out=dbg_q2[g], in_=q2Tg[:]), "dbg", 16)
        kb.wait("tensor", "p2qe")
        kb.wait("sync", "p2s2m", kb.val("p2s2m"))
        kb.wait("tensor", "p2xe", kb.val("p2xe"))

        kts = FFN_KTS[g]
        nit = len(kts)
        colbase = sum(len(FFN_KTS[gg]) for gg in range(g))
        cand = FFN_CAND[g]
        kb.wait("sync", "p2qm", kb.val("p2qm"))
        for h in range(16):
            kvh = h // 4
            expv = {}
            maskv = {}
            pend = [None]

            def emit_pv2(j2_, kts=kts, nit=nit, expv=expv, maskv=maskv,
                         kvh=kvh):
                if j2_ in maskv:
                    kb.wait("tensor", "p1mk", maskv[j2_])
                else:
                    kb.wait("tensor", "p1ex", expv[j2_])
                if j2_ == 0:
                    kb.wait("tensor", "p1on", kb.val("p1on"))
                kt = kts[j2_]
                for sub in range(4):
                    fn = lambda e, j2_=j2_, sub=sub, kt=kt, nit=nit, kvh=kvh: \
                        e.matmul(
                        pbk[2 + sub][:, 0:129],
                        Pb[j2_ % 3][:, sub * 128:(sub + 1) * 128],
                        vaug[:, kt * 528 + kvh * 132: kt * 528 + kvh * 132 + 129],
                        start=(j2_ == 0), stop=(j2_ == nit - 1))
                    kb.inc("tensor", fn, "p1pv") if sub == 3 else kb.do("tensor", fn)

            for it, kt in enumerate(kts):
                sbk = pbk[it % 2]
                if it >= 2:
                    kb.wait("tensor", "p1ex", expv[it - 2])
                kb.inc("tensor", lambda e, kt=kt, h=h, kvh=kvh, sbk=sbk:
                       e.matmul(
                    sbk[:],
                    kT[:, kvh * 4096 + kt * 128: kvh * 4096 + (kt + 1) * 128],
                    q2Tg[:, h * 512:(h + 1) * 512], start=True, stop=True),
                    "p1sm")
                pbuf = Pb[it % 3]
                kb.wait("scalar", "p1sm")
                kb.wait("scalar", "p1pv", kb.val("p1pv"))
                col = colbase + it
                expv[it] = kb.inc("scalar", lambda e, sbk=sbk, pbuf=pbuf,
                                  col=col: e.activation(
                    pbuf[:], sbk[:], AF.Exp, bias=vffn_s[:, col:col + 1],
                    scale=SCALE), "p1ex")
                if kt in cand:
                    jj = cand.index(kt)
                    s_ = jj % 4
                    slot = g * 8 + jj
                    kb.wait("vector", "p1ex", expv[it])
                    kb.do("vector", lambda e, pbuf=pbuf, s_=s_: e.tensor_tensor(
                        Pscr[:], pbuf[:], negu_s[:, s_ * 512:(s_ + 1) * 512],
                        op=ALU.mult))
                    maskv[it] = kb.inc("vector", lambda e, pbuf=pbuf, slot=slot:
                                       e.scalar_tensor_tensor(
                        pbuf[:], Pscr[:], sel_s[:, slot:slot + 1], pbuf[:],
                        op0=ALU.mult, op1=ALU.add), "p1mk")
                pend.append(it)
                if len(pend) > 3:
                    emit_pv2(pend.pop(1))
            for j_ in pend[1:]:
                emit_pv2(j_)
            kb.wait("vector", "p1pv")
            for sub in range(3):
                kb.do("vector", lambda e, sub=sub: e.reciprocal(
                    rcol[:, sub:sub + 1], pbk[2 + sub][:, 128:129]))
            kb.inc("vector", lambda e: e.reciprocal(
                rcol[:, 3:4], pbk[5][:, 128:129]), "p1rc")
            kb.wait("scalar", "p1rc")
            kb.inc("scalar", lambda e: e.activation(rcol[:, 4:8], rcol[:, 0:4],
                                                    AF.Identity), "p1lnc")
            kb.wait("vector", "p1lnc")
            kb.wait("vector", "p1oT", kb.val("p1oT") - 64)
            for sub in range(4):
                kb.inc("vector", lambda e, sub=sub: e.tensor_scalar(
                    onb[:, sub * 128:(sub + 1) * 128], pbk[2 + sub][:, 0:128],
                    rcol[:, 4 + sub:5 + sub], None, op0=ALU.mult), "p1on")
            kb.wait("sync", "p1on")
            for sub in range(4):
                kb.inc("sync", lambda e, h=h, sub=sub: e.dma_start_transpose(
                    o2Tg[:, h * 512 + sub * 128: h * 512 + (sub + 1) * 128],
                    onb[:, sub * 128:(sub + 1) * 128]), "p1oT", 16)
        kb.wait("tensor", "p1oT")
        if DBG_P2:
            kb.wait("sync", "p1oT")
            kb.inc("sync", lambda e, g=g: e.dma_start(
                out=dbg_o2[g], in_=o2Tg[:]), "dbg", 16)
            kb.wait("tensor", "dbg")
        for tsub in range(4):
            t = g * 4 + tsub
            bx = tsub % 2
            kb.wait("sync", "p2st", kb.val("p2st"))
            kb.inc("sync", lambda e, t=t, bx=bx: e.dma_start(
                out=xb[bx][:], in_=x1_d[t * 128:(t + 1) * 128, :]), "p2xr", 16)
            kb.wait("vector", "p2xr")
            kb.wait("scalar", "p2xr")
            _stats(kb, xb[bx][:], D, stat[bx], junkb, LN_EPS, True, bx)
            kb.wait("scalar", "sst%d" % bx)
            kb.inc("scalar", lambda e, tsub=tsub, bx=bx: e.activation(
                s2st[:, tsub * 2:tsub * 2 + 1], stat[bx][:, 4:5], AF.Identity),
                "p2st")
            kb.inc("scalar", lambda e, tsub=tsub, bx=bx: e.activation(
                s2st[:, tsub * 2 + 1:tsub * 2 + 2], stat[bx][:, 6:7],
                AF.Identity), "p2st")
        for db in range(4):
            wsl = wqo2[db % 2]
            sname = "p2wq%d" % (db % 2)
            kb.wait("gpsimd", "p2s2m", kb.val("p2s2m") - 4)
            kb.wait("gpsimd", "p2qm", kb.val("p2qm"))
            for hh in range(16):
                kb.inc("gpsimd", lambda e, hh=hh, db=db, wsl=wsl: e.dma_start(
                    out=wsl[:, hh * 512:(hh + 1) * 512],
                    in_=wo[hh * 128:(hh + 1) * 128, db * 512:(db + 1) * 512]),
                    sname, 16)
            kb.wait("tensor", sname)
            for tsub in range(4):
                t = g * 4 + tsub
                i2 = db * 4 + tsub
                bank = pbk[2 + i2 % 2]
                xsl = slice((i2 % 2) * 512, (i2 % 2) * 512 + 512)
                kb.wait("sync", "p2xs", kb.val("p2xs") - 16)
                kb.inc("sync", lambda e, t=t, db=db, xsl=xsl: e.dma_start(
                    out=x2s[:, xsl], in_=x1_d[t * 128:(t + 1) * 128,
                                              db * 512:(db + 1) * 512]),
                    "p2xr", 16)
                kb.wait("tensor", "p2xe", kb.val("p2xe") - 1)
                for hh in range(16):
                    fn = lambda e, hh=hh, db=db, tsub=tsub, bank=bank, wsl=wsl: \
                        e.matmul(
                        bank[:],
                        o2Tg[:, hh * 512 + tsub * 128: hh * 512 + (tsub + 1) * 128],
                        wsl[:, hh * 512:(hh + 1) * 512],
                        start=(hh == 0), stop=(hh == 15))
                    kb.inc("tensor", fn, "p2s2m") if hh == 15 else kb.do("tensor", fn)
                kb.wait("vector", "p2s2m")
                kb.wait("vector", "p2xr")
                kb.wait("vector", "p2st", kb.val("p2st"))
                kb.wait("vector", "p2xs", kb.val("p2xs") - 16)
                kb.do("vector", lambda e, xsl=xsl, tsub=tsub, bank=bank:
                      e.scalar_tensor_tensor(
                    x2s[:, xsl], x2s[:, xsl], s2st[:, tsub * 2:tsub * 2 + 1],
                    bank[:], op0=ALU.mult, op1=ALU.add))
                kb.inc("vector", lambda e, xsl=xsl, tsub=tsub: e.tensor_scalar(
                    x2s[:, xsl], x2s[:, xsl],
                    s2st[:, tsub * 2 + 1:tsub * 2 + 2], None,
                    op0=ALU.subtract), "p2xe")
                kb.wait("sync", "p2xe")
                kb.inc("sync", lambda e, t=t, db=db, xsl=xsl: e.dma_start(
                    out=x2_d[t * 128:(t + 1) * 128, db * 512:(db + 1) * 512],
                    in_=x2s[:, xsl]), "p2xs", 16)
    kb.flush()
    al.free("p2b")
    al.free("p2tb")
    al.free("attnsh")

    if n_phases == 2:
        _copy_out(kb, al, x2_d, y)
        stack.close()
        return nc

    kb.begin()
    identf_s = al.sb("p3", "identf_s", (128, 128), F32)
    ln2b_s = al.sb("p3", "ln2b_s", (128, D), F32)
    rbb_s = al.sb("p3", "rbb_s", (128, 8), F32)
    comb_s = al.sb("p3", "comb_s", (128, 16 * 8), F32)
    for s_, d_ in ((identf, identf_s), (ln2b, ln2b_s), (rbb, rbb_s)):
        kb.inc("sync", lambda e, s=s_, d=d_: e.dma_start(out=d[:], in_=s),
               "p3tab", 16)
    x2m = al.sb("p3", "x2m", (128, 8 * 2048), F32)
    h2T = al.sb("p3", "h2T", (128, 16 * 1024), BF16)
    h2f = xb[0]
    h2Tf = al.sb("p3", "h2Tf", (128, 128), F32)
    rws = al.sb("p3", "rws", (128, 16 * 8), F32)
    sig = al.sb("p3", "sig", (128, 8), F32)
    sg2 = al.sb("p3", "sg2", (128, 32), F32)
    sigall = al.sb("p3", "sigall", (128, 64), F32)
    rt3 = al.sb("p3", "rt3", (128, 224), F32)
    actT = al.sb("p3", "actT", (128, 8 * 1024), BF16)
    gsil = al.sb("p3", "gsil", (128, 1024), BF16)
    wgs = [al.sb("p3", "wgs%d" % i, (128, 16 * 128), BF16) for i in range(2)]
    wus = [al.sb("p3", "wus%d" % i, (128, 16 * 128), BF16) for i in range(2)]
    wds = al.sb("p3", "wds", (128, 8 * 2048), BF16)

    for ch in range(16):
        kb.inc("sync", lambda e, ch=ch: e.dma_start(
            out=rws[:, ch * 8:(ch + 1) * 8],
            in_=rw[ch * 128:(ch + 1) * 128, :]), "p3rw", 16)

    for hf in range(2):
        kb.wait("vector", "p3gm", kb.val("p3gm"))
        for tl in range(8):
            t = hf * 8 + tl
            kb.wait("sync", "p3ys", hf * 8 * 16)
            kb.inc("sync", lambda e, t=t, tl=tl: e.dma_start(
                out=x2m[:, tl * 2048:(tl + 1) * 2048],
                in_=x2_d[t * 128:(t + 1) * 128, :]), "p3xl", 16)
        for tl in range(8):
            t = hf * 8 + tl
            b = tl % 2
            kb.wait("vector", "p3xl", (hf * 8 + tl + 1) * 16)
            kb.wait("scalar", "p3xl", (hf * 8 + tl + 1) * 16)
            xt = x2m[:, tl * 2048:(tl + 1) * 2048]
            _stats(kb, xt, D, stat[b], junkb, RMS_EPS, False, b)
            kb.wait("vector", "sst%d" % b)
            kb.wait("vector", "p3tp", kb.val("p3tp"))
            kb.do("vector", lambda e, xt=xt, b=b: e.tensor_scalar(
                h2f[:], xt, stat[b][:, 4:5], None, op0=ALU.mult))
            kb.inc("vector", lambda e: e.tensor_tensor(
                h2f[:], h2f[:], ln2b_s[:], op=ALU.mult), "p3h2")
            kb.wait("tensor", "p3h2")
            kb.wait("tensor", "p3rw", 256)
            kb.wait("tensor", "p3tab", 48)
            kb.wait("tensor", "p3sg", kb.val("p3sg"))
            for dch in range(16):
                kb.wait("tensor", "p3he", kb.val("p3he"))
                kb.inc("tensor", lambda e, dch=dch: e.transpose(
                    pbk[6][:, 0:128], h2f[:, dch * 128:(dch + 1) * 128],
                    identf_s[:]), "p3tp")
                kb.wait("vector", "p3tp")
                kb.wait("vector", "p3rm", kb.val("p3rm"))
                kb.inc("vector", lambda e: e.tensor_copy(h2Tf[:],
                                                         pbk[6][:, 0:128]),
                       "p3he")
                kb.inc("vector", lambda e, dch=dch, tl=tl: e.tensor_copy(
                    h2T[:, dch * 1024 + tl * 128: dch * 1024 + (tl + 1) * 128],
                    h2Tf[:]), "p3hT")
                kb.wait("tensor", "p3he")
                kb.inc("tensor", lambda e, dch=dch: e.matmul(
                    pbk[7][:, 0:8], h2Tf[:], rws[:, dch * 8:(dch + 1) * 8],
                    start=(dch == 0), stop=(dch == 15)), "p3rm")
            kb.wait("scalar", "p3rm")
            if DBG_P3 and hf == 0 and tl == 7:
                kb.inc("scalar", lambda e: e.activation(
                    sg2[:, 18:26], pbk[7][:, 0:8], AF.Identity), "p3sg")
            kb.inc("scalar", lambda e, tl=tl: e.activation(
                sigall[:, tl * 8:(tl + 1) * 8], pbk[7][:, 0:8], AF.Sigmoid),
                "p3sg")
        kb.wait("vector", "p3sg")
        W0 = rt3[:, 0:64]
        W1 = rt3[:, 64:128]
        W2 = rt3[:, 128:192]
        R0 = rt3[:, 192:200]
        R1 = rt3[:, 200:208]
        R2 = rt3[:, 208:216]
        RT3W = 224

        def r1rep(off):
            return bass.AP(rt3, off, [[RT3W, 128], [1, 8], [0, 8]])
        rbbrep = bass.AP(rbb_s, 0, [[8, 128], [0, 8], [1, 8]])
        sig3 = bass.AP(sigall, 0, [[64, 128], [8, 8], [1, 8]])
        w03 = bass.AP(rt3, 0, [[RT3W, 128], [8, 8], [1, 8]])
        w13 = bass.AP(rt3, 64, [[RT3W, 128], [8, 8], [1, 8]])

        def vst(fn):
            kb.inc("vector", fn, "p3sg")
            kb.wait("scalar", "p3sg")

        def sco(dst, srcc):
            kb.inc("scalar", lambda e, d=dst, s=srcc: e.activation(
                d, s, AF.Identity), "p3sg")
            kb.wait("vector", "p3sg")

        vst(lambda e: e.tensor_tensor(w03, sig3, rbbrep, op=ALU.add))
        sco(W1, W0)
        for tl in range(8):
            fn = lambda e, tl=tl: e.reduce_max(
                R0[:, tl:tl + 1], W1[:, tl * 8:(tl + 1) * 8],
                axis=mybir.AxisListType.X)
            kb.inc("vector", fn, "p3sg") if tl == 7 else kb.do("vector", fn)
        kb.wait("scalar", "p3sg")
        sco(R1, R0)
        vst(lambda e: e.tensor_tensor(w03, w13, r1rep(200), op=ALU.is_ge))
        sco(W2, W0)
        vst(lambda e: e.scalar_tensor_tensor(W0, W2, NEG, W1,
                                             op0=ALU.mult, op1=ALU.add))
        sco(W1, W0)
        for tl in range(8):
            fn = lambda e, tl=tl: e.reduce_max(
                R0[:, tl:tl + 1], W1[:, tl * 8:(tl + 1) * 8],
                axis=mybir.AxisListType.X)
            kb.inc("vector", fn, "p3sg") if tl == 7 else kb.do("vector", fn)
        kb.wait("scalar", "p3sg")
        sco(R1, R0)
        vst(lambda e: e.tensor_tensor(w03, w13, r1rep(200), op=ALU.is_ge))
        sco(W1, W0)
        vst(lambda e: e.tensor_tensor(W0, W2, W1, op=ALU.add))
        sco(W2, W0)
        vst(lambda e: e.tensor_tensor(W0, W2, sigall[:], op=ALU.mult))
        sco(W1, W0)
        for tl in range(8):
            fn = lambda e, tl=tl: e.reduce_sum(
                R0[:, tl:tl + 1], W1[:, tl * 8:(tl + 1) * 8],
                axis=mybir.AxisListType.X)
            kb.inc("vector", fn, "p3sg") if tl == 7 else kb.do("vector", fn)
        kb.wait("scalar", "p3sg")
        sco(R1, R0)
        vst(lambda e: e.reciprocal(R2[:], R1[:]))
        sco(R0, R2)
        vst(lambda e: e.tensor_tensor(w03, w13, r1rep(192), op=ALU.mult))
        kb.inc("scalar", lambda e, hf=hf: e.activation(
            comb_s[:, hf * 64:(hf + 1) * 64], W0, AF.Identity), "p3cb")
        kb.wait("tensor", "p3hT", kb.val("p3hT"))
        if DBG_P3 and hf == 0:
            kb.wait("sync", "p3cb", kb.val("p3cb"))
            kb.wait("sync", "p3sg", kb.val("p3sg"))
            kb.inc("sync", lambda e: e.dma_start(
                out=dbg_sg[:, 0:24], in_=sg2[:, 0:24]), "p3x1", 16)
            kb.inc("sync", lambda e: e.dma_start(
                out=dbg_sg[:, 24:32], in_=sig[:]), "p3x1", 16)
            kb.wait("sync", "p3hT", kb.val("p3hT"))
            kb.inc("sync", lambda e: e.dma_start(out=dbg_cb, in_=comb_s[:]),
                   "p3x1", 16)
            kb.inc("sync", lambda e: e.dma_start(out=dbg_h2, in_=h2T[:]),
                   "p3x1", 16)
        for e_ in range(E):
            kb.wait("gpsimd", "p3dn", kb.val("p3dn"))
            for ich in range(8):
                kb.inc("gpsimd", lambda e, e_=e_, ich=ich: e.dma_start(
                    out=wds[:, ich * 2048:(ich + 1) * 2048],
                    in_=wdp[e_, ich * 128:(ich + 1) * 128, :]), "p3wd", 16)
            kb.wait("vector", "p3dn", kb.val("p3dn"))
            for i128 in range(8):
                sl_ = i128 % 2
                wb_ = wgs[sl_]
                wub = wus[sl_]
                kb.wait("gpsimd", "p3gm", kb.val("p3gm") - 4)
                kb.inc("gpsimd", lambda e, e_=e_, i128=i128, wb_=wb_: e.dma_start(
                    out=wb_[:], in_=wg_r[e_, i128, :, :]), "p3wg%d" % sl_, 16)
                kb.inc("gpsimd", lambda e, e_=e_, i128=i128, wub=wub: e.dma_start(
                    out=wub[:], in_=wu_r[e_, i128, :, :]), "p3wg%d" % sl_, 16)
                kb.wait("tensor", "p3wg%d" % sl_)
                kb.wait("tensor", "p3sl", kb.val("p3sl"))
                kb.wait("tensor", "p3ae", kb.val("p3ae"))
                for tg in range(2):
                    for dch in range(16):
                        fn = lambda e, dch=dch, tg=tg, wb_=wb_: e.matmul(
                            pbk[tg][:], wb_[:, dch * 128:(dch + 1) * 128],
                            h2T[:, dch * 1024 + tg * 512: dch * 1024 + (tg + 1) * 512],
                            start=(dch == 0), stop=(dch == 15))
                        kb.inc("tensor", fn, "p3gm") if dch == 15 \
                            else kb.do("tensor", fn)
                        fn2 = lambda e, dch=dch, tg=tg, wub=wub: e.matmul(
                            pbk[2 + tg][:], wub[:, dch * 128:(dch + 1) * 128],
                            h2T[:, dch * 1024 + tg * 512: dch * 1024 + (tg + 1) * 512],
                            start=(dch == 0), stop=(dch == 15))
                        kb.inc("tensor", fn2, "p3gm") if dch == 15 \
                            else kb.do("tensor", fn2)
                kb.wait("scalar", "p3gm")
                for tg in range(2):
                    kb.inc("scalar", lambda e, tg=tg: e.activation(
                        gsil[:, tg * 512:(tg + 1) * 512], pbk[tg][:], AF.Silu),
                        "p3sl")
                kb.wait("vector", "p3sl")
                for tg in range(2):
                    kb.inc("vector", lambda e, tg=tg, i128=i128: e.tensor_tensor(
                        actT[:, i128 * 1024 + tg * 512: i128 * 1024 + (tg + 1) * 512],
                        gsil[:, tg * 512:(tg + 1) * 512], pbk[2 + tg][:],
                        op=ALU.mult), "p3ae")
            kb.wait("tensor", "p3ae")
            kb.wait("tensor", "p3wd")
            for tl in range(8):
                t = hf * 8 + tl
                for db in range(4):
                    bank = pbk[4 + (tl * 4 + db) % 2]
                    kb.wait("tensor", "p3de", kb.val("p3de") - 1)
                    for ich in range(8):
                        fn = lambda e, ich=ich, tl=tl, db=db, bank=bank: \
                            e.matmul(
                            bank[:],
                            actT[:, ich * 1024 + tl * 128: ich * 1024 + (tl + 1) * 128],
                            wds[:, ich * 2048 + db * 512: ich * 2048 + (db + 1) * 512],
                            start=(ich == 0), stop=(ich == 7))
                        kb.inc("tensor", fn, "p3dn") if ich == 7 \
                            else kb.do("tensor", fn)
                    kb.wait("vector", "p3dn")
                    kb.wait("vector", "p3cb", hf + 1)
                    kb.inc("vector", lambda e, tl=tl, db=db, t=t, e_=e_,
                           bank=bank: e.scalar_tensor_tensor(
                        x2m[:, tl * 2048 + db * 512: tl * 2048 + (db + 1) * 512],
                        bank[:], comb_s[:, t * 8 + e_: t * 8 + e_ + 1],
                        x2m[:, tl * 2048 + db * 512: tl * 2048 + (db + 1) * 512],
                        op0=ALU.mult, op1=ALU.add), "p3de")
        for tl in range(8):
            t = hf * 8 + tl
            kb.wait("sync", "p3xa", kb.val("p3xa"))
            kb.inc("sync", lambda e, t=t: e.dma_start(
                out=xb[1][:], in_=x1_d[t * 128:(t + 1) * 128, :]), "p3x1", 16)
            kb.wait("vector", "p3x1")
            kb.wait("vector", "p3de", kb.val("p3de"))
            kb.inc("vector", lambda e, tl=tl: e.tensor_tensor(
                x2m[:, tl * 2048:(tl + 1) * 2048],
                x2m[:, tl * 2048:(tl + 1) * 2048], xb[1][:], op=ALU.add),
                "p3xa")
            kb.wait("sync", "p3xa")
            kb.inc("sync", lambda e, t=t, tl=tl: e.dma_start(
                out=y[t * 128:(t + 1) * 128, :],
                in_=x2m[:, tl * 2048:(tl + 1) * 2048]), "p3ys", 16)
    kb.flush()
    al.free("p3")
    al.free("base")
    stack.close()
    return nc


def _copy_out(kb, al, src_d, y):
    kb.begin()
    dbg = al.sb("dbg", "dbgt", (128, D), F32)
    for t in range(16):
        kb.wait("sync", "dst", (t - 1) * 16)
        kb.inc("sync", lambda e, t=t: e.dma_start(
            out=dbg[:], in_=src_d[t * 128:(t + 1) * 128, :]), "dld", 16)
        kb.wait("sync", "dld")
        kb.inc("sync", lambda e, t=t: e.dma_start(
            out=y[t * 128:(t + 1) * 128, :], in_=dbg[:]), "dst", 16)
    kb.flush()
    al.free("dbg")
    al.free("base")



def _host_tables(H):
    t = {}
    t["identf"] = np.eye(128, dtype=np.float32)
    i = np.arange(128)[:, None]
    j = np.arange(128)[None, :]
    t["tri01"] = (i <= j).astype(ml_dtypes.bfloat16)
    negu = np.zeros((128, 4, 512), np.float32)
    for s in range(4):
        m = np.ones((128, 512), np.float32)
        m[:, : s * 128] = 0.0
        m[:, s * 128:(s + 1) * 128] = (i <= j).astype(np.float32)
        negu[:, s, :] = -(1.0 - m)
    t["negu"] = negu.reshape(128, 4 * 512).astype(ml_dtypes.bfloat16)
    sel = np.zeros((128, 32), np.float32)
    for g in range(NG):
        for fam in range(2):
            for s in range(4):
                sel[:, g * 8 + fam * 4 + s] = 1.0 if fam == H else 0.0
    t["sel"] = sel
    amla_t = np.full((128, MLA_NC), NEG, np.float32)
    ii = np.arange(128)
    colb = 0
    for g in range(NG):
        Q0 = 2048 * H + 512 * g
        for it, kt in enumerate(MLA_KTS[g]):
            if kt < 4 * g + 4:
                kpos = kt * 128 + ii + 2048 * H
                amla_t[:, colb + it] = SLOPE * (kpos - Q0)
            elif H == 1:
                kpos = (kt * 128 + ii + 2048) % 4096
                amla_t[:, colb + it] = SLOPE * (kpos - Q0)
        colb += len(MLA_KTS[g])
    t["amla"] = amla_t
    vffn_t = np.full((128, FFN_NC), NEG, np.float32)
    colb = 0
    for g in range(NG):
        for it, kt in enumerate(FFN_KTS[g]):
            if kt <= 16 * H + 4 * g + 3:
                vffn_t[:, colb + it] = 0.0
        colb += len(FFN_KTS[g])
    t["vffn"] = vffn_t
    pos = (np.arange(T) + 2048 * H).astype(np.float64)
    invf = 1.0 / (1e6 ** (np.arange(0, 64, 2) / 64.0))
    ang = pos[:, None] * invf[None, :]
    cs = np.zeros((128, 16 * 64), np.float32)
    for tt in range(16):
        cs[:, tt * 64: tt * 64 + 32] = np.cos(ang[tt * 128:(tt + 1) * 128])
        cs[:, tt * 64 + 32: tt * 64 + 64] = np.sin(ang[tt * 128:(tt + 1) * 128])
    t["cossin"] = cs
    return t


_NC_CACHE = {}


def kernel(**inp):
    g = lambda k: np.ascontiguousarray(np.asarray(inp[k], dtype=np.float32))
    x0f = g("L0_attn_hidden_states")

    def restage(w):
        return np.ascontiguousarray(
            w.reshape(E, 16, 128, 8, 128).transpose(0, 3, 2, 1, 4)
            .reshape(E, 8, 128, 16 * 128))

    common = {
        "w_dkv": g("L0_attn_w_dkv"), "w_q": g("L0_attn_w_q"),
        "w_uk": g("L0_attn_w_uk"), "w_uv": g("L0_attn_w_uv"),
        "w_o": g("L0_attn_w_o"), "wq": g("L0_ffn_wq"), "wk": g("L0_ffn_wk"),
        "wv": g("L0_ffn_wv"), "wo": g("L0_ffn_wo"),
        "router_w": g("L0_ffn_router_w"),
        "wg_r": restage(g("L0_ffn_wg")), "wu_r": restage(g("L0_ffn_wu")),
        "wd": g("L0_ffn_wd"),
        "ln1b": np.ascontiguousarray(
            np.broadcast_to(g("L0_ffn_ln1_w"), (128, D))),
        "ln2b": np.ascontiguousarray(
            np.broadcast_to(g("L0_ffn_ln2_w"), (128, D))),
        "qnb": np.ascontiguousarray(
            np.broadcast_to(g("L0_ffn_qnorm_w"), (128, D))),
        "knb": np.ascontiguousarray(
            np.broadcast_to(g("L0_ffn_knorm_w"), (128, 512))),
        "rbb": np.ascontiguousarray(
            np.broadcast_to(g("L0_ffn_router_bias"), (128, 8))),
    }
    tabs = [_host_tables(H) for H in range(2)]

    in_maps = []
    for c in range(8):
        b, H = c // 2, c % 2
        m = dict(common)
        m.update(tabs[H])
        m["x0"] = np.ascontiguousarray(np.roll(x0f[b], -2048 * H, axis=0))
        in_maps.append(m)

    if "nc" not in _NC_CACHE:
        _NC_CACHE["nc"] = build_program()
    nc = _NC_CACHE["nc"]
    res = run_bass_kernel_spmd(nc, in_maps, core_ids=list(range(8)))
    out = np.zeros((B, S, D), np.float32)
    for c in range(8):
        b, H = c // 2, c % 2
        out[b, 2048 * H: 2048 * (H + 1), :] = res.results[c]["y"]
    return out


# version 07
# speedup vs baseline: 1.0891x; 1.0431x over previous
import math
import numpy as np
import ml_dtypes
from contextlib import ExitStack

import concourse.bass as bass
import concourse.mybir as mybir
from concourse.bass_utils import run_bass_kernel_spmd

F32 = mybir.dt.float32
BF16 = mybir.dt.bfloat16
AF = mybir.ActivationFunctionType
ALU = mybir.AluOpType

B, S, D = 4, 4096, 2048
T = 2048
E, I = 8, 1024
NG = 4
SLOPE = 0.0625
SCALE = 1.0 / math.sqrt(128.0)
NEG = -1e30
LN_EPS = 1e-5
RMS_EPS = 1e-6

MLA_KTS = [list(range(4 * g + 4)) + list(range(16, 32)) for g in range(NG)]
FFN_KTS = [list(range(20 + 4 * g)) for g in range(NG)]
FFN_CAND = [[4 * g + s for s in range(4)] + [16 + 4 * g + s for s in range(4)]
            for g in range(NG)]
MLA_NC = sum(len(k) for k in MLA_KTS)
FFN_NC = sum(len(k) for k in FFN_KTS)
VAW = 16 * 528
KVW = 4 * 2048 + VAW

N_PHASES = 3
DBG_P2 = False
DBG_P3 = False
ENGINES = ("sync", "scalar", "vector", "tensor", "gpsimd")


class KB:
    def __init__(self, nc, stack):
        self.nc = nc
        self.stack = stack
        self.sems = {}
        self.dma_sems = set()
        self.q = None

    def sem(self, name):
        if name not in self.sems:
            h = self.stack.enter_context(self.nc.semaphore(name))
            self.sems[name] = [h, 0]
        return self.sems[name]

    def on(self, eng, fn):
        self.q[eng].append(fn)

    def wait(self, eng, name, val=None):
        s = self.sem(name)
        v = s[1] if val is None else val
        if v <= 0:
            return
        self.on(eng, lambda e, h=s[0], v=v: e.wait_ge(h, v))

    def inc(self, eng, fn, name, k=1):
        s = self.sem(name)
        s[1] += k
        if k == 16:
            self.dma_sems.add(name)
        self.on(eng, lambda e, fn=fn, h=s[0], k=k: fn(e).then_inc(h, k))
        return s[1]

    def do(self, eng, fn):
        self.on(eng, fn)

    def val(self, name):
        return self.sem(name)[1]

    def begin(self):
        self.q = {e: [] for e in ENGINES}

    def flush(self):
        for name in sorted(self.dma_sems):
            self.wait("sync", name)
        with self.nc.Block() as blk:
            for e in ENGINES:
                fns = self.q[e]
                if fns:
                    getattr(blk, e)(lambda eng, fns=fns: [f(eng) for f in fns])
        self.q = None


class Alloc:
    def __init__(self, nc):
        self.nc = nc
        self.groups = {}
        self.order = []

    def sb(self, group, name, shape, dt):
        cm = self.nc.sbuf_tensor(name, list(shape), dt)
        h = cm.__enter__()
        if group not in self.groups:
            self.groups[group] = []
            self.order.append(group)
        self.groups[group].append(cm)
        return h

    def free(self, group):
        assert self.order and self.order[-1] == group, (self.order, group)
        for cm in reversed(self.groups.pop(group)):
            cm.__exit__(None, None, None)
        self.order.pop()


def _stats(kb, xt, width, st, junk, eps, mean, b):
    sem = "sst%d" % b
    if mean:
        kb.inc("vector", lambda e: e.reduce_sum(st[:, 0:1], xt,
                                                axis=mybir.AxisListType.X), sem)
    kb.do("scalar", lambda e: e.activation(junk[:, 0:width], xt, AF.Square,
                                           accum_out=st[:, 2:3]))
    kb.do("scalar", lambda e: e.activation(st[:, 3:4], st[:, 2:3],
                                           AF.Identity, scale=1.0 / width))
    if mean:
        kb.wait("scalar", sem)
        kb.do("scalar", lambda e: e.activation(st[:, 1:2], st[:, 0:1],
                                               AF.Identity, scale=1.0 / width))
        kb.inc("scalar", lambda e: e.activation(st[:, 7:8], st[:, 1:2],
                                                AF.Square), sem)
        kb.wait("vector", sem)
        kb.inc("vector", lambda e: e.tensor_scalar(
            st[:, 3:4], st[:, 3:4], st[:, 7:8], eps,
            op0=ALU.subtract, op1=ALU.add), sem)
    else:
        kb.inc("scalar", lambda e: e.activation(st[:, 3:4], st[:, 3:4],
                                                AF.Identity), sem)
        kb.wait("vector", sem)
        kb.inc("vector", lambda e: e.tensor_scalar_add(st[:, 3:4], st[:, 3:4],
                                                       eps), sem)
    kb.wait("scalar", sem)
    kb.do("scalar", lambda e: e.activation(st[:, 4:5], st[:, 3:4], AF.Ln))
    kb.inc("scalar", lambda e: e.activation(st[:, 4:5], st[:, 4:5], AF.Exp,
                                            scale=-0.5), sem)
    if mean:
        kb.wait("vector", sem)
        kb.inc("vector", lambda e: e.tensor_tensor(st[:, 5:6], st[:, 1:2],
                                                   st[:, 4:5], op=ALU.mult),
               sem)
        kb.wait("scalar", sem)
        kb.inc("scalar", lambda e: e.activation(st[:, 6:7], st[:, 5:6],
                                                AF.Identity), sem)


def _stats4(kb, xts, width, st4, junk, eps, mean):
    n = len(xts)
    sem = "sst0"
    if mean:
        for i, xt in enumerate(xts):
            fn = lambda e, i=i, xt=xt: e.reduce_sum(
                st4[:, i * 8:i * 8 + 1], xt, axis=mybir.AxisListType.X)
            kb.inc("vector", fn, sem) if i == n - 1 else kb.do("vector", fn)
        kb.wait("scalar", sem)
    for i, xt in enumerate(xts):
        kb.do("scalar", lambda e, i=i, xt=xt: e.activation(
            junk[:, 0:width], xt, AF.Square, accum_out=st4[:, i * 8 + 2:i * 8 + 3]))
        kb.do("scalar", lambda e, i=i: e.activation(
            st4[:, i * 8 + 3:i * 8 + 4], st4[:, i * 8 + 2:i * 8 + 3],
            AF.Identity, scale=1.0 / width))
    if mean:
        for i in range(n):
            kb.do("scalar", lambda e, i=i: e.activation(
                st4[:, i * 8 + 1:i * 8 + 2], st4[:, i * 8:i * 8 + 1],
                AF.Identity, scale=1.0 / width))
            kb.do("scalar", lambda e, i=i: e.activation(
                st4[:, i * 8 + 7:i * 8 + 8], st4[:, i * 8 + 1:i * 8 + 2],
                AF.Square))
    kb.inc("scalar", lambda e: e.activation(junk[:, 0:1], junk[:, 0:1],
                                            AF.Identity), sem)
    kb.wait("vector", sem)
    for i in range(n):
        if mean:
            fn = lambda e, i=i: e.tensor_scalar(
                st4[:, i * 8 + 3:i * 8 + 4], st4[:, i * 8 + 3:i * 8 + 4],
                st4[:, i * 8 + 7:i * 8 + 8], eps,
                op0=ALU.subtract, op1=ALU.add)
        else:
            fn = lambda e, i=i: e.tensor_scalar_add(
                st4[:, i * 8 + 3:i * 8 + 4], st4[:, i * 8 + 3:i * 8 + 4], eps)
        kb.inc("vector", fn, sem) if i == n - 1 else kb.do("vector", fn)
    kb.wait("scalar", sem)
    for i in range(n):
        kb.do("scalar", lambda e, i=i: e.activation(
            st4[:, i * 8 + 4:i * 8 + 5], st4[:, i * 8 + 3:i * 8 + 4], AF.Ln))
        fn = lambda e, i=i: e.activation(
            st4[:, i * 8 + 4:i * 8 + 5], st4[:, i * 8 + 4:i * 8 + 5],
            AF.Exp, scale=-0.5)
        kb.inc("scalar", fn, sem) if i == n - 1 else kb.do("scalar", fn)
    if mean:
        kb.wait("vector", sem)
        for i in range(n):
            fn = lambda e, i=i: e.tensor_tensor(
                st4[:, i * 8 + 5:i * 8 + 6], st4[:, i * 8 + 1:i * 8 + 2],
                st4[:, i * 8 + 4:i * 8 + 5], op=ALU.mult)
            kb.inc("vector", fn, sem) if i == n - 1 else kb.do("vector", fn)
        kb.wait("scalar", sem)
        for i in range(n):
            fn = lambda e, i=i: e.activation(
                st4[:, i * 8 + 6:i * 8 + 7], st4[:, i * 8 + 5:i * 8 + 6],
                AF.Identity)
            kb.inc("scalar", fn, sem) if i == n - 1 else kb.do("scalar", fn)


def build_program(n_phases=N_PHASES):
    nc = bass.Bass("TRN2", target_bir_lowering=False,
                   detect_race_conditions=False)
    stack = ExitStack()
    kb = KB(nc, stack)
    al = Alloc(nc)

    def din(name, shape, dt=F32):
        return nc.dram_tensor(name, list(shape), dt, kind="ExternalInput").ap()

    x0 = din("x0", (S, D))
    w_dkv = din("w_dkv", (128, D))
    w_q = din("w_q", (D, D))
    w_uk = din("w_uk", (D, 128))
    w_uv = din("w_uv", (D, 128))
    w_o = din("w_o", (D, D))
    wq = din("wq", (D, D))
    wk = din("wk", (D, 512))
    wv = din("wv", (D, 512))
    wo = din("wo", (D, D))
    rw = din("router_w", (D, 8))
    wg_r = din("wg_r", (E, 8, 128, 16 * 128))
    wu_r = din("wu_r", (E, 8, 128, 16 * 128))
    wdp = din("wd", (E, I, D))
    identf = din("identf", (128, 128))
    tri01 = din("tri01", (128, 128), BF16)
    negu = din("negu", (128, 4 * 512), BF16)
    seli = din("sel", (128, 32))
    amla = din("amla", (128, MLA_NC))
    vffn = din("vffn", (128, FFN_NC))
    cossin = din("cossin", (128, 16 * 64))
    ln1b = din("ln1b", (128, D))
    ln2b = din("ln2b", (128, D))
    qnb = din("qnb", (128, D))
    knb = din("knb", (128, 512))
    rbb = din("rbb", (128, 8))

    y = nc.dram_tensor("y", [T, D], F32, kind="ExternalOutput").ap()

    wqk_d = nc.dram_tensor("wqk_d", [16, 128, 16 * 128], BF16).ap()
    x1_d = nc.dram_tensor("x1_d", [T, D], F32).ap()
    x2_d = nc.dram_tensor("x2_d", [T, D], F32).ap()
    kvin_d = nc.dram_tensor("kvin_d", [128, KVW], BF16).ap()
    kvout_d = nc.dram_tensor("kvout_d", [2, 128, KVW], BF16).ap()
    if DBG_P3:
        dbg_cb = nc.dram_tensor("dbg_cb", [128, 128], F32,
                                kind="ExternalOutput").ap()
        dbg_sg = nc.dram_tensor("dbg_sg", [128, 32], F32,
                                kind="ExternalOutput").ap()
        dbg_h2 = nc.dram_tensor("dbg_h2", [128, 16 * 1024], BF16,
                                kind="ExternalOutput").ap()
    if DBG_P2:
        dbg_kt = nc.dram_tensor("dbg_kt", [128, 4 * 4096], BF16,
                                kind="ExternalOutput").ap()
        dbg_va = nc.dram_tensor("dbg_va", [128, 32 * 528], BF16,
                                kind="ExternalOutput").ap()
        dbg_q2 = nc.dram_tensor("dbg_q2", [4, 128, 16 * 512], BF16,
                                kind="ExternalOutput").ap()
        dbg_o2 = nc.dram_tensor("dbg_o2", [4, 128, 16 * 512], BF16,
                                kind="ExternalOutput").ap()

    bsb = lambda n, sh, dt: al.sb("base", n, sh, dt)
    xb = [bsb("xb%d" % i, (128, D), F32) for i in range(2)]
    zb = [bsb("zb%d" % i, (128, D), BF16) for i in range(2)]
    stat = [bsb("st%d" % i, (128, 8), F32) for i in range(2)]
    st4 = bsb("st4", (128, 32), F32)
    junkb = bsb("junkb", (128, D), BF16)
    asb = lambda n, sh, dt: al.sb("attnsh", n, sh, dt)
    amla_s = asb("amla_s", (128, MLA_NC), F32)
    tri_s = asb("tri_s", (128, 128), BF16)
    Pb = [asb("Pb%d" % i, (128, 512), BF16) for i in range(3)]
    Pscr = asb("Pscr", (128, 512), BF16)
    rcol = asb("rcol", (128, 8), F32)
    onb = asb("onb", (128, 512), BF16)
    ropescr = asb("ropescr", (128, 2048), BF16)

    pbk = [stack.enter_context(nc.psum_tensor("pbk%d" % i, [128, 512], F32))
           for i in range(8)]

    kb.begin()
    wdkvT_s = al.sb("p1per", "wdkvT_s", (128, 16 * 128), BF16)
    w2_s = al.sb("p1per", "w2_s", (128, 16 * 2048), BF16)
    ckvT_s = al.sb("p1per", "ckvT_s", (128, 4096), BF16)
    ckvaug_s = al.sb("p1per", "ckvaug_s", (128, 32 * 144), BF16)
    lnT_g = al.sb("p1per", "lnT_g", (128, 16 * 512), BF16)
    qtg = al.sb("p1per", "qtg", (128, 16 * 512), BF16)
    wqks = [al.sb("p1per", "wqks%d" % i, (128, 4096), BF16) for i in range(2)]
    oTg = al.sb("p1per", "oTg", (128, 16 * 512), BF16)
    x1t = [al.sb("p1per", "x1t0", (128, D), F32)] * 2

    wuv_s = al.sb("p0tmp", "p0_wuv", (128, 16 * 128), BF16)
    wuk_s = wuv_s
    evtile = al.sb("p0tmp", "p0_ev", (128, 128), BF16)
    t128 = [al.sb("p0tmp", "p0_t%d" % i, (128, 128), BF16) for i in range(2)]
    wslab = [al.sb("p0tmp", "p0_ws%d" % i, (128, 2048), BF16) for i in range(2)]
    woth = wslab[0]

    for s_, d_ in ((amla, amla_s), (tri01, tri_s)):
        kb.inc("sync", lambda e, s=s_, d=d_: e.dma_start(out=d[:], in_=s),
               "p0m", 16)
    kb.inc("vector", lambda e: e.memset(ckvaug_s[:], 2.0 ** -24), "p0aug", 1)

    for h in range(16):
        kb.inc("gpsimd", lambda e, h=h: e.dma_start(
            out=wuv_s[:, h * 128:(h + 1) * 128],
            in_=w_uv[h * 128:(h + 1) * 128, :]), "p0g", 16)

    kb.wait("tensor", "p0g")
    for h in range(16):
        for ci in range(16):
            sl_ = ci % 2
            tb = t128[sl_]
            kb.wait("gpsimd", "p0wot%d" % sl_, kb.val("p0wot%d" % sl_))
            kb.inc("gpsimd", lambda e, h=h, ci=ci, tb=tb: e.dma_start(
                out=tb[:], in_=w_o[ci * 128:(ci + 1) * 128,
                                   h * 128:(h + 1) * 128]), "p0wol%d" % sl_, 16)
            kb.wait("sync", "p0wol%d" % sl_)
            kb.wait("sync", "p0w2m", h * 4)
            kb.inc("sync", lambda e, ci=ci, tb=tb: e.dma_start_transpose(
                woth[:, ci * 128:(ci + 1) * 128], tb[:]), "p0wot%d" % sl_, 16)
        kb.wait("tensor", "p0wot0")
        kb.wait("tensor", "p0wot1")
        for db in range(4):
            bank = pbk[db % 2]
            kb.wait("tensor", "p0w2e", kb.val("p0w2e") - 1)
            kb.inc("tensor", lambda e, h=h, db=db, bank=bank: e.matmul(
                bank[:], wuv_s[:, h * 128:(h + 1) * 128],
                woth[:, db * 512:(db + 1) * 512],
                start=True, stop=True), "p0w2m")
            kb.wait("vector", "p0w2m")
            kb.inc("vector", lambda e, h=h, db=db, bank=bank: e.tensor_copy(
                w2_s[:, h * 2048 + db * 512: h * 2048 + (db + 1) * 512], bank[:]),
                "p0w2e")

    kb.wait("gpsimd", "p0w2m", kb.val("p0w2m"))
    for h in range(16):
        kb.inc("gpsimd", lambda e, h=h: e.dma_start(
            out=wuk_s[:, h * 128:(h + 1) * 128],
            in_=w_uk[h * 128:(h + 1) * 128, :]), "p0g", 16)
    for h in range(16):
        sl_ = h % 2
        sl = wslab[sl_]
        kb.wait("gpsimd", "p0qkm", (h - 1) * 16)
        kb.inc("gpsimd", lambda e, h=h, sl=sl: e.dma_start(
            out=sl[:], in_=w_q[h * 128:(h + 1) * 128, :]), "p0wq%d" % sl_, 16)
        kb.wait("tensor", "p0wq%d" % sl_)
        kb.wait("tensor", "p0g")
        for dch in range(16):
            bank = pbk[2 + dch % 2]
            kb.wait("tensor", "p0qke", kb.val("p0qke") - 1)
            kb.inc("tensor", lambda e, h=h, dch=dch, sl=sl, bank=bank: e.matmul(
                bank[:, 0:128], sl[:, dch * 128:(dch + 1) * 128],
                wuk_s[:, h * 128:(h + 1) * 128], start=True, stop=True), "p0qkm")
            kb.wait("vector", "p0qkm")
            kb.wait("vector", "p0qks", kb.val("p0qks"))
            kb.inc("vector", lambda e, bank=bank: e.tensor_copy(
                evtile[:], bank[:, 0:128]), "p0qke")
            kb.wait("sync", "p0qke")
            kb.inc("sync", lambda e, h=h, dch=dch: e.dma_start(
                out=wqk_d[h, :, dch * 128:(dch + 1) * 128], in_=evtile[:]),
                "p0qks", 16)

    kb.wait("gpsimd", "p0qkm", kb.val("p0qkm"))
    kb.inc("gpsimd", lambda e: e.dma_start(out=wslab[0][:], in_=w_dkv[:, :]),
           "p0g", 16)
    kb.wait("sync", "p0g")
    for dch in range(16):
        kb.inc("sync", lambda e, dch=dch: e.dma_start_transpose(
            wdkvT_s[:, dch * 128:(dch + 1) * 128],
            wslab[0][:, dch * 128:(dch + 1) * 128]), "p0m", 16)
    al.free("p0tmp")

    state = {"n": 0}

    def ln_tile(t, tsub):
        n = state["n"]
        state["n"] += 1
        b = n % 2
        kb.wait("sync", "paz", n - 1)
        kb.wait("sync", "patp", (n - 1) * 256)
        kb.inc("sync", lambda e, t=t, b=b: e.dma_start(
            out=xb[b][:], in_=x0[t * 128:(t + 1) * 128, :]), "pald", 16)
        kb.wait("vector", "pald")
        kb.wait("scalar", "pald")
        _stats(kb, xb[b][:], D, stat[b], junkb, LN_EPS, True, b)
        kb.wait("vector", "sst%d" % b)
        kb.inc("vector", lambda e, b=b: e.tensor_scalar(
            zb[b][:], xb[b][:], stat[b][:, 1:2], stat[b][:, 4:5],
            op0=ALU.subtract, op1=ALU.mult), "paz")
        kb.wait("sync", "paz")
        for dch in range(16):
            kb.inc("sync", lambda e, b=b, dch=dch, tsub=tsub: e.dma_start_transpose(
                lnT_g[:, dch * 512 + tsub * 128: dch * 512 + (tsub + 1) * 128],
                zb[b][:, dch * 128:(dch + 1) * 128]), "patp", 16)

    kb.wait("tensor", "p0m")
    kb.wait("vector", "p0m")
    kb.wait("scalar", "p0m")
    for ga in range(8):
        kb.wait("sync", "pack", kb.val("pack"))
        for ts4 in range(4):
            ln_tile(ga * 4 + ts4, ts4)
        kb.wait("tensor", "patp", state["n"] * 256)
        bank = pbk[ga % 2]
        kb.wait("tensor", "packe", kb.val("packe") - 1)
        for dch in range(16):
            fn = lambda e, dch=dch, bank=bank: e.matmul(
                bank[:], wdkvT_s[:, dch * 128:(dch + 1) * 128],
                lnT_g[:, dch * 512:(dch + 1) * 512],
                start=(dch == 0), stop=(dch == 15))
            kb.inc("tensor", fn, "pack") if dch == 15 else kb.do("tensor", fn)
        kb.wait("vector", "pack")
        kb.inc("vector", lambda e, ga=ga, bank=bank: e.tensor_copy(
            ckvT_s[:, ga * 512:(ga + 1) * 512], bank[:]), "packe")
        kb.wait("sync", "packe")
        kb.wait("sync", "p0aug", 1)
        for ks in range(4):
            kt = ga * 4 + ks
            kb.inc("sync", lambda e, ga=ga, ks=ks, kt=kt: e.dma_start_transpose(
                ckvaug_s[:, kt * 144: kt * 144 + 128],
                ckvT_s[:, ga * 512 + ks * 128: ga * 512 + (ks + 1) * 128]),
                "paaug", 16)
    kb.wait("tensor", "paaug")
    kb.wait("vector", "paaug")
    kb.wait("sync", "pack", kb.val("pack"))

    kb.wait("gpsimd", "p0qks")
    kb.wait("tensor", "p0w2e")
    for g in range(NG):
        kb.wait("sync", "pbqm", kb.val("pbqm"))
        kb.wait("sync", "p1xe", kb.val("p1xe"))
        for ts4 in range(4):
            ln_tile(g * 4 + ts4, ts4)
        kb.wait("tensor", "patp", state["n"] * 256)
        kb.wait("tensor", "p1xe", kb.val("p1xe"))
        for cp in range(8):
            sl_ = cp % 2
            sl = wqks[sl_]
            kb.wait("gpsimd", "pbqm", kb.val("pbqm") - 2)
            kb.inc("gpsimd", lambda e, cp=cp, sl=sl: e.dma_start(
                out=bass.AP(sl, 0, [[4096, 128], [2048, 2], [1, 2048]]),
                in_=wqk_d[2 * cp:2 * cp + 2, :, :].rearrange(
                    "two p c -> p two c")), "pbqk%d" % sl_, 16)
            kb.wait("tensor", "pbqk%d" % sl_)
            for half in range(2):
                cch = 2 * cp + half
                bank = pbk[2 + cch % 2]
                kb.wait("tensor", "pbqe", kb.val("pbqe") - 1)
                for dch in range(16):
                    fn = lambda e, dch=dch, sl=sl, half=half, bank=bank: \
                        e.matmul(
                        bank[:],
                        sl[:, half * 2048 + dch * 128: half * 2048 + (dch + 1) * 128],
                        lnT_g[:, dch * 512:(dch + 1) * 512],
                        start=(dch == 0), stop=(dch == 15))
                    kb.inc("tensor", fn, "pbqm") if dch == 15 \
                        else kb.do("tensor", fn)
                kb.wait("vector", "pbqm")
                kb.wait("vector", "p1sm", kb.val("p1sm"))
                kb.inc("vector", lambda e, cch=cch, bank=bank: e.tensor_copy(
                    qtg[:, cch * 512:(cch + 1) * 512], bank[:]), "pbqe")
        kb.wait("tensor", "pbqe")
        kb.wait("sync", "p1s2m", kb.val("p1s2m"))
        kb.wait("tensor", "p1xe", kb.val("p1xe"))

        kts = MLA_KTS[g]
        nit = len(kts)
        colbase = sum(len(MLA_KTS[gg]) for gg in range(g))
        for h in range(16):
            expv = {}
            maskv = {}
            pend = [None]

            def emit_pv(j, kts=kts, nit=nit, expv=expv, maskv=maskv):
                if j in maskv:
                    kb.wait("tensor", "p1mk", maskv[j])
                else:
                    kb.wait("tensor", "p1ex", expv[j])
                if j == 0:
                    kb.wait("tensor", "p1on", kb.val("p1on"))
                kt = kts[j]
                for sub in range(4):
                    fn = lambda e, j=j, sub=sub, kt=kt, nit=nit: e.matmul(
                        pbk[2 + sub][:, 0:129],
                        Pb[j % 3][:, sub * 128:(sub + 1) * 128],
                        ckvaug_s[:, kt * 144: kt * 144 + 129],
                        start=(j == 0), stop=(j == nit - 1))
                    kb.inc("tensor", fn, "p1pv") if sub == 3 else kb.do("tensor", fn)

            for it, kt in enumerate(kts):
                sbk = pbk[it % 2]
                if it >= 2:
                    kb.wait("tensor", "p1ex", expv[it - 2])
                kb.inc("tensor", lambda e, kt=kt, h=h, sbk=sbk: e.matmul(
                    sbk[:], ckvT_s[:, kt * 128:(kt + 1) * 128],
                    qtg[:, h * 512:(h + 1) * 512], start=True, stop=True),
                    "p1sm")
                pbuf = Pb[it % 3]
                kb.wait("scalar", "p1sm")
                kb.wait("scalar", "p1pv", kb.val("p1pv"))
                col = colbase + it
                expv[it] = kb.inc("scalar", lambda e, sbk=sbk, pbuf=pbuf,
                                  col=col: e.activation(
                    pbuf[:], sbk[:], AF.Exp, bias=amla_s[:, col:col + 1],
                    scale=SCALE), "p1ex")
                if 4 * g <= kt < 4 * g + 4:
                    s_ = kt - 4 * g
                    kb.wait("vector", "p1ex", expv[it])
                    if s_ > 0:
                        kb.do("vector", lambda e, pbuf=pbuf, s_=s_: e.memset(
                            pbuf[:, 0:s_ * 128], 0.0))
                    maskv[it] = kb.inc("vector", lambda e, pbuf=pbuf, s_=s_:
                                       e.tensor_tensor(
                        pbuf[:, s_ * 128:(s_ + 1) * 128],
                        pbuf[:, s_ * 128:(s_ + 1) * 128], tri_s[:],
                        op=ALU.mult), "p1mk")
                pend.append(it)
                if len(pend) > 3:
                    emit_pv(pend.pop(1))
            for j_ in pend[1:]:
                emit_pv(j_)
            kb.wait("vector", "p1pv")
            for sub in range(3):
                kb.do("vector", lambda e, sub=sub: e.reciprocal(
                    rcol[:, sub:sub + 1], pbk[2 + sub][:, 128:129]))
            kb.inc("vector", lambda e: e.reciprocal(
                rcol[:, 3:4], pbk[5][:, 128:129]), "p1rc")
            kb.wait("scalar", "p1rc")
            kb.inc("scalar", lambda e: e.activation(rcol[:, 4:8], rcol[:, 0:4],
                                                    AF.Identity), "p1lnc")
            kb.wait("vector", "p1lnc")
            kb.wait("vector", "p1oT", kb.val("p1oT") - 64)
            for sub in range(4):
                kb.inc("vector", lambda e, sub=sub: e.tensor_scalar(
                    onb[:, sub * 128:(sub + 1) * 128], pbk[2 + sub][:, 0:128],
                    rcol[:, 4 + sub:5 + sub], 2.0 ** -24,
                    op0=ALU.mult, op1=ALU.mult), "p1on")
            kb.wait("sync", "p1on")
            for sub in range(4):
                kb.inc("sync", lambda e, h=h, sub=sub: e.dma_start_transpose(
                    oTg[:, h * 512 + sub * 128: h * 512 + (sub + 1) * 128],
                    onb[:, sub * 128:(sub + 1) * 128]), "p1oT", 16)
        kb.wait("tensor", "p1oT")
        for tsub in range(4):
            t = g * 4 + tsub
            bx = tsub % 2
            kb.wait("sync", "p1xe", kb.val("p1xe") - 8)
            kb.inc("sync", lambda e, t=t, bx=bx: e.dma_start(
                out=xb[bx][:], in_=x0[t * 128:(t + 1) * 128, :]), "p1xr", 16)
            for db in range(4):
                bank = pbk[2 + db]
                kb.wait("tensor", "p1xe", kb.val("p1xe"))
                for h in range(16):
                    fn = lambda e, h=h, db=db, tsub=tsub, bank=bank: e.matmul(
                        bank[:],
                        oTg[:, h * 512 + tsub * 128: h * 512 + (tsub + 1) * 128],
                        w2_s[:, h * 2048 + db * 512: h * 2048 + (db + 1) * 512],
                        start=(h == 0), stop=(h == 15))
                    kb.inc("tensor", fn, "p1s2m") if h == 15 else kb.do("tensor", fn)
            kb.wait("vector", "p1s2m")
            kb.wait("vector", "p1xr")
            bo = 0
            kb.wait("vector", "p1xs", kb.val("p1xs"))
            for db in range(4):
                kb.inc("vector", lambda e, db=db, bo=bo, bx=bx: e.tensor_tensor(
                    x1t[bo][:, db * 512:(db + 1) * 512], pbk[2 + db][:],
                    xb[bx][:, db * 512:(db + 1) * 512], op=ALU.add), "p1xe")
            kb.wait("sync", "p1xe")
            kb.inc("sync", lambda e, t=t, bo=bo: e.dma_start(
                out=x1_d[t * 128:(t + 1) * 128, :], in_=x1t[bo][:]), "p1xs", 16)
    kb.flush()
    al.free("p1per")

    if n_phases == 1:
        al.free("attnsh")
        _copy_out(kb, al, x1_d, y)
        stack.close()
        return nc

    kb.begin()
    negu_s = al.sb("p2tb", "negu_s", (128, 4 * 512), BF16)
    sel_s = al.sb("p2tb", "sel_s", (128, 32), F32)
    vffn_s = al.sb("p2tb", "vffn_s", (128, FFN_NC), F32)
    cs_s = al.sb("p2tb", "cs_s", (128, 16 * 64), F32)
    ln1b_s = al.sb("p2tb", "ln1b_s", (128, D), BF16)
    qnb_s = al.sb("p2tb", "qnb_s", (128, D), BF16)
    knb_s = al.sb("p2tb", "knb_s", (128, 512), BF16)
    for s_, d_ in ((negu, negu_s), (seli, sel_s), (vffn, vffn_s),
                   (cossin, cs_s)):
        kb.inc("sync", lambda e, s=s_, d=d_: e.dma_start(out=d[:], in_=s),
               "p2tb", 16)
    for s_, d_ in ((ln1b, ln1b_s), (qnb, qnb_s), (knb, knb_s)):
        kb.inc("gpsimd", lambda e, s=s_, d=d_: e.dma_start(out=d[:], in_=s),
               "p2tg", 16)
    wkv_s = al.sb("p2a", "wkv_s", (128, 2 * 16 * 512), BF16)
    hTt = al.sb("p2a", "hTt", (128, 16 * 128), BF16)
    ktf = al.sb("p2a", "ktf", (128, 512), F32)
    krb = al.sb("p2a", "krb", (128, 512), BF16)
    kTown = al.sb("p2a", "kTown", (128, 4 * 2048), BF16)
    vaugo = al.sb("p2a", "vaugo", (128, VAW), BF16)

    for ci in range(16):
        kb.inc("gpsimd", lambda e, ci=ci: e.dma_start(
            out=wkv_s[:, ci * 512:(ci + 1) * 512],
            in_=wk[ci * 128:(ci + 1) * 128, :]), "p2tg", 16)
        kb.inc("gpsimd", lambda e, ci=ci: e.dma_start(
            out=wkv_s[:, 8192 + ci * 512: 8192 + (ci + 1) * 512],
            in_=wv[ci * 128:(ci + 1) * 128, :]), "p2tg", 16)
    kb.inc("vector", lambda e: e.memset(vaugo[:], 1.0), "p2vone", 1)

    def h_tile(tag, t, b):
        kb.inc("sync", lambda e, t=t, b=b: e.dma_start(
            out=xb[b][:], in_=x1_d[t * 128:(t + 1) * 128, :]), tag + "ld", 16)
        kb.wait("vector", tag + "ld")
        kb.wait("scalar", tag + "ld")
        _stats(kb, xb[b][:], D, stat[b], junkb, LN_EPS, True, b)
        kb.wait("vector", "sst%d" % b)
        kb.inc("vector", lambda e, b=b: e.tensor_scalar(
            zb[b][:], xb[b][:], stat[b][:, 1:2], stat[b][:, 4:5],
            op0=ALU.subtract, op1=ALU.mult), tag + "z")
        kb.wait("scalar", tag + "z")
        _stats(kb, zb[b][:], D, stat[b], junkb, RMS_EPS, False, b)
        kb.wait("vector", "sst%d" % b)
        kb.do("vector", lambda e, b=b: e.tensor_scalar(
            zb[b][:], zb[b][:], stat[b][:, 4:5], None, op0=ALU.mult))
        kb.inc("vector", lambda e, b=b: e.tensor_tensor(
            zb[b][:], zb[b][:], ln1b_s[:], op=ALU.mult), tag + "h")

    def rope_emit(src, srcw, soff, dst, dstw, doff, nh, tcol, tag):
        cosap = bass.AP(cs_s, tcol * 64, [[1024, 128], [0, nh], [1, 32]])
        sinap = bass.AP(cs_s, tcol * 64 + 32, [[1024, 128], [0, nh], [1, 32]])
        sc = [bass.AP(ropescr, 32 * nh * i, [[2048, 128], [32, nh], [1, 32]])
              for i in range(4)]
        q1 = bass.AP(src, soff, [[srcw, 128], [128, nh], [1, 32]])
        q2 = bass.AP(src, soff + 32, [[srcw, 128], [128, nh], [1, 32]])
        d1 = bass.AP(dst, doff, [[dstw, 128], [128, nh], [1, 32]])
        d2 = bass.AP(dst, doff + 32, [[dstw, 128], [128, nh], [1, 32]])
        kb.do("vector", lambda e: e.tensor_tensor(sc[0], q1, cosap, op=ALU.mult))
        kb.do("vector", lambda e: e.tensor_tensor(sc[1], q2, sinap, op=ALU.mult))
        kb.do("vector", lambda e: e.tensor_tensor(sc[2], q2, cosap, op=ALU.mult))
        kb.do("vector", lambda e: e.tensor_tensor(sc[3], q1, sinap, op=ALU.mult))
        kb.do("vector", lambda e: e.tensor_tensor(d1, sc[0], sc[1],
                                                  op=ALU.subtract))
        kb.do("vector", lambda e: e.tensor_tensor(d2, sc[2], sc[3], op=ALU.add))
        if src is not dst or soff != doff:
            pasrc = bass.AP(src, soff + 64, [[srcw, 128], [128, nh], [1, 64]])
            padst = bass.AP(dst, doff + 64, [[dstw, 128], [128, nh], [1, 64]])
            kb.do("vector", lambda e: e.tensor_copy(padst, pasrc))
        kb.inc("vector", lambda e: e.tensor_copy(
            bass.AP(dst, doff, [[dstw, 128], [1, 1]]),
            bass.AP(dst, doff, [[dstw, 128], [1, 1]])), tag)

    kb.wait("vector", "p2tg")
    kb.wait("vector", "p2tb")
    for t in range(16):
        b = t % 2
        kb.wait("sync", "p2ahT", (t - 1) * 256)
        kb.wait("sync", "p2az", t - 1)
        h_tile("p2a", t, b)
        kb.wait("sync", "p2ah")
        kb.wait("sync", "p2akm", kb.val("p2akm"))
        kb.wait("sync", "p2avm", kb.val("p2avm"))
        for dch in range(16):
            kb.inc("sync", lambda e, b=b, dch=dch: e.dma_start_transpose(
                hTt[:, dch * 128:(dch + 1) * 128],
                zb[b][:, dch * 128:(dch + 1) * 128]), "p2ahT", 16)
        kb.wait("tensor", "p2ahT")
        kb.wait("tensor", "p2tg")
        kb.wait("tensor", "p2ake", kb.val("p2ake") - 1)
        for dch in range(16):
            fn = lambda e, dch=dch: e.matmul(
                pbk[0][:], hTt[:, dch * 128:(dch + 1) * 128],
                wkv_s[:, dch * 512:(dch + 1) * 512],
                start=(dch == 0), stop=(dch == 15))
            kb.inc("tensor", fn, "p2akm") if dch == 15 else kb.do("tensor", fn)
        kb.wait("tensor", "p2ave", kb.val("p2ave") - 1)
        for dch in range(16):
            fn = lambda e, dch=dch: e.matmul(
                pbk[1][:], hTt[:, dch * 128:(dch + 1) * 128],
                wkv_s[:, 8192 + dch * 512: 8192 + (dch + 1) * 512],
                start=(dch == 0), stop=(dch == 15))
            kb.inc("tensor", fn, "p2avm") if dch == 15 else kb.do("tensor", fn)
        kb.wait("vector", "p2akm")
        kb.wait("vector", "p2akT", t * 64)
        kb.inc("vector", lambda e: e.tensor_copy(ktf[:], pbk[0][:]), "p2ake")
        kb.wait("scalar", "p2ake")
        _stats(kb, ktf[:], 512, stat[b], junkb, RMS_EPS, False, b)
        kb.wait("vector", "sst%d" % b)
        kb.do("vector", lambda e, b=b: e.tensor_scalar(
            ktf[:], ktf[:], stat[b][:, 4:5], None, op0=ALU.mult))
        kb.inc("vector", lambda e: e.tensor_tensor(
            ktf[:], ktf[:], knb_s[:], op=ALU.mult), "p2akn")
        rope_emit(ktf, 512, 0, krb, 512, 0, 4, t, "p2akr")
        kb.wait("sync", "p2akr")
        for kvh in range(4):
            kb.inc("sync", lambda e, kvh=kvh, t=t: e.dma_start_transpose(
                kTown[:, kvh * 2048 + t * 128: kvh * 2048 + (t + 1) * 128],
                krb[:, kvh * 128:(kvh + 1) * 128]), "p2akT", 16)
        kb.wait("vector", "p2avm")
        kb.wait("vector", "p2vone", 1)
        vdst = bass.AP(vaugo, t * 528, [[VAW, 128], [132, 4], [1, 128]])
        vsrc = bass.AP(pbk[1], 0, [[512, 128], [128, 4], [1, 128]])
        kb.inc("vector", lambda e, vdst=vdst, vsrc=vsrc: e.tensor_copy(
            vdst, vsrc), "p2ave")
    kb.wait("sync", "p2akT")
    kb.wait("sync", "p2ave")
    kb.inc("sync", lambda e: e.dma_start(out=kvin_d[:, 0:8192], in_=kTown[:]),
           "p2kvw", 16)
    kb.inc("sync", lambda e: e.dma_start(out=kvin_d[:, 8192:KVW], in_=vaugo[:]),
           "p2kvw", 16)
    kb.wait("gpsimd", "p2kvw")
    kb.inc("gpsimd", lambda e: e.collective_compute(
        "AllGather", ALU.bypass,
        replica_groups=[[0, 1], [2, 3], [4, 5], [6, 7]],
        ins=[kvin_d.opt()], outs=[kvout_d.opt()]), "p2cc", 1)
    kb.flush()
    al.free("p2a")

    kb.begin()
    kT = al.sb("p2b", "kT", (128, 4 * 4096), BF16)
    vaug = al.sb("p2b", "vaug", (128, 32 * 528), BF16)
    hTg2 = al.sb("p2b", "hTg2", (128, 16 * 512), BF16)
    o2Tg = hTg2
    qf4 = al.sb("p2b", "qf4", (128, 4 * 2048), BF16)
    q2Tg = al.sb("p2b", "q2Tg", (128, 16 * 512), BF16)
    wqo2 = [al.sb("p2b", "wqo%d" % i, (128, 16 * 512), BF16) for i in range(2)]
    x2s = al.sb("p2b", "x2s", (128, 1024), F32)
    s2st = al.sb("p2b", "s2st", (128, 8), F32)

    kb.wait("sync", "p2cc", 1)
    for r in range(2):
        for kvh in range(4):
            kb.inc("sync", lambda e, r=r, kvh=kvh: e.dma_start(
                out=kT[:, kvh * 4096 + r * 2048: kvh * 4096 + (r + 1) * 2048],
                in_=kvout_d[r, :, kvh * 2048:(kvh + 1) * 2048]), "p2kvr", 16)
        kb.inc("sync", lambda e, r=r: e.dma_start(
            out=vaug[:, r * VAW:(r + 1) * VAW],
            in_=kvout_d[r, :, 8192:KVW]), "p2kvr", 16)
    kb.wait("tensor", "p2kvr")
    if DBG_P2:
        kb.wait("sync", "p2kvr")
        kb.inc("sync", lambda e: e.dma_start(out=dbg_kt, in_=kT[:]), "dbg", 16)
        kb.inc("sync", lambda e: e.dma_start(out=dbg_va, in_=vaug[:]), "dbg", 16)

    st2 = {"n": 0}
    for g in range(NG):
        kb.wait("sync", "p2qm", kb.val("p2qm"))
        kb.wait("sync", "p2s2m", kb.val("p2s2m"))
        kb.wait("sync", "p2xe", kb.val("p2xe"))
        for ts4 in range(4):
            n2 = st2["n"]
            st2["n"] += 1
            t = g * 4 + ts4
            b = n2 % 2
            kb.wait("sync", "p2bhT", (n2 - 1) * 256)
            kb.wait("sync", "p2bz", n2 - 1)
            h_tile("p2b", t, b)
            kb.wait("sync", "p2bh")
            for dch in range(16):
                kb.inc("sync", lambda e, b=b, dch=dch, ts4=ts4: e.dma_start_transpose(
                    hTg2[:, dch * 512 + ts4 * 128: dch * 512 + (ts4 + 1) * 128],
                    zb[b][:, dch * 128:(dch + 1) * 128]), "p2bhT", 16)
        kb.wait("tensor", "p2bhT")
        kb.wait("vector", "p2qT", kb.val("p2qT"))
        for fb in range(4):
            wsl = wqo2[fb % 2]
            sname = "p2wq%d" % (fb % 2)
            kb.wait("gpsimd", "p2qm", kb.val("p2qm") - 4)
            kb.wait("gpsimd", "p2s2m", kb.val("p2s2m"))
            for dch in range(16):
                kb.inc("gpsimd", lambda e, dch=dch, fb=fb, wsl=wsl: e.dma_start(
                    out=wsl[:, dch * 512:(dch + 1) * 512],
                    in_=wq[dch * 128:(dch + 1) * 128, fb * 512:(fb + 1) * 512]),
                    sname, 16)
            kb.wait("tensor", sname)
            for ts4 in range(4):
                bank = pbk[ts4 % 2]
                kb.wait("tensor", "p2qe", kb.val("p2qe") - 1)
                for dch in range(16):
                    fn = lambda e, dch=dch, ts4=ts4, bank=bank, wsl=wsl: \
                        e.matmul(
                        bank[:],
                        hTg2[:, dch * 512 + ts4 * 128: dch * 512 + (ts4 + 1) * 128],
                        wsl[:, dch * 512:(dch + 1) * 512],
                        start=(dch == 0), stop=(dch == 15))
                    kb.inc("tensor", fn, "p2qm") if dch == 15 else kb.do("tensor", fn)
                kb.wait("vector", "p2qm")
                kb.inc("vector", lambda e, ts4=ts4, fb=fb, bank=bank: e.tensor_copy(
                    qf4[:, ts4 * 2048 + fb * 512: ts4 * 2048 + (fb + 1) * 512],
                    bank[:]), "p2qe")
        kb.wait("sync", "p1sm", kb.val("p1sm"))
        for ts4 in range(4):
            b = ts4 % 2
            t = g * 4 + ts4
            qsl = qf4[:, ts4 * 2048:(ts4 + 1) * 2048]
            kb.wait("vector", "p2qe", kb.val("p2qe"))
            kb.wait("scalar", "p2qe", kb.val("p2qe"))
            _stats(kb, qsl, D, stat[b], junkb, RMS_EPS, False, b)
            kb.wait("vector", "sst%d" % b)
            kb.do("vector", lambda e, qsl=qsl, b=b: e.tensor_scalar(
                qsl, qsl, stat[b][:, 4:5], None, op0=ALU.mult))
            kb.inc("vector", lambda e, qsl=qsl: e.tensor_tensor(
                qsl, qsl, qnb_s[:], op=ALU.mult), "p2qn")
            rope_emit(qf4, 4 * 2048, ts4 * 2048, qf4, 4 * 2048, ts4 * 2048,
                      16, t, "p2qr")
            kb.wait("sync", "p2qr")
            for hh in range(16):
                kb.inc("sync", lambda e, hh=hh, ts4=ts4: e.dma_start_transpose(
                    q2Tg[:, hh * 512 + ts4 * 128: hh * 512 + (ts4 + 1) * 128],
                    qf4[:, ts4 * 2048 + hh * 128: ts4 * 2048 + (hh + 1) * 128]),
                    "p2qT", 16)
        kb.wait("tensor", "p2qT")
        if DBG_P2:
            kb.wait("sync", "p2qT")
            kb.inc("sync", lambda e, g=g: e.dma_start(
                out=dbg_q2[g], in_=q2Tg[:]), "dbg", 16)
        kb.wait("tensor", "p2qe")
        kb.wait("sync", "p2s2m", kb.val("p2s2m"))
        kb.wait("tensor", "p2xe", kb.val("p2xe"))

        kts = FFN_KTS[g]
        nit = len(kts)
        colbase = sum(len(FFN_KTS[gg]) for gg in range(g))
        cand = FFN_CAND[g]
        kb.wait("sync", "p2qm", kb.val("p2qm"))
        for h in range(16):
            kvh = h // 4
            expv = {}
            maskv = {}
            pend = [None]

            def emit_pv2(j2_, kts=kts, nit=nit, expv=expv, maskv=maskv,
                         kvh=kvh):
                if j2_ in maskv:
                    kb.wait("tensor", "p1mk", maskv[j2_])
                else:
                    kb.wait("tensor", "p1ex", expv[j2_])
                if j2_ == 0:
                    kb.wait("tensor", "p1on", kb.val("p1on"))
                kt = kts[j2_]
                for sub in range(4):
                    fn = lambda e, j2_=j2_, sub=sub, kt=kt, nit=nit, kvh=kvh: \
                        e.matmul(
                        pbk[2 + sub][:, 0:129],
                        Pb[j2_ % 3][:, sub * 128:(sub + 1) * 128],
                        vaug[:, kt * 528 + kvh * 132: kt * 528 + kvh * 132 + 129],
                        start=(j2_ == 0), stop=(j2_ == nit - 1))
                    kb.inc("tensor", fn, "p1pv") if sub == 3 else kb.do("tensor", fn)

            for it, kt in enumerate(kts):
                sbk = pbk[it % 2]
                if it >= 2:
                    kb.wait("tensor", "p1ex", expv[it - 2])
                kb.inc("tensor", lambda e, kt=kt, h=h, kvh=kvh, sbk=sbk:
                       e.matmul(
                    sbk[:],
                    kT[:, kvh * 4096 + kt * 128: kvh * 4096 + (kt + 1) * 128],
                    q2Tg[:, h * 512:(h + 1) * 512], start=True, stop=True),
                    "p1sm")
                pbuf = Pb[it % 3]
                kb.wait("scalar", "p1sm")
                kb.wait("scalar", "p1pv", kb.val("p1pv"))
                col = colbase + it
                expv[it] = kb.inc("scalar", lambda e, sbk=sbk, pbuf=pbuf,
                                  col=col: e.activation(
                    pbuf[:], sbk[:], AF.Exp, bias=vffn_s[:, col:col + 1],
                    scale=SCALE), "p1ex")
                if kt in cand:
                    jj = cand.index(kt)
                    s_ = jj % 4
                    slot = g * 8 + jj
                    kb.wait("vector", "p1ex", expv[it])
                    kb.do("vector", lambda e, pbuf=pbuf, s_=s_: e.tensor_tensor(
                        Pscr[:], pbuf[:], negu_s[:, s_ * 512:(s_ + 1) * 512],
                        op=ALU.mult))
                    maskv[it] = kb.inc("vector", lambda e, pbuf=pbuf, slot=slot:
                                       e.scalar_tensor_tensor(
                        pbuf[:], Pscr[:], sel_s[:, slot:slot + 1], pbuf[:],
                        op0=ALU.mult, op1=ALU.add), "p1mk")
                pend.append(it)
                if len(pend) > 3:
                    emit_pv2(pend.pop(1))
            for j_ in pend[1:]:
                emit_pv2(j_)
            kb.wait("vector", "p1pv")
            for sub in range(3):
                kb.do("vector", lambda e, sub=sub: e.reciprocal(
                    rcol[:, sub:sub + 1], pbk[2 + sub][:, 128:129]))
            kb.inc("vector", lambda e: e.reciprocal(
                rcol[:, 3:4], pbk[5][:, 128:129]), "p1rc")
            kb.wait("scalar", "p1rc")
            kb.inc("scalar", lambda e: e.activation(rcol[:, 4:8], rcol[:, 0:4],
                                                    AF.Identity), "p1lnc")
            kb.wait("vector", "p1lnc")
            kb.wait("vector", "p1oT", kb.val("p1oT") - 64)
            for sub in range(4):
                kb.inc("vector", lambda e, sub=sub: e.tensor_scalar(
                    onb[:, sub * 128:(sub + 1) * 128], pbk[2 + sub][:, 0:128],
                    rcol[:, 4 + sub:5 + sub], None, op0=ALU.mult), "p1on")
            kb.wait("sync", "p1on")
            for sub in range(4):
                kb.inc("sync", lambda e, h=h, sub=sub: e.dma_start_transpose(
                    o2Tg[:, h * 512 + sub * 128: h * 512 + (sub + 1) * 128],
                    onb[:, sub * 128:(sub + 1) * 128]), "p1oT", 16)
        kb.wait("tensor", "p1oT")
        if DBG_P2:
            kb.wait("sync", "p1oT")
            kb.inc("sync", lambda e, g=g: e.dma_start(
                out=dbg_o2[g], in_=o2Tg[:]), "dbg", 16)
            kb.wait("tensor", "dbg")
        for tsub in range(4):
            t = g * 4 + tsub
            bx = tsub % 2
            kb.wait("sync", "p2st", kb.val("p2st"))
            kb.inc("sync", lambda e, t=t, bx=bx: e.dma_start(
                out=xb[bx][:], in_=x1_d[t * 128:(t + 1) * 128, :]), "p2xr", 16)
            kb.wait("vector", "p2xr")
            kb.wait("scalar", "p2xr")
            _stats(kb, xb[bx][:], D, stat[bx], junkb, LN_EPS, True, bx)
            kb.wait("scalar", "sst%d" % bx)
            kb.inc("scalar", lambda e, tsub=tsub, bx=bx: e.activation(
                s2st[:, tsub * 2:tsub * 2 + 1], stat[bx][:, 4:5], AF.Identity),
                "p2st")
            kb.inc("scalar", lambda e, tsub=tsub, bx=bx: e.activation(
                s2st[:, tsub * 2 + 1:tsub * 2 + 2], stat[bx][:, 6:7],
                AF.Identity), "p2st")
        for db in range(4):
            wsl = wqo2[db % 2]
            sname = "p2wq%d" % (db % 2)
            kb.wait("gpsimd", "p2s2m", kb.val("p2s2m") - 4)
            kb.wait("gpsimd", "p2qm", kb.val("p2qm"))
            for hh in range(16):
                kb.inc("gpsimd", lambda e, hh=hh, db=db, wsl=wsl: e.dma_start(
                    out=wsl[:, hh * 512:(hh + 1) * 512],
                    in_=wo[hh * 128:(hh + 1) * 128, db * 512:(db + 1) * 512]),
                    sname, 16)
            kb.wait("tensor", sname)
            for tsub in range(4):
                t = g * 4 + tsub
                i2 = db * 4 + tsub
                bank = pbk[2 + i2 % 2]
                xsl = slice((i2 % 2) * 512, (i2 % 2) * 512 + 512)
                kb.wait("sync", "p2xs", kb.val("p2xs") - 16)
                kb.inc("sync", lambda e, t=t, db=db, xsl=xsl: e.dma_start(
                    out=x2s[:, xsl], in_=x1_d[t * 128:(t + 1) * 128,
                                              db * 512:(db + 1) * 512]),
                    "p2xr", 16)
                kb.wait("tensor", "p2xe", kb.val("p2xe") - 1)
                for hh in range(16):
                    fn = lambda e, hh=hh, db=db, tsub=tsub, bank=bank, wsl=wsl: \
                        e.matmul(
                        bank[:],
                        o2Tg[:, hh * 512 + tsub * 128: hh * 512 + (tsub + 1) * 128],
                        wsl[:, hh * 512:(hh + 1) * 512],
                        start=(hh == 0), stop=(hh == 15))
                    kb.inc("tensor", fn, "p2s2m") if hh == 15 else kb.do("tensor", fn)
                kb.wait("vector", "p2s2m")
                kb.wait("vector", "p2xr")
                kb.wait("vector", "p2st", kb.val("p2st"))
                kb.wait("vector", "p2xs", kb.val("p2xs") - 16)
                kb.do("vector", lambda e, xsl=xsl, tsub=tsub, bank=bank:
                      e.scalar_tensor_tensor(
                    x2s[:, xsl], x2s[:, xsl], s2st[:, tsub * 2:tsub * 2 + 1],
                    bank[:], op0=ALU.mult, op1=ALU.add))
                kb.inc("vector", lambda e, xsl=xsl, tsub=tsub: e.tensor_scalar(
                    x2s[:, xsl], x2s[:, xsl],
                    s2st[:, tsub * 2 + 1:tsub * 2 + 2], None,
                    op0=ALU.subtract), "p2xe")
                kb.wait("sync", "p2xe")
                kb.inc("sync", lambda e, t=t, db=db, xsl=xsl: e.dma_start(
                    out=x2_d[t * 128:(t + 1) * 128, db * 512:(db + 1) * 512],
                    in_=x2s[:, xsl]), "p2xs", 16)
    kb.flush()
    al.free("p2b")
    al.free("p2tb")
    al.free("attnsh")

    if n_phases == 2:
        _copy_out(kb, al, x2_d, y)
        stack.close()
        return nc

    kb.begin()
    ln2b_s = al.sb("p3", "ln2b_s", (128, D), F32)
    rbb_s = al.sb("p3", "rbb_s", (128, 8), F32)
    comb_s = al.sb("p3", "comb_s", (128, 16 * 8), F32)
    for s_, d_ in ((ln2b, ln2b_s), (rbb, rbb_s)):
        kb.inc("sync", lambda e, s=s_, d=d_: e.dma_start(out=d[:], in_=s),
               "p3tab", 16)
    x2m = al.sb("p3", "x2m", (128, 8 * 2048), F32)
    h2T = al.sb("p3", "h2T", (128, 16 * 1024), BF16)
    h2f = xb[0]
    h2Tf = al.sb("p3", "h2Tf", (128, 128), F32)
    rws = al.sb("p3", "rws", (128, 16 * 8), BF16)
    sig = al.sb("p3", "sig", (128, 8), F32)
    sg2 = al.sb("p3", "sg2", (128, 32), F32)
    sigall = al.sb("p3", "sigall", (128, 64), F32)
    rt3 = al.sb("p3", "rt3", (128, 224), F32)
    actT = al.sb("p3", "actT", (128, 8 * 1024), BF16)
    gsil = al.sb("p3", "gsil", (128, 1024), BF16)
    wgs = [al.sb("p3", "wgs%d" % i, (128, 16 * 128), BF16) for i in range(2)]
    wus = [al.sb("p3", "wus%d" % i, (128, 16 * 128), BF16) for i in range(2)]
    wds = al.sb("p3", "wds", (128, 8 * 2048), BF16)

    for ch in range(16):
        kb.inc("gpsimd", lambda e, ch=ch: e.dma_start(
            out=rws[:, ch * 8:(ch + 1) * 8],
            in_=rw[ch * 128:(ch + 1) * 128, :]), "p3rw", 16)

    for hf in range(2):
        kb.wait("vector", "p3gm", kb.val("p3gm"))
        for tl in range(8):
            t = hf * 8 + tl
            kb.wait("sync", "p3ys", hf * 8 * 16)
            kb.inc("sync", lambda e, t=t, tl=tl: e.dma_start(
                out=x2m[:, tl * 2048:(tl + 1) * 2048],
                in_=x2_d[t * 128:(t + 1) * 128, :]), "p3xl", 16)
        for w in range(2):
            tiles = [w * 4 + i for i in range(4)]
            kb.wait("vector", "p3xl", (hf * 8 + tiles[-1] + 1) * 16)
            kb.wait("scalar", "p3xl", (hf * 8 + tiles[-1] + 1) * 16)
            _stats4(kb, [x2m[:, tl * 2048:(tl + 1) * 2048] for tl in tiles],
                    D, st4, junkb, RMS_EPS, False)
            kb.wait("vector", "sst0")
            for i, tl in enumerate(tiles):
                b = tl % 2
                tg_ = hf * 8 + tl
                kb.wait("vector", "p3hT", (tg_ - 1) * 256)
                xt = x2m[:, tl * 2048:(tl + 1) * 2048]
                kb.do("vector", lambda e, xt=xt, i=i, b=b: e.tensor_scalar(
                    zb[b][:], xt, st4[:, i * 8 + 4:i * 8 + 5], None,
                    op0=ALU.mult))
                kb.inc("vector", lambda e, b=b: e.tensor_tensor(
                    zb[b][:], zb[b][:], ln2b_s[:], op=ALU.mult), "p3h2")
                kb.wait("sync", "p3h2")
                for dch in range(16):
                    kb.inc("sync", lambda e, b=b, dch=dch, tl=tl:
                           e.dma_start_transpose(
                        h2T[:, dch * 1024 + tl * 128: dch * 1024 + (tl + 1) * 128],
                        zb[b][:, dch * 128:(dch + 1) * 128]), "p3hT", 16)
                kb.wait("tensor", "p3hT")
                kb.wait("tensor", "p3rw", 256)
                kb.wait("tensor", "p3tab", 32)
                kb.wait("tensor", "p3sg", kb.val("p3sg"))
                for dch in range(16):
                    fn = lambda e, dch=dch, tl=tl: e.matmul(
                        pbk[7][:, 0:8],
                        h2T[:, dch * 1024 + tl * 128: dch * 1024 + (tl + 1) * 128],
                        rws[:, dch * 8:(dch + 1) * 8],
                        start=(dch == 0), stop=(dch == 15))
                    kb.inc("tensor", fn, "p3rm") if dch == 15 \
                        else kb.do("tensor", fn)
                kb.wait("scalar", "p3rm")
                kb.inc("scalar", lambda e, tl=tl: e.activation(
                    sigall[:, tl * 8:(tl + 1) * 8], pbk[7][:, 0:8], AF.Sigmoid),
                    "p3sg")
        kb.wait("vector", "p3sg")
        W0 = rt3[:, 0:64]
        W1 = rt3[:, 64:128]
        W2 = rt3[:, 128:192]
        R0 = rt3[:, 192:200]
        R1 = rt3[:, 200:208]
        R2 = rt3[:, 208:216]
        RT3W = 224

        def r1rep(off):
            return bass.AP(rt3, off, [[RT3W, 128], [1, 8], [0, 8]])
        rbbrep = bass.AP(rbb_s, 0, [[8, 128], [0, 8], [1, 8]])
        sig3 = bass.AP(sigall, 0, [[64, 128], [8, 8], [1, 8]])
        w03 = bass.AP(rt3, 0, [[RT3W, 128], [8, 8], [1, 8]])
        w13 = bass.AP(rt3, 64, [[RT3W, 128], [8, 8], [1, 8]])

        def vst(fn):
            kb.inc("vector", fn, "p3sg")
            kb.wait("scalar", "p3sg")

        def sco(dst, srcc):
            kb.inc("scalar", lambda e, d=dst, s=srcc: e.activation(
                d, s, AF.Identity), "p3sg")
            kb.wait("vector", "p3sg")

        vst(lambda e: e.tensor_tensor(w03, sig3, rbbrep, op=ALU.add))
        sco(W1, W0)
        for tl in range(8):
            fn = lambda e, tl=tl: e.reduce_max(
                R0[:, tl:tl + 1], W1[:, tl * 8:(tl + 1) * 8],
                axis=mybir.AxisListType.X)
            kb.inc("vector", fn, "p3sg") if tl == 7 else kb.do("vector", fn)
        kb.wait("scalar", "p3sg")
        sco(R1, R0)
        vst(lambda e: e.tensor_tensor(w03, w13, r1rep(200), op=ALU.is_ge))
        sco(W2, W0)
        vst(lambda e: e.scalar_tensor_tensor(W0, W2, NEG, W1,
                                             op0=ALU.mult, op1=ALU.add))
        sco(W1, W0)
        for tl in range(8):
            fn = lambda e, tl=tl: e.reduce_max(
                R0[:, tl:tl + 1], W1[:, tl * 8:(tl + 1) * 8],
                axis=mybir.AxisListType.X)
            kb.inc("vector", fn, "p3sg") if tl == 7 else kb.do("vector", fn)
        kb.wait("scalar", "p3sg")
        sco(R1, R0)
        vst(lambda e: e.tensor_tensor(w03, w13, r1rep(200), op=ALU.is_ge))
        sco(W1, W0)
        vst(lambda e: e.tensor_tensor(W0, W2, W1, op=ALU.add))
        sco(W2, W0)
        vst(lambda e: e.tensor_tensor(W0, W2, sigall[:], op=ALU.mult))
        sco(W1, W0)
        for tl in range(8):
            fn = lambda e, tl=tl: e.reduce_sum(
                R0[:, tl:tl + 1], W1[:, tl * 8:(tl + 1) * 8],
                axis=mybir.AxisListType.X)
            kb.inc("vector", fn, "p3sg") if tl == 7 else kb.do("vector", fn)
        kb.wait("scalar", "p3sg")
        sco(R1, R0)
        vst(lambda e: e.reciprocal(R2[:], R1[:]))
        sco(R0, R2)
        vst(lambda e: e.tensor_tensor(w03, w13, r1rep(192), op=ALU.mult))
        kb.inc("scalar", lambda e, hf=hf: e.activation(
            comb_s[:, hf * 64:(hf + 1) * 64], W0, AF.Identity), "p3cb")
        kb.wait("tensor", "p3hT", kb.val("p3hT"))
        if DBG_P3 and hf == 0:
            kb.wait("sync", "p3cb", kb.val("p3cb"))
            kb.wait("sync", "p3sg", kb.val("p3sg"))
            kb.inc("sync", lambda e: e.dma_start(
                out=dbg_sg[:, 0:24], in_=sg2[:, 0:24]), "p3x1", 16)
            kb.inc("sync", lambda e: e.dma_start(
                out=dbg_sg[:, 24:32], in_=sig[:]), "p3x1", 16)
            kb.wait("sync", "p3hT", kb.val("p3hT"))
            kb.inc("sync", lambda e: e.dma_start(out=dbg_cb, in_=comb_s[:]),
                   "p3x1", 16)
            kb.inc("sync", lambda e: e.dma_start(out=dbg_h2, in_=h2T[:]),
                   "p3x1", 16)
        for e_ in range(E):
            kb.wait("gpsimd", "p3dn", kb.val("p3dn"))
            for ich in range(8):
                kb.inc("gpsimd", lambda e, e_=e_, ich=ich: e.dma_start(
                    out=wds[:, ich * 2048:(ich + 1) * 2048],
                    in_=wdp[e_, ich * 128:(ich + 1) * 128, :]), "p3wd", 16)
            kb.wait("vector", "p3dn", kb.val("p3dn"))
            for i128 in range(8):
                sl_ = i128 % 2
                wb_ = wgs[sl_]
                wub = wus[sl_]
                kb.wait("gpsimd", "p3gm", kb.val("p3gm") - 4)
                kb.inc("gpsimd", lambda e, e_=e_, i128=i128, wb_=wb_: e.dma_start(
                    out=wb_[:], in_=wg_r[e_, i128, :, :]), "p3wg%d" % sl_, 16)
                kb.inc("gpsimd", lambda e, e_=e_, i128=i128, wub=wub: e.dma_start(
                    out=wub[:], in_=wu_r[e_, i128, :, :]), "p3wg%d" % sl_, 16)
                kb.wait("tensor", "p3wg%d" % sl_)
                kb.wait("tensor", "p3sl", kb.val("p3sl"))
                kb.wait("tensor", "p3ae", kb.val("p3ae"))
                for tg in range(2):
                    for dch in range(16):
                        fn = lambda e, dch=dch, tg=tg, wb_=wb_: e.matmul(
                            pbk[tg][:], wb_[:, dch * 128:(dch + 1) * 128],
                            h2T[:, dch * 1024 + tg * 512: dch * 1024 + (tg + 1) * 512],
                            start=(dch == 0), stop=(dch == 15))
                        kb.inc("tensor", fn, "p3gm") if dch == 15 \
                            else kb.do("tensor", fn)
                        fn2 = lambda e, dch=dch, tg=tg, wub=wub: e.matmul(
                            pbk[2 + tg][:], wub[:, dch * 128:(dch + 1) * 128],
                            h2T[:, dch * 1024 + tg * 512: dch * 1024 + (tg + 1) * 512],
                            start=(dch == 0), stop=(dch == 15))
                        kb.inc("tensor", fn2, "p3gm") if dch == 15 \
                            else kb.do("tensor", fn2)
                kb.wait("scalar", "p3gm")
                for tg in range(2):
                    kb.inc("scalar", lambda e, tg=tg: e.activation(
                        gsil[:, tg * 512:(tg + 1) * 512], pbk[tg][:], AF.Silu),
                        "p3sl")
                kb.wait("vector", "p3sl")
                for tg in range(2):
                    kb.inc("vector", lambda e, tg=tg, i128=i128: e.tensor_tensor(
                        actT[:, i128 * 1024 + tg * 512: i128 * 1024 + (tg + 1) * 512],
                        gsil[:, tg * 512:(tg + 1) * 512], pbk[2 + tg][:],
                        op=ALU.mult), "p3ae")
            kb.wait("tensor", "p3ae")
            kb.wait("tensor", "p3wd")
            for tl in range(8):
                t = hf * 8 + tl
                for db in range(4):
                    bank = pbk[4 + (tl * 4 + db) % 2]
                    kb.wait("tensor", "p3de", kb.val("p3de") - 1)
                    for ich in range(8):
                        fn = lambda e, ich=ich, tl=tl, db=db, bank=bank: \
                            e.matmul(
                            bank[:],
                            actT[:, ich * 1024 + tl * 128: ich * 1024 + (tl + 1) * 128],
                            wds[:, ich * 2048 + db * 512: ich * 2048 + (db + 1) * 512],
                            start=(ich == 0), stop=(ich == 7))
                        kb.inc("tensor", fn, "p3dn") if ich == 7 \
                            else kb.do("tensor", fn)
                    kb.wait("vector", "p3dn")
                    kb.wait("vector", "p3cb", hf + 1)
                    kb.inc("vector", lambda e, tl=tl, db=db, t=t, e_=e_,
                           bank=bank: e.scalar_tensor_tensor(
                        x2m[:, tl * 2048 + db * 512: tl * 2048 + (db + 1) * 512],
                        bank[:], comb_s[:, t * 8 + e_: t * 8 + e_ + 1],
                        x2m[:, tl * 2048 + db * 512: tl * 2048 + (db + 1) * 512],
                        op0=ALU.mult, op1=ALU.add), "p3de")
        for tl in range(8):
            t = hf * 8 + tl
            kb.wait("sync", "p3xa", kb.val("p3xa"))
            kb.inc("sync", lambda e, t=t: e.dma_start(
                out=xb[1][:], in_=x1_d[t * 128:(t + 1) * 128, :]), "p3x1", 16)
            kb.wait("vector", "p3x1")
            kb.wait("vector", "p3de", kb.val("p3de"))
            kb.inc("vector", lambda e, tl=tl: e.tensor_tensor(
                x2m[:, tl * 2048:(tl + 1) * 2048],
                x2m[:, tl * 2048:(tl + 1) * 2048], xb[1][:], op=ALU.add),
                "p3xa")
            kb.wait("sync", "p3xa")
            kb.inc("sync", lambda e, t=t, tl=tl: e.dma_start(
                out=y[t * 128:(t + 1) * 128, :],
                in_=x2m[:, tl * 2048:(tl + 1) * 2048]), "p3ys", 16)
    kb.flush()
    al.free("p3")
    al.free("base")
    stack.close()
    return nc


def _copy_out(kb, al, src_d, y):
    kb.begin()
    dbg = al.sb("dbg", "dbgt", (128, D), F32)
    for t in range(16):
        kb.wait("sync", "dst", (t - 1) * 16)
        kb.inc("sync", lambda e, t=t: e.dma_start(
            out=dbg[:], in_=src_d[t * 128:(t + 1) * 128, :]), "dld", 16)
        kb.wait("sync", "dld")
        kb.inc("sync", lambda e, t=t: e.dma_start(
            out=y[t * 128:(t + 1) * 128, :], in_=dbg[:]), "dst", 16)
    kb.flush()
    al.free("dbg")
    al.free("base")



def _host_tables(H):
    t = {}
    t["identf"] = np.eye(128, dtype=np.float32)
    i = np.arange(128)[:, None]
    j = np.arange(128)[None, :]
    t["tri01"] = (i <= j).astype(ml_dtypes.bfloat16)
    negu = np.zeros((128, 4, 512), np.float32)
    for s in range(4):
        m = np.ones((128, 512), np.float32)
        m[:, : s * 128] = 0.0
        m[:, s * 128:(s + 1) * 128] = (i <= j).astype(np.float32)
        negu[:, s, :] = -(1.0 - m)
    t["negu"] = negu.reshape(128, 4 * 512).astype(ml_dtypes.bfloat16)
    sel = np.zeros((128, 32), np.float32)
    for g in range(NG):
        for fam in range(2):
            for s in range(4):
                sel[:, g * 8 + fam * 4 + s] = 1.0 if fam == H else 0.0
    t["sel"] = sel
    amla_t = np.full((128, MLA_NC), NEG, np.float32)
    ii = np.arange(128)
    colb = 0
    for g in range(NG):
        Q0 = 2048 * H + 512 * g
        for it, kt in enumerate(MLA_KTS[g]):
            if kt < 4 * g + 4:
                kpos = kt * 128 + ii + 2048 * H
                amla_t[:, colb + it] = SLOPE * (kpos - Q0)
            elif H == 1:
                kpos = (kt * 128 + ii + 2048) % 4096
                amla_t[:, colb + it] = SLOPE * (kpos - Q0)
        colb += len(MLA_KTS[g])
    t["amla"] = amla_t
    vffn_t = np.full((128, FFN_NC), NEG, np.float32)
    colb = 0
    for g in range(NG):
        for it, kt in enumerate(FFN_KTS[g]):
            if kt <= 16 * H + 4 * g + 3:
                vffn_t[:, colb + it] = 0.0
        colb += len(FFN_KTS[g])
    t["vffn"] = vffn_t
    pos = (np.arange(T) + 2048 * H).astype(np.float64)
    invf = 1.0 / (1e6 ** (np.arange(0, 64, 2) / 64.0))
    ang = pos[:, None] * invf[None, :]
    cs = np.zeros((128, 16 * 64), np.float32)
    for tt in range(16):
        cs[:, tt * 64: tt * 64 + 32] = np.cos(ang[tt * 128:(tt + 1) * 128])
        cs[:, tt * 64 + 32: tt * 64 + 64] = np.sin(ang[tt * 128:(tt + 1) * 128])
    t["cossin"] = cs
    return t


_NC_CACHE = {}


def kernel(**inp):
    g = lambda k: np.ascontiguousarray(np.asarray(inp[k], dtype=np.float32))
    x0f = g("L0_attn_hidden_states")

    def restage(w):
        return np.ascontiguousarray(
            w.reshape(E, 16, 128, 8, 128).transpose(0, 3, 2, 1, 4)
            .reshape(E, 8, 128, 16 * 128))

    common = {
        "w_dkv": g("L0_attn_w_dkv"), "w_q": g("L0_attn_w_q"),
        "w_uk": g("L0_attn_w_uk"), "w_uv": g("L0_attn_w_uv"),
        "w_o": g("L0_attn_w_o"), "wq": g("L0_ffn_wq"), "wk": g("L0_ffn_wk"),
        "wv": g("L0_ffn_wv"), "wo": g("L0_ffn_wo"),
        "router_w": g("L0_ffn_router_w"),
        "wg_r": restage(g("L0_ffn_wg")), "wu_r": restage(g("L0_ffn_wu")),
        "wd": g("L0_ffn_wd"),
        "ln1b": np.ascontiguousarray(
            np.broadcast_to(g("L0_ffn_ln1_w"), (128, D))),
        "ln2b": np.ascontiguousarray(
            np.broadcast_to(g("L0_ffn_ln2_w"), (128, D))),
        "qnb": np.ascontiguousarray(
            np.broadcast_to(g("L0_ffn_qnorm_w"), (128, D))),
        "knb": np.ascontiguousarray(
            np.broadcast_to(g("L0_ffn_knorm_w"), (128, 512))),
        "rbb": np.ascontiguousarray(
            np.broadcast_to(g("L0_ffn_router_bias"), (128, 8))),
    }
    tabs = [_host_tables(H) for H in range(2)]

    in_maps = []
    for c in range(8):
        b, H = c // 2, c % 2
        m = dict(common)
        m.update(tabs[H])
        m["x0"] = np.ascontiguousarray(np.roll(x0f[b], -2048 * H, axis=0))
        in_maps.append(m)

    if "nc" not in _NC_CACHE:
        _NC_CACHE["nc"] = build_program()
    nc = _NC_CACHE["nc"]
    res = run_bass_kernel_spmd(nc, in_maps, core_ids=list(range(8)))
    out = np.zeros((B, S, D), np.float32)
    for c in range(8):
        b, H = c // 2, c % 2
        out[b, 2048 * H: 2048 * (H + 1), :] = res.results[c]["y"]
    return out


# version 08
# speedup vs baseline: 1.1070x; 1.0165x over previous
import math
import numpy as np
import ml_dtypes
from contextlib import ExitStack

import concourse.bass as bass
import concourse.mybir as mybir
from concourse.bass_utils import run_bass_kernel_spmd

F32 = mybir.dt.float32
BF16 = mybir.dt.bfloat16
AF = mybir.ActivationFunctionType
ALU = mybir.AluOpType

B, S, D = 4, 4096, 2048
T = 2048
E, I = 8, 1024
NG = 4
SLOPE = 0.0625
SCALE = 1.0 / math.sqrt(128.0)
NEG = -1e30
LN_EPS = 1e-5
RMS_EPS = 1e-6

MLA_KTS = [list(range(4 * g + 4)) + list(range(16, 32)) for g in range(NG)]
FFN_KTS = [list(range(20 + 4 * g)) for g in range(NG)]
FFN_CAND = [[4 * g + s for s in range(4)] + [16 + 4 * g + s for s in range(4)]
            for g in range(NG)]
MLA_NC = sum(len(k) for k in MLA_KTS)
FFN_NC = sum(len(k) for k in FFN_KTS)
VAW = 16 * 528
KVW = 4 * 2048 + VAW

N_PHASES = 3
DBG_P2 = False
DBG_P3 = False
ENGINES = ("sync", "scalar", "vector", "tensor", "gpsimd")


class KB:
    def __init__(self, nc, stack):
        self.nc = nc
        self.stack = stack
        self.sems = {}
        self.dma_sems = set()
        self.q = None

    def sem(self, name):
        if name not in self.sems:
            h = self.stack.enter_context(self.nc.semaphore(name))
            self.sems[name] = [h, 0]
        return self.sems[name]

    def on(self, eng, fn):
        self.q[eng].append(fn)

    def wait(self, eng, name, val=None):
        s = self.sem(name)
        v = s[1] if val is None else val
        if v <= 0:
            return
        self.on(eng, lambda e, h=s[0], v=v: e.wait_ge(h, v))

    def inc(self, eng, fn, name, k=1):
        s = self.sem(name)
        s[1] += k
        if k == 16:
            self.dma_sems.add(name)
        self.on(eng, lambda e, fn=fn, h=s[0], k=k: fn(e).then_inc(h, k))
        return s[1]

    def do(self, eng, fn):
        self.on(eng, fn)

    def val(self, name):
        return self.sem(name)[1]

    def begin(self):
        self.q = {e: [] for e in ENGINES}

    def flush(self):
        for name in sorted(self.dma_sems):
            self.wait("sync", name)
        with self.nc.Block() as blk:
            for e in ENGINES:
                fns = self.q[e]
                if fns:
                    getattr(blk, e)(lambda eng, fns=fns: [f(eng) for f in fns])
        self.q = None


class Alloc:
    def __init__(self, nc):
        self.nc = nc
        self.groups = {}
        self.order = []

    def sb(self, group, name, shape, dt):
        cm = self.nc.sbuf_tensor(name, list(shape), dt)
        h = cm.__enter__()
        if group not in self.groups:
            self.groups[group] = []
            self.order.append(group)
        self.groups[group].append(cm)
        return h

    def free(self, group):
        assert self.order and self.order[-1] == group, (self.order, group)
        for cm in reversed(self.groups.pop(group)):
            cm.__exit__(None, None, None)
        self.order.pop()


def _stats(kb, xt, width, st, junk, eps, mean, b):
    sem = "sst%d" % b
    if mean:
        kb.inc("vector", lambda e: e.reduce_sum(st[:, 0:1], xt,
                                                axis=mybir.AxisListType.X), sem)
    kb.do("scalar", lambda e: e.activation(junk[:, 0:width], xt, AF.Square,
                                           accum_out=st[:, 2:3]))
    kb.do("scalar", lambda e: e.activation(st[:, 3:4], st[:, 2:3],
                                           AF.Identity, scale=1.0 / width))
    if mean:
        kb.wait("scalar", sem)
        kb.do("scalar", lambda e: e.activation(st[:, 1:2], st[:, 0:1],
                                               AF.Identity, scale=1.0 / width))
        kb.inc("scalar", lambda e: e.activation(st[:, 7:8], st[:, 1:2],
                                                AF.Square), sem)
        kb.wait("vector", sem)
        kb.inc("vector", lambda e: e.tensor_scalar(
            st[:, 3:4], st[:, 3:4], st[:, 7:8], eps,
            op0=ALU.subtract, op1=ALU.add), sem)
    else:
        kb.inc("scalar", lambda e: e.activation(st[:, 3:4], st[:, 3:4],
                                                AF.Identity), sem)
        kb.wait("vector", sem)
        kb.inc("vector", lambda e: e.tensor_scalar_add(st[:, 3:4], st[:, 3:4],
                                                       eps), sem)
    kb.wait("scalar", sem)
    kb.do("scalar", lambda e: e.activation(st[:, 4:5], st[:, 3:4], AF.Ln))
    kb.inc("scalar", lambda e: e.activation(st[:, 4:5], st[:, 4:5], AF.Exp,
                                            scale=-0.5), sem)
    if mean:
        kb.wait("vector", sem)
        kb.inc("vector", lambda e: e.tensor_tensor(st[:, 5:6], st[:, 1:2],
                                                   st[:, 4:5], op=ALU.mult),
               sem)
        kb.wait("scalar", sem)
        kb.inc("scalar", lambda e: e.activation(st[:, 6:7], st[:, 5:6],
                                                AF.Identity), sem)


def _stats4(kb, xts, width, st4, junk, eps, mean):
    n = len(xts)
    sem = "sst0"
    if mean:
        for i, xt in enumerate(xts):
            fn = lambda e, i=i, xt=xt: e.reduce_sum(
                st4[:, i * 8:i * 8 + 1], xt, axis=mybir.AxisListType.X)
            kb.inc("vector", fn, sem) if i == n - 1 else kb.do("vector", fn)
        kb.wait("scalar", sem)
    for i, xt in enumerate(xts):
        kb.do("scalar", lambda e, i=i, xt=xt: e.activation(
            junk[:, 0:width], xt, AF.Square, accum_out=st4[:, i * 8 + 2:i * 8 + 3]))
        kb.do("scalar", lambda e, i=i: e.activation(
            st4[:, i * 8 + 3:i * 8 + 4], st4[:, i * 8 + 2:i * 8 + 3],
            AF.Identity, scale=1.0 / width))
    if mean:
        for i in range(n):
            kb.do("scalar", lambda e, i=i: e.activation(
                st4[:, i * 8 + 1:i * 8 + 2], st4[:, i * 8:i * 8 + 1],
                AF.Identity, scale=1.0 / width))
            kb.do("scalar", lambda e, i=i: e.activation(
                st4[:, i * 8 + 7:i * 8 + 8], st4[:, i * 8 + 1:i * 8 + 2],
                AF.Square))
    kb.inc("scalar", lambda e: e.activation(junk[:, 0:1], junk[:, 0:1],
                                            AF.Identity), sem)
    kb.wait("vector", sem)
    for i in range(n):
        if mean:
            fn = lambda e, i=i: e.tensor_scalar(
                st4[:, i * 8 + 3:i * 8 + 4], st4[:, i * 8 + 3:i * 8 + 4],
                st4[:, i * 8 + 7:i * 8 + 8], eps,
                op0=ALU.subtract, op1=ALU.add)
        else:
            fn = lambda e, i=i: e.tensor_scalar_add(
                st4[:, i * 8 + 3:i * 8 + 4], st4[:, i * 8 + 3:i * 8 + 4], eps)
        kb.inc("vector", fn, sem) if i == n - 1 else kb.do("vector", fn)
    kb.wait("scalar", sem)
    for i in range(n):
        kb.do("scalar", lambda e, i=i: e.activation(
            st4[:, i * 8 + 4:i * 8 + 5], st4[:, i * 8 + 3:i * 8 + 4], AF.Ln))
        fn = lambda e, i=i: e.activation(
            st4[:, i * 8 + 4:i * 8 + 5], st4[:, i * 8 + 4:i * 8 + 5],
            AF.Exp, scale=-0.5)
        kb.inc("scalar", fn, sem) if i == n - 1 else kb.do("scalar", fn)
    if mean:
        kb.wait("vector", sem)
        for i in range(n):
            fn = lambda e, i=i: e.tensor_tensor(
                st4[:, i * 8 + 5:i * 8 + 6], st4[:, i * 8 + 1:i * 8 + 2],
                st4[:, i * 8 + 4:i * 8 + 5], op=ALU.mult)
            kb.inc("vector", fn, sem) if i == n - 1 else kb.do("vector", fn)
        kb.wait("scalar", sem)
        for i in range(n):
            fn = lambda e, i=i: e.activation(
                st4[:, i * 8 + 6:i * 8 + 7], st4[:, i * 8 + 5:i * 8 + 6],
                AF.Identity)
            kb.inc("scalar", fn, sem) if i == n - 1 else kb.do("scalar", fn)


def build_program(n_phases=N_PHASES):
    nc = bass.Bass("TRN2", target_bir_lowering=False,
                   detect_race_conditions=False)
    stack = ExitStack()
    kb = KB(nc, stack)
    al = Alloc(nc)

    def din(name, shape, dt=F32):
        return nc.dram_tensor(name, list(shape), dt, kind="ExternalInput").ap()

    x0 = din("x0", (S, D))
    w_dkv = din("w_dkv", (128, D))
    w_q = din("w_q", (D, D))
    w_uk = din("w_uk", (D, 128))
    w_uv = din("w_uv", (D, 128))
    w_o = din("w_o", (D, D))
    wq = din("wq", (D, D))
    wk = din("wk", (D, 512))
    wv = din("wv", (D, 512))
    wo = din("wo", (D, D))
    rw = din("router_w", (D, 8))
    wg_r = din("wg_r", (E, 8, 128, 16 * 128))
    wu_r = din("wu_r", (E, 8, 128, 16 * 128))
    wdp = din("wd", (E, I, D))
    identf = din("identf", (128, 128))
    tri01 = din("tri01", (128, 128), BF16)
    negu = din("negu", (128, 4 * 512), BF16)
    seli = din("sel", (128, 32))
    amla = din("amla", (128, MLA_NC))
    vffn = din("vffn", (128, FFN_NC))
    cossin = din("cossin", (128, 16 * 64))
    ln1b = din("ln1b", (128, D))
    ln2b = din("ln2b", (128, D))
    qnb = din("qnb", (128, D))
    knb = din("knb", (128, 512))
    rbb = din("rbb", (128, 8))

    y = nc.dram_tensor("y", [T, D], F32, kind="ExternalOutput").ap()

    wqk_d = nc.dram_tensor("wqk_d", [16, 128, 16 * 128], BF16).ap()
    x1_d = nc.dram_tensor("x1_d", [T, D], F32).ap()
    x2_d = nc.dram_tensor("x2_d", [T, D], F32).ap()
    kvin_d = nc.dram_tensor("kvin_d", [128, KVW], BF16).ap()
    kvout_d = nc.dram_tensor("kvout_d", [2, 128, KVW], BF16).ap()
    if DBG_P3:
        dbg_cb = nc.dram_tensor("dbg_cb", [128, 128], F32,
                                kind="ExternalOutput").ap()
        dbg_sg = nc.dram_tensor("dbg_sg", [128, 32], F32,
                                kind="ExternalOutput").ap()
        dbg_h2 = nc.dram_tensor("dbg_h2", [128, 16 * 1024], BF16,
                                kind="ExternalOutput").ap()
    if DBG_P2:
        dbg_kt = nc.dram_tensor("dbg_kt", [128, 4 * 4096], BF16,
                                kind="ExternalOutput").ap()
        dbg_va = nc.dram_tensor("dbg_va", [128, 32 * 528], BF16,
                                kind="ExternalOutput").ap()
        dbg_q2 = nc.dram_tensor("dbg_q2", [4, 128, 16 * 512], BF16,
                                kind="ExternalOutput").ap()
        dbg_o2 = nc.dram_tensor("dbg_o2", [4, 128, 16 * 512], BF16,
                                kind="ExternalOutput").ap()

    bsb = lambda n, sh, dt: al.sb("base", n, sh, dt)
    xb = [bsb("xb%d" % i, (128, D), F32) for i in range(2)]
    zb = [bsb("zb%d" % i, (128, D), BF16) for i in range(2)]
    stat = [bsb("st%d" % i, (128, 8), F32) for i in range(2)]
    st4 = bsb("st4", (128, 32), F32)
    junkb = bsb("junkb", (128, D), BF16)
    asb = lambda n, sh, dt: al.sb("attnsh", n, sh, dt)
    amla_s = asb("amla_s", (128, MLA_NC), F32)
    tri_s = asb("tri_s", (128, 128), BF16)
    Pb = [asb("Pb%d" % i, (128, 512), BF16) for i in range(3)]
    Pscr = asb("Pscr", (128, 512), BF16)
    rcol = asb("rcol", (128, 8), F32)
    onb = asb("onb", (128, 512), BF16)

    pbk = [stack.enter_context(nc.psum_tensor("pbk%d" % i, [128, 512], F32))
           for i in range(8)]

    kb.begin()
    wdkvT_s = al.sb("p1per", "wdkvT_s", (128, 16 * 128), BF16)
    w2_s = al.sb("p1per", "w2_s", (128, 16 * 2048), BF16)
    ckvT_s = al.sb("p1per", "ckvT_s", (128, 4096), BF16)
    ckvaug_s = al.sb("p1per", "ckvaug_s", (128, 32 * 144), BF16)
    lnT_g = al.sb("p1per", "lnT_g", (128, 16 * 512), BF16)
    xb4 = al.sb("p1per", "xb4", (128, 4 * 2048), BF16)
    qtg = al.sb("p1per", "qtg", (128, 16 * 512), BF16)
    wqks = [al.sb("p1per", "wqks%d" % i, (128, 2048), BF16) for i in range(2)]
    oTg = al.sb("p1per", "oTg", (128, 16 * 512), BF16)
    x1t = [al.sb("p1per", "x1t0", (128, D), F32)] * 2

    wuv_s = al.sb("p0tmp", "p0_wuv", (128, 16 * 128), BF16)
    wuk_s = wuv_s
    evtile = al.sb("p0tmp", "p0_ev", (128, 128), BF16)
    t128 = [al.sb("p0tmp", "p0_t%d" % i, (128, 128), BF16) for i in range(2)]
    wslab = [al.sb("p0tmp", "p0_ws0", (128, 2048), BF16)] * 2

    for s_, d_ in ((amla, amla_s), (tri01, tri_s)):
        kb.inc("sync", lambda e, s=s_, d=d_: e.dma_start(out=d[:], in_=s),
               "p0m", 16)
    kb.inc("vector", lambda e: e.memset(ckvaug_s[:], 2.0 ** -24), "p0aug", 1)

    for h in range(16):
        kb.inc("gpsimd", lambda e, h=h: e.dma_start(
            out=wuv_s[:, h * 128:(h + 1) * 128],
            in_=w_uv[h * 128:(h + 1) * 128, :]), "p0g", 16)

    kb.wait("tensor", "p0g")
    for ci in range(16):
        sl_ = ci % 2
        wsl = wslab[sl_]
        kb.wait("gpsimd", "p0w2m", kb.val("p0w2m"))
        kb.inc("gpsimd", lambda e, ci=ci, wsl=wsl: e.dma_start(
            out=wsl[:], in_=w_o[ci * 128:(ci + 1) * 128, :]), "p0wol%d" % sl_,
            16)
        kb.wait("scalar", "p0wol%d" % sl_)
        for h in range(16):
            tb = t128[h % 2]
            kb.wait("scalar", "p0w2m", kb.val("p0w2m") - 1)
            kb.inc("scalar", lambda e, h=h, tb=tb, wsl=wsl:
                   e.dma_start_transpose(
                tb[:], wsl[:, h * 128:(h + 1) * 128]), "p0wot%d" % (h % 2), 16)
            kb.wait("tensor", "p0wot%d" % (h % 2))
            bank = pbk[h % 2]
            kb.wait("tensor", "p0w2e", kb.val("p0w2e") - 1)
            kb.inc("tensor", lambda e, h=h, ci=ci, tb=tb, bank=bank: e.matmul(
                bank[:, 0:128], wuv_s[:, h * 128:(h + 1) * 128], tb[:],
                start=True, stop=True), "p0w2m")
            kb.wait("vector", "p0w2m")
            kb.inc("vector", lambda e, h=h, ci=ci, bank=bank: e.tensor_copy(
                w2_s[:, h * 2048 + ci * 128: h * 2048 + (ci + 1) * 128],
                bank[:, 0:128]), "p0w2e")

    kb.wait("gpsimd", "p0w2m", kb.val("p0w2m"))
    for h in range(16):
        kb.inc("gpsimd", lambda e, h=h: e.dma_start(
            out=wuk_s[:, h * 128:(h + 1) * 128],
            in_=w_uk[h * 128:(h + 1) * 128, :]), "p0g", 16)
    for h in range(16):
        sl_ = h % 2
        sl = wslab[sl_]
        kb.wait("gpsimd", "p0qkm", kb.val("p0qkm"))
        kb.inc("gpsimd", lambda e, h=h, sl=sl: e.dma_start(
            out=sl[:], in_=w_q[h * 128:(h + 1) * 128, :]), "p0wq%d" % sl_, 16)
        kb.wait("tensor", "p0wq%d" % sl_)
        kb.wait("tensor", "p0g")
        for dch in range(16):
            bank = pbk[2 + dch % 2]
            kb.wait("tensor", "p0qke", kb.val("p0qke") - 1)
            kb.inc("tensor", lambda e, h=h, dch=dch, sl=sl, bank=bank: e.matmul(
                bank[:, 0:128], sl[:, dch * 128:(dch + 1) * 128],
                wuk_s[:, h * 128:(h + 1) * 128], start=True, stop=True), "p0qkm")
            kb.wait("vector", "p0qkm")
            kb.wait("vector", "p0qks", kb.val("p0qks"))
            kb.inc("vector", lambda e, bank=bank: e.tensor_copy(
                evtile[:], bank[:, 0:128]), "p0qke")
            kb.wait("sync", "p0qke")
            kb.inc("sync", lambda e, h=h, dch=dch: e.dma_start(
                out=wqk_d[h, :, dch * 128:(dch + 1) * 128], in_=evtile[:]),
                "p0qks", 16)

    kb.wait("gpsimd", "p0qkm", kb.val("p0qkm"))
    kb.inc("gpsimd", lambda e: e.dma_start(out=wslab[0][:], in_=w_dkv[:, :]),
           "p0g", 16)
    kb.wait("sync", "p0g")
    for dch in range(16):
        kb.inc("sync", lambda e, dch=dch: e.dma_start_transpose(
            wdkvT_s[:, dch * 128:(dch + 1) * 128],
            wslab[0][:, dch * 128:(dch + 1) * 128]), "p0m", 16)
    al.free("p0tmp")

    state = {"n": 0}

    def ln_wave(tiles):
        n0 = state["n"]
        state["n"] += 1
        kb.wait("gpsimd", "patp", n0 * 1024)
        for i, t in enumerate(tiles):
            kb.inc("gpsimd", lambda e, t=t, i=i: e.dma_start(
                out=xb4[:, i * 2048:(i + 1) * 2048],
                in_=x0[t * 128:(t + 1) * 128, :]), "pald", 16)
        kb.wait("vector", "pald")
        kb.wait("scalar", "pald")
        _stats4(kb, [xb4[:, i * 2048:(i + 1) * 2048] for i in range(4)],
                D, st4, junkb, LN_EPS, True)
        kb.wait("vector", "sst0")
        for i in range(4):
            fn = lambda e, i=i: e.tensor_scalar(
                xb4[:, i * 2048:(i + 1) * 2048],
                xb4[:, i * 2048:(i + 1) * 2048],
                st4[:, i * 8 + 1:i * 8 + 2], st4[:, i * 8 + 4:i * 8 + 5],
                op0=ALU.subtract, op1=ALU.mult)
            kb.inc("vector", fn, "paz") if i == 3 else kb.do("vector", fn)
        kb.wait("sync", "paz")
        for i in range(4):
            for dch in range(16):
                kb.inc("sync", lambda e, i=i, dch=dch: e.dma_start_transpose(
                    lnT_g[:, dch * 512 + i * 128: dch * 512 + (i + 1) * 128],
                    xb4[:, i * 2048 + dch * 128: i * 2048 + (dch + 1) * 128]),
                    "patp", 16)


    kb.wait("tensor", "p0m")
    kb.wait("vector", "p0m")
    kb.wait("scalar", "p0m")
    for ga in range(8):
        kb.wait("sync", "pack", kb.val("pack"))
        ln_wave([ga * 4 + i for i in range(4)])
        kb.wait("tensor", "patp", state["n"] * 1024)
        bank = pbk[ga % 2]
        kb.wait("tensor", "packe", kb.val("packe") - 1)
        for dch in range(16):
            fn = lambda e, dch=dch, bank=bank: e.matmul(
                bank[:], wdkvT_s[:, dch * 128:(dch + 1) * 128],
                lnT_g[:, dch * 512:(dch + 1) * 512],
                start=(dch == 0), stop=(dch == 15))
            kb.inc("tensor", fn, "pack") if dch == 15 else kb.do("tensor", fn)
        kb.wait("vector", "pack")
        kb.inc("vector", lambda e, ga=ga, bank=bank: e.tensor_copy(
            ckvT_s[:, ga * 512:(ga + 1) * 512], bank[:]), "packe")
        kb.wait("sync", "packe")
        kb.wait("sync", "p0aug", 1)
        for ks in range(4):
            kt = ga * 4 + ks
            kb.inc("sync", lambda e, ga=ga, ks=ks, kt=kt: e.dma_start_transpose(
                ckvaug_s[:, kt * 144: kt * 144 + 128],
                ckvT_s[:, ga * 512 + ks * 128: ga * 512 + (ks + 1) * 128]),
                "paaug", 16)
    kb.wait("tensor", "paaug")
    kb.wait("vector", "paaug")
    kb.wait("sync", "pack", kb.val("pack"))

    kb.wait("gpsimd", "p0qks")
    kb.wait("tensor", "p0w2e")
    for g in range(NG):
        kb.wait("sync", "pbqm", kb.val("pbqm"))
        kb.wait("sync", "p1xe", kb.val("p1xe"))
        ln_wave([g * 4 + i for i in range(4)])
        kb.wait("tensor", "patp", state["n"] * 1024)
        kb.wait("tensor", "p1xe", kb.val("p1xe"))
        for cch in range(16):
            sl_ = cch % 2
            sl = wqks[sl_]
            kb.wait("gpsimd", "pbqm", kb.val("pbqm") - 1)
            kb.inc("gpsimd", lambda e, cch=cch, sl=sl: e.dma_start(
                out=sl[:], in_=wqk_d[cch, :, :]), "pbqk%d" % sl_, 16)
            kb.wait("tensor", "pbqk%d" % sl_)
            bank = pbk[2 + cch % 2]
            kb.wait("tensor", "pbqe", kb.val("pbqe") - 1)
            for dch in range(16):
                fn = lambda e, dch=dch, sl=sl, bank=bank: e.matmul(
                    bank[:], sl[:, dch * 128:(dch + 1) * 128],
                    lnT_g[:, dch * 512:(dch + 1) * 512],
                    start=(dch == 0), stop=(dch == 15))
                kb.inc("tensor", fn, "pbqm") if dch == 15 else kb.do("tensor", fn)
            kb.wait("vector", "pbqm")
            kb.wait("vector", "p1sm", kb.val("p1sm"))
            kb.inc("vector", lambda e, cch=cch, bank=bank: e.tensor_copy(
                qtg[:, cch * 512:(cch + 1) * 512], bank[:]), "pbqe")
        kb.wait("tensor", "pbqe")
        kb.wait("sync", "p1s2m", kb.val("p1s2m"))
        kb.wait("tensor", "p1xe", kb.val("p1xe"))

        kts = MLA_KTS[g]
        nit = len(kts)
        colbase = sum(len(MLA_KTS[gg]) for gg in range(g))
        for h in range(16):
            expv = {}
            maskv = {}
            pend = [None]

            def emit_pv(j, kts=kts, nit=nit, expv=expv, maskv=maskv):
                if j in maskv:
                    kb.wait("tensor", "p1mk", maskv[j])
                else:
                    kb.wait("tensor", "p1ex", expv[j])
                if j == 0:
                    kb.wait("tensor", "p1on", kb.val("p1on"))
                kt = kts[j]
                for sub in range(4):
                    fn = lambda e, j=j, sub=sub, kt=kt, nit=nit: e.matmul(
                        pbk[2 + sub][:, 0:129],
                        Pb[j % 3][:, sub * 128:(sub + 1) * 128],
                        ckvaug_s[:, kt * 144: kt * 144 + 129],
                        start=(j == 0), stop=(j == nit - 1))
                    kb.inc("tensor", fn, "p1pv") if sub == 3 else kb.do("tensor", fn)

            for it, kt in enumerate(kts):
                sbk = pbk[it % 2]
                if it >= 2:
                    kb.wait("tensor", "p1ex", expv[it - 2])
                kb.inc("tensor", lambda e, kt=kt, h=h, sbk=sbk: e.matmul(
                    sbk[:], ckvT_s[:, kt * 128:(kt + 1) * 128],
                    qtg[:, h * 512:(h + 1) * 512], start=True, stop=True),
                    "p1sm")
                pbuf = Pb[it % 3]
                kb.wait("scalar", "p1sm")
                kb.wait("scalar", "p1pv", kb.val("p1pv"))
                col = colbase + it
                expv[it] = kb.inc("scalar", lambda e, sbk=sbk, pbuf=pbuf,
                                  col=col: e.activation(
                    pbuf[:], sbk[:], AF.Exp, bias=amla_s[:, col:col + 1],
                    scale=SCALE), "p1ex")
                if 4 * g <= kt < 4 * g + 4:
                    s_ = kt - 4 * g
                    kb.wait("vector", "p1ex", expv[it])
                    if s_ > 0:
                        kb.do("vector", lambda e, pbuf=pbuf, s_=s_: e.memset(
                            pbuf[:, 0:s_ * 128], 0.0))
                    maskv[it] = kb.inc("vector", lambda e, pbuf=pbuf, s_=s_:
                                       e.tensor_tensor(
                        pbuf[:, s_ * 128:(s_ + 1) * 128],
                        pbuf[:, s_ * 128:(s_ + 1) * 128], tri_s[:],
                        op=ALU.mult), "p1mk")
                pend.append(it)
                if len(pend) > 3:
                    emit_pv(pend.pop(1))
            for j_ in pend[1:]:
                emit_pv(j_)
            kb.wait("vector", "p1pv")
            for sub in range(3):
                kb.do("vector", lambda e, sub=sub: e.reciprocal(
                    rcol[:, sub:sub + 1], pbk[2 + sub][:, 128:129]))
            kb.inc("vector", lambda e: e.reciprocal(
                rcol[:, 3:4], pbk[5][:, 128:129]), "p1rc")
            kb.wait("scalar", "p1rc")
            kb.inc("scalar", lambda e: e.activation(rcol[:, 4:8], rcol[:, 0:4],
                                                    AF.Identity), "p1lnc")
            kb.wait("vector", "p1lnc")
            kb.wait("vector", "p1oT", kb.val("p1oT") - 64)
            for sub in range(4):
                kb.inc("vector", lambda e, sub=sub: e.tensor_scalar(
                    onb[:, sub * 128:(sub + 1) * 128], pbk[2 + sub][:, 0:128],
                    rcol[:, 4 + sub:5 + sub], 2.0 ** -24,
                    op0=ALU.mult, op1=ALU.mult), "p1on")
            kb.wait("sync", "p1on")
            for sub in range(4):
                kb.inc("sync", lambda e, h=h, sub=sub: e.dma_start_transpose(
                    oTg[:, h * 512 + sub * 128: h * 512 + (sub + 1) * 128],
                    onb[:, sub * 128:(sub + 1) * 128]), "p1oT", 16)
        kb.wait("tensor", "p1oT")
        for tsub in range(4):
            t = g * 4 + tsub
            bx = tsub % 2
            kb.wait("sync", "p1xe", kb.val("p1xe") - 8)
            kb.inc("sync", lambda e, t=t, bx=bx: e.dma_start(
                out=xb[bx][:], in_=x0[t * 128:(t + 1) * 128, :]), "p1xr", 16)
            for db in range(4):
                bank = pbk[2 + db]
                kb.wait("tensor", "p1xe", kb.val("p1xe"))
                for h in range(16):
                    fn = lambda e, h=h, db=db, tsub=tsub, bank=bank: e.matmul(
                        bank[:],
                        oTg[:, h * 512 + tsub * 128: h * 512 + (tsub + 1) * 128],
                        w2_s[:, h * 2048 + db * 512: h * 2048 + (db + 1) * 512],
                        start=(h == 0), stop=(h == 15))
                    kb.inc("tensor", fn, "p1s2m") if h == 15 else kb.do("tensor", fn)
            kb.wait("vector", "p1s2m")
            kb.wait("vector", "p1xr")
            bo = 0
            kb.wait("vector", "p1xs", kb.val("p1xs"))
            for db in range(4):
                kb.inc("vector", lambda e, db=db, bo=bo, bx=bx: e.tensor_tensor(
                    x1t[bo][:, db * 512:(db + 1) * 512], pbk[2 + db][:],
                    xb[bx][:, db * 512:(db + 1) * 512], op=ALU.add), "p1xe")
            kb.wait("sync", "p1xe")
            kb.inc("sync", lambda e, t=t, bo=bo: e.dma_start(
                out=x1_d[t * 128:(t + 1) * 128, :], in_=x1t[bo][:]), "p1xs", 16)
    kb.flush()
    al.free("p1per")

    if n_phases == 1:
        al.free("attnsh")
        _copy_out(kb, al, x1_d, y)
        stack.close()
        return nc

    kb.begin()
    ropescr = al.sb("p2tb", "ropescr", (128, 2048), BF16)
    negu_s = al.sb("p2tb", "negu_s", (128, 4 * 512), BF16)
    sel_s = al.sb("p2tb", "sel_s", (128, 32), F32)
    vffn_s = al.sb("p2tb", "vffn_s", (128, FFN_NC), F32)
    cs_s = al.sb("p2tb", "cs_s", (128, 16 * 64), F32)
    ln1b_s = al.sb("p2tb", "ln1b_s", (128, D), BF16)
    qnb_s = al.sb("p2tb", "qnb_s", (128, D), BF16)
    knb_s = al.sb("p2tb", "knb_s", (128, 512), BF16)
    for s_, d_ in ((negu, negu_s), (seli, sel_s), (vffn, vffn_s),
                   (cossin, cs_s)):
        kb.inc("sync", lambda e, s=s_, d=d_: e.dma_start(out=d[:], in_=s),
               "p2tb", 16)
    for s_, d_ in ((ln1b, ln1b_s), (qnb, qnb_s), (knb, knb_s)):
        kb.inc("gpsimd", lambda e, s=s_, d=d_: e.dma_start(out=d[:], in_=s),
               "p2tg", 16)
    wkv_s = al.sb("p2a", "wkv_s", (128, 2 * 16 * 512), BF16)
    hTt = al.sb("p2a", "hTt", (128, 16 * 128), BF16)
    ktf = al.sb("p2a", "ktf", (128, 512), F32)
    krb = al.sb("p2a", "krb", (128, 512), BF16)
    kTown = al.sb("p2a", "kTown", (128, 4 * 2048), BF16)
    vaugo = al.sb("p2a", "vaugo", (128, VAW), BF16)

    for ci in range(16):
        kb.inc("gpsimd", lambda e, ci=ci: e.dma_start(
            out=wkv_s[:, ci * 512:(ci + 1) * 512],
            in_=wk[ci * 128:(ci + 1) * 128, :]), "p2tg", 16)
        kb.inc("gpsimd", lambda e, ci=ci: e.dma_start(
            out=wkv_s[:, 8192 + ci * 512: 8192 + (ci + 1) * 512],
            in_=wv[ci * 128:(ci + 1) * 128, :]), "p2tg", 16)
    kb.inc("vector", lambda e: e.memset(vaugo[:], 1.0), "p2vone", 1)

    def h_tile(tag, t, b):
        kb.inc("sync", lambda e, t=t, b=b: e.dma_start(
            out=xb[b][:], in_=x1_d[t * 128:(t + 1) * 128, :]), tag + "ld", 16)
        kb.wait("vector", tag + "ld")
        kb.wait("scalar", tag + "ld")
        _stats(kb, xb[b][:], D, stat[b], junkb, LN_EPS, True, b)
        kb.wait("vector", "sst%d" % b)
        kb.inc("vector", lambda e, b=b: e.tensor_scalar(
            zb[b][:], xb[b][:], stat[b][:, 1:2], stat[b][:, 4:5],
            op0=ALU.subtract, op1=ALU.mult), tag + "z")
        kb.wait("scalar", tag + "z")
        _stats(kb, zb[b][:], D, stat[b], junkb, RMS_EPS, False, b)
        kb.wait("vector", "sst%d" % b)
        kb.do("vector", lambda e, b=b: e.tensor_scalar(
            zb[b][:], zb[b][:], stat[b][:, 4:5], None, op0=ALU.mult))
        kb.inc("vector", lambda e, b=b: e.tensor_tensor(
            zb[b][:], zb[b][:], ln1b_s[:], op=ALU.mult), tag + "h")

    def rope_emit(src, srcw, soff, dst, dstw, doff, nh, tcol, tag):
        cosap = bass.AP(cs_s, tcol * 64, [[1024, 128], [0, nh], [1, 32]])
        sinap = bass.AP(cs_s, tcol * 64 + 32, [[1024, 128], [0, nh], [1, 32]])
        sc = [bass.AP(ropescr, 32 * nh * i, [[2048, 128], [32, nh], [1, 32]])
              for i in range(4)]
        q1 = bass.AP(src, soff, [[srcw, 128], [128, nh], [1, 32]])
        q2 = bass.AP(src, soff + 32, [[srcw, 128], [128, nh], [1, 32]])
        d1 = bass.AP(dst, doff, [[dstw, 128], [128, nh], [1, 32]])
        d2 = bass.AP(dst, doff + 32, [[dstw, 128], [128, nh], [1, 32]])
        kb.do("vector", lambda e: e.tensor_tensor(sc[0], q1, cosap, op=ALU.mult))
        kb.do("vector", lambda e: e.tensor_tensor(sc[1], q2, sinap, op=ALU.mult))
        kb.do("vector", lambda e: e.tensor_tensor(sc[2], q2, cosap, op=ALU.mult))
        kb.do("vector", lambda e: e.tensor_tensor(sc[3], q1, sinap, op=ALU.mult))
        kb.do("vector", lambda e: e.tensor_tensor(d1, sc[0], sc[1],
                                                  op=ALU.subtract))
        kb.do("vector", lambda e: e.tensor_tensor(d2, sc[2], sc[3], op=ALU.add))
        if src is not dst or soff != doff:
            pasrc = bass.AP(src, soff + 64, [[srcw, 128], [128, nh], [1, 64]])
            padst = bass.AP(dst, doff + 64, [[dstw, 128], [128, nh], [1, 64]])
            kb.do("vector", lambda e: e.tensor_copy(padst, pasrc))
        kb.inc("vector", lambda e: e.tensor_copy(
            bass.AP(dst, doff, [[dstw, 128], [1, 1]]),
            bass.AP(dst, doff, [[dstw, 128], [1, 1]])), tag)

    kb.wait("vector", "p2tg")
    kb.wait("vector", "p2tb")
    for t in range(16):
        b = t % 2
        kb.wait("sync", "p2ahT", (t - 1) * 256)
        kb.wait("sync", "p2az", t - 1)
        h_tile("p2a", t, b)
        kb.wait("sync", "p2ah")
        kb.wait("sync", "p2akm", kb.val("p2akm"))
        kb.wait("sync", "p2avm", kb.val("p2avm"))
        for dch in range(16):
            kb.inc("sync", lambda e, b=b, dch=dch: e.dma_start_transpose(
                hTt[:, dch * 128:(dch + 1) * 128],
                zb[b][:, dch * 128:(dch + 1) * 128]), "p2ahT", 16)
        kb.wait("tensor", "p2ahT")
        kb.wait("tensor", "p2tg")
        kb.wait("tensor", "p2ake", kb.val("p2ake") - 1)
        for dch in range(16):
            fn = lambda e, dch=dch: e.matmul(
                pbk[0][:], hTt[:, dch * 128:(dch + 1) * 128],
                wkv_s[:, dch * 512:(dch + 1) * 512],
                start=(dch == 0), stop=(dch == 15))
            kb.inc("tensor", fn, "p2akm") if dch == 15 else kb.do("tensor", fn)
        kb.wait("tensor", "p2ave", kb.val("p2ave") - 1)
        for dch in range(16):
            fn = lambda e, dch=dch: e.matmul(
                pbk[1][:], hTt[:, dch * 128:(dch + 1) * 128],
                wkv_s[:, 8192 + dch * 512: 8192 + (dch + 1) * 512],
                start=(dch == 0), stop=(dch == 15))
            kb.inc("tensor", fn, "p2avm") if dch == 15 else kb.do("tensor", fn)
        kb.wait("vector", "p2akm")
        kb.wait("vector", "p2akT", t * 64)
        kb.inc("vector", lambda e: e.tensor_copy(ktf[:], pbk[0][:]), "p2ake")
        kb.wait("scalar", "p2ake")
        _stats(kb, ktf[:], 512, stat[b], junkb, RMS_EPS, False, b)
        kb.wait("vector", "sst%d" % b)
        kb.do("vector", lambda e, b=b: e.tensor_scalar(
            ktf[:], ktf[:], stat[b][:, 4:5], None, op0=ALU.mult))
        kb.inc("vector", lambda e: e.tensor_tensor(
            ktf[:], ktf[:], knb_s[:], op=ALU.mult), "p2akn")
        rope_emit(ktf, 512, 0, krb, 512, 0, 4, t, "p2akr")
        kb.wait("sync", "p2akr")
        for kvh in range(4):
            kb.inc("sync", lambda e, kvh=kvh, t=t: e.dma_start_transpose(
                kTown[:, kvh * 2048 + t * 128: kvh * 2048 + (t + 1) * 128],
                krb[:, kvh * 128:(kvh + 1) * 128]), "p2akT", 16)
        kb.wait("vector", "p2avm")
        kb.wait("vector", "p2vone", 1)
        vdst = bass.AP(vaugo, t * 528, [[VAW, 128], [132, 4], [1, 128]])
        vsrc = bass.AP(pbk[1], 0, [[512, 128], [128, 4], [1, 128]])
        kb.inc("vector", lambda e, vdst=vdst, vsrc=vsrc: e.tensor_copy(
            vdst, vsrc), "p2ave")
    kb.wait("sync", "p2akT")
    kb.wait("sync", "p2ave")
    kb.inc("sync", lambda e: e.dma_start(out=kvin_d[:, 0:8192], in_=kTown[:]),
           "p2kvw", 16)
    kb.inc("sync", lambda e: e.dma_start(out=kvin_d[:, 8192:KVW], in_=vaugo[:]),
           "p2kvw", 16)
    kb.wait("gpsimd", "p2kvw")
    kb.inc("gpsimd", lambda e: e.collective_compute(
        "AllGather", ALU.bypass,
        replica_groups=[[0, 1], [2, 3], [4, 5], [6, 7]],
        ins=[kvin_d.opt()], outs=[kvout_d.opt()]), "p2cc", 1)
    kb.flush()
    al.free("p2a")

    kb.begin()
    kT = al.sb("p2b", "kT", (128, 4 * 4096), BF16)
    vaug = al.sb("p2b", "vaug", (128, 32 * 528), BF16)
    hTg2 = al.sb("p2b", "hTg2", (128, 16 * 512), BF16)
    o2Tg = hTg2
    qf4 = al.sb("p2b", "qf4", (128, 4 * 2048), BF16)
    q2Tg = al.sb("p2b", "q2Tg", (128, 16 * 512), BF16)
    wqo2 = [al.sb("p2b", "wqo%d" % i, (128, 16 * 512), BF16) for i in range(2)]
    x2s = al.sb("p2b", "x2s", (128, 1024), F32)
    s2st = al.sb("p2b", "s2st", (128, 8), F32)

    kb.wait("sync", "p2cc", 1)
    for r in range(2):
        for kvh in range(4):
            kb.inc("sync", lambda e, r=r, kvh=kvh: e.dma_start(
                out=kT[:, kvh * 4096 + r * 2048: kvh * 4096 + (r + 1) * 2048],
                in_=kvout_d[r, :, kvh * 2048:(kvh + 1) * 2048]), "p2kvr", 16)
        kb.inc("sync", lambda e, r=r: e.dma_start(
            out=vaug[:, r * VAW:(r + 1) * VAW],
            in_=kvout_d[r, :, 8192:KVW]), "p2kvr", 16)
    kb.wait("tensor", "p2kvr")
    if DBG_P2:
        kb.wait("sync", "p2kvr")
        kb.inc("sync", lambda e: e.dma_start(out=dbg_kt, in_=kT[:]), "dbg", 16)
        kb.inc("sync", lambda e: e.dma_start(out=dbg_va, in_=vaug[:]), "dbg", 16)

    st2 = {"n": 0}
    for g in range(NG):
        kb.wait("sync", "p2qm", kb.val("p2qm"))
        kb.wait("sync", "p2s2m", kb.val("p2s2m"))
        kb.wait("sync", "p2xe", kb.val("p2xe"))
        for ts4 in range(4):
            n2 = st2["n"]
            st2["n"] += 1
            t = g * 4 + ts4
            b = n2 % 2
            kb.wait("sync", "p2bhT", (n2 - 1) * 256)
            kb.wait("sync", "p2bz", n2 - 1)
            h_tile("p2b", t, b)
            kb.wait("sync", "p2bh")
            for dch in range(16):
                kb.inc("sync", lambda e, b=b, dch=dch, ts4=ts4: e.dma_start_transpose(
                    hTg2[:, dch * 512 + ts4 * 128: dch * 512 + (ts4 + 1) * 128],
                    zb[b][:, dch * 128:(dch + 1) * 128]), "p2bhT", 16)
        kb.wait("tensor", "p2bhT")
        kb.wait("vector", "p2qT", kb.val("p2qT"))
        for fb in range(4):
            wsl = wqo2[fb % 2]
            sname = "p2wq%d" % (fb % 2)
            kb.wait("gpsimd", "p2qm", kb.val("p2qm") - 4)
            kb.wait("gpsimd", "p2s2m", kb.val("p2s2m"))
            for dch in range(16):
                kb.inc("gpsimd", lambda e, dch=dch, fb=fb, wsl=wsl: e.dma_start(
                    out=wsl[:, dch * 512:(dch + 1) * 512],
                    in_=wq[dch * 128:(dch + 1) * 128, fb * 512:(fb + 1) * 512]),
                    sname, 16)
            kb.wait("tensor", sname)
            for ts4 in range(4):
                bank = pbk[ts4 % 2]
                kb.wait("tensor", "p2qe", kb.val("p2qe") - 1)
                for dch in range(16):
                    fn = lambda e, dch=dch, ts4=ts4, bank=bank, wsl=wsl: \
                        e.matmul(
                        bank[:],
                        hTg2[:, dch * 512 + ts4 * 128: dch * 512 + (ts4 + 1) * 128],
                        wsl[:, dch * 512:(dch + 1) * 512],
                        start=(dch == 0), stop=(dch == 15))
                    kb.inc("tensor", fn, "p2qm") if dch == 15 else kb.do("tensor", fn)
                kb.wait("vector", "p2qm")
                kb.inc("vector", lambda e, ts4=ts4, fb=fb, bank=bank: e.tensor_copy(
                    qf4[:, ts4 * 2048 + fb * 512: ts4 * 2048 + (fb + 1) * 512],
                    bank[:]), "p2qe")
        kb.wait("sync", "p1sm", kb.val("p1sm"))
        for ts4 in range(4):
            b = ts4 % 2
            t = g * 4 + ts4
            qsl = qf4[:, ts4 * 2048:(ts4 + 1) * 2048]
            kb.wait("vector", "p2qe", kb.val("p2qe"))
            kb.wait("scalar", "p2qe", kb.val("p2qe"))
            _stats(kb, qsl, D, stat[b], junkb, RMS_EPS, False, b)
            kb.wait("vector", "sst%d" % b)
            kb.do("vector", lambda e, qsl=qsl, b=b: e.tensor_scalar(
                qsl, qsl, stat[b][:, 4:5], None, op0=ALU.mult))
            kb.inc("vector", lambda e, qsl=qsl: e.tensor_tensor(
                qsl, qsl, qnb_s[:], op=ALU.mult), "p2qn")
            rope_emit(qf4, 4 * 2048, ts4 * 2048, qf4, 4 * 2048, ts4 * 2048,
                      16, t, "p2qr")
            kb.wait("sync", "p2qr")
            for hh in range(16):
                kb.inc("sync", lambda e, hh=hh, ts4=ts4: e.dma_start_transpose(
                    q2Tg[:, hh * 512 + ts4 * 128: hh * 512 + (ts4 + 1) * 128],
                    qf4[:, ts4 * 2048 + hh * 128: ts4 * 2048 + (hh + 1) * 128]),
                    "p2qT", 16)
        kb.wait("tensor", "p2qT")
        if DBG_P2:
            kb.wait("sync", "p2qT")
            kb.inc("sync", lambda e, g=g: e.dma_start(
                out=dbg_q2[g], in_=q2Tg[:]), "dbg", 16)
        kb.wait("tensor", "p2qe")
        kb.wait("sync", "p2s2m", kb.val("p2s2m"))
        kb.wait("tensor", "p2xe", kb.val("p2xe"))

        kts = FFN_KTS[g]
        nit = len(kts)
        colbase = sum(len(FFN_KTS[gg]) for gg in range(g))
        cand = FFN_CAND[g]
        kb.wait("sync", "p2qm", kb.val("p2qm"))
        for h in range(16):
            kvh = h // 4
            expv = {}
            maskv = {}
            pend = [None]

            def emit_pv2(j2_, kts=kts, nit=nit, expv=expv, maskv=maskv,
                         kvh=kvh):
                if j2_ in maskv:
                    kb.wait("tensor", "p1mk", maskv[j2_])
                else:
                    kb.wait("tensor", "p1ex", expv[j2_])
                if j2_ == 0:
                    kb.wait("tensor", "p1on", kb.val("p1on"))
                kt = kts[j2_]
                for sub in range(4):
                    fn = lambda e, j2_=j2_, sub=sub, kt=kt, nit=nit, kvh=kvh: \
                        e.matmul(
                        pbk[2 + sub][:, 0:129],
                        Pb[j2_ % 3][:, sub * 128:(sub + 1) * 128],
                        vaug[:, kt * 528 + kvh * 132: kt * 528 + kvh * 132 + 129],
                        start=(j2_ == 0), stop=(j2_ == nit - 1))
                    kb.inc("tensor", fn, "p1pv") if sub == 3 else kb.do("tensor", fn)

            for it, kt in enumerate(kts):
                sbk = pbk[it % 2]
                if it >= 2:
                    kb.wait("tensor", "p1ex", expv[it - 2])
                kb.inc("tensor", lambda e, kt=kt, h=h, kvh=kvh, sbk=sbk:
                       e.matmul(
                    sbk[:],
                    kT[:, kvh * 4096 + kt * 128: kvh * 4096 + (kt + 1) * 128],
                    q2Tg[:, h * 512:(h + 1) * 512], start=True, stop=True),
                    "p1sm")
                pbuf = Pb[it % 3]
                kb.wait("scalar", "p1sm")
                kb.wait("scalar", "p1pv", kb.val("p1pv"))
                col = colbase + it
                expv[it] = kb.inc("scalar", lambda e, sbk=sbk, pbuf=pbuf,
                                  col=col: e.activation(
                    pbuf[:], sbk[:], AF.Exp, bias=vffn_s[:, col:col + 1],
                    scale=SCALE), "p1ex")
                if kt in cand:
                    jj = cand.index(kt)
                    s_ = jj % 4
                    slot = g * 8 + jj
                    kb.wait("vector", "p1ex", expv[it])
                    kb.do("vector", lambda e, pbuf=pbuf, s_=s_: e.tensor_tensor(
                        Pscr[:], pbuf[:], negu_s[:, s_ * 512:(s_ + 1) * 512],
                        op=ALU.mult))
                    maskv[it] = kb.inc("vector", lambda e, pbuf=pbuf, slot=slot:
                                       e.scalar_tensor_tensor(
                        pbuf[:], Pscr[:], sel_s[:, slot:slot + 1], pbuf[:],
                        op0=ALU.mult, op1=ALU.add), "p1mk")
                pend.append(it)
                if len(pend) > 3:
                    emit_pv2(pend.pop(1))
            for j_ in pend[1:]:
                emit_pv2(j_)
            kb.wait("vector", "p1pv")
            for sub in range(3):
                kb.do("vector", lambda e, sub=sub: e.reciprocal(
                    rcol[:, sub:sub + 1], pbk[2 + sub][:, 128:129]))
            kb.inc("vector", lambda e: e.reciprocal(
                rcol[:, 3:4], pbk[5][:, 128:129]), "p1rc")
            kb.wait("scalar", "p1rc")
            kb.inc("scalar", lambda e: e.activation(rcol[:, 4:8], rcol[:, 0:4],
                                                    AF.Identity), "p1lnc")
            kb.wait("vector", "p1lnc")
            kb.wait("vector", "p1oT", kb.val("p1oT") - 64)
            for sub in range(4):
                kb.inc("vector", lambda e, sub=sub: e.tensor_scalar(
                    onb[:, sub * 128:(sub + 1) * 128], pbk[2 + sub][:, 0:128],
                    rcol[:, 4 + sub:5 + sub], None, op0=ALU.mult), "p1on")
            kb.wait("sync", "p1on")
            for sub in range(4):
                kb.inc("sync", lambda e, h=h, sub=sub: e.dma_start_transpose(
                    o2Tg[:, h * 512 + sub * 128: h * 512 + (sub + 1) * 128],
                    onb[:, sub * 128:(sub + 1) * 128]), "p1oT", 16)
        kb.wait("tensor", "p1oT")
        if DBG_P2:
            kb.wait("sync", "p1oT")
            kb.inc("sync", lambda e, g=g: e.dma_start(
                out=dbg_o2[g], in_=o2Tg[:]), "dbg", 16)
            kb.wait("tensor", "dbg")
        for tsub in range(4):
            t = g * 4 + tsub
            bx = tsub % 2
            kb.wait("sync", "p2st", kb.val("p2st"))
            kb.inc("sync", lambda e, t=t, bx=bx: e.dma_start(
                out=xb[bx][:], in_=x1_d[t * 128:(t + 1) * 128, :]), "p2xr", 16)
            kb.wait("vector", "p2xr")
            kb.wait("scalar", "p2xr")
            _stats(kb, xb[bx][:], D, stat[bx], junkb, LN_EPS, True, bx)
            kb.wait("scalar", "sst%d" % bx)
            kb.inc("scalar", lambda e, tsub=tsub, bx=bx: e.activation(
                s2st[:, tsub * 2:tsub * 2 + 1], stat[bx][:, 4:5], AF.Identity),
                "p2st")
            kb.inc("scalar", lambda e, tsub=tsub, bx=bx: e.activation(
                s2st[:, tsub * 2 + 1:tsub * 2 + 2], stat[bx][:, 6:7],
                AF.Identity), "p2st")
        for db in range(4):
            wsl = wqo2[db % 2]
            sname = "p2wq%d" % (db % 2)
            kb.wait("gpsimd", "p2s2m", kb.val("p2s2m") - 4)
            kb.wait("gpsimd", "p2qm", kb.val("p2qm"))
            for hh in range(16):
                kb.inc("gpsimd", lambda e, hh=hh, db=db, wsl=wsl: e.dma_start(
                    out=wsl[:, hh * 512:(hh + 1) * 512],
                    in_=wo[hh * 128:(hh + 1) * 128, db * 512:(db + 1) * 512]),
                    sname, 16)
            kb.wait("tensor", sname)
            for tsub in range(4):
                t = g * 4 + tsub
                i2 = db * 4 + tsub
                bank = pbk[2 + i2 % 2]
                xsl = slice((i2 % 2) * 512, (i2 % 2) * 512 + 512)
                kb.wait("sync", "p2xs", kb.val("p2xs") - 16)
                kb.inc("sync", lambda e, t=t, db=db, xsl=xsl: e.dma_start(
                    out=x2s[:, xsl], in_=x1_d[t * 128:(t + 1) * 128,
                                              db * 512:(db + 1) * 512]),
                    "p2xr", 16)
                kb.wait("tensor", "p2xe", kb.val("p2xe") - 1)
                for hh in range(16):
                    fn = lambda e, hh=hh, db=db, tsub=tsub, bank=bank, wsl=wsl: \
                        e.matmul(
                        bank[:],
                        o2Tg[:, hh * 512 + tsub * 128: hh * 512 + (tsub + 1) * 128],
                        wsl[:, hh * 512:(hh + 1) * 512],
                        start=(hh == 0), stop=(hh == 15))
                    kb.inc("tensor", fn, "p2s2m") if hh == 15 else kb.do("tensor", fn)
                kb.wait("vector", "p2s2m")
                kb.wait("vector", "p2xr")
                kb.wait("vector", "p2st", kb.val("p2st"))
                kb.wait("vector", "p2xs", kb.val("p2xs") - 16)
                kb.do("vector", lambda e, xsl=xsl, tsub=tsub, bank=bank:
                      e.scalar_tensor_tensor(
                    x2s[:, xsl], x2s[:, xsl], s2st[:, tsub * 2:tsub * 2 + 1],
                    bank[:], op0=ALU.mult, op1=ALU.add))
                kb.inc("vector", lambda e, xsl=xsl, tsub=tsub: e.tensor_scalar(
                    x2s[:, xsl], x2s[:, xsl],
                    s2st[:, tsub * 2 + 1:tsub * 2 + 2], None,
                    op0=ALU.subtract), "p2xe")
                kb.wait("sync", "p2xe")
                kb.inc("sync", lambda e, t=t, db=db, xsl=xsl: e.dma_start(
                    out=x2_d[t * 128:(t + 1) * 128, db * 512:(db + 1) * 512],
                    in_=x2s[:, xsl]), "p2xs", 16)
    kb.flush()
    al.free("p2b")
    al.free("p2tb")
    al.free("attnsh")

    if n_phases == 2:
        _copy_out(kb, al, x2_d, y)
        stack.close()
        return nc

    kb.begin()
    ln2b_s = al.sb("p3", "ln2b_s", (128, D), F32)
    rbb_s = al.sb("p3", "rbb_s", (128, 8), F32)
    comb_s = al.sb("p3", "comb_s", (128, 16 * 8), F32)
    for s_, d_ in ((ln2b, ln2b_s), (rbb, rbb_s)):
        kb.inc("sync", lambda e, s=s_, d=d_: e.dma_start(out=d[:], in_=s),
               "p3tab", 16)
    x2m = al.sb("p3", "x2m", (128, 8 * 2048), F32)
    h2T = al.sb("p3", "h2T", (128, 16 * 1024), BF16)
    h2f = xb[0]
    h2Tf = al.sb("p3", "h2Tf", (128, 128), F32)
    rws = al.sb("p3", "rws", (128, 16 * 8), BF16)
    sig = al.sb("p3", "sig", (128, 8), F32)
    sg2 = al.sb("p3", "sg2", (128, 32), F32)
    sigall = al.sb("p3", "sigall", (128, 64), F32)
    rt3 = al.sb("p3", "rt3", (128, 224), F32)
    actT = al.sb("p3", "actT", (128, 8 * 1024), BF16)
    gsil = al.sb("p3", "gsil", (128, 1024), BF16)
    wgs = [al.sb("p3", "wgs%d" % i, (128, 16 * 128), BF16) for i in range(2)]
    wus = [al.sb("p3", "wus%d" % i, (128, 16 * 128), BF16) for i in range(2)]
    wds = al.sb("p3", "wds", (128, 8 * 2048), BF16)

    for ch in range(16):
        kb.inc("gpsimd", lambda e, ch=ch: e.dma_start(
            out=rws[:, ch * 8:(ch + 1) * 8],
            in_=rw[ch * 128:(ch + 1) * 128, :]), "p3rw", 16)

    kb.wait("vector", "p3tab")
    for hf in range(2):
        kb.wait("vector", "p3gm", kb.val("p3gm"))
        for tl in range(8):
            t = hf * 8 + tl
            kb.wait("sync", "p3ys", hf * 8 * 16)
            kb.inc("sync", lambda e, t=t, tl=tl: e.dma_start(
                out=x2m[:, tl * 2048:(tl + 1) * 2048],
                in_=x2_d[t * 128:(t + 1) * 128, :]), "p3xl", 16)
        for w in range(2):
            tiles = [w * 4 + i for i in range(4)]
            kb.wait("vector", "p3xl", (hf * 8 + tiles[-1] + 1) * 16)
            kb.wait("scalar", "p3xl", (hf * 8 + tiles[-1] + 1) * 16)
            _stats4(kb, [x2m[:, tl * 2048:(tl + 1) * 2048] for tl in tiles],
                    D, st4, junkb, RMS_EPS, False)
            kb.wait("vector", "sst0")
            for i, tl in enumerate(tiles):
                b = tl % 2
                tg_ = hf * 8 + tl
                kb.wait("vector", "p3hT", (tg_ - 1) * 256)
                xt = x2m[:, tl * 2048:(tl + 1) * 2048]
                kb.do("vector", lambda e, xt=xt, i=i, b=b: e.tensor_scalar(
                    zb[b][:], xt, st4[:, i * 8 + 4:i * 8 + 5], None,
                    op0=ALU.mult))
                kb.inc("vector", lambda e, b=b: e.tensor_tensor(
                    zb[b][:], zb[b][:], ln2b_s[:], op=ALU.mult), "p3h2")
                kb.wait("sync", "p3h2")
                for dch in range(16):
                    kb.inc("sync", lambda e, b=b, dch=dch, tl=tl:
                           e.dma_start_transpose(
                        h2T[:, dch * 1024 + tl * 128: dch * 1024 + (tl + 1) * 128],
                        zb[b][:, dch * 128:(dch + 1) * 128]), "p3hT", 16)
                kb.wait("tensor", "p3hT")
                kb.wait("tensor", "p3rw", 256)
                kb.wait("tensor", "p3tab", 32)
                kb.wait("tensor", "p3sg", kb.val("p3sg"))
                for dch in range(16):
                    fn = lambda e, dch=dch, tl=tl: e.matmul(
                        pbk[7][:, 0:8],
                        h2T[:, dch * 1024 + tl * 128: dch * 1024 + (tl + 1) * 128],
                        rws[:, dch * 8:(dch + 1) * 8],
                        start=(dch == 0), stop=(dch == 15))
                    kb.inc("tensor", fn, "p3rm") if dch == 15 \
                        else kb.do("tensor", fn)
                kb.wait("scalar", "p3rm")
                kb.inc("scalar", lambda e, tl=tl: e.activation(
                    sigall[:, tl * 8:(tl + 1) * 8], pbk[7][:, 0:8], AF.Sigmoid),
                    "p3sg")
        kb.wait("vector", "p3sg")
        W0 = rt3[:, 0:64]
        W1 = rt3[:, 64:128]
        W2 = rt3[:, 128:192]
        R0 = rt3[:, 192:200]
        R1 = rt3[:, 200:208]
        R2 = rt3[:, 208:216]
        RT3W = 224

        def r1rep(off):
            return bass.AP(rt3, off, [[RT3W, 128], [1, 8], [0, 8]])
        rbbrep = bass.AP(rbb_s, 0, [[8, 128], [0, 8], [1, 8]])
        sig3 = bass.AP(sigall, 0, [[64, 128], [8, 8], [1, 8]])
        w03 = bass.AP(rt3, 0, [[RT3W, 128], [8, 8], [1, 8]])
        w13 = bass.AP(rt3, 64, [[RT3W, 128], [8, 8], [1, 8]])

        def vst(fn):
            kb.inc("vector", fn, "p3sg")
            kb.wait("scalar", "p3sg")

        def sco(dst, srcc):
            kb.inc("scalar", lambda e, d=dst, s=srcc: e.activation(
                d, s, AF.Identity), "p3sg")
            kb.wait("vector", "p3sg")

        vst(lambda e: e.tensor_tensor(w03, sig3, rbbrep, op=ALU.add))
        sco(W1, W0)
        for tl in range(8):
            fn = lambda e, tl=tl: e.reduce_max(
                R0[:, tl:tl + 1], W1[:, tl * 8:(tl + 1) * 8],
                axis=mybir.AxisListType.X)
            kb.inc("vector", fn, "p3sg") if tl == 7 else kb.do("vector", fn)
        kb.wait("scalar", "p3sg")
        sco(R1, R0)
        vst(lambda e: e.tensor_tensor(w03, w13, r1rep(200), op=ALU.is_ge))
        sco(W2, W0)
        vst(lambda e: e.scalar_tensor_tensor(W0, W2, NEG, W1,
                                             op0=ALU.mult, op1=ALU.add))
        sco(W1, W0)
        for tl in range(8):
            fn = lambda e, tl=tl: e.reduce_max(
                R0[:, tl:tl + 1], W1[:, tl * 8:(tl + 1) * 8],
                axis=mybir.AxisListType.X)
            kb.inc("vector", fn, "p3sg") if tl == 7 else kb.do("vector", fn)
        kb.wait("scalar", "p3sg")
        sco(R1, R0)
        vst(lambda e: e.tensor_tensor(w03, w13, r1rep(200), op=ALU.is_ge))
        sco(W1, W0)
        vst(lambda e: e.tensor_tensor(W0, W2, W1, op=ALU.add))
        sco(W2, W0)
        vst(lambda e: e.tensor_tensor(W0, W2, sigall[:], op=ALU.mult))
        sco(W1, W0)
        for tl in range(8):
            fn = lambda e, tl=tl: e.reduce_sum(
                R0[:, tl:tl + 1], W1[:, tl * 8:(tl + 1) * 8],
                axis=mybir.AxisListType.X)
            kb.inc("vector", fn, "p3sg") if tl == 7 else kb.do("vector", fn)
        kb.wait("scalar", "p3sg")
        sco(R1, R0)
        vst(lambda e: e.reciprocal(R2[:], R1[:]))
        sco(R0, R2)
        vst(lambda e: e.tensor_tensor(w03, w13, r1rep(192), op=ALU.mult))
        kb.inc("scalar", lambda e, hf=hf: e.activation(
            comb_s[:, hf * 64:(hf + 1) * 64], W0, AF.Identity), "p3cb")
        kb.wait("tensor", "p3hT", kb.val("p3hT"))
        if DBG_P3 and hf == 0:
            kb.wait("sync", "p3cb", kb.val("p3cb"))
            kb.wait("sync", "p3sg", kb.val("p3sg"))
            kb.inc("sync", lambda e: e.dma_start(
                out=dbg_sg[:, 0:24], in_=sg2[:, 0:24]), "p3x1", 16)
            kb.inc("sync", lambda e: e.dma_start(
                out=dbg_sg[:, 24:32], in_=sig[:]), "p3x1", 16)
            kb.wait("sync", "p3hT", kb.val("p3hT"))
            kb.inc("sync", lambda e: e.dma_start(out=dbg_cb, in_=comb_s[:]),
                   "p3x1", 16)
            kb.inc("sync", lambda e: e.dma_start(out=dbg_h2, in_=h2T[:]),
                   "p3x1", 16)
        for e_ in range(E):
            kb.wait("gpsimd", "p3dn", kb.val("p3dn"))
            for ich in range(8):
                kb.inc("gpsimd", lambda e, e_=e_, ich=ich: e.dma_start(
                    out=wds[:, ich * 2048:(ich + 1) * 2048],
                    in_=wdp[e_, ich * 128:(ich + 1) * 128, :]), "p3wd", 16)
            kb.wait("vector", "p3dn", kb.val("p3dn"))
            for i128 in range(8):
                sl_ = i128 % 2
                wb_ = wgs[sl_]
                wub = wus[sl_]
                kb.wait("gpsimd", "p3gm", kb.val("p3gm") - 4)
                kb.inc("gpsimd", lambda e, e_=e_, i128=i128, wb_=wb_: e.dma_start(
                    out=wb_[:], in_=wg_r[e_, i128, :, :]), "p3wg%d" % sl_, 16)
                kb.inc("gpsimd", lambda e, e_=e_, i128=i128, wub=wub: e.dma_start(
                    out=wub[:], in_=wu_r[e_, i128, :, :]), "p3wg%d" % sl_, 16)
                kb.wait("tensor", "p3wg%d" % sl_)
                kb.wait("tensor", "p3sl", kb.val("p3sl"))
                kb.wait("tensor", "p3ae", kb.val("p3ae"))
                for tg in range(2):
                    for dch in range(16):
                        fn = lambda e, dch=dch, tg=tg, wb_=wb_: e.matmul(
                            pbk[tg][:], wb_[:, dch * 128:(dch + 1) * 128],
                            h2T[:, dch * 1024 + tg * 512: dch * 1024 + (tg + 1) * 512],
                            start=(dch == 0), stop=(dch == 15))
                        kb.inc("tensor", fn, "p3gm") if dch == 15 \
                            else kb.do("tensor", fn)
                        fn2 = lambda e, dch=dch, tg=tg, wub=wub: e.matmul(
                            pbk[2 + tg][:], wub[:, dch * 128:(dch + 1) * 128],
                            h2T[:, dch * 1024 + tg * 512: dch * 1024 + (tg + 1) * 512],
                            start=(dch == 0), stop=(dch == 15))
                        kb.inc("tensor", fn2, "p3gm") if dch == 15 \
                            else kb.do("tensor", fn2)
                kb.wait("scalar", "p3gm")
                for tg in range(2):
                    kb.inc("scalar", lambda e, tg=tg: e.activation(
                        gsil[:, tg * 512:(tg + 1) * 512], pbk[tg][:], AF.Silu),
                        "p3sl")
                kb.wait("vector", "p3sl")
                for tg in range(2):
                    kb.inc("vector", lambda e, tg=tg, i128=i128: e.tensor_tensor(
                        actT[:, i128 * 1024 + tg * 512: i128 * 1024 + (tg + 1) * 512],
                        gsil[:, tg * 512:(tg + 1) * 512], pbk[2 + tg][:],
                        op=ALU.mult), "p3ae")
            kb.wait("tensor", "p3ae")
            kb.wait("tensor", "p3wd")
            for tl in range(8):
                t = hf * 8 + tl
                for db in range(4):
                    bank = pbk[4 + (tl * 4 + db) % 2]
                    kb.wait("tensor", "p3de", kb.val("p3de") - 1)
                    for ich in range(8):
                        fn = lambda e, ich=ich, tl=tl, db=db, bank=bank: \
                            e.matmul(
                            bank[:],
                            actT[:, ich * 1024 + tl * 128: ich * 1024 + (tl + 1) * 128],
                            wds[:, ich * 2048 + db * 512: ich * 2048 + (db + 1) * 512],
                            start=(ich == 0), stop=(ich == 7))
                        kb.inc("tensor", fn, "p3dn") if ich == 7 \
                            else kb.do("tensor", fn)
                    kb.wait("vector", "p3dn")
                    kb.wait("vector", "p3cb", hf + 1)
                    kb.inc("vector", lambda e, tl=tl, db=db, t=t, e_=e_,
                           bank=bank: e.scalar_tensor_tensor(
                        x2m[:, tl * 2048 + db * 512: tl * 2048 + (db + 1) * 512],
                        bank[:], comb_s[:, t * 8 + e_: t * 8 + e_ + 1],
                        x2m[:, tl * 2048 + db * 512: tl * 2048 + (db + 1) * 512],
                        op0=ALU.mult, op1=ALU.add), "p3de")
        for tl in range(8):
            t = hf * 8 + tl
            kb.wait("sync", "p3xa", kb.val("p3xa"))
            kb.inc("sync", lambda e, t=t: e.dma_start(
                out=xb[1][:], in_=x1_d[t * 128:(t + 1) * 128, :]), "p3x1", 16)
            kb.wait("vector", "p3x1")
            kb.wait("vector", "p3de", kb.val("p3de"))
            kb.inc("vector", lambda e, tl=tl: e.tensor_tensor(
                x2m[:, tl * 2048:(tl + 1) * 2048],
                x2m[:, tl * 2048:(tl + 1) * 2048], xb[1][:], op=ALU.add),
                "p3xa")
            kb.wait("sync", "p3xa")
            kb.inc("sync", lambda e, t=t, tl=tl: e.dma_start(
                out=y[t * 128:(t + 1) * 128, :],
                in_=x2m[:, tl * 2048:(tl + 1) * 2048]), "p3ys", 16)
    kb.flush()
    al.free("p3")
    al.free("base")
    stack.close()
    return nc


def _copy_out(kb, al, src_d, y):
    kb.begin()
    dbg = al.sb("dbg", "dbgt", (128, D), F32)
    for t in range(16):
        kb.wait("sync", "dst", (t - 1) * 16)
        kb.inc("sync", lambda e, t=t: e.dma_start(
            out=dbg[:], in_=src_d[t * 128:(t + 1) * 128, :]), "dld", 16)
        kb.wait("sync", "dld")
        kb.inc("sync", lambda e, t=t: e.dma_start(
            out=y[t * 128:(t + 1) * 128, :], in_=dbg[:]), "dst", 16)
    kb.flush()
    al.free("dbg")
    al.free("base")



def _host_tables(H):
    t = {}
    t["identf"] = np.eye(128, dtype=np.float32)
    i = np.arange(128)[:, None]
    j = np.arange(128)[None, :]
    t["tri01"] = (i <= j).astype(ml_dtypes.bfloat16)
    negu = np.zeros((128, 4, 512), np.float32)
    for s in range(4):
        m = np.ones((128, 512), np.float32)
        m[:, : s * 128] = 0.0
        m[:, s * 128:(s + 1) * 128] = (i <= j).astype(np.float32)
        negu[:, s, :] = -(1.0 - m)
    t["negu"] = negu.reshape(128, 4 * 512).astype(ml_dtypes.bfloat16)
    sel = np.zeros((128, 32), np.float32)
    for g in range(NG):
        for fam in range(2):
            for s in range(4):
                sel[:, g * 8 + fam * 4 + s] = 1.0 if fam == H else 0.0
    t["sel"] = sel
    amla_t = np.full((128, MLA_NC), NEG, np.float32)
    ii = np.arange(128)
    colb = 0
    for g in range(NG):
        Q0 = 2048 * H + 512 * g
        for it, kt in enumerate(MLA_KTS[g]):
            if kt < 4 * g + 4:
                kpos = kt * 128 + ii + 2048 * H
                amla_t[:, colb + it] = SLOPE * (kpos - Q0)
            elif H == 1:
                kpos = (kt * 128 + ii + 2048) % 4096
                amla_t[:, colb + it] = SLOPE * (kpos - Q0)
        colb += len(MLA_KTS[g])
    t["amla"] = amla_t
    vffn_t = np.full((128, FFN_NC), NEG, np.float32)
    colb = 0
    for g in range(NG):
        for it, kt in enumerate(FFN_KTS[g]):
            if kt <= 16 * H + 4 * g + 3:
                vffn_t[:, colb + it] = 0.0
        colb += len(FFN_KTS[g])
    t["vffn"] = vffn_t
    pos = (np.arange(T) + 2048 * H).astype(np.float64)
    invf = 1.0 / (1e6 ** (np.arange(0, 64, 2) / 64.0))
    ang = pos[:, None] * invf[None, :]
    cs = np.zeros((128, 16 * 64), np.float32)
    for tt in range(16):
        cs[:, tt * 64: tt * 64 + 32] = np.cos(ang[tt * 128:(tt + 1) * 128])
        cs[:, tt * 64 + 32: tt * 64 + 64] = np.sin(ang[tt * 128:(tt + 1) * 128])
    t["cossin"] = cs
    return t


_NC_CACHE = {}


def kernel(**inp):
    g = lambda k: np.ascontiguousarray(np.asarray(inp[k], dtype=np.float32))
    x0f = g("L0_attn_hidden_states")

    def restage(w):
        return np.ascontiguousarray(
            w.reshape(E, 16, 128, 8, 128).transpose(0, 3, 2, 1, 4)
            .reshape(E, 8, 128, 16 * 128))

    common = {
        "w_dkv": g("L0_attn_w_dkv"), "w_q": g("L0_attn_w_q"),
        "w_uk": g("L0_attn_w_uk"), "w_uv": g("L0_attn_w_uv"),
        "w_o": g("L0_attn_w_o"), "wq": g("L0_ffn_wq"), "wk": g("L0_ffn_wk"),
        "wv": g("L0_ffn_wv"), "wo": g("L0_ffn_wo"),
        "router_w": g("L0_ffn_router_w"),
        "wg_r": restage(g("L0_ffn_wg")), "wu_r": restage(g("L0_ffn_wu")),
        "wd": g("L0_ffn_wd"),
        "ln1b": np.ascontiguousarray(
            np.broadcast_to(g("L0_ffn_ln1_w"), (128, D))),
        "ln2b": np.ascontiguousarray(
            np.broadcast_to(g("L0_ffn_ln2_w"), (128, D))),
        "qnb": np.ascontiguousarray(
            np.broadcast_to(g("L0_ffn_qnorm_w"), (128, D))),
        "knb": np.ascontiguousarray(
            np.broadcast_to(g("L0_ffn_knorm_w"), (128, 512))),
        "rbb": np.ascontiguousarray(
            np.broadcast_to(g("L0_ffn_router_bias"), (128, 8))),
    }
    tabs = [_host_tables(H) for H in range(2)]

    in_maps = []
    for c in range(8):
        b, H = c // 2, c % 2
        m = dict(common)
        m.update(tabs[H])
        m["x0"] = np.ascontiguousarray(np.roll(x0f[b], -2048 * H, axis=0))
        in_maps.append(m)

    if "nc" not in _NC_CACHE:
        _NC_CACHE["nc"] = build_program()
    nc = _NC_CACHE["nc"]
    res = run_bass_kernel_spmd(nc, in_maps, core_ids=list(range(8)))
    out = np.zeros((B, S, D), np.float32)
    for c in range(8):
        b, H = c // 2, c % 2
        out[b, 2048 * H: 2048 * (H + 1), :] = res.results[c]["y"]
    return out


# version 09
# speedup vs baseline: 1.1631x; 1.0507x over previous
import math
import numpy as np
import ml_dtypes
from contextlib import ExitStack

import concourse.bass as bass
import concourse.mybir as mybir
from concourse.bass_utils import run_bass_kernel_spmd

F32 = mybir.dt.float32
BF16 = mybir.dt.bfloat16
AF = mybir.ActivationFunctionType
ALU = mybir.AluOpType

B, S, D = 4, 4096, 2048
T = 2048
E, I = 8, 1024
NG = 4
SLOPE = 0.0625
SCALE = 1.0 / math.sqrt(128.0)
NEG = -1e30
LN_EPS = 1e-5
RMS_EPS = 1e-6

MLA_KTS = [list(range(4 * g + 4)) + list(range(16, 32)) for g in range(NG)]
FFN_KTS = [list(range(20 + 4 * g)) for g in range(NG)]
FFN_CAND = [[4 * g + s for s in range(4)] + [16 + 4 * g + s for s in range(4)]
            for g in range(NG)]
MLA_NC = sum(len(k) for k in MLA_KTS)
FFN_NC = sum(len(k) for k in FFN_KTS)
VAW = 16 * 528
KVW = 4 * 2048 + VAW

N_PHASES = 3
DBG_P2 = False
DBG_P3 = False
ENGINES = ("sync", "scalar", "vector", "tensor", "gpsimd")


class KB:
    def __init__(self, nc, stack):
        self.nc = nc
        self.stack = stack
        self.sems = {}
        self.dma_sems = set()
        self.q = None

    def sem(self, name):
        if name not in self.sems:
            h = self.stack.enter_context(self.nc.semaphore(name))
            self.sems[name] = [h, 0]
        return self.sems[name]

    def on(self, eng, fn):
        self.q[eng].append(fn)

    def wait(self, eng, name, val=None):
        s = self.sem(name)
        v = s[1] if val is None else val
        if v <= 0:
            return
        self.on(eng, lambda e, h=s[0], v=v: e.wait_ge(h, v))

    def inc(self, eng, fn, name, k=1):
        s = self.sem(name)
        s[1] += k
        if k == 16:
            self.dma_sems.add(name)
        self.on(eng, lambda e, fn=fn, h=s[0], k=k: fn(e).then_inc(h, k))
        return s[1]

    def do(self, eng, fn):
        self.on(eng, fn)

    def val(self, name):
        return self.sem(name)[1]

    def begin(self):
        self.q = {e: [] for e in ENGINES}

    def flush(self):
        for name in sorted(self.dma_sems):
            self.wait("sync", name)
        with self.nc.Block() as blk:
            for e in ENGINES:
                fns = self.q[e]
                if fns:
                    getattr(blk, e)(lambda eng, fns=fns: [f(eng) for f in fns])
        self.q = None


class Alloc:
    def __init__(self, nc):
        self.nc = nc
        self.groups = {}
        self.order = []

    def sb(self, group, name, shape, dt):
        cm = self.nc.sbuf_tensor(name, list(shape), dt)
        h = cm.__enter__()
        if group not in self.groups:
            self.groups[group] = []
            self.order.append(group)
        self.groups[group].append(cm)
        return h

    def free(self, group):
        assert self.order and self.order[-1] == group, (self.order, group)
        for cm in reversed(self.groups.pop(group)):
            cm.__exit__(None, None, None)
        self.order.pop()


def _stats(kb, xt, width, st, junk, eps, mean, b):
    sem = "sst%d" % b
    if mean:
        kb.inc("vector", lambda e: e.reduce_sum(st[:, 0:1], xt,
                                                axis=mybir.AxisListType.X), sem)
    kb.do("scalar", lambda e: e.activation(junk[:, 0:width], xt, AF.Square,
                                           accum_out=st[:, 2:3]))
    kb.do("scalar", lambda e: e.activation(st[:, 3:4], st[:, 2:3],
                                           AF.Identity, scale=1.0 / width))
    if mean:
        kb.wait("scalar", sem)
        kb.do("scalar", lambda e: e.activation(st[:, 1:2], st[:, 0:1],
                                               AF.Identity, scale=1.0 / width))
        kb.inc("scalar", lambda e: e.activation(st[:, 7:8], st[:, 1:2],
                                                AF.Square), sem)
        kb.wait("vector", sem)
        kb.inc("vector", lambda e: e.tensor_scalar(
            st[:, 3:4], st[:, 3:4], st[:, 7:8], eps,
            op0=ALU.subtract, op1=ALU.add), sem)
    else:
        kb.inc("scalar", lambda e: e.activation(st[:, 3:4], st[:, 3:4],
                                                AF.Identity), sem)
        kb.wait("vector", sem)
        kb.inc("vector", lambda e: e.tensor_scalar_add(st[:, 3:4], st[:, 3:4],
                                                       eps), sem)
    kb.wait("scalar", sem)
    kb.do("scalar", lambda e: e.activation(st[:, 4:5], st[:, 3:4], AF.Ln))
    kb.inc("scalar", lambda e: e.activation(st[:, 4:5], st[:, 4:5], AF.Exp,
                                            scale=-0.5), sem)
    if mean:
        kb.wait("vector", sem)
        kb.inc("vector", lambda e: e.tensor_tensor(st[:, 5:6], st[:, 1:2],
                                                   st[:, 4:5], op=ALU.mult),
               sem)
        kb.wait("scalar", sem)
        kb.inc("scalar", lambda e: e.activation(st[:, 6:7], st[:, 5:6],
                                                AF.Identity), sem)


def _stats4(kb, xts, width, st4, junk, eps, mean):
    n = len(xts)
    sem = "sst0"
    if mean:
        for i, xt in enumerate(xts):
            fn = lambda e, i=i, xt=xt: e.reduce_sum(
                st4[:, i * 8:i * 8 + 1], xt, axis=mybir.AxisListType.X)
            kb.inc("vector", fn, sem) if i == n - 1 else kb.do("vector", fn)
        kb.wait("scalar", sem)
    for i, xt in enumerate(xts):
        kb.do("scalar", lambda e, i=i, xt=xt: e.activation(
            junk[:, 0:width], xt, AF.Square, accum_out=st4[:, i * 8 + 2:i * 8 + 3]))
        kb.do("scalar", lambda e, i=i: e.activation(
            st4[:, i * 8 + 3:i * 8 + 4], st4[:, i * 8 + 2:i * 8 + 3],
            AF.Identity, scale=1.0 / width))
    if mean:
        for i in range(n):
            kb.do("scalar", lambda e, i=i: e.activation(
                st4[:, i * 8 + 1:i * 8 + 2], st4[:, i * 8:i * 8 + 1],
                AF.Identity, scale=1.0 / width))
            kb.do("scalar", lambda e, i=i: e.activation(
                st4[:, i * 8 + 7:i * 8 + 8], st4[:, i * 8 + 1:i * 8 + 2],
                AF.Square))
    kb.inc("scalar", lambda e: e.activation(junk[:, 0:1], junk[:, 0:1],
                                            AF.Identity), sem)
    kb.wait("vector", sem)
    for i in range(n):
        if mean:
            fn = lambda e, i=i: e.tensor_scalar(
                st4[:, i * 8 + 3:i * 8 + 4], st4[:, i * 8 + 3:i * 8 + 4],
                st4[:, i * 8 + 7:i * 8 + 8], eps,
                op0=ALU.subtract, op1=ALU.add)
        else:
            fn = lambda e, i=i: e.tensor_scalar_add(
                st4[:, i * 8 + 3:i * 8 + 4], st4[:, i * 8 + 3:i * 8 + 4], eps)
        kb.inc("vector", fn, sem) if i == n - 1 else kb.do("vector", fn)
    kb.wait("scalar", sem)
    for i in range(n):
        kb.do("scalar", lambda e, i=i: e.activation(
            st4[:, i * 8 + 4:i * 8 + 5], st4[:, i * 8 + 3:i * 8 + 4], AF.Ln))
        fn = lambda e, i=i: e.activation(
            st4[:, i * 8 + 4:i * 8 + 5], st4[:, i * 8 + 4:i * 8 + 5],
            AF.Exp, scale=-0.5)
        kb.inc("scalar", fn, sem) if i == n - 1 else kb.do("scalar", fn)
    if mean:
        kb.wait("vector", sem)
        for i in range(n):
            fn = lambda e, i=i: e.tensor_tensor(
                st4[:, i * 8 + 5:i * 8 + 6], st4[:, i * 8 + 1:i * 8 + 2],
                st4[:, i * 8 + 4:i * 8 + 5], op=ALU.mult)
            kb.inc("vector", fn, sem) if i == n - 1 else kb.do("vector", fn)
        kb.wait("scalar", sem)
        for i in range(n):
            fn = lambda e, i=i: e.activation(
                st4[:, i * 8 + 6:i * 8 + 7], st4[:, i * 8 + 5:i * 8 + 6],
                AF.Identity)
            kb.inc("scalar", fn, sem) if i == n - 1 else kb.do("scalar", fn)


def build_program(n_phases=N_PHASES):
    nc = bass.Bass("TRN2", target_bir_lowering=False,
                   detect_race_conditions=False)
    stack = ExitStack()
    kb = KB(nc, stack)
    al = Alloc(nc)

    def din(name, shape, dt=F32):
        return nc.dram_tensor(name, list(shape), dt, kind="ExternalInput").ap()

    x0 = din("x0", (S, D))
    w_dkv = din("w_dkv", (128, D))
    w_q = din("w_q", (D, D))
    w_uk = din("w_uk", (D, 128))
    w_uv = din("w_uv", (D, 128))
    w_o = din("w_o", (D, D))
    wq = din("wq", (D, D))
    wk = din("wk", (D, 512))
    wv = din("wv", (D, 512))
    wo = din("wo", (D, D))
    rw = din("router_w", (D, 8))
    wg_r = din("wg_r", (E, 8, 128, 16 * 128))
    wu_r = din("wu_r", (E, 8, 128, 16 * 128))
    wdp = din("wd", (E, I, D))
    identf = din("identf", (128, 128))
    tri01 = din("tri01", (128, 128), BF16)
    negu = din("negu", (128, 4 * 512), BF16)
    seli = din("sel", (128, 32))
    amla = din("amla", (128, MLA_NC))
    vffn = din("vffn", (128, FFN_NC))
    cossin = din("cossin", (128, 16 * 64))
    ln1b = din("ln1b", (128, D))
    ln2b = din("ln2b", (128, D))
    qnb = din("qnb", (128, D))
    knb = din("knb", (128, 512))
    rbb = din("rbb", (128, 8))

    y = nc.dram_tensor("y", [T, D], F32, kind="ExternalOutput").ap()

    wqk_d = nc.dram_tensor("wqk_d", [16, 128, 16 * 128], BF16).ap()
    x1_d = nc.dram_tensor("x1_d", [T, D], F32).ap()
    x2_d = nc.dram_tensor("x2_d", [T, D], F32).ap()
    kvin_d = nc.dram_tensor("kvin_d", [128, KVW], BF16).ap()
    kvout_d = nc.dram_tensor("kvout_d", [2, 128, KVW], BF16).ap()
    if DBG_P3:
        dbg_cb = nc.dram_tensor("dbg_cb", [128, 128], F32,
                                kind="ExternalOutput").ap()
        dbg_sg = nc.dram_tensor("dbg_sg", [128, 32], F32,
                                kind="ExternalOutput").ap()
        dbg_h2 = nc.dram_tensor("dbg_h2", [128, 16 * 1024], BF16,
                                kind="ExternalOutput").ap()
    if DBG_P2:
        dbg_kt = nc.dram_tensor("dbg_kt", [128, 4 * 4096], BF16,
                                kind="ExternalOutput").ap()
        dbg_va = nc.dram_tensor("dbg_va", [128, 32 * 528], BF16,
                                kind="ExternalOutput").ap()
        dbg_q2 = nc.dram_tensor("dbg_q2", [4, 128, 16 * 512], BF16,
                                kind="ExternalOutput").ap()
        dbg_o2 = nc.dram_tensor("dbg_o2", [4, 128, 16 * 512], BF16,
                                kind="ExternalOutput").ap()

    bsb = lambda n, sh, dt: al.sb("base", n, sh, dt)
    xb = [bsb("xb%d" % i, (128, D), F32) for i in range(2)]
    zb = [bsb("zb%d" % i, (128, D), BF16) for i in range(2)]
    stat = [bsb("st%d" % i, (128, 8), F32) for i in range(2)]
    st4 = bsb("st4", (128, 32), F32)
    junkb = bsb("junkb", (128, D), BF16)
    asb = lambda n, sh, dt: al.sb("attnsh", n, sh, dt)
    amla_s = asb("amla_s", (128, MLA_NC), F32)
    tri_s = asb("tri_s", (128, 128), BF16)
    Pb = [asb("Pb%d" % i, (128, 512), BF16) for i in range(3)]
    Pscr = asb("Pscr", (128, 512), BF16)
    rcol = asb("rcol", (128, 8), F32)
    onb = asb("onb", (128, 512), BF16)

    pbk = [stack.enter_context(nc.psum_tensor("pbk%d" % i, [128, 512], F32))
           for i in range(8)]

    kb.begin()
    wdkvT_s = al.sb("p1per", "wdkvT_s", (128, 16 * 128), BF16)
    w2_s = al.sb("p1per", "w2_s", (128, 16 * 2048), BF16)
    ckvT_s = al.sb("p1per", "ckvT_s", (128, 4096), BF16)
    ckvaug_s = al.sb("p1per", "ckvaug_s", (128, 32 * 144), BF16)
    lnT_g = al.sb("p1per", "lnT_g", (128, 16 * 512), BF16)
    xb4 = al.sb("p1per", "xb4", (128, 4 * 2048), BF16)
    qtg = al.sb("p1per", "qtg", (128, 16 * 512), BF16)
    wqks = [al.sb("p1per", "wqks%d" % i, (128, 2048), BF16) for i in range(2)]
    oTg = al.sb("p1per", "oTg", (128, 16 * 512), BF16)
    x1t = [al.sb("p1per", "x1t0", (128, D), F32)] * 2

    wuv_s = al.sb("p0tmp", "p0_wuv", (128, 16 * 128), BF16)
    wuk_s = wuv_s
    evtile = al.sb("p0tmp", "p0_ev", (128, 128), BF16)
    t128 = [al.sb("p0tmp", "p0_t%d" % i, (128, 128), BF16) for i in range(2)]
    wslab = [al.sb("p0tmp", "p0_ws0", (128, 2048), BF16)] * 2

    for s_, d_ in ((amla, amla_s), (tri01, tri_s)):
        kb.inc("sync", lambda e, s=s_, d=d_: e.dma_start(out=d[:], in_=s),
               "p0m", 16)
    kb.inc("vector", lambda e: e.memset(ckvaug_s[:], 2.0 ** -24), "p0aug", 1)

    for h in range(16):
        kb.inc("gpsimd", lambda e, h=h: e.dma_start(
            out=wuv_s[:, h * 128:(h + 1) * 128],
            in_=w_uv[h * 128:(h + 1) * 128, :]), "p0g", 16)

    kb.wait("tensor", "p0g")
    for ci in range(16):
        sl_ = ci % 2
        wsl = wslab[sl_]
        kb.wait("gpsimd", "p0w2m", kb.val("p0w2m"))
        kb.inc("gpsimd", lambda e, ci=ci, wsl=wsl: e.dma_start(
            out=wsl[:], in_=w_o[ci * 128:(ci + 1) * 128, :]), "p0wol%d" % sl_,
            16)
        kb.wait("scalar", "p0wol%d" % sl_)
        for h in range(16):
            tb = t128[h % 2]
            kb.wait("scalar", "p0w2m", kb.val("p0w2m") - 1)
            kb.inc("scalar", lambda e, h=h, tb=tb, wsl=wsl:
                   e.dma_start_transpose(
                tb[:], wsl[:, h * 128:(h + 1) * 128]), "p0wot%d" % (h % 2), 16)
            kb.wait("tensor", "p0wot%d" % (h % 2))
            bank = pbk[h % 2]
            kb.wait("tensor", "p0w2e", kb.val("p0w2e") - 1)
            kb.inc("tensor", lambda e, h=h, ci=ci, tb=tb, bank=bank: e.matmul(
                bank[:, 0:128], wuv_s[:, h * 128:(h + 1) * 128], tb[:],
                start=True, stop=True), "p0w2m")
            kb.wait("vector", "p0w2m")
            kb.inc("vector", lambda e, h=h, ci=ci, bank=bank: e.tensor_copy(
                w2_s[:, h * 2048 + ci * 128: h * 2048 + (ci + 1) * 128],
                bank[:, 0:128]), "p0w2e")

    kb.wait("gpsimd", "p0w2m", kb.val("p0w2m"))
    for h in range(16):
        kb.inc("gpsimd", lambda e, h=h: e.dma_start(
            out=wuk_s[:, h * 128:(h + 1) * 128],
            in_=w_uk[h * 128:(h + 1) * 128, :]), "p0g", 16)
    for h in range(16):
        sl_ = h % 2
        sl = wslab[sl_]
        kb.wait("gpsimd", "p0qkm", kb.val("p0qkm"))
        kb.inc("gpsimd", lambda e, h=h, sl=sl: e.dma_start(
            out=sl[:], in_=w_q[h * 128:(h + 1) * 128, :]), "p0wq%d" % sl_, 16)
        kb.wait("tensor", "p0wq%d" % sl_)
        kb.wait("tensor", "p0g")
        for dch in range(16):
            bank = pbk[2 + dch % 2]
            kb.wait("tensor", "p0qke", kb.val("p0qke") - 1)
            kb.inc("tensor", lambda e, h=h, dch=dch, sl=sl, bank=bank: e.matmul(
                bank[:, 0:128], sl[:, dch * 128:(dch + 1) * 128],
                wuk_s[:, h * 128:(h + 1) * 128], start=True, stop=True), "p0qkm")
            kb.wait("vector", "p0qkm")
            kb.wait("vector", "p0qks", kb.val("p0qks"))
            kb.inc("vector", lambda e, bank=bank: e.tensor_copy(
                evtile[:], bank[:, 0:128]), "p0qke")
            kb.wait("sync", "p0qke")
            kb.inc("sync", lambda e, h=h, dch=dch: e.dma_start(
                out=wqk_d[h, :, dch * 128:(dch + 1) * 128], in_=evtile[:]),
                "p0qks", 16)

    kb.wait("gpsimd", "p0qkm", kb.val("p0qkm"))
    kb.inc("gpsimd", lambda e: e.dma_start(out=wslab[0][:], in_=w_dkv[:, :]),
           "p0g", 16)
    kb.wait("sync", "p0g")
    for dch in range(16):
        kb.inc("sync", lambda e, dch=dch: e.dma_start_transpose(
            wdkvT_s[:, dch * 128:(dch + 1) * 128],
            wslab[0][:, dch * 128:(dch + 1) * 128]), "p0m", 16)
    al.free("p0tmp")

    state = {"n": 0}

    def ln_wave(tiles):
        n0 = state["n"]
        state["n"] += 1
        kb.wait("gpsimd", "patp", n0 * 1024)
        for i, t in enumerate(tiles):
            kb.inc("gpsimd", lambda e, t=t, i=i: e.dma_start(
                out=xb4[:, i * 2048:(i + 1) * 2048],
                in_=x0[t * 128:(t + 1) * 128, :]), "pald", 16)
        kb.wait("vector", "pald")
        kb.wait("scalar", "pald")
        _stats4(kb, [xb4[:, i * 2048:(i + 1) * 2048] for i in range(4)],
                D, st4, junkb, LN_EPS, True)
        kb.wait("vector", "sst0")
        for i in range(4):
            fn = lambda e, i=i: e.tensor_scalar(
                xb4[:, i * 2048:(i + 1) * 2048],
                xb4[:, i * 2048:(i + 1) * 2048],
                st4[:, i * 8 + 1:i * 8 + 2], st4[:, i * 8 + 4:i * 8 + 5],
                op0=ALU.subtract, op1=ALU.mult)
            kb.inc("vector", fn, "paz") if i == 3 else kb.do("vector", fn)
        kb.wait("sync", "paz")
        for i in range(4):
            for dch in range(16):
                kb.inc("sync", lambda e, i=i, dch=dch: e.dma_start_transpose(
                    lnT_g[:, dch * 512 + i * 128: dch * 512 + (i + 1) * 128],
                    xb4[:, i * 2048 + dch * 128: i * 2048 + (dch + 1) * 128]),
                    "patp", 16)


    kb.wait("tensor", "p0m")
    kb.wait("vector", "p0m")
    kb.wait("scalar", "p0m")
    for ga in range(8):
        kb.wait("sync", "pack", kb.val("pack"))
        ln_wave([ga * 4 + i for i in range(4)])
        kb.wait("tensor", "patp", state["n"] * 1024)
        bank = pbk[ga % 2]
        kb.wait("tensor", "packe", kb.val("packe") - 1)
        for dch in range(16):
            fn = lambda e, dch=dch, bank=bank: e.matmul(
                bank[:], wdkvT_s[:, dch * 128:(dch + 1) * 128],
                lnT_g[:, dch * 512:(dch + 1) * 512],
                start=(dch == 0), stop=(dch == 15))
            kb.inc("tensor", fn, "pack") if dch == 15 else kb.do("tensor", fn)
        kb.wait("vector", "pack")
        kb.inc("vector", lambda e, ga=ga, bank=bank: e.tensor_copy(
            ckvT_s[:, ga * 512:(ga + 1) * 512], bank[:]), "packe")
        kb.wait("sync", "packe")
        kb.wait("sync", "p0aug", 1)
        for ks in range(4):
            kt = ga * 4 + ks
            kb.inc("sync", lambda e, ga=ga, ks=ks, kt=kt: e.dma_start_transpose(
                ckvaug_s[:, kt * 144: kt * 144 + 128],
                ckvT_s[:, ga * 512 + ks * 128: ga * 512 + (ks + 1) * 128]),
                "paaug", 16)
    kb.wait("tensor", "paaug")
    kb.wait("vector", "paaug")
    kb.wait("sync", "pack", kb.val("pack"))

    kb.wait("gpsimd", "p0qks")
    kb.wait("tensor", "p0w2e")
    for g in range(NG):
        kb.wait("sync", "pbqm", kb.val("pbqm"))
        kb.wait("sync", "p1xe", kb.val("p1xe"))
        ln_wave([g * 4 + i for i in range(4)])
        kb.wait("tensor", "patp", state["n"] * 1024)
        kb.wait("tensor", "p1xe", kb.val("p1xe"))
        for cch in range(16):
            sl_ = cch % 2
            sl = wqks[sl_]
            kb.wait("gpsimd", "pbqm", kb.val("pbqm") - 1)
            kb.inc("gpsimd", lambda e, cch=cch, sl=sl: e.dma_start(
                out=sl[:], in_=wqk_d[cch, :, :]), "pbqk%d" % sl_, 16)
            kb.wait("tensor", "pbqk%d" % sl_)
            bank = pbk[2 + cch % 2]
            kb.wait("tensor", "pbqe", kb.val("pbqe") - 1)
            for dch in range(16):
                fn = lambda e, dch=dch, sl=sl, bank=bank: e.matmul(
                    bank[:], sl[:, dch * 128:(dch + 1) * 128],
                    lnT_g[:, dch * 512:(dch + 1) * 512],
                    start=(dch == 0), stop=(dch == 15))
                kb.inc("tensor", fn, "pbqm") if dch == 15 else kb.do("tensor", fn)
            kb.wait("vector", "pbqm")
            kb.wait("vector", "p1sm", kb.val("p1sm"))
            kb.inc("vector", lambda e, cch=cch, bank=bank: e.tensor_copy(
                qtg[:, cch * 512:(cch + 1) * 512], bank[:]), "pbqe")
        kb.wait("tensor", "pbqe")
        kb.wait("sync", "p1s2m", kb.val("p1s2m"))
        kb.wait("tensor", "p1xe", kb.val("p1xe"))

        kts = MLA_KTS[g]
        nit = len(kts)
        colbase = sum(len(MLA_KTS[gg]) for gg in range(g))
        for h in range(16):
            expv = {}
            maskv = {}
            pend = [None]

            def emit_pv(j, kts=kts, nit=nit, expv=expv, maskv=maskv):
                if j in maskv:
                    kb.wait("tensor", "p1mk", maskv[j])
                else:
                    kb.wait("tensor", "p1ex", expv[j])
                if j == 0:
                    kb.wait("tensor", "p1on", kb.val("p1on"))
                kt = kts[j]
                for sub in range(4):
                    fn = lambda e, j=j, sub=sub, kt=kt, nit=nit: e.matmul(
                        pbk[2 + sub][:, 0:129],
                        Pb[j % 3][:, sub * 128:(sub + 1) * 128],
                        ckvaug_s[:, kt * 144: kt * 144 + 129],
                        start=(j == 0), stop=(j == nit - 1))
                    kb.inc("tensor", fn, "p1pv") if sub == 3 else kb.do("tensor", fn)

            for it, kt in enumerate(kts):
                sbk = pbk[it % 2]
                if it >= 2:
                    kb.wait("tensor", "p1ex", expv[it - 2])
                kb.inc("tensor", lambda e, kt=kt, h=h, sbk=sbk: e.matmul(
                    sbk[:], ckvT_s[:, kt * 128:(kt + 1) * 128],
                    qtg[:, h * 512:(h + 1) * 512], start=True, stop=True),
                    "p1sm")
                pbuf = Pb[it % 3]
                kb.wait("scalar", "p1sm")
                kb.wait("scalar", "p1pv", kb.val("p1pv"))
                col = colbase + it
                expv[it] = kb.inc("scalar", lambda e, sbk=sbk, pbuf=pbuf,
                                  col=col: e.activation(
                    pbuf[:], sbk[:], AF.Exp, bias=amla_s[:, col:col + 1],
                    scale=SCALE), "p1ex")
                if 4 * g <= kt < 4 * g + 4:
                    s_ = kt - 4 * g
                    kb.wait("vector", "p1ex", expv[it])
                    if s_ > 0:
                        kb.do("vector", lambda e, pbuf=pbuf, s_=s_: e.memset(
                            pbuf[:, 0:s_ * 128], 0.0))
                    maskv[it] = kb.inc("vector", lambda e, pbuf=pbuf, s_=s_:
                                       e.tensor_tensor(
                        pbuf[:, s_ * 128:(s_ + 1) * 128],
                        pbuf[:, s_ * 128:(s_ + 1) * 128], tri_s[:],
                        op=ALU.mult), "p1mk")
                pend.append(it)
                if len(pend) > 3:
                    emit_pv(pend.pop(1))
            for j_ in pend[1:]:
                emit_pv(j_)
            kb.wait("vector", "p1pv")
            for sub in range(3):
                kb.do("vector", lambda e, sub=sub: e.reciprocal(
                    rcol[:, sub:sub + 1], pbk[2 + sub][:, 128:129]))
            kb.inc("vector", lambda e: e.reciprocal(
                rcol[:, 3:4], pbk[5][:, 128:129]), "p1rc")
            kb.wait("scalar", "p1rc")
            kb.inc("scalar", lambda e: e.activation(rcol[:, 4:8], rcol[:, 0:4],
                                                    AF.Identity), "p1lnc")
            kb.wait("vector", "p1lnc")
            kb.wait("vector", "p1oT", kb.val("p1oT") - 64)
            for sub in range(4):
                kb.inc("vector", lambda e, sub=sub: e.tensor_scalar(
                    onb[:, sub * 128:(sub + 1) * 128], pbk[2 + sub][:, 0:128],
                    rcol[:, 4 + sub:5 + sub], 2.0 ** -24,
                    op0=ALU.mult, op1=ALU.mult), "p1on")
            kb.wait("sync", "p1on")
            for sub in range(4):
                kb.inc("sync", lambda e, h=h, sub=sub: e.dma_start_transpose(
                    oTg[:, h * 512 + sub * 128: h * 512 + (sub + 1) * 128],
                    onb[:, sub * 128:(sub + 1) * 128]), "p1oT", 16)
        kb.wait("tensor", "p1oT")
        for tsub in range(4):
            t = g * 4 + tsub
            bx = tsub % 2
            kb.wait("sync", "p1xe", kb.val("p1xe") - 8)
            kb.inc("sync", lambda e, t=t, bx=bx: e.dma_start(
                out=xb[bx][:], in_=x0[t * 128:(t + 1) * 128, :]), "p1xr", 16)
            for db in range(4):
                bank = pbk[2 + db]
                kb.wait("tensor", "p1xe", kb.val("p1xe"))
                for h in range(16):
                    fn = lambda e, h=h, db=db, tsub=tsub, bank=bank: e.matmul(
                        bank[:],
                        oTg[:, h * 512 + tsub * 128: h * 512 + (tsub + 1) * 128],
                        w2_s[:, h * 2048 + db * 512: h * 2048 + (db + 1) * 512],
                        start=(h == 0), stop=(h == 15))
                    kb.inc("tensor", fn, "p1s2m") if h == 15 else kb.do("tensor", fn)
            kb.wait("vector", "p1s2m")
            kb.wait("vector", "p1xr")
            bo = 0
            kb.wait("vector", "p1xs", kb.val("p1xs"))
            for db in range(4):
                kb.inc("vector", lambda e, db=db, bo=bo, bx=bx: e.tensor_tensor(
                    x1t[bo][:, db * 512:(db + 1) * 512], pbk[2 + db][:],
                    xb[bx][:, db * 512:(db + 1) * 512], op=ALU.add), "p1xe")
            kb.wait("sync", "p1xe")
            kb.inc("sync", lambda e, t=t, bo=bo: e.dma_start(
                out=x1_d[t * 128:(t + 1) * 128, :], in_=x1t[bo][:]), "p1xs", 16)
    kb.flush()
    al.free("p1per")

    if n_phases == 1:
        al.free("attnsh")
        _copy_out(kb, al, x1_d, y)
        stack.close()
        return nc

    kb.begin()
    ropescr = al.sb("p2tb", "ropescr", (128, 2048), BF16)
    negu_s = al.sb("p2tb", "negu_s", (128, 4 * 512), BF16)
    sel_s = al.sb("p2tb", "sel_s", (128, 32), F32)
    vffn_s = al.sb("p2tb", "vffn_s", (128, FFN_NC), F32)
    cs_s = al.sb("p2tb", "cs_s", (128, 16 * 64), F32)
    ln1b_s = al.sb("p2tb", "ln1b_s", (128, D), BF16)
    qnb_s = al.sb("p2tb", "qnb_s", (128, D), BF16)
    knb_s = al.sb("p2tb", "knb_s", (128, 512), BF16)
    for s_, d_ in ((negu, negu_s), (seli, sel_s), (vffn, vffn_s),
                   (cossin, cs_s)):
        kb.inc("sync", lambda e, s=s_, d=d_: e.dma_start(out=d[:], in_=s),
               "p2tb", 16)
    for s_, d_ in ((ln1b, ln1b_s), (qnb, qnb_s), (knb, knb_s)):
        kb.inc("gpsimd", lambda e, s=s_, d=d_: e.dma_start(out=d[:], in_=s),
               "p2tg", 16)
    wkv_s = al.sb("p2a", "wkv_s", (128, 2 * 16 * 512), BF16)
    hTt = [al.sb("p2a", "hTt%d" % i, (128, 16 * 128), BF16) for i in range(2)]
    xw4 = al.sb("p2a", "xw4", (128, 4 * 2048), BF16)
    ktf4 = al.sb("p2a", "ktf4", (128, 4 * 512), F32)
    krb = al.sb("p2a", "krb", (128, 512), BF16)
    kTown = al.sb("p2a", "kTown", (128, 4 * 2048), BF16)
    vaugo = al.sb("p2a", "vaugo", (128, VAW), BF16)

    for ci in range(16):
        kb.inc("gpsimd", lambda e, ci=ci: e.dma_start(
            out=wkv_s[:, ci * 512:(ci + 1) * 512],
            in_=wk[ci * 128:(ci + 1) * 128, :]), "p2tg", 16)
        kb.inc("gpsimd", lambda e, ci=ci: e.dma_start(
            out=wkv_s[:, 8192 + ci * 512: 8192 + (ci + 1) * 512],
            in_=wv[ci * 128:(ci + 1) * 128, :]), "p2tg", 16)
    kb.inc("vector", lambda e: e.memset(vaugo[:], 1.0), "p2vone", 1)

    def h_tile(tag, t, b):
        kb.inc("sync", lambda e, t=t, b=b: e.dma_start(
            out=xb[b][:], in_=x1_d[t * 128:(t + 1) * 128, :]), tag + "ld", 16)
        kb.wait("vector", tag + "ld")
        kb.wait("scalar", tag + "ld")
        _stats(kb, xb[b][:], D, stat[b], junkb, LN_EPS, True, b)
        kb.wait("vector", "sst%d" % b)
        kb.inc("vector", lambda e, b=b: e.tensor_scalar(
            zb[b][:], xb[b][:], stat[b][:, 1:2], stat[b][:, 4:5],
            op0=ALU.subtract, op1=ALU.mult), tag + "z")
        kb.wait("scalar", tag + "z")
        _stats(kb, zb[b][:], D, stat[b], junkb, RMS_EPS, False, b)
        kb.wait("vector", "sst%d" % b)
        kb.do("vector", lambda e, b=b: e.tensor_scalar(
            zb[b][:], zb[b][:], stat[b][:, 4:5], None, op0=ALU.mult))
        kb.inc("vector", lambda e, b=b: e.tensor_tensor(
            zb[b][:], zb[b][:], ln1b_s[:], op=ALU.mult), tag + "h")

    def rope_emit(src, srcw, soff, dst, dstw, doff, nh, tcol, tag):
        cosap = bass.AP(cs_s, tcol * 64, [[1024, 128], [0, nh], [1, 32]])
        sinap = bass.AP(cs_s, tcol * 64 + 32, [[1024, 128], [0, nh], [1, 32]])
        sc = [bass.AP(ropescr, 32 * nh * i, [[2048, 128], [32, nh], [1, 32]])
              for i in range(4)]
        q1 = bass.AP(src, soff, [[srcw, 128], [128, nh], [1, 32]])
        q2 = bass.AP(src, soff + 32, [[srcw, 128], [128, nh], [1, 32]])
        d1 = bass.AP(dst, doff, [[dstw, 128], [128, nh], [1, 32]])
        d2 = bass.AP(dst, doff + 32, [[dstw, 128], [128, nh], [1, 32]])
        kb.do("vector", lambda e: e.tensor_tensor(sc[0], q1, cosap, op=ALU.mult))
        kb.do("vector", lambda e: e.tensor_tensor(sc[1], q2, sinap, op=ALU.mult))
        kb.do("vector", lambda e: e.tensor_tensor(sc[2], q2, cosap, op=ALU.mult))
        kb.do("vector", lambda e: e.tensor_tensor(sc[3], q1, sinap, op=ALU.mult))
        kb.do("vector", lambda e: e.tensor_tensor(d1, sc[0], sc[1],
                                                  op=ALU.subtract))
        kb.do("vector", lambda e: e.tensor_tensor(d2, sc[2], sc[3], op=ALU.add))
        if src is not dst or soff != doff:
            pasrc = bass.AP(src, soff + 64, [[srcw, 128], [128, nh], [1, 64]])
            padst = bass.AP(dst, doff + 64, [[dstw, 128], [128, nh], [1, 64]])
            kb.do("vector", lambda e: e.tensor_copy(padst, pasrc))
        kb.inc("vector", lambda e: e.tensor_copy(
            bass.AP(dst, doff, [[dstw, 128], [1, 1]]),
            bass.AP(dst, doff, [[dstw, 128], [1, 1]])), tag)

    kb.wait("vector", "p2tg")
    kb.wait("vector", "p2tb")
    kb.wait("scalar", "p2tg")
    kb.wait("scalar", "p2tb")
    for wvi in range(4):
        tiles = [wvi * 4 + i for i in range(4)]
        kb.wait("gpsimd", "p2ahT", wvi * 4 * 256)
        for i, t in enumerate(tiles):
            kb.inc("gpsimd", lambda e, t=t, i=i: e.dma_start(
                out=xw4[:, i * 2048:(i + 1) * 2048],
                in_=x1_d[t * 128:(t + 1) * 128, :]), "p2ald", 16)
        kb.wait("vector", "p2ald")
        kb.wait("scalar", "p2ald")
        xsl = [xw4[:, i * 2048:(i + 1) * 2048] for i in range(4)]
        _stats4(kb, xsl, D, st4, junkb, LN_EPS, True)
        kb.wait("vector", "sst0")
        for i in range(4):
            fn = lambda e, i=i: e.tensor_scalar(
                xsl[i], xsl[i], st4[:, i * 8 + 1:i * 8 + 2],
                st4[:, i * 8 + 4:i * 8 + 5], op0=ALU.subtract, op1=ALU.mult)
            kb.inc("vector", fn, "p2az") if i == 3 else kb.do("vector", fn)
        kb.wait("scalar", "p2az")
        _stats4(kb, xsl, D, st4, junkb, RMS_EPS, False)
        kb.wait("vector", "sst0")
        for i in range(4):
            kb.do("vector", lambda e, i=i: e.tensor_scalar(
                xsl[i], xsl[i], st4[:, i * 8 + 4:i * 8 + 5], None,
                op0=ALU.mult))
            fn = lambda e, i=i: e.tensor_tensor(xsl[i], xsl[i], ln1b_s[:],
                                                op=ALU.mult)
            kb.inc("vector", fn, "p2ah") if i == 3 else kb.do("vector", fn)
        kb.wait("sync", "p2ah")
        for i, t in enumerate(tiles):
            ht = hTt[t % 2]
            kb.wait("sync", "p2akm", max(0, t - 1))
            kb.wait("sync", "p2avm", max(0, t - 1))
            for dch in range(16):
                kb.inc("sync", lambda e, i=i, dch=dch, ht=ht:
                       e.dma_start_transpose(
                    ht[:, dch * 128:(dch + 1) * 128],
                    xw4[:, i * 2048 + dch * 128: i * 2048 + (dch + 1) * 128]),
                    "p2ahT", 16)
            kb.wait("tensor", "p2ahT")
            kb.wait("tensor", "p2tg")
            kb.wait("tensor", "p2ake", t)
            for dch in range(16):
                fn = lambda e, dch=dch, ht=ht: e.matmul(
                    pbk[0][:], ht[:, dch * 128:(dch + 1) * 128],
                    wkv_s[:, dch * 512:(dch + 1) * 512],
                    start=(dch == 0), stop=(dch == 15))
                kb.inc("tensor", fn, "p2akm") if dch == 15 else kb.do("tensor", fn)
            kb.wait("tensor", "p2ave", t)
            for dch in range(16):
                fn = lambda e, dch=dch, ht=ht: e.matmul(
                    pbk[1][:], ht[:, dch * 128:(dch + 1) * 128],
                    wkv_s[:, 8192 + dch * 512: 8192 + (dch + 1) * 512],
                    start=(dch == 0), stop=(dch == 15))
                kb.inc("tensor", fn, "p2avm") if dch == 15 else kb.do("tensor", fn)
            kb.wait("vector", "p2akm")
            kb.inc("vector", lambda e, i=i: e.tensor_copy(
                ktf4[:, i * 512:(i + 1) * 512], pbk[0][:]), "p2ake")
            kb.wait("vector", "p2avm")
            kb.wait("vector", "p2vone", 1)
            vdst = bass.AP(vaugo, t * 528, [[VAW, 128], [132, 4], [1, 128]])
            vsrc = bass.AP(pbk[1], 0, [[512, 128], [128, 4], [1, 128]])
            kb.inc("vector", lambda e, vdst=vdst, vsrc=vsrc: e.tensor_copy(
                vdst, vsrc), "p2ave")
        kb.wait("scalar", "p2ake")
        ksl = [ktf4[:, i * 512:(i + 1) * 512] for i in range(4)]
        _stats4(kb, ksl, 512, st4, junkb, RMS_EPS, False)
        kb.wait("vector", "sst0")
        for i, t in enumerate(tiles):
            kb.do("vector", lambda e, i=i: e.tensor_scalar(
                ksl[i], ksl[i], st4[:, i * 8 + 4:i * 8 + 5], None,
                op0=ALU.mult))
            kb.do("vector", lambda e, i=i: e.tensor_tensor(
                ksl[i], ksl[i], knb_s[:], op=ALU.mult))
            kb.wait("vector", "p2akT", (wvi * 4 + i) * 64)
            rope_emit(ktf4, 4 * 512, i * 512, krb, 512, 0, 4, t, "p2akr")
            kb.wait("sync", "p2akr")
            for kvh in range(4):
                kb.inc("sync", lambda e, kvh=kvh, t=t: e.dma_start_transpose(
                    kTown[:, kvh * 2048 + t * 128: kvh * 2048 + (t + 1) * 128],
                    krb[:, kvh * 128:(kvh + 1) * 128]), "p2akT", 16)
    kb.wait("sync", "p2akT")
    kb.wait("sync", "p2ave")
    kb.inc("sync", lambda e: e.dma_start(out=kvin_d[:, 0:8192], in_=kTown[:]),
           "p2kvw", 16)
    kb.inc("sync", lambda e: e.dma_start(out=kvin_d[:, 8192:KVW], in_=vaugo[:]),
           "p2kvw", 16)
    kb.wait("gpsimd", "p2kvw")
    kb.inc("gpsimd", lambda e: e.collective_compute(
        "AllGather", ALU.bypass,
        replica_groups=[[0, 1], [2, 3], [4, 5], [6, 7]],
        ins=[kvin_d.opt()], outs=[kvout_d.opt()]), "p2cc", 1)
    kb.flush()
    al.free("p2a")

    kb.begin()
    kT = al.sb("p2b", "kT", (128, 4 * 4096), BF16)
    vaug = al.sb("p2b", "vaug", (128, 32 * 528), BF16)
    hTg2 = al.sb("p2b", "hTg2", (128, 16 * 512), BF16)
    o2Tg = hTg2
    qf4 = al.sb("p2b", "qf4", (128, 4 * 2048), BF16)
    q2Tg = al.sb("p2b", "q2Tg", (128, 16 * 512), BF16)
    wqo2 = [al.sb("p2b", "wqo%d" % i, (128, 16 * 512), BF16) for i in range(2)]
    x2s = al.sb("p2b", "x2s", (128, 1024), F32)
    s2st = al.sb("p2b", "s2st", (128, 8), F32)

    kb.wait("sync", "p2cc", 1)
    for r in range(2):
        for kvh in range(4):
            kb.inc("sync", lambda e, r=r, kvh=kvh: e.dma_start(
                out=kT[:, kvh * 4096 + r * 2048: kvh * 4096 + (r + 1) * 2048],
                in_=kvout_d[r, :, kvh * 2048:(kvh + 1) * 2048]), "p2kvr", 16)
        kb.inc("sync", lambda e, r=r: e.dma_start(
            out=vaug[:, r * VAW:(r + 1) * VAW],
            in_=kvout_d[r, :, 8192:KVW]), "p2kvr", 16)
    kb.wait("tensor", "p2kvr")
    if DBG_P2:
        kb.wait("sync", "p2kvr")
        kb.inc("sync", lambda e: e.dma_start(out=dbg_kt, in_=kT[:]), "dbg", 16)
        kb.inc("sync", lambda e: e.dma_start(out=dbg_va, in_=vaug[:]), "dbg", 16)

    for g in range(NG):
        kb.wait("sync", "p2qm", kb.val("p2qm"))
        kb.wait("sync", "p2s2m", kb.val("p2s2m"))
        kb.wait("sync", "p2xe", kb.val("p2xe"))
        kb.wait("gpsimd", "p2qT", kb.val("p2qT"))
        kb.wait("gpsimd", "sst0", kb.val("sst0"))
        for ts4 in range(4):
            t = g * 4 + ts4
            kb.inc("gpsimd", lambda e, t=t, ts4=ts4: e.dma_start(
                out=qf4[:, ts4 * 2048:(ts4 + 1) * 2048],
                in_=x1_d[t * 128:(t + 1) * 128, :]), "p2ald", 16)
        kb.wait("vector", "p2ald")
        kb.wait("scalar", "p2ald")
        hsl = [qf4[:, i * 2048:(i + 1) * 2048] for i in range(4)]
        _stats4(kb, hsl, D, st4, junkb, LN_EPS, True)
        kb.wait("vector", "sst0")
        for i in range(4):
            fn = lambda e, i=i: e.tensor_scalar(
                hsl[i], hsl[i], st4[:, i * 8 + 1:i * 8 + 2],
                st4[:, i * 8 + 4:i * 8 + 5], op0=ALU.subtract, op1=ALU.mult)
            kb.inc("vector", fn, "p2az") if i == 3 else kb.do("vector", fn)
        kb.wait("scalar", "p2az")
        _stats4(kb, hsl, D, st4, junkb, RMS_EPS, False)
        kb.wait("vector", "sst0")
        for i in range(4):
            kb.do("vector", lambda e, i=i: e.tensor_scalar(
                hsl[i], hsl[i], st4[:, i * 8 + 4:i * 8 + 5], None,
                op0=ALU.mult))
            fn = lambda e, i=i: e.tensor_tensor(hsl[i], hsl[i], ln1b_s[:],
                                                op=ALU.mult)
            kb.inc("vector", fn, "p2bh") if i == 3 else kb.do("vector", fn)
        kb.wait("sync", "p2bh")
        for ts4 in range(4):
            for dch in range(16):
                kb.inc("sync", lambda e, dch=dch, ts4=ts4: e.dma_start_transpose(
                    hTg2[:, dch * 512 + ts4 * 128: dch * 512 + (ts4 + 1) * 128],
                    qf4[:, ts4 * 2048 + dch * 128: ts4 * 2048 + (dch + 1) * 128]),
                    "p2bhT", 16)
        kb.wait("tensor", "p2bhT")
        kb.wait("vector", "p2bhT", kb.val("p2bhT"))
        for fb in range(4):
            wsl = wqo2[fb % 2]
            sname = "p2wq%d" % (fb % 2)
            kb.wait("gpsimd", "p2qm", kb.val("p2qm") - 4)
            kb.wait("gpsimd", "p2s2m", kb.val("p2s2m"))
            for dch in range(16):
                kb.inc("gpsimd", lambda e, dch=dch, fb=fb, wsl=wsl: e.dma_start(
                    out=wsl[:, dch * 512:(dch + 1) * 512],
                    in_=wq[dch * 128:(dch + 1) * 128, fb * 512:(fb + 1) * 512]),
                    sname, 16)
            kb.wait("tensor", sname)
            for ts4 in range(4):
                bank = pbk[ts4 % 2]
                kb.wait("tensor", "p2qe", kb.val("p2qe") - 1)
                for dch in range(16):
                    fn = lambda e, dch=dch, ts4=ts4, bank=bank, wsl=wsl: \
                        e.matmul(
                        bank[:],
                        hTg2[:, dch * 512 + ts4 * 128: dch * 512 + (ts4 + 1) * 128],
                        wsl[:, dch * 512:(dch + 1) * 512],
                        start=(dch == 0), stop=(dch == 15))
                    kb.inc("tensor", fn, "p2qm") if dch == 15 else kb.do("tensor", fn)
                kb.wait("vector", "p2qm")
                kb.inc("vector", lambda e, ts4=ts4, fb=fb, bank=bank: e.tensor_copy(
                    qf4[:, ts4 * 2048 + fb * 512: ts4 * 2048 + (fb + 1) * 512],
                    bank[:]), "p2qe")
        kb.wait("sync", "p1sm", kb.val("p1sm"))
        kb.wait("vector", "p2qe", kb.val("p2qe"))
        kb.wait("scalar", "p2qe", kb.val("p2qe"))
        qsl4 = [qf4[:, i * 2048:(i + 1) * 2048] for i in range(4)]
        _stats4(kb, qsl4, D, st4, junkb, RMS_EPS, False)
        kb.wait("vector", "sst0")
        for ts4 in range(4):
            t = g * 4 + ts4
            qsl = qsl4[ts4]
            kb.do("vector", lambda e, qsl=qsl, ts4=ts4: e.tensor_scalar(
                qsl, qsl, st4[:, ts4 * 8 + 4:ts4 * 8 + 5], None, op0=ALU.mult))
            kb.inc("vector", lambda e, qsl=qsl: e.tensor_tensor(
                qsl, qsl, qnb_s[:], op=ALU.mult), "p2qn")
            rope_emit(qf4, 4 * 2048, ts4 * 2048, qf4, 4 * 2048, ts4 * 2048,
                      16, t, "p2qr")
            kb.wait("sync", "p2qr")
            for hh in range(16):
                kb.inc("sync", lambda e, hh=hh, ts4=ts4: e.dma_start_transpose(
                    q2Tg[:, hh * 512 + ts4 * 128: hh * 512 + (ts4 + 1) * 128],
                    qf4[:, ts4 * 2048 + hh * 128: ts4 * 2048 + (hh + 1) * 128]),
                    "p2qT", 16)
        kb.wait("tensor", "p2qT")
        if DBG_P2:
            kb.wait("sync", "p2qT")
            kb.inc("sync", lambda e, g=g: e.dma_start(
                out=dbg_q2[g], in_=q2Tg[:]), "dbg", 16)
        kb.wait("tensor", "p2qe")
        kb.wait("sync", "p2s2m", kb.val("p2s2m"))
        kb.wait("tensor", "p2xe", kb.val("p2xe"))

        kts = FFN_KTS[g]
        nit = len(kts)
        colbase = sum(len(FFN_KTS[gg]) for gg in range(g))
        cand = FFN_CAND[g]
        kb.wait("sync", "p2qm", kb.val("p2qm"))
        for h in range(16):
            kvh = h // 4
            expv = {}
            maskv = {}
            pend = [None]

            def emit_pv2(j2_, kts=kts, nit=nit, expv=expv, maskv=maskv,
                         kvh=kvh):
                if j2_ in maskv:
                    kb.wait("tensor", "p1mk", maskv[j2_])
                else:
                    kb.wait("tensor", "p1ex", expv[j2_])
                if j2_ == 0:
                    kb.wait("tensor", "p1on", kb.val("p1on"))
                kt = kts[j2_]
                for sub in range(4):
                    fn = lambda e, j2_=j2_, sub=sub, kt=kt, nit=nit, kvh=kvh: \
                        e.matmul(
                        pbk[2 + sub][:, 0:129],
                        Pb[j2_ % 3][:, sub * 128:(sub + 1) * 128],
                        vaug[:, kt * 528 + kvh * 132: kt * 528 + kvh * 132 + 129],
                        start=(j2_ == 0), stop=(j2_ == nit - 1))
                    kb.inc("tensor", fn, "p1pv") if sub == 3 else kb.do("tensor", fn)

            for it, kt in enumerate(kts):
                sbk = pbk[it % 2]
                if it >= 2:
                    kb.wait("tensor", "p1ex", expv[it - 2])
                kb.inc("tensor", lambda e, kt=kt, h=h, kvh=kvh, sbk=sbk:
                       e.matmul(
                    sbk[:],
                    kT[:, kvh * 4096 + kt * 128: kvh * 4096 + (kt + 1) * 128],
                    q2Tg[:, h * 512:(h + 1) * 512], start=True, stop=True),
                    "p1sm")
                pbuf = Pb[it % 3]
                kb.wait("scalar", "p1sm")
                kb.wait("scalar", "p1pv", kb.val("p1pv"))
                col = colbase + it
                expv[it] = kb.inc("scalar", lambda e, sbk=sbk, pbuf=pbuf,
                                  col=col: e.activation(
                    pbuf[:], sbk[:], AF.Exp, bias=vffn_s[:, col:col + 1],
                    scale=SCALE), "p1ex")
                if kt in cand:
                    jj = cand.index(kt)
                    s_ = jj % 4
                    slot = g * 8 + jj
                    kb.wait("vector", "p1ex", expv[it])
                    kb.do("vector", lambda e, pbuf=pbuf, s_=s_: e.tensor_tensor(
                        Pscr[:], pbuf[:], negu_s[:, s_ * 512:(s_ + 1) * 512],
                        op=ALU.mult))
                    maskv[it] = kb.inc("vector", lambda e, pbuf=pbuf, slot=slot:
                                       e.scalar_tensor_tensor(
                        pbuf[:], Pscr[:], sel_s[:, slot:slot + 1], pbuf[:],
                        op0=ALU.mult, op1=ALU.add), "p1mk")
                pend.append(it)
                if len(pend) > 3:
                    emit_pv2(pend.pop(1))
            for j_ in pend[1:]:
                emit_pv2(j_)
            kb.wait("vector", "p1pv")
            for sub in range(3):
                kb.do("vector", lambda e, sub=sub: e.reciprocal(
                    rcol[:, sub:sub + 1], pbk[2 + sub][:, 128:129]))
            kb.inc("vector", lambda e: e.reciprocal(
                rcol[:, 3:4], pbk[5][:, 128:129]), "p1rc")
            kb.wait("scalar", "p1rc")
            kb.inc("scalar", lambda e: e.activation(rcol[:, 4:8], rcol[:, 0:4],
                                                    AF.Identity), "p1lnc")
            kb.wait("vector", "p1lnc")
            kb.wait("vector", "p1oT", kb.val("p1oT") - 64)
            for sub in range(4):
                kb.inc("vector", lambda e, sub=sub: e.tensor_scalar(
                    onb[:, sub * 128:(sub + 1) * 128], pbk[2 + sub][:, 0:128],
                    rcol[:, 4 + sub:5 + sub], None, op0=ALU.mult), "p1on")
            kb.wait("sync", "p1on")
            for sub in range(4):
                kb.inc("sync", lambda e, h=h, sub=sub: e.dma_start_transpose(
                    o2Tg[:, h * 512 + sub * 128: h * 512 + (sub + 1) * 128],
                    onb[:, sub * 128:(sub + 1) * 128]), "p1oT", 16)
        kb.wait("tensor", "p1oT")
        if DBG_P2:
            kb.wait("sync", "p1oT")
            kb.inc("sync", lambda e, g=g: e.dma_start(
                out=dbg_o2[g], in_=o2Tg[:]), "dbg", 16)
            kb.wait("tensor", "dbg")
        kb.wait("gpsimd", "p2qT", kb.val("p2qT"))
        for tsub in range(4):
            t = g * 4 + tsub
            kb.inc("gpsimd", lambda e, t=t, tsub=tsub: e.dma_start(
                out=qf4[:, tsub * 2048:(tsub + 1) * 2048],
                in_=x1_d[t * 128:(t + 1) * 128, :]), "p2ald", 16)
        kb.wait("vector", "p2ald")
        kb.wait("scalar", "p2ald")
        ssl = [qf4[:, i * 2048:(i + 1) * 2048] for i in range(4)]
        _stats4(kb, ssl, D, st4, junkb, LN_EPS, True)
        kb.wait("scalar", "sst0")
        kb.wait("scalar", "p2xe", kb.val("p2xe"))
        for tsub in range(4):
            kb.do("scalar", lambda e, tsub=tsub: e.activation(
                s2st[:, tsub * 2:tsub * 2 + 1],
                st4[:, tsub * 8 + 4:tsub * 8 + 5], AF.Identity))
            fn = lambda e, tsub=tsub: e.activation(
                s2st[:, tsub * 2 + 1:tsub * 2 + 2],
                st4[:, tsub * 8 + 6:tsub * 8 + 7], AF.Identity)
            kb.inc("scalar", fn, "p2st") if tsub == 3 else kb.do("scalar", fn)
        for db in range(4):
            wsl = wqo2[db % 2]
            sname = "p2wq%d" % (db % 2)
            kb.wait("gpsimd", "p2s2m", kb.val("p2s2m") - 4)
            kb.wait("gpsimd", "p2qm", kb.val("p2qm"))
            for hh in range(16):
                kb.inc("gpsimd", lambda e, hh=hh, db=db, wsl=wsl: e.dma_start(
                    out=wsl[:, hh * 512:(hh + 1) * 512],
                    in_=wo[hh * 128:(hh + 1) * 128, db * 512:(db + 1) * 512]),
                    sname, 16)
            kb.wait("tensor", sname)
            for tsub in range(4):
                t = g * 4 + tsub
                i2 = db * 4 + tsub
                bank = pbk[2 + i2 % 2]
                xsl = slice((i2 % 2) * 512, (i2 % 2) * 512 + 512)
                kb.wait("sync", "p2xs", kb.val("p2xs") - 16)
                kb.inc("sync", lambda e, t=t, db=db, xsl=xsl: e.dma_start(
                    out=x2s[:, xsl], in_=x1_d[t * 128:(t + 1) * 128,
                                              db * 512:(db + 1) * 512]),
                    "p2xr", 16)
                kb.wait("tensor", "p2xe", kb.val("p2xe") - 1)
                for hh in range(16):
                    fn = lambda e, hh=hh, db=db, tsub=tsub, bank=bank, wsl=wsl: \
                        e.matmul(
                        bank[:],
                        o2Tg[:, hh * 512 + tsub * 128: hh * 512 + (tsub + 1) * 128],
                        wsl[:, hh * 512:(hh + 1) * 512],
                        start=(hh == 0), stop=(hh == 15))
                    kb.inc("tensor", fn, "p2s2m") if hh == 15 else kb.do("tensor", fn)
                kb.wait("vector", "p2s2m")
                kb.wait("vector", "p2xr")
                kb.wait("vector", "p2st", kb.val("p2st"))
                kb.wait("vector", "p2xs", kb.val("p2xs") - 16)
                kb.do("vector", lambda e, xsl=xsl, tsub=tsub, bank=bank:
                      e.scalar_tensor_tensor(
                    x2s[:, xsl], x2s[:, xsl], s2st[:, tsub * 2:tsub * 2 + 1],
                    bank[:], op0=ALU.mult, op1=ALU.add))
                kb.inc("vector", lambda e, xsl=xsl, tsub=tsub: e.tensor_scalar(
                    x2s[:, xsl], x2s[:, xsl],
                    s2st[:, tsub * 2 + 1:tsub * 2 + 2], None,
                    op0=ALU.subtract), "p2xe")
                kb.wait("sync", "p2xe")
                kb.inc("sync", lambda e, t=t, db=db, xsl=xsl: e.dma_start(
                    out=x2_d[t * 128:(t + 1) * 128, db * 512:(db + 1) * 512],
                    in_=x2s[:, xsl]), "p2xs", 16)
    kb.flush()
    al.free("p2b")
    al.free("p2tb")
    al.free("attnsh")

    if n_phases == 2:
        _copy_out(kb, al, x2_d, y)
        stack.close()
        return nc

    kb.begin()
    ln2b_s = al.sb("p3", "ln2b_s", (128, D), F32)
    rbb_s = al.sb("p3", "rbb_s", (128, 8), F32)
    comb_s = al.sb("p3", "comb_s", (128, 16 * 8), F32)
    for s_, d_ in ((ln2b, ln2b_s), (rbb, rbb_s)):
        kb.inc("sync", lambda e, s=s_, d=d_: e.dma_start(out=d[:], in_=s),
               "p3tab", 16)
    x2m = al.sb("p3", "x2m", (128, 8 * 2048), F32)
    h2T = al.sb("p3", "h2T", (128, 16 * 1024), BF16)
    h2f = xb[0]
    h2Tf = al.sb("p3", "h2Tf", (128, 128), F32)
    rws = al.sb("p3", "rws", (128, 16 * 8), BF16)
    sig = al.sb("p3", "sig", (128, 8), F32)
    sg2 = al.sb("p3", "sg2", (128, 32), F32)
    sigall = al.sb("p3", "sigall", (128, 64), F32)
    rt3 = al.sb("p3", "rt3", (128, 224), F32)
    actT = al.sb("p3", "actT", (128, 8 * 1024), BF16)
    gsil = al.sb("p3", "gsil", (128, 1024), BF16)
    wgs = [al.sb("p3", "wgs%d" % i, (128, 16 * 128), BF16) for i in range(2)]
    wus = [al.sb("p3", "wus%d" % i, (128, 16 * 128), BF16) for i in range(2)]
    wds = al.sb("p3", "wds", (128, 8 * 2048), BF16)

    for ch in range(16):
        kb.inc("gpsimd", lambda e, ch=ch: e.dma_start(
            out=rws[:, ch * 8:(ch + 1) * 8],
            in_=rw[ch * 128:(ch + 1) * 128, :]), "p3rw", 16)

    kb.wait("vector", "p3tab")
    for hf in range(2):
        kb.wait("vector", "p3gm", kb.val("p3gm"))
        for tl in range(8):
            t = hf * 8 + tl
            kb.wait("sync", "p3ys", hf * 8 * 16)
            kb.inc("sync", lambda e, t=t, tl=tl: e.dma_start(
                out=x2m[:, tl * 2048:(tl + 1) * 2048],
                in_=x2_d[t * 128:(t + 1) * 128, :]), "p3xl", 16)
        for w in range(2):
            tiles = [w * 4 + i for i in range(4)]
            kb.wait("vector", "p3xl", (hf * 8 + tiles[-1] + 1) * 16)
            kb.wait("scalar", "p3xl", (hf * 8 + tiles[-1] + 1) * 16)
            _stats4(kb, [x2m[:, tl * 2048:(tl + 1) * 2048] for tl in tiles],
                    D, st4, junkb, RMS_EPS, False)
            kb.wait("vector", "sst0")
            for i, tl in enumerate(tiles):
                b = tl % 2
                tg_ = hf * 8 + tl
                kb.wait("vector", "p3hT", (tg_ - 1) * 256)
                xt = x2m[:, tl * 2048:(tl + 1) * 2048]
                kb.do("vector", lambda e, xt=xt, i=i, b=b: e.tensor_scalar(
                    zb[b][:], xt, st4[:, i * 8 + 4:i * 8 + 5], None,
                    op0=ALU.mult))
                kb.inc("vector", lambda e, b=b: e.tensor_tensor(
                    zb[b][:], zb[b][:], ln2b_s[:], op=ALU.mult), "p3h2")
                kb.wait("sync", "p3h2")
                for dch in range(16):
                    kb.inc("sync", lambda e, b=b, dch=dch, tl=tl:
                           e.dma_start_transpose(
                        h2T[:, dch * 1024 + tl * 128: dch * 1024 + (tl + 1) * 128],
                        zb[b][:, dch * 128:(dch + 1) * 128]), "p3hT", 16)
                kb.wait("tensor", "p3hT")
                kb.wait("tensor", "p3rw", 256)
                kb.wait("tensor", "p3tab", 32)
                kb.wait("tensor", "p3sg", kb.val("p3sg"))
                for dch in range(16):
                    fn = lambda e, dch=dch, tl=tl: e.matmul(
                        pbk[7][:, 0:8],
                        h2T[:, dch * 1024 + tl * 128: dch * 1024 + (tl + 1) * 128],
                        rws[:, dch * 8:(dch + 1) * 8],
                        start=(dch == 0), stop=(dch == 15))
                    kb.inc("tensor", fn, "p3rm") if dch == 15 \
                        else kb.do("tensor", fn)
                kb.wait("scalar", "p3rm")
                kb.inc("scalar", lambda e, tl=tl: e.activation(
                    sigall[:, tl * 8:(tl + 1) * 8], pbk[7][:, 0:8], AF.Sigmoid),
                    "p3sg")
        kb.wait("vector", "p3sg")
        W0 = rt3[:, 0:64]
        W1 = rt3[:, 64:128]
        W2 = rt3[:, 128:192]
        R0 = rt3[:, 192:200]
        R1 = rt3[:, 200:208]
        R2 = rt3[:, 208:216]
        RT3W = 224

        def r1rep(off):
            return bass.AP(rt3, off, [[RT3W, 128], [1, 8], [0, 8]])
        rbbrep = bass.AP(rbb_s, 0, [[8, 128], [0, 8], [1, 8]])
        sig3 = bass.AP(sigall, 0, [[64, 128], [8, 8], [1, 8]])
        w03 = bass.AP(rt3, 0, [[RT3W, 128], [8, 8], [1, 8]])
        w13 = bass.AP(rt3, 64, [[RT3W, 128], [8, 8], [1, 8]])

        def vst(fn):
            kb.inc("vector", fn, "p3sg")
            kb.wait("scalar", "p3sg")

        def sco(dst, srcc):
            kb.inc("scalar", lambda e, d=dst, s=srcc: e.activation(
                d, s, AF.Identity), "p3sg")
            kb.wait("vector", "p3sg")

        vst(lambda e: e.tensor_tensor(w03, sig3, rbbrep, op=ALU.add))
        sco(W1, W0)
        for tl in range(8):
            fn = lambda e, tl=tl: e.reduce_max(
                R0[:, tl:tl + 1], W1[:, tl * 8:(tl + 1) * 8],
                axis=mybir.AxisListType.X)
            kb.inc("vector", fn, "p3sg") if tl == 7 else kb.do("vector", fn)
        kb.wait("scalar", "p3sg")
        sco(R1, R0)
        vst(lambda e: e.tensor_tensor(w03, w13, r1rep(200), op=ALU.is_ge))
        sco(W2, W0)
        vst(lambda e: e.scalar_tensor_tensor(W0, W2, NEG, W1,
                                             op0=ALU.mult, op1=ALU.add))
        sco(W1, W0)
        for tl in range(8):
            fn = lambda e, tl=tl: e.reduce_max(
                R0[:, tl:tl + 1], W1[:, tl * 8:(tl + 1) * 8],
                axis=mybir.AxisListType.X)
            kb.inc("vector", fn, "p3sg") if tl == 7 else kb.do("vector", fn)
        kb.wait("scalar", "p3sg")
        sco(R1, R0)
        vst(lambda e: e.tensor_tensor(w03, w13, r1rep(200), op=ALU.is_ge))
        sco(W1, W0)
        vst(lambda e: e.tensor_tensor(W0, W2, W1, op=ALU.add))
        sco(W2, W0)
        vst(lambda e: e.tensor_tensor(W0, W2, sigall[:], op=ALU.mult))
        sco(W1, W0)
        for tl in range(8):
            fn = lambda e, tl=tl: e.reduce_sum(
                R0[:, tl:tl + 1], W1[:, tl * 8:(tl + 1) * 8],
                axis=mybir.AxisListType.X)
            kb.inc("vector", fn, "p3sg") if tl == 7 else kb.do("vector", fn)
        kb.wait("scalar", "p3sg")
        sco(R1, R0)
        vst(lambda e: e.reciprocal(R2[:], R1[:]))
        sco(R0, R2)
        vst(lambda e: e.tensor_tensor(w03, w13, r1rep(192), op=ALU.mult))
        kb.inc("scalar", lambda e, hf=hf: e.activation(
            comb_s[:, hf * 64:(hf + 1) * 64], W0, AF.Identity), "p3cb")
        kb.wait("tensor", "p3hT", kb.val("p3hT"))
        if DBG_P3 and hf == 0:
            kb.wait("sync", "p3cb", kb.val("p3cb"))
            kb.wait("sync", "p3sg", kb.val("p3sg"))
            kb.inc("sync", lambda e: e.dma_start(
                out=dbg_sg[:, 0:24], in_=sg2[:, 0:24]), "p3x1", 16)
            kb.inc("sync", lambda e: e.dma_start(
                out=dbg_sg[:, 24:32], in_=sig[:]), "p3x1", 16)
            kb.wait("sync", "p3hT", kb.val("p3hT"))
            kb.inc("sync", lambda e: e.dma_start(out=dbg_cb, in_=comb_s[:]),
                   "p3x1", 16)
            kb.inc("sync", lambda e: e.dma_start(out=dbg_h2, in_=h2T[:]),
                   "p3x1", 16)
        for e_ in range(E):
            kb.wait("gpsimd", "p3dn", kb.val("p3dn"))
            for ich in range(8):
                kb.inc("gpsimd", lambda e, e_=e_, ich=ich: e.dma_start(
                    out=wds[:, ich * 2048:(ich + 1) * 2048],
                    in_=wdp[e_, ich * 128:(ich + 1) * 128, :]), "p3wd", 16)
            kb.wait("vector", "p3dn", kb.val("p3dn"))
            for i128 in range(8):
                sl_ = i128 % 2
                wb_ = wgs[sl_]
                wub = wus[sl_]
                kb.wait("gpsimd", "p3gm", kb.val("p3gm") - 4)
                kb.inc("gpsimd", lambda e, e_=e_, i128=i128, wb_=wb_: e.dma_start(
                    out=wb_[:], in_=wg_r[e_, i128, :, :]), "p3wg%d" % sl_, 16)
                kb.inc("gpsimd", lambda e, e_=e_, i128=i128, wub=wub: e.dma_start(
                    out=wub[:], in_=wu_r[e_, i128, :, :]), "p3wg%d" % sl_, 16)
                kb.wait("tensor", "p3wg%d" % sl_)
                kb.wait("tensor", "p3sl", kb.val("p3sl"))
                kb.wait("tensor", "p3ae", kb.val("p3ae"))
                for tg in range(2):
                    for dch in range(16):
                        fn = lambda e, dch=dch, tg=tg, wb_=wb_: e.matmul(
                            pbk[tg][:], wb_[:, dch * 128:(dch + 1) * 128],
                            h2T[:, dch * 1024 + tg * 512: dch * 1024 + (tg + 1) * 512],
                            start=(dch == 0), stop=(dch == 15))
                        kb.inc("tensor", fn, "p3gm") if dch == 15 \
                            else kb.do("tensor", fn)
                        fn2 = lambda e, dch=dch, tg=tg, wub=wub: e.matmul(
                            pbk[2 + tg][:], wub[:, dch * 128:(dch + 1) * 128],
                            h2T[:, dch * 1024 + tg * 512: dch * 1024 + (tg + 1) * 512],
                            start=(dch == 0), stop=(dch == 15))
                        kb.inc("tensor", fn2, "p3gm") if dch == 15 \
                            else kb.do("tensor", fn2)
                kb.wait("scalar", "p3gm")
                for tg in range(2):
                    kb.inc("scalar", lambda e, tg=tg: e.activation(
                        gsil[:, tg * 512:(tg + 1) * 512], pbk[tg][:], AF.Silu),
                        "p3sl")
                kb.wait("vector", "p3sl")
                for tg in range(2):
                    kb.inc("vector", lambda e, tg=tg, i128=i128: e.tensor_tensor(
                        actT[:, i128 * 1024 + tg * 512: i128 * 1024 + (tg + 1) * 512],
                        gsil[:, tg * 512:(tg + 1) * 512], pbk[2 + tg][:],
                        op=ALU.mult), "p3ae")
            kb.wait("tensor", "p3ae")
            kb.wait("tensor", "p3wd")
            for tl in range(8):
                t = hf * 8 + tl
                for db in range(4):
                    bank = pbk[4 + (tl * 4 + db) % 2]
                    kb.wait("tensor", "p3de", kb.val("p3de") - 1)
                    for ich in range(8):
                        fn = lambda e, ich=ich, tl=tl, db=db, bank=bank: \
                            e.matmul(
                            bank[:],
                            actT[:, ich * 1024 + tl * 128: ich * 1024 + (tl + 1) * 128],
                            wds[:, ich * 2048 + db * 512: ich * 2048 + (db + 1) * 512],
                            start=(ich == 0), stop=(ich == 7))
                        kb.inc("tensor", fn, "p3dn") if ich == 7 \
                            else kb.do("tensor", fn)
                    kb.wait("vector", "p3dn")
                    kb.wait("vector", "p3cb", hf + 1)
                    kb.inc("vector", lambda e, tl=tl, db=db, t=t, e_=e_,
                           bank=bank: e.scalar_tensor_tensor(
                        x2m[:, tl * 2048 + db * 512: tl * 2048 + (db + 1) * 512],
                        bank[:], comb_s[:, t * 8 + e_: t * 8 + e_ + 1],
                        x2m[:, tl * 2048 + db * 512: tl * 2048 + (db + 1) * 512],
                        op0=ALU.mult, op1=ALU.add), "p3de")
        for tl in range(8):
            t = hf * 8 + tl
            kb.wait("sync", "p3xa", kb.val("p3xa"))
            kb.inc("sync", lambda e, t=t: e.dma_start(
                out=xb[1][:], in_=x1_d[t * 128:(t + 1) * 128, :]), "p3x1", 16)
            kb.wait("vector", "p3x1")
            kb.wait("vector", "p3de", kb.val("p3de"))
            kb.inc("vector", lambda e, tl=tl: e.tensor_tensor(
                x2m[:, tl * 2048:(tl + 1) * 2048],
                x2m[:, tl * 2048:(tl + 1) * 2048], xb[1][:], op=ALU.add),
                "p3xa")
            kb.wait("sync", "p3xa")
            kb.inc("sync", lambda e, t=t, tl=tl: e.dma_start(
                out=y[t * 128:(t + 1) * 128, :],
                in_=x2m[:, tl * 2048:(tl + 1) * 2048]), "p3ys", 16)
    kb.flush()
    al.free("p3")
    al.free("base")
    stack.close()
    return nc


def _copy_out(kb, al, src_d, y):
    kb.begin()
    dbg = al.sb("dbg", "dbgt", (128, D), F32)
    for t in range(16):
        kb.wait("sync", "dst", (t - 1) * 16)
        kb.inc("sync", lambda e, t=t: e.dma_start(
            out=dbg[:], in_=src_d[t * 128:(t + 1) * 128, :]), "dld", 16)
        kb.wait("sync", "dld")
        kb.inc("sync", lambda e, t=t: e.dma_start(
            out=y[t * 128:(t + 1) * 128, :], in_=dbg[:]), "dst", 16)
    kb.flush()
    al.free("dbg")
    al.free("base")



def _host_tables(H):
    t = {}
    t["identf"] = np.eye(128, dtype=np.float32)
    i = np.arange(128)[:, None]
    j = np.arange(128)[None, :]
    t["tri01"] = (i <= j).astype(ml_dtypes.bfloat16)
    negu = np.zeros((128, 4, 512), np.float32)
    for s in range(4):
        m = np.ones((128, 512), np.float32)
        m[:, : s * 128] = 0.0
        m[:, s * 128:(s + 1) * 128] = (i <= j).astype(np.float32)
        negu[:, s, :] = -(1.0 - m)
    t["negu"] = negu.reshape(128, 4 * 512).astype(ml_dtypes.bfloat16)
    sel = np.zeros((128, 32), np.float32)
    for g in range(NG):
        for fam in range(2):
            for s in range(4):
                sel[:, g * 8 + fam * 4 + s] = 1.0 if fam == H else 0.0
    t["sel"] = sel
    amla_t = np.full((128, MLA_NC), NEG, np.float32)
    ii = np.arange(128)
    colb = 0
    for g in range(NG):
        Q0 = 2048 * H + 512 * g
        for it, kt in enumerate(MLA_KTS[g]):
            if kt < 4 * g + 4:
                kpos = kt * 128 + ii + 2048 * H
                amla_t[:, colb + it] = SLOPE * (kpos - Q0)
            elif H == 1:
                kpos = (kt * 128 + ii + 2048) % 4096
                amla_t[:, colb + it] = SLOPE * (kpos - Q0)
        colb += len(MLA_KTS[g])
    t["amla"] = amla_t
    vffn_t = np.full((128, FFN_NC), NEG, np.float32)
    colb = 0
    for g in range(NG):
        for it, kt in enumerate(FFN_KTS[g]):
            if kt <= 16 * H + 4 * g + 3:
                vffn_t[:, colb + it] = 0.0
        colb += len(FFN_KTS[g])
    t["vffn"] = vffn_t
    pos = (np.arange(T) + 2048 * H).astype(np.float64)
    invf = 1.0 / (1e6 ** (np.arange(0, 64, 2) / 64.0))
    ang = pos[:, None] * invf[None, :]
    cs = np.zeros((128, 16 * 64), np.float32)
    for tt in range(16):
        cs[:, tt * 64: tt * 64 + 32] = np.cos(ang[tt * 128:(tt + 1) * 128])
        cs[:, tt * 64 + 32: tt * 64 + 64] = np.sin(ang[tt * 128:(tt + 1) * 128])
    t["cossin"] = cs
    return t


_NC_CACHE = {}


def kernel(**inp):
    g = lambda k: np.ascontiguousarray(np.asarray(inp[k], dtype=np.float32))
    x0f = g("L0_attn_hidden_states")

    def restage(w):
        return np.ascontiguousarray(
            w.reshape(E, 16, 128, 8, 128).transpose(0, 3, 2, 1, 4)
            .reshape(E, 8, 128, 16 * 128))

    common = {
        "w_dkv": g("L0_attn_w_dkv"), "w_q": g("L0_attn_w_q"),
        "w_uk": g("L0_attn_w_uk"), "w_uv": g("L0_attn_w_uv"),
        "w_o": g("L0_attn_w_o"), "wq": g("L0_ffn_wq"), "wk": g("L0_ffn_wk"),
        "wv": g("L0_ffn_wv"), "wo": g("L0_ffn_wo"),
        "router_w": g("L0_ffn_router_w"),
        "wg_r": restage(g("L0_ffn_wg")), "wu_r": restage(g("L0_ffn_wu")),
        "wd": g("L0_ffn_wd"),
        "ln1b": np.ascontiguousarray(
            np.broadcast_to(g("L0_ffn_ln1_w"), (128, D))),
        "ln2b": np.ascontiguousarray(
            np.broadcast_to(g("L0_ffn_ln2_w"), (128, D))),
        "qnb": np.ascontiguousarray(
            np.broadcast_to(g("L0_ffn_qnorm_w"), (128, D))),
        "knb": np.ascontiguousarray(
            np.broadcast_to(g("L0_ffn_knorm_w"), (128, 512))),
        "rbb": np.ascontiguousarray(
            np.broadcast_to(g("L0_ffn_router_bias"), (128, 8))),
    }
    tabs = [_host_tables(H) for H in range(2)]

    in_maps = []
    for c in range(8):
        b, H = c // 2, c % 2
        m = dict(common)
        m.update(tabs[H])
        m["x0"] = np.ascontiguousarray(np.roll(x0f[b], -2048 * H, axis=0))
        in_maps.append(m)

    if "nc" not in _NC_CACHE:
        _NC_CACHE["nc"] = build_program()
    nc = _NC_CACHE["nc"]
    res = run_bass_kernel_spmd(nc, in_maps, core_ids=list(range(8)))
    out = np.zeros((B, S, D), np.float32)
    for c in range(8):
        b, H = c // 2, c % 2
        out[b, 2048 * H: 2048 * (H + 1), :] = res.results[c]["y"]
    return out


# version 13 (final)
# speedup vs baseline: 1.2244x; 1.0527x over previous
import math
import numpy as np
import ml_dtypes
from contextlib import ExitStack

import concourse.bass as bass
import concourse.mybir as mybir
from concourse.bass_utils import run_bass_kernel_spmd

F32 = mybir.dt.float32
BF16 = mybir.dt.bfloat16
AF = mybir.ActivationFunctionType
ALU = mybir.AluOpType

B, S, D = 4, 4096, 2048
T = 2048
E, I = 8, 1024
NG = 4
SLOPE = 0.0625
SCALE = 1.0 / math.sqrt(128.0)
NEG = -1e30
LN_EPS = 1e-5
RMS_EPS = 1e-6

MLA_KTS = [list(range(4 * g + 4)) + list(range(16, 32)) for g in range(NG)]
FFN_KTS = [list(range(20 + 4 * g)) for g in range(NG)]
FFN_CAND = [[4 * g + s for s in range(4)] + [16 + 4 * g + s for s in range(4)]
            for g in range(NG)]
MLA_NC = sum(len(k) for k in MLA_KTS)
FFN_NC = sum(len(k) for k in FFN_KTS)
VAW = 16 * 528
KVW = 4 * 2048 + VAW

N_PHASES = 3
DBG_P2 = False
DBG_P3 = False
ENGINES = ("sync", "scalar", "vector", "tensor", "gpsimd")


class KB:
    def __init__(self, nc, stack):
        self.nc = nc
        self.stack = stack
        self.sems = {}
        self.dma_sems = set()
        self.q = None

    def sem(self, name):
        if name not in self.sems:
            h = self.stack.enter_context(self.nc.semaphore(name))
            self.sems[name] = [h, 0]
        return self.sems[name]

    def on(self, eng, fn):
        self.q[eng].append(fn)

    def wait(self, eng, name, val=None):
        s = self.sem(name)
        v = s[1] if val is None else val
        if v <= 0:
            return
        self.on(eng, lambda e, h=s[0], v=v: e.wait_ge(h, v))

    def inc(self, eng, fn, name, k=1):
        s = self.sem(name)
        s[1] += k
        if k == 16:
            self.dma_sems.add(name)
        self.on(eng, lambda e, fn=fn, h=s[0], k=k: fn(e).then_inc(h, k))
        return s[1]

    def do(self, eng, fn):
        self.on(eng, fn)

    def val(self, name):
        return self.sem(name)[1]

    def begin(self):
        self.q = {e: [] for e in ENGINES}

    def flush(self):
        for name in sorted(self.dma_sems):
            self.wait("sync", name)
        with self.nc.Block() as blk:
            for e in ENGINES:
                fns = self.q[e]
                if fns:
                    getattr(blk, e)(lambda eng, fns=fns: [f(eng) for f in fns])
        self.q = None


class Alloc:
    def __init__(self, nc):
        self.nc = nc
        self.groups = {}
        self.order = []

    def sb(self, group, name, shape, dt):
        cm = self.nc.sbuf_tensor(name, list(shape), dt)
        h = cm.__enter__()
        if group not in self.groups:
            self.groups[group] = []
            self.order.append(group)
        self.groups[group].append(cm)
        return h

    def free(self, group):
        assert self.order and self.order[-1] == group, (self.order, group)
        for cm in reversed(self.groups.pop(group)):
            cm.__exit__(None, None, None)
        self.order.pop()


def _stats(kb, xt, width, st, junk, eps, mean, b):
    sem = "sst%d" % b
    if mean:
        kb.inc("vector", lambda e: e.reduce_sum(st[:, 0:1], xt,
                                                axis=mybir.AxisListType.X), sem)
    kb.do("scalar", lambda e: e.activation(junk[:, 0:width], xt, AF.Square,
                                           accum_out=st[:, 2:3]))
    kb.do("scalar", lambda e: e.activation(st[:, 3:4], st[:, 2:3],
                                           AF.Identity, scale=1.0 / width))
    if mean:
        kb.wait("scalar", sem)
        kb.do("scalar", lambda e: e.activation(st[:, 1:2], st[:, 0:1],
                                               AF.Identity, scale=1.0 / width))
        kb.inc("scalar", lambda e: e.activation(st[:, 7:8], st[:, 1:2],
                                                AF.Square), sem)
        kb.wait("vector", sem)
        kb.inc("vector", lambda e: e.tensor_scalar(
            st[:, 3:4], st[:, 3:4], st[:, 7:8], eps,
            op0=ALU.subtract, op1=ALU.add), sem)
    else:
        kb.inc("scalar", lambda e: e.activation(st[:, 3:4], st[:, 3:4],
                                                AF.Identity), sem)
        kb.wait("vector", sem)
        kb.inc("vector", lambda e: e.tensor_scalar_add(st[:, 3:4], st[:, 3:4],
                                                       eps), sem)
    kb.wait("scalar", sem)
    kb.do("scalar", lambda e: e.activation(st[:, 4:5], st[:, 3:4], AF.Ln))
    kb.inc("scalar", lambda e: e.activation(st[:, 4:5], st[:, 4:5], AF.Exp,
                                            scale=-0.5), sem)
    if mean:
        kb.wait("vector", sem)
        kb.inc("vector", lambda e: e.tensor_tensor(st[:, 5:6], st[:, 1:2],
                                                   st[:, 4:5], op=ALU.mult),
               sem)
        kb.wait("scalar", sem)
        kb.inc("scalar", lambda e: e.activation(st[:, 6:7], st[:, 5:6],
                                                AF.Identity), sem)


def _stats4(kb, xts, width, st4, junk, eps, mean):
    n = len(xts)
    sem = "sst0"
    if mean:
        for i, xt in enumerate(xts):
            fn = lambda e, i=i, xt=xt: e.reduce_sum(
                st4[:, i * 8:i * 8 + 1], xt, axis=mybir.AxisListType.X)
            kb.inc("vector", fn, sem) if i == n - 1 else kb.do("vector", fn)
        kb.wait("scalar", sem)
    for i, xt in enumerate(xts):
        kb.do("scalar", lambda e, i=i, xt=xt: e.activation(
            junk[:, 0:width], xt, AF.Square, accum_out=st4[:, i * 8 + 2:i * 8 + 3]))
        kb.do("scalar", lambda e, i=i: e.activation(
            st4[:, i * 8 + 3:i * 8 + 4], st4[:, i * 8 + 2:i * 8 + 3],
            AF.Identity, scale=1.0 / width))
    if mean:
        for i in range(n):
            kb.do("scalar", lambda e, i=i: e.activation(
                st4[:, i * 8 + 1:i * 8 + 2], st4[:, i * 8:i * 8 + 1],
                AF.Identity, scale=1.0 / width))
            kb.do("scalar", lambda e, i=i: e.activation(
                st4[:, i * 8 + 7:i * 8 + 8], st4[:, i * 8 + 1:i * 8 + 2],
                AF.Square))
    kb.inc("scalar", lambda e: e.activation(junk[:, 0:1], junk[:, 0:1],
                                            AF.Identity), sem)
    kb.wait("vector", sem)
    for i in range(n):
        if mean:
            fn = lambda e, i=i: e.tensor_scalar(
                st4[:, i * 8 + 3:i * 8 + 4], st4[:, i * 8 + 3:i * 8 + 4],
                st4[:, i * 8 + 7:i * 8 + 8], eps,
                op0=ALU.subtract, op1=ALU.add)
        else:
            fn = lambda e, i=i: e.tensor_scalar_add(
                st4[:, i * 8 + 3:i * 8 + 4], st4[:, i * 8 + 3:i * 8 + 4], eps)
        kb.inc("vector", fn, sem) if i == n - 1 else kb.do("vector", fn)
    kb.wait("scalar", sem)
    for i in range(n):
        kb.do("scalar", lambda e, i=i: e.activation(
            st4[:, i * 8 + 4:i * 8 + 5], st4[:, i * 8 + 3:i * 8 + 4], AF.Ln))
        fn = lambda e, i=i: e.activation(
            st4[:, i * 8 + 4:i * 8 + 5], st4[:, i * 8 + 4:i * 8 + 5],
            AF.Exp, scale=-0.5)
        kb.inc("scalar", fn, sem) if i == n - 1 else kb.do("scalar", fn)
    if mean:
        kb.wait("vector", sem)
        for i in range(n):
            fn = lambda e, i=i: e.tensor_tensor(
                st4[:, i * 8 + 5:i * 8 + 6], st4[:, i * 8 + 1:i * 8 + 2],
                st4[:, i * 8 + 4:i * 8 + 5], op=ALU.mult)
            kb.inc("vector", fn, sem) if i == n - 1 else kb.do("vector", fn)
        kb.wait("scalar", sem)
        for i in range(n):
            fn = lambda e, i=i: e.activation(
                st4[:, i * 8 + 6:i * 8 + 7], st4[:, i * 8 + 5:i * 8 + 6],
                AF.Identity)
            kb.inc("scalar", fn, sem) if i == n - 1 else kb.do("scalar", fn)


def build_program(n_phases=N_PHASES):
    nc = bass.Bass("TRN2", target_bir_lowering=False,
                   detect_race_conditions=False)
    stack = ExitStack()
    kb = KB(nc, stack)
    al = Alloc(nc)

    def din(name, shape, dt=F32):
        return nc.dram_tensor(name, list(shape), dt, kind="ExternalInput").ap()

    x0 = din("x0", (S, D))
    w_dkv = din("w_dkv", (128, D))
    w_q = din("w_q", (D, D))
    w_uk = din("w_uk", (D, 128))
    w_uv = din("w_uv", (D, 128))
    w_o = din("w_o", (D, D))
    wq = din("wq", (D, D))
    wk = din("wk", (D, 512))
    wv = din("wv", (D, 512))
    wo = din("wo", (D, D))
    rw = din("router_w", (D, 8))
    wg_r = din("wg_r", (E, 8, 128, 16 * 128))
    wu_r = din("wu_r", (E, 8, 128, 16 * 128))
    wdp = din("wd", (E, I, D))
    identf = din("identf", (128, 128))
    tri01 = din("tri01", (128, 128), BF16)
    negu = din("negu", (128, 4 * 512), BF16)
    seli = din("sel", (128, 32))
    amla = din("amla", (128, MLA_NC))
    vffn = din("vffn", (128, FFN_NC))
    cossin = din("cossin", (128, 16 * 64))
    ln1b = din("ln1b", (128, D))
    ln2b = din("ln2b", (128, D))
    qnb = din("qnb", (128, D))
    knb = din("knb", (128, 512))
    rbb = din("rbb", (128, 8))

    y = nc.dram_tensor("y", [T, D], F32, kind="ExternalOutput").ap()

    wqk_d = nc.dram_tensor("wqk_d", [16, 128, 16 * 128], BF16).ap()
    x1_d = nc.dram_tensor("x1_d", [T, D], F32).ap()
    x2_d = nc.dram_tensor("x2_d", [T, D], F32).ap()
    kvin_d = nc.dram_tensor("kvin_d", [128, KVW], BF16).ap()
    kvout_d = nc.dram_tensor("kvout_d", [2, 128, KVW], BF16).ap()
    if DBG_P3:
        dbg_cb = nc.dram_tensor("dbg_cb", [128, 128], F32,
                                kind="ExternalOutput").ap()
        dbg_sg = nc.dram_tensor("dbg_sg", [128, 32], F32,
                                kind="ExternalOutput").ap()
        dbg_h2 = nc.dram_tensor("dbg_h2", [128, 16 * 1024], BF16,
                                kind="ExternalOutput").ap()
    if DBG_P2:
        dbg_kt = nc.dram_tensor("dbg_kt", [128, 4 * 4096], BF16,
                                kind="ExternalOutput").ap()
        dbg_va = nc.dram_tensor("dbg_va", [128, 32 * 528], BF16,
                                kind="ExternalOutput").ap()
        dbg_q2 = nc.dram_tensor("dbg_q2", [4, 128, 16 * 512], BF16,
                                kind="ExternalOutput").ap()
        dbg_o2 = nc.dram_tensor("dbg_o2", [4, 128, 16 * 512], BF16,
                                kind="ExternalOutput").ap()

    bsb = lambda n, sh, dt: al.sb("base", n, sh, dt)
    xb = [bsb("xb%d" % i, (128, D), F32) for i in range(2)]
    zb = [bsb("zb%d" % i, (128, D), BF16) for i in range(2)]
    stat = [bsb("st%d" % i, (128, 8), F32) for i in range(2)]
    st4 = bsb("st4", (128, 32), F32)
    junkb = bsb("junkb", (128, D), BF16)
    asb = lambda n, sh, dt: al.sb("attnsh", n, sh, dt)
    amla_s = asb("amla_s", (128, MLA_NC), F32)
    tri_s = asb("tri_s", (128, 128), BF16)
    Pb = [asb("Pb%d" % i, (128, 512), BF16) for i in range(3)]
    Pscr = asb("Pscr", (128, 512), BF16)
    rcol = asb("rcol", (128, 8), F32)
    onb = asb("onb", (128, 512), BF16)

    pbk = [stack.enter_context(nc.psum_tensor("pbk%d" % i, [128, 512], F32))
           for i in range(8)]

    kb.begin()
    wdkvT_s = al.sb("p1per", "wdkvT_s", (128, 16 * 128), BF16)
    w2_s = al.sb("p1per", "w2_s", (128, 16 * 2048), BF16)
    ckvT_s = al.sb("p1per", "ckvT_s", (128, 4096), BF16)
    ckvaug_s = al.sb("p1per", "ckvaug_s", (128, 32 * 144), BF16)
    lnT_g = al.sb("p1per", "lnT_g", (128, 16 * 512), BF16)
    xb4 = al.sb("p1per", "xb4", (128, 4 * 2048), BF16)
    qtg = al.sb("p1per", "qtg", (128, 16 * 512), BF16)
    wqks = [al.sb("p1per", "wqks%d" % i, (128, 2048), BF16) for i in range(2)]
    oTg = al.sb("p1per", "oTg", (128, 16 * 512), BF16)
    x1t = [al.sb("p1per", "x1t0", (128, D), F32)] * 2

    wuv_s = al.sb("p0tmp", "p0_wuv", (128, 16 * 128), BF16)
    wuk_s = wuv_s
    wslab = [al.sb("p0tmp", "p0_ws0", (128, 2048), BF16)] * 2
    woTs = [qtg[:, 0:2048], qtg[:, 2048:4096]]
    qksl = [oTg[:, 0:2048], oTg[:, 2048:4096]]

    for s_, d_ in ((amla, amla_s), (tri01, tri_s)):
        kb.inc("sync", lambda e, s=s_, d=d_: e.dma_start(out=d[:], in_=s),
               "p0m", 16)
    kb.inc("vector", lambda e: e.memset(ckvaug_s[:], 2.0 ** -24), "p0aug", 1)

    for h in range(16):
        kb.inc("gpsimd", lambda e, h=h: e.dma_start(
            out=wuv_s[:, h * 128:(h + 1) * 128],
            in_=w_uv[h * 128:(h + 1) * 128, :]), "p0g", 16)

    QT2 = ("sync", "scalar")
    kb.wait("tensor", "p0g")
    for ci in range(16):
        sl_ = ci % 2
        wsl = wslab[sl_]
        kb.wait("gpsimd", "p0wot", ci * 256)
        kb.inc("gpsimd", lambda e, ci=ci, wsl=wsl: e.dma_start(
            out=wsl[:], in_=w_o[ci * 128:(ci + 1) * 128, :]), "p0wol", 16)
        for q in QT2:
            kb.wait(q, "p0wol", (ci + 1) * 16)
            kb.wait(q, "p0w2m", max(0, (ci - 1) * 16))
        for h in range(16):
            kb.inc(QT2[h % 2], lambda e, h=h, sl_=sl_, wsl=wsl:
                   e.dma_start_transpose(
                woTs[sl_][:, h * 128:(h + 1) * 128],
                wsl[:, h * 128:(h + 1) * 128]), "p0wot", 16)
        kb.wait("tensor", "p0wot", (ci + 1) * 256)
        for h in range(16):
            bank = pbk[h % 2]
            kb.wait("tensor", "p0w2e", kb.val("p0w2e") - 1)
            kb.inc("tensor", lambda e, h=h, sl_=sl_, bank=bank: e.matmul(
                bank[:, 0:128], wuv_s[:, h * 128:(h + 1) * 128],
                woTs[sl_][:, h * 128:(h + 1) * 128],
                start=True, stop=True), "p0w2m")
            kb.wait("vector", "p0w2m")
            kb.inc("vector", lambda e, h=h, ci=ci, bank=bank: e.tensor_copy(
                w2_s[:, h * 2048 + ci * 128: h * 2048 + (ci + 1) * 128],
                bank[:, 0:128]), "p0w2e")

    kb.wait("gpsimd", "p0w2m", kb.val("p0w2m"))
    for h in range(16):
        kb.inc("gpsimd", lambda e, h=h: e.dma_start(
            out=wuk_s[:, h * 128:(h + 1) * 128],
            in_=w_uk[h * 128:(h + 1) * 128, :]), "p0g", 16)
    kb.wait("gpsimd", "p0wot", 16 * 256)
    kb.wait("tensor", "p0g")
    for h in range(16):
        sl_ = h % 2
        sl = wslab[sl_]
        kb.wait("gpsimd", "p0qkm", h * 16)
        kb.inc("gpsimd", lambda e, h=h, sl=sl: e.dma_start(
            out=sl[:], in_=w_q[h * 128:(h + 1) * 128, :]), "p0wq", 16)
        kb.wait("tensor", "p0wq", (h + 1) * 16)
        kb.wait("vector", "p0qks", max(0, (h - 1) * 16))
        for dch in range(16):
            bank = pbk[2 + dch % 2]
            kb.wait("tensor", "p0qke", kb.val("p0qke") - 1)
            kb.inc("tensor", lambda e, h=h, dch=dch, sl=sl, bank=bank: e.matmul(
                bank[:, 0:128], sl[:, dch * 128:(dch + 1) * 128],
                wuk_s[:, h * 128:(h + 1) * 128], start=True, stop=True), "p0qkm")
            kb.wait("vector", "p0qkm")
            kb.inc("vector", lambda e, dch=dch, sl_=sl_, bank=bank:
                   e.tensor_copy(
                qksl[sl_][:, dch * 128:(dch + 1) * 128], bank[:, 0:128]),
                "p0qke")
        kb.wait("sync", "p0qke")
        kb.inc("sync", lambda e, h=h, sl_=sl_: e.dma_start(
            out=wqk_d[h, :, :], in_=qksl[sl_][:]), "p0qks", 16)

    kb.wait("gpsimd", "p0qkm", kb.val("p0qkm"))
    kb.inc("gpsimd", lambda e: e.dma_start(out=wslab[0][:], in_=w_dkv[:, :]),
           "p0g", 16)
    kb.wait("sync", "p0g")
    for dch in range(16):
        kb.inc("sync", lambda e, dch=dch: e.dma_start_transpose(
            wdkvT_s[:, dch * 128:(dch + 1) * 128],
            wslab[0][:, dch * 128:(dch + 1) * 128]), "p0m", 16)
    al.free("p0tmp")

    state = {"n": 0}

    def ln_wave(tiles):
        n0 = state["n"]
        state["n"] += 1
        kb.wait("gpsimd", "patp", n0 * 1024)
        for i, t in enumerate(tiles):
            kb.inc("gpsimd", lambda e, t=t, i=i: e.dma_start(
                out=xb4[:, i * 2048:(i + 1) * 2048],
                in_=x0[t * 128:(t + 1) * 128, :]), "pald", 16)
        kb.wait("vector", "pald")
        kb.wait("scalar", "pald")
        _stats4(kb, [xb4[:, i * 2048:(i + 1) * 2048] for i in range(4)],
                D, st4, junkb, LN_EPS, True)
        kb.wait("vector", "sst0")
        for i in range(4):
            fn = lambda e, i=i: e.tensor_scalar(
                xb4[:, i * 2048:(i + 1) * 2048],
                xb4[:, i * 2048:(i + 1) * 2048],
                st4[:, i * 8 + 1:i * 8 + 2], st4[:, i * 8 + 4:i * 8 + 5],
                op0=ALU.subtract, op1=ALU.mult)
            kb.inc("vector", fn, "paz") if i == 3 else kb.do("vector", fn)
        kb.wait("sync", "paz")
        for i in range(4):
            for dch in range(16):
                kb.inc("sync", lambda e, i=i, dch=dch: e.dma_start_transpose(
                    lnT_g[:, dch * 512 + i * 128: dch * 512 + (i + 1) * 128],
                    xb4[:, i * 2048 + dch * 128: i * 2048 + (dch + 1) * 128]),
                    "patp", 16)


    kb.wait("tensor", "p0m")
    kb.wait("vector", "p0m")
    kb.wait("scalar", "p0m")
    for ga in range(8):
        kb.wait("sync", "pack", kb.val("pack"))
        ln_wave([ga * 4 + i for i in range(4)])
        kb.wait("tensor", "patp", state["n"] * 1024)
        bank = pbk[ga % 2]
        kb.wait("tensor", "packe", kb.val("packe") - 1)
        for dch in range(16):
            fn = lambda e, dch=dch, bank=bank: e.matmul(
                bank[:], wdkvT_s[:, dch * 128:(dch + 1) * 128],
                lnT_g[:, dch * 512:(dch + 1) * 512],
                start=(dch == 0), stop=(dch == 15))
            kb.inc("tensor", fn, "pack") if dch == 15 else kb.do("tensor", fn)
        kb.wait("vector", "pack")
        kb.inc("vector", lambda e, ga=ga, bank=bank: e.tensor_copy(
            ckvT_s[:, ga * 512:(ga + 1) * 512], bank[:]), "packe")
        kb.wait("sync", "packe")
        kb.wait("sync", "p0aug", 1)
        for ks in range(4):
            kt = ga * 4 + ks
            kb.inc("sync", lambda e, ga=ga, ks=ks, kt=kt: e.dma_start_transpose(
                ckvaug_s[:, kt * 144: kt * 144 + 128],
                ckvT_s[:, ga * 512 + ks * 128: ga * 512 + (ks + 1) * 128]),
                "paaug", 16)
    kb.wait("tensor", "paaug")
    kb.wait("vector", "paaug")
    kb.wait("sync", "pack", kb.val("pack"))

    kb.wait("gpsimd", "p0qks")
    kb.wait("tensor", "p0w2e")
    for g in range(NG):
        kb.wait("sync", "pbqm", kb.val("pbqm"))
        kb.wait("sync", "p1xe", kb.val("p1xe"))
        ln_wave([g * 4 + i for i in range(4)])
        kb.wait("tensor", "patp", state["n"] * 1024)
        kb.wait("tensor", "p1xe", kb.val("p1xe"))
        for cch in range(16):
            sl_ = cch % 2
            sl = wqks[sl_]
            kb.wait("gpsimd", "pbqm", kb.val("pbqm") - 1)
            kb.inc("gpsimd", lambda e, cch=cch, sl=sl: e.dma_start(
                out=sl[:], in_=wqk_d[cch, :, :]), "pbqk%d" % sl_, 16)
            kb.wait("tensor", "pbqk%d" % sl_)
            bank = pbk[2 + cch % 2]
            kb.wait("tensor", "pbqe", kb.val("pbqe") - 1)
            for dch in range(16):
                fn = lambda e, dch=dch, sl=sl, bank=bank: e.matmul(
                    bank[:], sl[:, dch * 128:(dch + 1) * 128],
                    lnT_g[:, dch * 512:(dch + 1) * 512],
                    start=(dch == 0), stop=(dch == 15))
                kb.inc("tensor", fn, "pbqm") if dch == 15 else kb.do("tensor", fn)
            kb.wait("vector", "pbqm")
            kb.wait("vector", "p1sm", kb.val("p1sm"))
            kb.inc("vector", lambda e, cch=cch, bank=bank: e.tensor_copy(
                qtg[:, cch * 512:(cch + 1) * 512], bank[:]), "pbqe")
        kb.wait("tensor", "pbqe")
        kb.wait("sync", "p1s2m", kb.val("p1s2m"))
        kb.wait("tensor", "p1xe", kb.val("p1xe"))

        kts = MLA_KTS[g]
        nit = len(kts)
        colbase = sum(len(MLA_KTS[gg]) for gg in range(g))
        for h in range(16):
            expv = {}
            maskv = {}
            pend = [None]

            def emit_pv(j, kts=kts, nit=nit, expv=expv, maskv=maskv):
                if j in maskv:
                    kb.wait("tensor", "p1mk", maskv[j])
                else:
                    kb.wait("tensor", "p1ex", expv[j])
                if j == 0:
                    kb.wait("tensor", "p1on", kb.val("p1on"))
                kt = kts[j]
                for sub in range(4):
                    fn = lambda e, j=j, sub=sub, kt=kt, nit=nit: e.matmul(
                        pbk[2 + sub][:, 0:129],
                        Pb[j % 3][:, sub * 128:(sub + 1) * 128],
                        ckvaug_s[:, kt * 144: kt * 144 + 129],
                        start=(j == 0), stop=(j == nit - 1))
                    kb.inc("tensor", fn, "p1pv") if sub == 3 else kb.do("tensor", fn)

            for it, kt in enumerate(kts):
                sbk = pbk[it % 2]
                if it >= 2:
                    kb.wait("tensor", "p1ex", expv[it - 2])
                kb.inc("tensor", lambda e, kt=kt, h=h, sbk=sbk: e.matmul(
                    sbk[:], ckvT_s[:, kt * 128:(kt + 1) * 128],
                    qtg[:, h * 512:(h + 1) * 512], start=True, stop=True),
                    "p1sm")
                pbuf = Pb[it % 3]
                kb.wait("scalar", "p1sm")
                kb.wait("scalar", "p1pv", kb.val("p1pv"))
                col = colbase + it
                expv[it] = kb.inc("scalar", lambda e, sbk=sbk, pbuf=pbuf,
                                  col=col: e.activation(
                    pbuf[:], sbk[:], AF.Exp, bias=amla_s[:, col:col + 1],
                    scale=SCALE), "p1ex")
                if 4 * g <= kt < 4 * g + 4:
                    s_ = kt - 4 * g
                    kb.wait("vector", "p1ex", expv[it])
                    if s_ > 0:
                        kb.do("vector", lambda e, pbuf=pbuf, s_=s_: e.memset(
                            pbuf[:, 0:s_ * 128], 0.0))
                    maskv[it] = kb.inc("vector", lambda e, pbuf=pbuf, s_=s_:
                                       e.tensor_tensor(
                        pbuf[:, s_ * 128:(s_ + 1) * 128],
                        pbuf[:, s_ * 128:(s_ + 1) * 128], tri_s[:],
                        op=ALU.mult), "p1mk")
                pend.append(it)
                if len(pend) > 3:
                    emit_pv(pend.pop(1))
            for j_ in pend[1:]:
                emit_pv(j_)
            kb.wait("vector", "p1pv")
            for sub in range(3):
                kb.do("vector", lambda e, sub=sub: e.reciprocal(
                    rcol[:, sub:sub + 1], pbk[2 + sub][:, 128:129]))
            kb.inc("vector", lambda e: e.reciprocal(
                rcol[:, 3:4], pbk[5][:, 128:129]), "p1rc")
            kb.wait("scalar", "p1rc")
            kb.inc("scalar", lambda e: e.activation(rcol[:, 4:8], rcol[:, 0:4],
                                                    AF.Identity), "p1lnc")
            kb.wait("vector", "p1lnc")
            kb.wait("vector", "p1oT", kb.val("p1oT") - 64)
            for sub in range(4):
                kb.inc("vector", lambda e, sub=sub: e.tensor_scalar(
                    onb[:, sub * 128:(sub + 1) * 128], pbk[2 + sub][:, 0:128],
                    rcol[:, 4 + sub:5 + sub], 2.0 ** -24,
                    op0=ALU.mult, op1=ALU.mult), "p1on")
            kb.wait("sync", "p1on")
            for sub in range(4):
                kb.inc("sync", lambda e, h=h, sub=sub: e.dma_start_transpose(
                    oTg[:, h * 512 + sub * 128: h * 512 + (sub + 1) * 128],
                    onb[:, sub * 128:(sub + 1) * 128]), "p1oT", 16)
        kb.wait("tensor", "p1oT")
        for tsub in range(4):
            t = g * 4 + tsub
            bx = tsub % 2
            kb.wait("sync", "p1xe", kb.val("p1xe") - 8)
            kb.inc("sync", lambda e, t=t, bx=bx: e.dma_start(
                out=xb[bx][:], in_=x0[t * 128:(t + 1) * 128, :]), "p1xr", 16)
            for db in range(4):
                bank = pbk[2 + db]
                kb.wait("tensor", "p1xe", kb.val("p1xe"))
                for h in range(16):
                    fn = lambda e, h=h, db=db, tsub=tsub, bank=bank: e.matmul(
                        bank[:],
                        oTg[:, h * 512 + tsub * 128: h * 512 + (tsub + 1) * 128],
                        w2_s[:, h * 2048 + db * 512: h * 2048 + (db + 1) * 512],
                        start=(h == 0), stop=(h == 15))
                    kb.inc("tensor", fn, "p1s2m") if h == 15 else kb.do("tensor", fn)
            kb.wait("vector", "p1s2m")
            kb.wait("vector", "p1xr")
            bo = 0
            kb.wait("vector", "p1xs", kb.val("p1xs"))
            for db in range(4):
                kb.inc("vector", lambda e, db=db, bo=bo, bx=bx: e.tensor_tensor(
                    x1t[bo][:, db * 512:(db + 1) * 512], pbk[2 + db][:],
                    xb[bx][:, db * 512:(db + 1) * 512], op=ALU.add), "p1xe")
            kb.wait("sync", "p1xe")
            kb.inc("sync", lambda e, t=t, bo=bo: e.dma_start(
                out=x1_d[t * 128:(t + 1) * 128, :], in_=x1t[bo][:]), "p1xs", 16)
    kb.flush()
    al.free("p1per")

    if n_phases == 1:
        al.free("attnsh")
        _copy_out(kb, al, x1_d, y)
        stack.close()
        return nc

    kb.begin()
    ropescr = al.sb("p2tb", "ropescr", (128, 2048), BF16)
    negu_s = al.sb("p2tb", "negu_s", (128, 4 * 512), BF16)
    sel_s = al.sb("p2tb", "sel_s", (128, 32), F32)
    vffn_s = al.sb("p2tb", "vffn_s", (128, FFN_NC), F32)
    cs_s = al.sb("p2tb", "cs_s", (128, 16 * 64), F32)
    ln1b_s = al.sb("p2tb", "ln1b_s", (128, D), BF16)
    qnb_s = al.sb("p2tb", "qnb_s", (128, D), BF16)
    knb_s = al.sb("p2tb", "knb_s", (128, 512), BF16)
    for s_, d_ in ((negu, negu_s), (seli, sel_s), (vffn, vffn_s),
                   (cossin, cs_s)):
        kb.inc("sync", lambda e, s=s_, d=d_: e.dma_start(out=d[:], in_=s),
               "p2tb", 16)
    for s_, d_ in ((ln1b, ln1b_s), (qnb, qnb_s), (knb, knb_s)):
        kb.inc("gpsimd", lambda e, s=s_, d=d_: e.dma_start(out=d[:], in_=s),
               "p2tg", 16)
    wkv_s = al.sb("p2a", "wkv_s", (128, 2 * 16 * 512), BF16)
    hTt = [al.sb("p2a", "hTt%d" % i, (128, 16 * 128), BF16) for i in range(2)]
    xw4 = al.sb("p2a", "xw4", (128, 4 * 2048), BF16)
    ktf4 = al.sb("p2a", "ktf4", (128, 4 * 512), F32)
    krb = al.sb("p2a", "krb", (128, 512), BF16)
    kTown = al.sb("p2a", "kTown", (128, 4 * 2048), BF16)
    vaugo = al.sb("p2a", "vaugo", (128, VAW), BF16)

    for ci in range(16):
        kb.inc("gpsimd", lambda e, ci=ci: e.dma_start(
            out=wkv_s[:, ci * 512:(ci + 1) * 512],
            in_=wk[ci * 128:(ci + 1) * 128, :]), "p2tg", 16)
        kb.inc("gpsimd", lambda e, ci=ci: e.dma_start(
            out=wkv_s[:, 8192 + ci * 512: 8192 + (ci + 1) * 512],
            in_=wv[ci * 128:(ci + 1) * 128, :]), "p2tg", 16)
    kb.inc("vector", lambda e: e.memset(vaugo[:], 1.0), "p2vone", 1)

    def h_tile(tag, t, b):
        kb.inc("sync", lambda e, t=t, b=b: e.dma_start(
            out=xb[b][:], in_=x1_d[t * 128:(t + 1) * 128, :]), tag + "ld", 16)
        kb.wait("vector", tag + "ld")
        kb.wait("scalar", tag + "ld")
        _stats(kb, xb[b][:], D, stat[b], junkb, LN_EPS, True, b)
        kb.wait("vector", "sst%d" % b)
        kb.inc("vector", lambda e, b=b: e.tensor_scalar(
            zb[b][:], xb[b][:], stat[b][:, 1:2], stat[b][:, 4:5],
            op0=ALU.subtract, op1=ALU.mult), tag + "z")
        kb.wait("scalar", tag + "z")
        _stats(kb, zb[b][:], D, stat[b], junkb, RMS_EPS, False, b)
        kb.wait("vector", "sst%d" % b)
        kb.do("vector", lambda e, b=b: e.tensor_scalar(
            zb[b][:], zb[b][:], stat[b][:, 4:5], None, op0=ALU.mult))
        kb.inc("vector", lambda e, b=b: e.tensor_tensor(
            zb[b][:], zb[b][:], ln1b_s[:], op=ALU.mult), tag + "h")

    def rope_emit(src, srcw, soff, dst, dstw, doff, nh, tcol, tag):
        cosap = bass.AP(cs_s, tcol * 64, [[1024, 128], [0, nh], [1, 32]])
        sinap = bass.AP(cs_s, tcol * 64 + 32, [[1024, 128], [0, nh], [1, 32]])
        sc = [bass.AP(ropescr, 32 * nh * i, [[2048, 128], [32, nh], [1, 32]])
              for i in range(4)]
        q1 = bass.AP(src, soff, [[srcw, 128], [128, nh], [1, 32]])
        q2 = bass.AP(src, soff + 32, [[srcw, 128], [128, nh], [1, 32]])
        d1 = bass.AP(dst, doff, [[dstw, 128], [128, nh], [1, 32]])
        d2 = bass.AP(dst, doff + 32, [[dstw, 128], [128, nh], [1, 32]])
        kb.do("vector", lambda e: e.tensor_tensor(sc[0], q1, cosap, op=ALU.mult))
        kb.do("vector", lambda e: e.tensor_tensor(sc[1], q2, sinap, op=ALU.mult))
        kb.do("vector", lambda e: e.tensor_tensor(sc[2], q2, cosap, op=ALU.mult))
        kb.do("vector", lambda e: e.tensor_tensor(sc[3], q1, sinap, op=ALU.mult))
        kb.do("vector", lambda e: e.tensor_tensor(d1, sc[0], sc[1],
                                                  op=ALU.subtract))
        kb.do("vector", lambda e: e.tensor_tensor(d2, sc[2], sc[3], op=ALU.add))
        if src is not dst or soff != doff:
            pasrc = bass.AP(src, soff + 64, [[srcw, 128], [128, nh], [1, 64]])
            padst = bass.AP(dst, doff + 64, [[dstw, 128], [128, nh], [1, 64]])
            kb.do("vector", lambda e: e.tensor_copy(padst, pasrc))
        kb.inc("vector", lambda e: e.tensor_copy(
            bass.AP(dst, doff, [[dstw, 128], [1, 1]]),
            bass.AP(dst, doff, [[dstw, 128], [1, 1]])), tag)

    kb.wait("vector", "p2tg")
    kb.wait("vector", "p2tb")
    kb.wait("scalar", "p2tg")
    kb.wait("scalar", "p2tb")
    for wvi in range(4):
        tiles = [wvi * 4 + i for i in range(4)]
        kb.wait("gpsimd", "p2ahT", wvi * 4 * 256)
        for i, t in enumerate(tiles):
            kb.inc("gpsimd", lambda e, t=t, i=i: e.dma_start(
                out=xw4[:, i * 2048:(i + 1) * 2048],
                in_=x1_d[t * 128:(t + 1) * 128, :]), "p2ald", 16)
        kb.wait("vector", "p2ald")
        kb.wait("scalar", "p2ald")
        xsl = [xw4[:, i * 2048:(i + 1) * 2048] for i in range(4)]
        _stats4(kb, xsl, D, st4, junkb, LN_EPS, True)
        kb.wait("vector", "sst0")
        for i in range(4):
            fn = lambda e, i=i: e.tensor_scalar(
                xsl[i], xsl[i], st4[:, i * 8 + 1:i * 8 + 2],
                st4[:, i * 8 + 4:i * 8 + 5], op0=ALU.subtract, op1=ALU.mult)
            kb.inc("vector", fn, "p2az") if i == 3 else kb.do("vector", fn)
        kb.wait("scalar", "p2az")
        _stats4(kb, xsl, D, st4, junkb, RMS_EPS, False)
        kb.wait("vector", "sst0")
        for i in range(4):
            kb.do("vector", lambda e, i=i: e.tensor_scalar(
                xsl[i], xsl[i], st4[:, i * 8 + 4:i * 8 + 5], None,
                op0=ALU.mult))
            fn = lambda e, i=i: e.tensor_tensor(xsl[i], xsl[i], ln1b_s[:],
                                                op=ALU.mult)
            kb.inc("vector", fn, "p2ah") if i == 3 else kb.do("vector", fn)
        kb.wait("sync", "p2ah")
        for i, t in enumerate(tiles):
            ht = hTt[t % 2]
            kb.wait("sync", "p2akm", max(0, t - 1))
            kb.wait("sync", "p2avm", max(0, t - 1))
            for dch in range(16):
                kb.inc("sync", lambda e, i=i, dch=dch, ht=ht:
                       e.dma_start_transpose(
                    ht[:, dch * 128:(dch + 1) * 128],
                    xw4[:, i * 2048 + dch * 128: i * 2048 + (dch + 1) * 128]),
                    "p2ahT", 16)
            kb.wait("tensor", "p2ahT")
            kb.wait("tensor", "p2tg")
            kb.wait("tensor", "p2ake", t)
            for dch in range(16):
                fn = lambda e, dch=dch, ht=ht: e.matmul(
                    pbk[0][:], ht[:, dch * 128:(dch + 1) * 128],
                    wkv_s[:, dch * 512:(dch + 1) * 512],
                    start=(dch == 0), stop=(dch == 15))
                kb.inc("tensor", fn, "p2akm") if dch == 15 else kb.do("tensor", fn)
            kb.wait("tensor", "p2ave", t)
            for dch in range(16):
                fn = lambda e, dch=dch, ht=ht: e.matmul(
                    pbk[1][:], ht[:, dch * 128:(dch + 1) * 128],
                    wkv_s[:, 8192 + dch * 512: 8192 + (dch + 1) * 512],
                    start=(dch == 0), stop=(dch == 15))
                kb.inc("tensor", fn, "p2avm") if dch == 15 else kb.do("tensor", fn)
            kb.wait("vector", "p2akm")
            kb.inc("vector", lambda e, i=i: e.tensor_copy(
                ktf4[:, i * 512:(i + 1) * 512], pbk[0][:]), "p2ake")
            kb.wait("vector", "p2avm")
            kb.wait("vector", "p2vone", 1)
            vdst = bass.AP(vaugo, t * 528, [[VAW, 128], [132, 4], [1, 128]])
            vsrc = bass.AP(pbk[1], 0, [[512, 128], [128, 4], [1, 128]])
            kb.inc("vector", lambda e, vdst=vdst, vsrc=vsrc: e.tensor_copy(
                vdst, vsrc), "p2ave")
        kb.wait("scalar", "p2ake")
        ksl = [ktf4[:, i * 512:(i + 1) * 512] for i in range(4)]
        _stats4(kb, ksl, 512, st4, junkb, RMS_EPS, False)
        kb.wait("vector", "sst0")
        for i, t in enumerate(tiles):
            kb.do("vector", lambda e, i=i: e.tensor_scalar(
                ksl[i], ksl[i], st4[:, i * 8 + 4:i * 8 + 5], None,
                op0=ALU.mult))
            kb.do("vector", lambda e, i=i: e.tensor_tensor(
                ksl[i], ksl[i], knb_s[:], op=ALU.mult))
            kb.wait("vector", "p2akT", (wvi * 4 + i) * 64)
            rope_emit(ktf4, 4 * 512, i * 512, krb, 512, 0, 4, t, "p2akr")
            kb.wait("sync", "p2akr")
            for kvh in range(4):
                kb.inc("sync", lambda e, kvh=kvh, t=t: e.dma_start_transpose(
                    kTown[:, kvh * 2048 + t * 128: kvh * 2048 + (t + 1) * 128],
                    krb[:, kvh * 128:(kvh + 1) * 128]), "p2akT", 16)
    kb.wait("sync", "p2akT")
    kb.wait("sync", "p2ave")
    kb.inc("sync", lambda e: e.dma_start(out=kvin_d[:, 0:8192], in_=kTown[:]),
           "p2kvw", 16)
    kb.inc("sync", lambda e: e.dma_start(out=kvin_d[:, 8192:KVW], in_=vaugo[:]),
           "p2kvw", 16)
    kb.wait("gpsimd", "p2kvw")
    kb.inc("gpsimd", lambda e: e.collective_compute(
        "AllGather", ALU.bypass,
        replica_groups=[[0, 1], [2, 3], [4, 5], [6, 7]],
        ins=[kvin_d.opt()], outs=[kvout_d.opt()]), "p2cc", 1)
    kb.flush()
    al.free("p2a")

    kb.begin()
    kT = al.sb("p2b", "kT", (128, 4 * 4096), BF16)
    vaug = al.sb("p2b", "vaug", (128, 32 * 528), BF16)
    hTg2 = al.sb("p2b", "hTg2", (128, 16 * 512), BF16)
    o2Tg = hTg2
    qf4 = al.sb("p2b", "qf4", (128, 4 * 2048), BF16)
    q2Tg = al.sb("p2b", "q2Tg", (128, 16 * 512), BF16)
    wqo2 = [al.sb("p2b", "wqo%d" % i, (128, 16 * 512), BF16) for i in range(2)]
    x2s = al.sb("p2b", "x2s", (128, 1024), F32)
    s2st = al.sb("p2b", "s2st", (128, 8), F32)

    kb.wait("sync", "p2cc", 1)
    for r in range(2):
        for kvh in range(4):
            kb.inc("sync", lambda e, r=r, kvh=kvh: e.dma_start(
                out=kT[:, kvh * 4096 + r * 2048: kvh * 4096 + (r + 1) * 2048],
                in_=kvout_d[r, :, kvh * 2048:(kvh + 1) * 2048]), "p2kvr", 16)
        kb.inc("sync", lambda e, r=r: e.dma_start(
            out=vaug[:, r * VAW:(r + 1) * VAW],
            in_=kvout_d[r, :, 8192:KVW]), "p2kvr", 16)
    kb.wait("tensor", "p2kvr")
    if DBG_P2:
        kb.wait("sync", "p2kvr")
        kb.inc("sync", lambda e: e.dma_start(out=dbg_kt, in_=kT[:]), "dbg", 16)
        kb.inc("sync", lambda e: e.dma_start(out=dbg_va, in_=vaug[:]), "dbg", 16)

    for g in range(NG):
        kb.wait("sync", "p2qm", kb.val("p2qm"))
        kb.wait("sync", "p2s2m", kb.val("p2s2m"))
        kb.wait("sync", "p2xe", kb.val("p2xe"))
        kb.wait("gpsimd", "p2qT", kb.val("p2qT"))
        kb.wait("gpsimd", "sst0", kb.val("sst0"))
        for ts4 in range(4):
            t = g * 4 + ts4
            kb.inc("gpsimd", lambda e, t=t, ts4=ts4: e.dma_start(
                out=qf4[:, ts4 * 2048:(ts4 + 1) * 2048],
                in_=x1_d[t * 128:(t + 1) * 128, :]), "p2ald", 16)
        kb.wait("vector", "p2ald")
        kb.wait("scalar", "p2ald")
        hsl = [qf4[:, i * 2048:(i + 1) * 2048] for i in range(4)]
        _stats4(kb, hsl, D, st4, junkb, LN_EPS, True)
        kb.wait("vector", "sst0")
        for i in range(4):
            fn = lambda e, i=i: e.tensor_scalar(
                hsl[i], hsl[i], st4[:, i * 8 + 1:i * 8 + 2],
                st4[:, i * 8 + 4:i * 8 + 5], op0=ALU.subtract, op1=ALU.mult)
            kb.inc("vector", fn, "p2az") if i == 3 else kb.do("vector", fn)
        kb.wait("scalar", "p2az")
        _stats4(kb, hsl, D, st4, junkb, RMS_EPS, False)
        kb.wait("vector", "sst0")
        for i in range(4):
            kb.do("vector", lambda e, i=i: e.tensor_scalar(
                hsl[i], hsl[i], st4[:, i * 8 + 4:i * 8 + 5], None,
                op0=ALU.mult))
            fn = lambda e, i=i: e.tensor_tensor(hsl[i], hsl[i], ln1b_s[:],
                                                op=ALU.mult)
            kb.inc("vector", fn, "p2bh") if i == 3 else kb.do("vector", fn)
        kb.wait("sync", "p2bh")
        for ts4 in range(4):
            for dch in range(16):
                kb.inc("sync", lambda e, dch=dch, ts4=ts4: e.dma_start_transpose(
                    hTg2[:, dch * 512 + ts4 * 128: dch * 512 + (ts4 + 1) * 128],
                    qf4[:, ts4 * 2048 + dch * 128: ts4 * 2048 + (dch + 1) * 128]),
                    "p2bhT", 16)
        kb.wait("tensor", "p2bhT")
        kb.wait("vector", "p2bhT", kb.val("p2bhT"))
        for fb in range(4):
            wsl = wqo2[fb % 2]
            sname = "p2wq%d" % (fb % 2)
            kb.wait("gpsimd", "p2qm", kb.val("p2qm") - 4)
            kb.wait("gpsimd", "p2s2m", kb.val("p2s2m"))
            for dch in range(16):
                kb.inc("gpsimd", lambda e, dch=dch, fb=fb, wsl=wsl: e.dma_start(
                    out=wsl[:, dch * 512:(dch + 1) * 512],
                    in_=wq[dch * 128:(dch + 1) * 128, fb * 512:(fb + 1) * 512]),
                    sname, 16)
            kb.wait("tensor", sname)
            for ts4 in range(4):
                bank = pbk[ts4 % 2]
                kb.wait("tensor", "p2qe", kb.val("p2qe") - 1)
                for dch in range(16):
                    fn = lambda e, dch=dch, ts4=ts4, bank=bank, wsl=wsl: \
                        e.matmul(
                        bank[:],
                        hTg2[:, dch * 512 + ts4 * 128: dch * 512 + (ts4 + 1) * 128],
                        wsl[:, dch * 512:(dch + 1) * 512],
                        start=(dch == 0), stop=(dch == 15))
                    kb.inc("tensor", fn, "p2qm") if dch == 15 else kb.do("tensor", fn)
                kb.wait("vector", "p2qm")
                kb.inc("vector", lambda e, ts4=ts4, fb=fb, bank=bank: e.tensor_copy(
                    qf4[:, ts4 * 2048 + fb * 512: ts4 * 2048 + (fb + 1) * 512],
                    bank[:]), "p2qe")
        kb.wait("sync", "p1sm", kb.val("p1sm"))
        kb.wait("vector", "p2qe", kb.val("p2qe"))
        kb.wait("scalar", "p2qe", kb.val("p2qe"))
        qsl4 = [qf4[:, i * 2048:(i + 1) * 2048] for i in range(4)]
        _stats4(kb, qsl4, D, st4, junkb, RMS_EPS, False)
        kb.wait("vector", "sst0")
        for ts4 in range(4):
            t = g * 4 + ts4
            qsl = qsl4[ts4]
            kb.do("vector", lambda e, qsl=qsl, ts4=ts4: e.tensor_scalar(
                qsl, qsl, st4[:, ts4 * 8 + 4:ts4 * 8 + 5], None, op0=ALU.mult))
            kb.inc("vector", lambda e, qsl=qsl: e.tensor_tensor(
                qsl, qsl, qnb_s[:], op=ALU.mult), "p2qn")
            rope_emit(qf4, 4 * 2048, ts4 * 2048, qf4, 4 * 2048, ts4 * 2048,
                      16, t, "p2qr")
            kb.wait("sync", "p2qr")
            for hh in range(16):
                kb.inc("sync", lambda e, hh=hh, ts4=ts4: e.dma_start_transpose(
                    q2Tg[:, hh * 512 + ts4 * 128: hh * 512 + (ts4 + 1) * 128],
                    qf4[:, ts4 * 2048 + hh * 128: ts4 * 2048 + (hh + 1) * 128]),
                    "p2qT", 16)
        kb.wait("tensor", "p2qT")
        if DBG_P2:
            kb.wait("sync", "p2qT")
            kb.inc("sync", lambda e, g=g: e.dma_start(
                out=dbg_q2[g], in_=q2Tg[:]), "dbg", 16)
        kb.wait("tensor", "p2qe")
        kb.wait("sync", "p2s2m", kb.val("p2s2m"))
        kb.wait("tensor", "p2xe", kb.val("p2xe"))

        kts = FFN_KTS[g]
        nit = len(kts)
        colbase = sum(len(FFN_KTS[gg]) for gg in range(g))
        cand = FFN_CAND[g]
        kb.wait("sync", "p2qm", kb.val("p2qm"))
        for h in range(16):
            kvh = h // 4
            expv = {}
            maskv = {}
            pend = [None]

            def emit_pv2(j2_, kts=kts, nit=nit, expv=expv, maskv=maskv,
                         kvh=kvh):
                if j2_ in maskv:
                    kb.wait("tensor", "p1mk", maskv[j2_])
                else:
                    kb.wait("tensor", "p1ex", expv[j2_])
                if j2_ == 0:
                    kb.wait("tensor", "p1on", kb.val("p1on"))
                kt = kts[j2_]
                for sub in range(4):
                    fn = lambda e, j2_=j2_, sub=sub, kt=kt, nit=nit, kvh=kvh: \
                        e.matmul(
                        pbk[2 + sub][:, 0:129],
                        Pb[j2_ % 3][:, sub * 128:(sub + 1) * 128],
                        vaug[:, kt * 528 + kvh * 132: kt * 528 + kvh * 132 + 129],
                        start=(j2_ == 0), stop=(j2_ == nit - 1))
                    kb.inc("tensor", fn, "p1pv") if sub == 3 else kb.do("tensor", fn)

            for it, kt in enumerate(kts):
                sbk = pbk[it % 2]
                if it >= 2:
                    kb.wait("tensor", "p1ex", expv[it - 2])
                kb.inc("tensor", lambda e, kt=kt, h=h, kvh=kvh, sbk=sbk:
                       e.matmul(
                    sbk[:],
                    kT[:, kvh * 4096 + kt * 128: kvh * 4096 + (kt + 1) * 128],
                    q2Tg[:, h * 512:(h + 1) * 512], start=True, stop=True),
                    "p1sm")
                pbuf = Pb[it % 3]
                kb.wait("scalar", "p1sm")
                kb.wait("scalar", "p1pv", kb.val("p1pv"))
                col = colbase + it
                expv[it] = kb.inc("scalar", lambda e, sbk=sbk, pbuf=pbuf,
                                  col=col: e.activation(
                    pbuf[:], sbk[:], AF.Exp, bias=vffn_s[:, col:col + 1],
                    scale=SCALE), "p1ex")
                if kt in cand:
                    jj = cand.index(kt)
                    s_ = jj % 4
                    slot = g * 8 + jj
                    kb.wait("vector", "p1ex", expv[it])
                    kb.do("vector", lambda e, pbuf=pbuf, s_=s_: e.tensor_tensor(
                        Pscr[:], pbuf[:], negu_s[:, s_ * 512:(s_ + 1) * 512],
                        op=ALU.mult))
                    maskv[it] = kb.inc("vector", lambda e, pbuf=pbuf, slot=slot:
                                       e.scalar_tensor_tensor(
                        pbuf[:], Pscr[:], sel_s[:, slot:slot + 1], pbuf[:],
                        op0=ALU.mult, op1=ALU.add), "p1mk")
                pend.append(it)
                if len(pend) > 3:
                    emit_pv2(pend.pop(1))
            for j_ in pend[1:]:
                emit_pv2(j_)
            kb.wait("vector", "p1pv")
            for sub in range(3):
                kb.do("vector", lambda e, sub=sub: e.reciprocal(
                    rcol[:, sub:sub + 1], pbk[2 + sub][:, 128:129]))
            kb.inc("vector", lambda e: e.reciprocal(
                rcol[:, 3:4], pbk[5][:, 128:129]), "p1rc")
            kb.wait("scalar", "p1rc")
            kb.inc("scalar", lambda e: e.activation(rcol[:, 4:8], rcol[:, 0:4],
                                                    AF.Identity), "p1lnc")
            kb.wait("vector", "p1lnc")
            kb.wait("vector", "p1oT", kb.val("p1oT") - 64)
            for sub in range(4):
                kb.inc("vector", lambda e, sub=sub: e.tensor_scalar(
                    onb[:, sub * 128:(sub + 1) * 128], pbk[2 + sub][:, 0:128],
                    rcol[:, 4 + sub:5 + sub], None, op0=ALU.mult), "p1on")
            kb.wait("sync", "p1on")
            for sub in range(4):
                kb.inc("sync", lambda e, h=h, sub=sub: e.dma_start_transpose(
                    o2Tg[:, h * 512 + sub * 128: h * 512 + (sub + 1) * 128],
                    onb[:, sub * 128:(sub + 1) * 128]), "p1oT", 16)
        kb.wait("tensor", "p1oT")
        if DBG_P2:
            kb.wait("sync", "p1oT")
            kb.inc("sync", lambda e, g=g: e.dma_start(
                out=dbg_o2[g], in_=o2Tg[:]), "dbg", 16)
            kb.wait("tensor", "dbg")
        kb.wait("gpsimd", "p2qT", kb.val("p2qT"))
        for tsub in range(4):
            t = g * 4 + tsub
            kb.inc("gpsimd", lambda e, t=t, tsub=tsub: e.dma_start(
                out=qf4[:, tsub * 2048:(tsub + 1) * 2048],
                in_=x1_d[t * 128:(t + 1) * 128, :]), "p2ald", 16)
        kb.wait("vector", "p2ald")
        kb.wait("scalar", "p2ald")
        ssl = [qf4[:, i * 2048:(i + 1) * 2048] for i in range(4)]
        _stats4(kb, ssl, D, st4, junkb, LN_EPS, True)
        kb.wait("scalar", "sst0")
        kb.wait("scalar", "p2xe", kb.val("p2xe"))
        for tsub in range(4):
            kb.do("scalar", lambda e, tsub=tsub: e.activation(
                s2st[:, tsub * 2:tsub * 2 + 1],
                st4[:, tsub * 8 + 4:tsub * 8 + 5], AF.Identity))
            fn = lambda e, tsub=tsub: e.activation(
                s2st[:, tsub * 2 + 1:tsub * 2 + 2],
                st4[:, tsub * 8 + 6:tsub * 8 + 7], AF.Identity)
            kb.inc("scalar", fn, "p2st") if tsub == 3 else kb.do("scalar", fn)
        for db in range(4):
            wsl = wqo2[db % 2]
            sname = "p2wq%d" % (db % 2)
            kb.wait("gpsimd", "p2s2m", kb.val("p2s2m") - 4)
            kb.wait("gpsimd", "p2qm", kb.val("p2qm"))
            for hh in range(16):
                kb.inc("gpsimd", lambda e, hh=hh, db=db, wsl=wsl: e.dma_start(
                    out=wsl[:, hh * 512:(hh + 1) * 512],
                    in_=wo[hh * 128:(hh + 1) * 128, db * 512:(db + 1) * 512]),
                    sname, 16)
            kb.wait("tensor", sname)
            for tsub in range(4):
                t = g * 4 + tsub
                i2 = db * 4 + tsub
                bank = pbk[2 + i2 % 2]
                xsl = slice((i2 % 2) * 512, (i2 % 2) * 512 + 512)
                kb.wait("sync", "p2xs", kb.val("p2xs") - 16)
                kb.inc("sync", lambda e, t=t, db=db, xsl=xsl: e.dma_start(
                    out=x2s[:, xsl], in_=x1_d[t * 128:(t + 1) * 128,
                                              db * 512:(db + 1) * 512]),
                    "p2xr", 16)
                kb.wait("tensor", "p2xe", kb.val("p2xe") - 1)
                for hh in range(16):
                    fn = lambda e, hh=hh, db=db, tsub=tsub, bank=bank, wsl=wsl: \
                        e.matmul(
                        bank[:],
                        o2Tg[:, hh * 512 + tsub * 128: hh * 512 + (tsub + 1) * 128],
                        wsl[:, hh * 512:(hh + 1) * 512],
                        start=(hh == 0), stop=(hh == 15))
                    kb.inc("tensor", fn, "p2s2m") if hh == 15 else kb.do("tensor", fn)
                kb.wait("vector", "p2s2m")
                kb.wait("vector", "p2xr")
                kb.wait("vector", "p2st", kb.val("p2st"))
                kb.wait("vector", "p2xs", kb.val("p2xs") - 16)
                kb.do("vector", lambda e, xsl=xsl, tsub=tsub, bank=bank:
                      e.scalar_tensor_tensor(
                    x2s[:, xsl], x2s[:, xsl], s2st[:, tsub * 2:tsub * 2 + 1],
                    bank[:], op0=ALU.mult, op1=ALU.add))
                kb.inc("vector", lambda e, xsl=xsl, tsub=tsub: e.tensor_scalar(
                    x2s[:, xsl], x2s[:, xsl],
                    s2st[:, tsub * 2 + 1:tsub * 2 + 2], None,
                    op0=ALU.subtract), "p2xe")
                kb.wait("sync", "p2xe")
                kb.inc("sync", lambda e, t=t, db=db, xsl=xsl: e.dma_start(
                    out=x2_d[t * 128:(t + 1) * 128, db * 512:(db + 1) * 512],
                    in_=x2s[:, xsl]), "p2xs", 16)
    kb.flush()
    al.free("p2b")
    al.free("p2tb")
    al.free("attnsh")

    if n_phases == 2:
        _copy_out(kb, al, x2_d, y)
        stack.close()
        return nc

    kb.begin()
    ln2b_s = al.sb("p3", "ln2b_s", (128, D), F32)
    rbb_s = al.sb("p3", "rbb_s", (128, 8), F32)
    comb_s = al.sb("p3", "comb_s", (128, 16 * 8), F32)
    for s_, d_ in ((ln2b, ln2b_s), (rbb, rbb_s)):
        kb.inc("sync", lambda e, s=s_, d=d_: e.dma_start(out=d[:], in_=s),
               "p3tab", 16)
    x2m = al.sb("p3", "x2m", (128, 8 * 2048), F32)
    h2T = al.sb("p3", "h2T", (128, 16 * 1024), BF16)
    h2f = xb[0]
    h2Tf = al.sb("p3", "h2Tf", (128, 128), F32)
    rws = al.sb("p3", "rws", (128, 16 * 8), BF16)
    sig = al.sb("p3", "sig", (128, 8), F32)
    sg2 = al.sb("p3", "sg2", (128, 32), F32)
    sigall = al.sb("p3", "sigall", (128, 64), F32)
    rt3 = al.sb("p3", "rt3", (128, 224), F32)
    actT = al.sb("p3", "actT", (128, 8 * 1024), BF16)
    gsil = al.sb("p3", "gsil", (128, 1024), BF16)
    wgs = [al.sb("p3", "wgs%d" % i, (128, 16 * 128), BF16) for i in range(2)]
    wus = [al.sb("p3", "wus%d" % i, (128, 16 * 128), BF16) for i in range(2)]
    wds = al.sb("p3", "wds", (128, 8 * 2048), BF16)

    for ch in range(16):
        kb.inc("gpsimd", lambda e, ch=ch: e.dma_start(
            out=rws[:, ch * 8:(ch + 1) * 8],
            in_=rw[ch * 128:(ch + 1) * 128, :]), "p3rw", 16)

    kb.wait("vector", "p3tab")
    for hf in range(2):
        kb.wait("vector", "p3gm", kb.val("p3gm"))
        for tl in range(8):
            t = hf * 8 + tl
            kb.wait("sync", "p3ys", hf * 8 * 16)
            kb.inc("sync", lambda e, t=t, tl=tl: e.dma_start(
                out=x2m[:, tl * 2048:(tl + 1) * 2048],
                in_=x2_d[t * 128:(t + 1) * 128, :]), "p3xl", 16)
        for w in range(2):
            tiles = [w * 4 + i for i in range(4)]
            kb.wait("vector", "p3xl", (hf * 8 + tiles[-1] + 1) * 16)
            kb.wait("scalar", "p3xl", (hf * 8 + tiles[-1] + 1) * 16)
            _stats4(kb, [x2m[:, tl * 2048:(tl + 1) * 2048] for tl in tiles],
                    D, st4, junkb, RMS_EPS, False)
            kb.wait("vector", "sst0")
            for i, tl in enumerate(tiles):
                b = tl % 2
                tg_ = hf * 8 + tl
                kb.wait("vector", "p3hT", (tg_ - 1) * 256)
                xt = x2m[:, tl * 2048:(tl + 1) * 2048]
                kb.do("vector", lambda e, xt=xt, i=i, b=b: e.tensor_scalar(
                    zb[b][:], xt, st4[:, i * 8 + 4:i * 8 + 5], None,
                    op0=ALU.mult))
                kb.inc("vector", lambda e, b=b: e.tensor_tensor(
                    zb[b][:], zb[b][:], ln2b_s[:], op=ALU.mult), "p3h2")
                kb.wait("sync", "p3h2")
                for dch in range(16):
                    kb.inc("sync", lambda e, b=b, dch=dch, tl=tl:
                           e.dma_start_transpose(
                        h2T[:, dch * 1024 + tl * 128: dch * 1024 + (tl + 1) * 128],
                        zb[b][:, dch * 128:(dch + 1) * 128]), "p3hT", 16)
                kb.wait("tensor", "p3hT")
                kb.wait("tensor", "p3rw", 256)
                kb.wait("tensor", "p3tab", 32)
                kb.wait("tensor", "p3sg", kb.val("p3sg"))
                for dch in range(16):
                    fn = lambda e, dch=dch, tl=tl: e.matmul(
                        pbk[7][:, 0:8],
                        h2T[:, dch * 1024 + tl * 128: dch * 1024 + (tl + 1) * 128],
                        rws[:, dch * 8:(dch + 1) * 8],
                        start=(dch == 0), stop=(dch == 15))
                    kb.inc("tensor", fn, "p3rm") if dch == 15 \
                        else kb.do("tensor", fn)
                kb.wait("scalar", "p3rm")
                kb.inc("scalar", lambda e, tl=tl: e.activation(
                    sigall[:, tl * 8:(tl + 1) * 8], pbk[7][:, 0:8], AF.Sigmoid),
                    "p3sg")
        kb.wait("vector", "p3sg")
        W0 = rt3[:, 0:64]
        W1 = rt3[:, 64:128]
        W2 = rt3[:, 128:192]
        R0 = rt3[:, 192:200]
        R1 = rt3[:, 200:208]
        R2 = rt3[:, 208:216]
        RT3W = 224

        def r1rep(off):
            return bass.AP(rt3, off, [[RT3W, 128], [1, 8], [0, 8]])
        rbbrep = bass.AP(rbb_s, 0, [[8, 128], [0, 8], [1, 8]])
        sig3 = bass.AP(sigall, 0, [[64, 128], [8, 8], [1, 8]])
        w03 = bass.AP(rt3, 0, [[RT3W, 128], [8, 8], [1, 8]])
        w13 = bass.AP(rt3, 64, [[RT3W, 128], [8, 8], [1, 8]])

        def vst(fn):
            kb.inc("vector", fn, "p3sg")
            kb.wait("scalar", "p3sg")

        def sco(dst, srcc):
            kb.inc("scalar", lambda e, d=dst, s=srcc: e.activation(
                d, s, AF.Identity), "p3sg")
            kb.wait("vector", "p3sg")

        vst(lambda e: e.tensor_tensor(w03, sig3, rbbrep, op=ALU.add))
        sco(W1, W0)
        for tl in range(8):
            fn = lambda e, tl=tl: e.reduce_max(
                R0[:, tl:tl + 1], W1[:, tl * 8:(tl + 1) * 8],
                axis=mybir.AxisListType.X)
            kb.inc("vector", fn, "p3sg") if tl == 7 else kb.do("vector", fn)
        kb.wait("scalar", "p3sg")
        sco(R1, R0)
        vst(lambda e: e.tensor_tensor(w03, w13, r1rep(200), op=ALU.is_ge))
        sco(W2, W0)
        vst(lambda e: e.scalar_tensor_tensor(W0, W2, NEG, W1,
                                             op0=ALU.mult, op1=ALU.add))
        sco(W1, W0)
        for tl in range(8):
            fn = lambda e, tl=tl: e.reduce_max(
                R0[:, tl:tl + 1], W1[:, tl * 8:(tl + 1) * 8],
                axis=mybir.AxisListType.X)
            kb.inc("vector", fn, "p3sg") if tl == 7 else kb.do("vector", fn)
        kb.wait("scalar", "p3sg")
        sco(R1, R0)
        vst(lambda e: e.tensor_tensor(w03, w13, r1rep(200), op=ALU.is_ge))
        sco(W1, W0)
        vst(lambda e: e.tensor_tensor(W0, W2, W1, op=ALU.add))
        sco(W2, W0)
        vst(lambda e: e.tensor_tensor(W0, W2, sigall[:], op=ALU.mult))
        sco(W1, W0)
        for tl in range(8):
            fn = lambda e, tl=tl: e.reduce_sum(
                R0[:, tl:tl + 1], W1[:, tl * 8:(tl + 1) * 8],
                axis=mybir.AxisListType.X)
            kb.inc("vector", fn, "p3sg") if tl == 7 else kb.do("vector", fn)
        kb.wait("scalar", "p3sg")
        sco(R1, R0)
        vst(lambda e: e.reciprocal(R2[:], R1[:]))
        sco(R0, R2)
        vst(lambda e: e.tensor_tensor(w03, w13, r1rep(192), op=ALU.mult))
        kb.inc("scalar", lambda e, hf=hf: e.activation(
            comb_s[:, hf * 64:(hf + 1) * 64], W0, AF.Identity), "p3cb")
        kb.wait("tensor", "p3hT", kb.val("p3hT"))
        if DBG_P3 and hf == 0:
            kb.wait("sync", "p3cb", kb.val("p3cb"))
            kb.wait("sync", "p3sg", kb.val("p3sg"))
            kb.inc("sync", lambda e: e.dma_start(
                out=dbg_sg[:, 0:24], in_=sg2[:, 0:24]), "p3x1", 16)
            kb.inc("sync", lambda e: e.dma_start(
                out=dbg_sg[:, 24:32], in_=sig[:]), "p3x1", 16)
            kb.wait("sync", "p3hT", kb.val("p3hT"))
            kb.inc("sync", lambda e: e.dma_start(out=dbg_cb, in_=comb_s[:]),
                   "p3x1", 16)
            kb.inc("sync", lambda e: e.dma_start(out=dbg_h2, in_=h2T[:]),
                   "p3x1", 16)
        for e_ in range(E):
            kb.wait("gpsimd", "p3dn", kb.val("p3dn"))
            for ich in range(8):
                kb.inc("gpsimd", lambda e, e_=e_, ich=ich: e.dma_start(
                    out=wds[:, ich * 2048:(ich + 1) * 2048],
                    in_=wdp[e_, ich * 128:(ich + 1) * 128, :]), "p3wd", 16)
            kb.wait("vector", "p3dn", kb.val("p3dn"))
            for i128 in range(8):
                sl_ = i128 % 2
                wb_ = wgs[sl_]
                wub = wus[sl_]
                kb.wait("gpsimd", "p3gm", kb.val("p3gm") - 4)
                kb.inc("gpsimd", lambda e, e_=e_, i128=i128, wb_=wb_: e.dma_start(
                    out=wb_[:], in_=wg_r[e_, i128, :, :]), "p3wg%d" % sl_, 16)
                kb.inc("gpsimd", lambda e, e_=e_, i128=i128, wub=wub: e.dma_start(
                    out=wub[:], in_=wu_r[e_, i128, :, :]), "p3wg%d" % sl_, 16)
                kb.wait("tensor", "p3wg%d" % sl_)
                kb.wait("tensor", "p3sl", kb.val("p3sl"))
                kb.wait("tensor", "p3ae", kb.val("p3ae"))
                for tg in range(2):
                    for dch in range(16):
                        fn = lambda e, dch=dch, tg=tg, wb_=wb_: e.matmul(
                            pbk[tg][:], wb_[:, dch * 128:(dch + 1) * 128],
                            h2T[:, dch * 1024 + tg * 512: dch * 1024 + (tg + 1) * 512],
                            start=(dch == 0), stop=(dch == 15))
                        kb.inc("tensor", fn, "p3gm") if dch == 15 \
                            else kb.do("tensor", fn)
                        fn2 = lambda e, dch=dch, tg=tg, wub=wub: e.matmul(
                            pbk[2 + tg][:], wub[:, dch * 128:(dch + 1) * 128],
                            h2T[:, dch * 1024 + tg * 512: dch * 1024 + (tg + 1) * 512],
                            start=(dch == 0), stop=(dch == 15))
                        kb.inc("tensor", fn2, "p3gm") if dch == 15 \
                            else kb.do("tensor", fn2)
                kb.wait("scalar", "p3gm")
                for tg in range(2):
                    kb.inc("scalar", lambda e, tg=tg: e.activation(
                        gsil[:, tg * 512:(tg + 1) * 512], pbk[tg][:], AF.Silu),
                        "p3sl")
                kb.wait("vector", "p3sl")
                for tg in range(2):
                    kb.inc("vector", lambda e, tg=tg, i128=i128: e.tensor_tensor(
                        actT[:, i128 * 1024 + tg * 512: i128 * 1024 + (tg + 1) * 512],
                        gsil[:, tg * 512:(tg + 1) * 512], pbk[2 + tg][:],
                        op=ALU.mult), "p3ae")
            kb.wait("tensor", "p3ae")
            kb.wait("tensor", "p3wd")
            for tl in range(8):
                t = hf * 8 + tl
                for db in range(4):
                    bank = pbk[4 + (tl * 4 + db) % 2]
                    kb.wait("tensor", "p3de", kb.val("p3de") - 1)
                    for ich in range(8):
                        fn = lambda e, ich=ich, tl=tl, db=db, bank=bank: \
                            e.matmul(
                            bank[:],
                            actT[:, ich * 1024 + tl * 128: ich * 1024 + (tl + 1) * 128],
                            wds[:, ich * 2048 + db * 512: ich * 2048 + (db + 1) * 512],
                            start=(ich == 0), stop=(ich == 7))
                        kb.inc("tensor", fn, "p3dn") if ich == 7 \
                            else kb.do("tensor", fn)
                    kb.wait("vector", "p3dn")
                    kb.wait("vector", "p3cb", hf + 1)
                    kb.inc("vector", lambda e, tl=tl, db=db, t=t, e_=e_,
                           bank=bank: e.scalar_tensor_tensor(
                        x2m[:, tl * 2048 + db * 512: tl * 2048 + (db + 1) * 512],
                        bank[:], comb_s[:, t * 8 + e_: t * 8 + e_ + 1],
                        x2m[:, tl * 2048 + db * 512: tl * 2048 + (db + 1) * 512],
                        op0=ALU.mult, op1=ALU.add), "p3de")
        for tl in range(8):
            t = hf * 8 + tl
            kb.wait("sync", "p3xa", kb.val("p3xa"))
            kb.inc("sync", lambda e, t=t: e.dma_start(
                out=xb[1][:], in_=x1_d[t * 128:(t + 1) * 128, :]), "p3x1", 16)
            kb.wait("vector", "p3x1")
            kb.wait("vector", "p3de", kb.val("p3de"))
            kb.inc("vector", lambda e, tl=tl: e.tensor_tensor(
                x2m[:, tl * 2048:(tl + 1) * 2048],
                x2m[:, tl * 2048:(tl + 1) * 2048], xb[1][:], op=ALU.add),
                "p3xa")
            kb.wait("sync", "p3xa")
            kb.inc("sync", lambda e, t=t, tl=tl: e.dma_start(
                out=y[t * 128:(t + 1) * 128, :],
                in_=x2m[:, tl * 2048:(tl + 1) * 2048]), "p3ys", 16)
    kb.flush()
    al.free("p3")
    al.free("base")
    stack.close()
    return nc


def _copy_out(kb, al, src_d, y):
    kb.begin()
    dbg = al.sb("dbg", "dbgt", (128, D), F32)
    for t in range(16):
        kb.wait("sync", "dst", (t - 1) * 16)
        kb.inc("sync", lambda e, t=t: e.dma_start(
            out=dbg[:], in_=src_d[t * 128:(t + 1) * 128, :]), "dld", 16)
        kb.wait("sync", "dld")
        kb.inc("sync", lambda e, t=t: e.dma_start(
            out=y[t * 128:(t + 1) * 128, :], in_=dbg[:]), "dst", 16)
    kb.flush()
    al.free("dbg")
    al.free("base")



def _host_tables(H):
    t = {}
    t["identf"] = np.eye(128, dtype=np.float32)
    i = np.arange(128)[:, None]
    j = np.arange(128)[None, :]
    t["tri01"] = (i <= j).astype(ml_dtypes.bfloat16)
    negu = np.zeros((128, 4, 512), np.float32)
    for s in range(4):
        m = np.ones((128, 512), np.float32)
        m[:, : s * 128] = 0.0
        m[:, s * 128:(s + 1) * 128] = (i <= j).astype(np.float32)
        negu[:, s, :] = -(1.0 - m)
    t["negu"] = negu.reshape(128, 4 * 512).astype(ml_dtypes.bfloat16)
    sel = np.zeros((128, 32), np.float32)
    for g in range(NG):
        for fam in range(2):
            for s in range(4):
                sel[:, g * 8 + fam * 4 + s] = 1.0 if fam == H else 0.0
    t["sel"] = sel
    amla_t = np.full((128, MLA_NC), NEG, np.float32)
    ii = np.arange(128)
    colb = 0
    for g in range(NG):
        Q0 = 2048 * H + 512 * g
        for it, kt in enumerate(MLA_KTS[g]):
            if kt < 4 * g + 4:
                kpos = kt * 128 + ii + 2048 * H
                amla_t[:, colb + it] = SLOPE * (kpos - Q0)
            elif H == 1:
                kpos = (kt * 128 + ii + 2048) % 4096
                amla_t[:, colb + it] = SLOPE * (kpos - Q0)
        colb += len(MLA_KTS[g])
    t["amla"] = amla_t
    vffn_t = np.full((128, FFN_NC), NEG, np.float32)
    colb = 0
    for g in range(NG):
        for it, kt in enumerate(FFN_KTS[g]):
            if kt <= 16 * H + 4 * g + 3:
                vffn_t[:, colb + it] = 0.0
        colb += len(FFN_KTS[g])
    t["vffn"] = vffn_t
    pos = (np.arange(T) + 2048 * H).astype(np.float64)
    invf = 1.0 / (1e6 ** (np.arange(0, 64, 2) / 64.0))
    ang = pos[:, None] * invf[None, :]
    cs = np.zeros((128, 16 * 64), np.float32)
    for tt in range(16):
        cs[:, tt * 64: tt * 64 + 32] = np.cos(ang[tt * 128:(tt + 1) * 128])
        cs[:, tt * 64 + 32: tt * 64 + 64] = np.sin(ang[tt * 128:(tt + 1) * 128])
    t["cossin"] = cs
    return t


_NC_CACHE = {}


def kernel(**inp):
    g = lambda k: np.ascontiguousarray(np.asarray(inp[k], dtype=np.float32))
    x0f = g("L0_attn_hidden_states")

    def restage(w):
        return np.ascontiguousarray(
            w.reshape(E, 16, 128, 8, 128).transpose(0, 3, 2, 1, 4)
            .reshape(E, 8, 128, 16 * 128))

    common = {
        "w_dkv": g("L0_attn_w_dkv"), "w_q": g("L0_attn_w_q"),
        "w_uk": g("L0_attn_w_uk"), "w_uv": g("L0_attn_w_uv"),
        "w_o": g("L0_attn_w_o"), "wq": g("L0_ffn_wq"), "wk": g("L0_ffn_wk"),
        "wv": g("L0_ffn_wv"), "wo": g("L0_ffn_wo"),
        "router_w": g("L0_ffn_router_w"),
        "wg_r": restage(g("L0_ffn_wg")), "wu_r": restage(g("L0_ffn_wu")),
        "wd": g("L0_ffn_wd"),
        "ln1b": np.ascontiguousarray(
            np.broadcast_to(g("L0_ffn_ln1_w"), (128, D))),
        "ln2b": np.ascontiguousarray(
            np.broadcast_to(g("L0_ffn_ln2_w"), (128, D))),
        "qnb": np.ascontiguousarray(
            np.broadcast_to(g("L0_ffn_qnorm_w"), (128, D))),
        "knb": np.ascontiguousarray(
            np.broadcast_to(g("L0_ffn_knorm_w"), (128, 512))),
        "rbb": np.ascontiguousarray(
            np.broadcast_to(g("L0_ffn_router_bias"), (128, 8))),
    }
    tabs = [_host_tables(H) for H in range(2)]

    in_maps = []
    for c in range(8):
        b, H = c // 2, c % 2
        m = dict(common)
        m.update(tabs[H])
        m["x0"] = np.ascontiguousarray(np.roll(x0f[b], -2048 * H, axis=0))
        in_maps.append(m)

    if "nc" not in _NC_CACHE:
        _NC_CACHE["nc"] = build_program()
    nc = _NC_CACHE["nc"]
    res = run_bass_kernel_spmd(nc, in_maps, core_ids=list(range(8)))
    out = np.zeros((B, S, D), np.float32)
    for c in range(8):
        b, H = c // 2, c % 2
        out[b, 2048 * H: 2048 * (H + 1), :] = res.results[c]["y"]
    return out
